# version 06
# baseline (speedup 1.0000x reference)
import numpy as np
import ml_dtypes
import concourse.bass as bass
import concourse.mybir as mybir
from concourse.bass_utils import run_bass_kernel_spmd

F32 = mybir.dt.float32
BF16 = mybir.dt.bfloat16
AF = mybir.ActivationFunctionType
ALU = mybir.AluOpType
AX = mybir.AxisListType

NCORES = 8
B = 2
N = 8192
T = 2048
NT = 16
D = 1024
DC = 8
H = 16
DH = 64
M = 64
DFF = 4096
FC = 32
HALF = 1024
DN = 2.0 ** -1.5
RATIO = 0.125
EPSK = 1e-6
EPS_LN0 = 1e-5
EPS_LN = 1e-6
NEG = -3.0e38

ENGINES = ("sync", "gpsimd", "scalar", "vector", "tensor")
KB = 1024


class Em:
    def __init__(self):
        self.steps = {e: [] for e in ENGINES}
        self.sems = {}
        self._maxwait = {e: {} for e in ENGINES}

    def now(self, sem):
        return self.sems.setdefault(sem, 0)

    def emit(self, eng, method, *args, waits=(), inc=None, incv=1, serial=True, **kw):
        w = []
        for (s, v) in waits:
            self.sems.setdefault(s, 0)
            assert v <= self.sems[s], (eng, method, s, v, self.sems[s])
            if v > 0 and v > self._maxwait[eng].get(s, -1):
                w.append((s, v))
                self._maxwait[eng][s] = v
        if method == "sem_inc":
            self.steps[eng].append((method, args, kw, tuple(w), None, 0))
            self.sems[args[0]] = self.sems.setdefault(args[0], 0) + args[1]
            return
        if method is None and inc is not None:
            for i in range(len(self.steps[eng]) - 1, -1, -1):
                m2, a2, k2, w2, i2, v2 = self.steps[eng][i]
                if m2 is not None:
                    assert i2 is None, ("double marker on", eng, m2, i2, inc)
                    self.steps[eng][i] = (m2, a2, k2, w2, inc, incv)
                    break
            else:
                raise AssertionError("marker with no prior instruction")
            if w:
                self.steps[eng].append((None, (), {}, tuple(w), None, 0))
            self.sems[inc] = self.sems.setdefault(inc, 0) + incv
            return
        self.steps[eng].append((method, args, kw, tuple(w), inc, incv))
        if inc is not None:
            self.sems[inc] = self.sems.setdefault(inc, 0) + incv
        if method == "dma_start" and inc is not None and serial:
            self.ewait(eng, (inc, self.sems[inc]))

    def ewait(self, eng, *waits):
        self.emit(eng, None, waits=waits)

    def fence(self, eng, sem):
        for i in range(len(self.steps[eng]) - 1, -1, -1):
            m2, a2, k2, w2, i2, v2 = self.steps[eng][i]
            if m2 is not None:
                if i2 is None:
                    self.steps[eng][i] = (m2, a2, k2, w2, sem, 1)
                    self.sems[sem] = self.sems.setdefault(sem, 0) + 1
                    self.ewait(eng, (sem, self.now(sem)))
                else:
                    self.ewait(eng, (i2, self.now(i2)))
                return
        raise AssertionError("fence with no prior instruction")

    def allw(self):
        return [("tm", self.now("tm")), ("vm", self.now("vm")), ("sm", self.now("sm"))]

    def check(self):
        idx = {e: 0 for e in ENGINES}
        cnt = {s: 0 for s in self.sems}
        progress = True
        while progress:
            progress = False
            for e in ENGINES:
                while idx[e] < len(self.steps[e]):
                    method, args, kw, waits, inc, incv = self.steps[e][idx[e]]
                    if all(cnt[s] >= v for (s, v) in waits):
                        if method == "sem_inc":
                            cnt[args[0]] += args[1]
                        if inc is not None:
                            cnt[inc] += incv
                        idx[e] += 1
                        progress = True
                    else:
                        break
        stuck = {e: idx[e] for e in ENGINES if idx[e] < len(self.steps[e])}
        if stuck:
            for e, i in stuck.items():
                method, args, kw, waits, inc, incv = self.steps[e][i]
                unmet = [(s, v, cnt[s]) for (s, v) in waits if cnt[s] < v]
                print(f"DEADLOCK {e} step {i}/{len(self.steps[e])} {method} unmet={unmet}")
            raise RuntimeError("static deadlock detected")
        for s, v in cnt.items():
            assert v == self.sems[s], (s, v, self.sems[s])

    def realize(self, nc):
        self.check()
        handles = {name: nc.alloc_semaphore(name) for name in self.sems}
        with nc.Block() as block:
            for engname in ENGINES:
                def make(engname=engname):
                    def f(eng):
                        for (method, args, kw, waits, inc, incv) in self.steps[engname]:
                            for (s, v) in waits:
                                eng.wait_ge(handles[s], v)
                            if method is None:
                                continue
                            if method == "sem_inc":
                                eng.sem_inc(handles[args[0]], args[1])
                                continue
                            ins = getattr(eng, method)(*args, **kw)
                            if inc is not None:
                                ins.then_inc(handles[inc], incv)
                    return f
                getattr(block, engname)(make())


class Arena:
    def __init__(self, nc, cap_bytes):
        self.cap = cap_bytes
        self.t = nc.alloc_sbuf_tensor("arena", [128, cap_bytes], mybir.dt.uint8)

    def view(self, off_kb, cols, dt, parts=128):
        esz = 4 if dt == F32 else 2
        off = int(off_kb * KB)
        assert off % 32 == 0, off_kb
        assert off + cols * esz <= self.cap, (off_kb, cols, dt)
        return self.t[0:parts, off:off + cols * esz].bitcast(dt)


def build(taps=(), upto=None, nta=NT):
    taps = set(taps)
    nc = bass.Bass("TRN2", target_bir_lowering=False, detect_race_conditions=False)
    em = Em()

    x_hbm = nc.dram_tensor("x", [T, D], F32, kind="ExternalInput").ap()
    wq_hbm = nc.dram_tensor("wq", [D, D], F32, kind="ExternalInput").ap()
    wk_hbm = nc.dram_tensor("wk", [D, D], F32, kind="ExternalInput").ap()
    wv_hbm = nc.dram_tensor("wv", [D, D], F32, kind="ExternalInput").ap()
    wo_hbm = nc.dram_tensor("wo", [D, D], F32, kind="ExternalInput").ap()
    proj_hbm = nc.dram_tensor("proj", [M, DH], F32, kind="ExternalInput").ap()
    w1_hbm = nc.dram_tensor("w1", [FC, 128, DC, 128], F32, kind="ExternalInput").ap()
    w2_hbm = nc.dram_tensor("w2", [FC, 128, D], F32, kind="ExternalInput").ap()
    w1t_hbm = nc.dram_tensor("w1t", [DC, 128, DFF], F32, kind="ExternalInput").ap()
    w2t_hbm = nc.dram_tensor("w2t", [DC, 128, DFF], F32, kind="ExternalInput").ap()
    h1t_hbm = nc.dram_tensor("h1t", [FC, 128, FC, 128], F32, kind="ExternalInput").ap()
    h2t_hbm = nc.dram_tensor("h2t", [FC, 128, FC, 128], F32, kind="ExternalInput").ap()
    w3t_hbm = nc.dram_tensor("w3t", [DC, 128, FC, 128], F32, kind="ExternalInput").ap()
    vecs_hbm = {}
    for nm, ncols in [("b1v", FC), ("w1s", FC), ("w2s", FC), ("h1s", FC), ("h2s", FC),
                      ("b2v", DC), ("ln1g", DC), ("ln1b", DC), ("ln2g", DC), ("ln2b", DC),
                      ("w3s", DC)]:
        vecs_hbm[nm] = nc.dram_tensor(nm, [128, ncols], F32, kind="ExternalInput").ap()

    out_hbm = nc.dram_tensor("out", [T, D], F32, kind="ExternalOutput").ap()

    xnf_sp = nc.dram_tensor("xnf_sp", [DC, 128, T], F32).ap()
    xnb_sp = nc.dram_tensor("xnb_sp", [DC, 128, T], BF16).ap()
    sum1_sp = nc.dram_tensor("sum1_sp", [DC, 128, T], F32).ap()
    out1f_sp = nc.dram_tensor("out1f_sp", [DC, 128, T], F32).ap()
    sum2_sp = nc.dram_tensor("sum2_sp", [2, DC, 128, HALF], F32).ap()
    attnb_sp = nc.dram_tensor("attnb_sp", [2, DC, 128, HALF], F32).ap()

    w1bf = nc.dram_tensor("w1bf", [FC, 128, DC, 128], BF16).ap()
    w2bf = nc.dram_tensor("w2bf", [FC, 128, D], BF16).ap()
    w1tbf = nc.dram_tensor("w1tbf", [DC, 128, DFF], BF16).ap()
    w2tbf = nc.dram_tensor("w2tbf", [DC, 128, DFF], BF16).ap()
    h1tbf = nc.dram_tensor("h1tbf", [FC, 128, FC, 128], BF16).ap()
    h2tbf = nc.dram_tensor("h2tbf", [FC, 128, FC, 128], BF16).ap()
    w3tbf = nc.dram_tensor("w3tbf", [DC, 128, FC, 128], BF16).ap()
    ccmax_in = nc.dram_tensor("ccmax_in", [1, 8], F32)
    ccmax_out = nc.dram_tensor("ccmax_out", [1, 8], F32)
    cckv_in = nc.dram_tensor("cckv_in", [128, 520], F32)
    cckv_out = nc.dram_tensor("cckv_out", [128, 520], F32)

    id_f32_d = nc.inline_tensor(np.eye(128, dtype=np.float32), name="c_idf")
    id_bf_d = nc.inline_tensor(np.eye(128, dtype=ml_dtypes.bfloat16).view(np.uint16),
                               name="c_idb")
    bd_np = np.zeros((128, 128), np.float32)
    for kr in range(8):
        bd_np[0:64, kr * 16 + 2 * kr] = 1.0 / 16.0
        bd_np[64:128, kr * 16 + 2 * kr + 1] = 1.0 / 16.0
    bd_d = nc.inline_tensor(bd_np, name="c_bd")
    e2_np = np.zeros((16, 8 * 128), np.float32)
    for c in range(8):
        e2_np[2 * c, c * 128:c * 128 + 64] = 1.0
        e2_np[2 * c + 1, c * 128 + 64:c * 128 + 128] = 1.0
    e2_d = nc.inline_tensor(e2_np, name="c_e2")

    AR = Arena(nc, 206 * KB)
    V_ = AR.view
    id_f = V_(0.0, 128, F32)
    id_b = V_(0.5, 128, BF16)
    bd_c = V_(0.75, 128, F32)
    pd2 = V_(1.25, 128, BF16)
    kvbd = V_(1.5, 1024, BF16)
    ksbd = V_(3.5, 128, BF16)
    kvr = V_(3.75, 520, F32)
    ones_col = V_(5.8125, 1, F32)
    ones_row = V_(5.875, 128, F32)[0:1, :]
    mx128 = V_(6.375, 1, F32)
    mxloc = V_(6.4375, 16, F32)
    mxst = V_(6.5, 16, F32)
    e2_c = V_(6.5625, 1024, F32, parts=16)
    diagq_tm = V_(10.5625, 256, F32)
    diagk_tm = V_(11.5625, 256, F32)
    par = {}
    poff = 12.5625
    for nm, ncols in [("b1v", FC), ("negb1", FC), ("b1m1", FC), ("w1s", FC), ("w2s", FC),
                      ("h1s", FC), ("h2s", FC), ("b2v", DC), ("ln1g", DC), ("ln1b", DC),
                      ("ln2g", DC), ("ln2b", DC), ("w3s", DC)]:
        par[nm] = V_(poff, ncols, F32)
        poff += ncols * 4 / KB
    assert poff <= 14.0, poff

    XNBF = 14.0
    SCR = 46.0
    R_A = 70.0
    R_B = 134.0

    xn_bf = [V_(XNBF + 4 * c, T, BF16) for c in range(DC)]

    psA_t = nc.alloc_psum_tensor("psA", [128, 2048], F32)
    psB_t = nc.alloc_psum_tensor("psB", [128, 2048], F32)
    psA = psA_t[:, :]
    psB = psB_t[:, :]
    psA_bf = psA.bitcast(BF16)


    def _stop_here():
        em.emit("sync", "dma_start", out=out_hbm[0:128, :],
                in_=AR.t[0:128, 0:4096].bitcast(F32),
                waits=em.allw() + [("sp", em.now("sp"))], inc="st", incv=16)
        em.ewait("sync", ("st", em.now("st")))
        em.ewait("gpsimd", ("st", em.now("st")))
        if em.now("tst"):
            em.ewait("gpsimd", ("tst", em.now("tst")))
        em.realize(nc)
        return nc

    em.emit("sync", "dma_start", out=id_f, in_=id_f32_d.ap(), inc="xl", incv=16)
    em.emit("sync", "dma_start", out=id_b, in_=id_bf_d.ap().bitcast(BF16), inc="xl", incv=16)
    em.emit("sync", "dma_start", out=bd_c, in_=bd_d.ap(), inc="xl", incv=16)
    em.emit("sync", "dma_start", out=e2_c, in_=e2_d.ap(), inc="xl", incv=16)
    for nm in ["b1v", "w1s", "w2s", "h1s", "h2s", "b2v", "ln1g", "ln1b", "ln2g", "ln2b", "w3s"]:
        em.emit("sync", "dma_start", out=par[nm], in_=vecs_hbm[nm], inc="xl", incv=16)
    XL_CONSTS = em.now("xl")
    em.emit("vector", "tensor_scalar_add", par["b1m1"], par["b1v"], -1.0,
            waits=[("xl", XL_CONSTS)], inc="vm")
    em.emit("vector", "tensor_scalar_mul", par["negb1"], par["b1v"], -1.0, inc="vm")
    em.emit("vector", "memset", ones_col, 1.0, inc="vm")
    em.emit("vector", "memset", ones_row, 1.0, inc="vm")

    wv_sb = V_(R_B, 8 * 1024, BF16)
    wq_sb = V_(R_B + 16, 8 * 1024, BF16)
    wk_sb = V_(R_B + 32, 8 * 1024, BF16)
    proj_sb = V_(R_B + 48, 64, F32, parts=64)
    em.emit("gpsimd", "dma_start", out=wv_sb.rearrange("p (k n) -> p k n", k=8),
            in_=wv_hbm.rearrange("(k p) n -> p k n", p=128), inc="wl", incv=16)
    em.emit("gpsimd", "dma_start", out=wq_sb.rearrange("p (k n) -> p k n", k=8),
            in_=wq_hbm.rearrange("(k p) n -> p k n", p=128), inc="wl", incv=16)
    WL_WQ = em.now("wl")
    em.emit("gpsimd", "dma_start", out=wk_sb.rearrange("p (k n) -> p k n", k=8),
            in_=wk_hbm.rearrange("(k p) n -> p k n", p=128), inc="wl", incv=16)
    WL_WK = em.now("wl")
    WL_WV = 16
    em.emit("gpsimd", "dma_start", out=proj_sb, in_=proj_hbm, inc="wl", incv=16)
    WL_PROJ = em.now("wl")
    for srcb, dstb, nch in [(w1t_hbm, w1tbf, 2), (w2t_hbm, w2tbf, 2),
                            (w1_hbm, w1bf, 4), (w2_hbm, w2bf, 4),
                            (h1t_hbm, h1tbf, 8), (h2t_hbm, h2tbf, 8),
                            (w3t_hbm, w3tbf, 2)]:
        n0 = srcb.shape[0]
        step = n0 // nch
        for i in range(nch):
            em.emit("gpsimd", "dma_start", out=dstb[i * step:(i + 1) * step],
                    in_=srcb[i * step:(i + 1) * step], inc="pc", incv=16,
                    serial=False)
    PC_ALL = em.now("pc")

    em.emit("tensor", "transpose", psB[0:64, 0:64], proj_sb, id_f[0:64, 0:64],
            waits=[("wl", WL_PROJ), ("xl", XL_CONSTS)], inc="tm")
    TM_PROJ = em.now("tm")
    em.emit("vector", "memset", pd2, 0.0, inc="vm")
    em.fence("vector", "vm")
    em.emit("vector", "tensor_scalar_mul", pd2[0:64, 0:64], psB[0:64, 0:64], DN,
            waits=[("tm", TM_PROJ)], inc="vm")
    em.emit("vector", "tensor_scalar_mul", pd2[64:128, 64:128], psB[0:64, 0:64], DN, inc="vm")
    VM_PD2 = em.now("vm")

    xbuf = [V_(SCR, 1024, F32), V_(SCR + 4, 1024, F32)]
    xsq = V_(SCR + 8, 1024, F32)
    xnt = V_(SCR + 12, 1024, F32)
    srow = [V_(SCR + 16 + 0.0625 * i, 1, F32) for i in range(5)]
    xn_f32 = [V_(R_A + 8 * c, T, F32) for c in range(DC)]

    a_mark = []
    for t in range(nta):
        buf = xbuf[t % 2]
        w = []
        if t >= 2:
            w += [("vm", a_mark[t - 2][1]), ("sm", a_mark[t - 2][2])]
        em.emit("sync", "dma_start", out=buf, in_=x_hbm[t * 128:(t + 1) * 128, :],
                waits=w, inc="xl", incv=16)
        xlv = em.now("xl")
        em.emit("vector", "scalar_tensor_tensor", xsq, buf, 1.0, buf,
                op0=ALU.mult, op1=ALU.mult, accum_out=srow[2],
                waits=[("xl", xlv)], inc="vm")
        em.emit("vector", "tensor_reduce", srow[0], buf, axis=AX.X, op=ALU.add, inc="vm")
        em.fence("vector", "vm")
        em.emit("vector", "tensor_scalar_mul", srow[1], srow[0], 1.0 / D, inc="vm")
        em.emit("vector", "tensor_scalar", srow[2], srow[2], 1.0 / D, EPS_LN0,
                op0=ALU.mult, op1=ALU.add, inc="vm")
        em.fence("vector", "vm")
        em.emit("vector", "scalar_tensor_tensor", srow[3], srow[1], -1.0, srow[1],
                op0=ALU.mult, op1=ALU.mult, inc="vm")
        em.fence("vector", "vm")
        em.emit("vector", "tensor_add", srow[2], srow[2], srow[3], inc="vm")
        em.fence("vector", "vm")
        em.emit("scalar", "activation", srow[3], srow[2], AF.Sqrt,
                waits=[("vm", em.now("vm"))], inc="sm")
        em.emit("vector", "reciprocal", srow[4], srow[3],
                waits=[("sm", em.now("sm"))], inc="vm")
        em.fence("vector", "vm")
        em.emit("vector", "scalar_tensor_tensor", srow[3], srow[1], -1.0, srow[4],
                op0=ALU.mult, op1=ALU.mult, inc="vm")
        em.fence("vector", "vm")
        em.emit("scalar", "activation", xnt, buf, AF.Identity,
                scale=srow[4][:, 0:1], bias=srow[3][:, 0:1],
                waits=[("vm", em.now("vm"))], inc="sm")
        em.emit("vector", None, waits=[("sm", em.now("sm"))])
        em.emit("vector", "sem_inc", "vm", 1)
        vm_xnt = em.now("vm")
        if "a0" in taps and t == 0:
            tap_a0x = nc.dram_tensor("tap_a0x", [128, 1024], F32, kind="ExternalOutput").ap()
            tap_a0n = nc.dram_tensor("tap_a0n", [128, 1024], F32, kind="ExternalOutput").ap()
            tap_a0s = nc.dram_tensor("tap_a0s", [128, 8], F32, kind="ExternalOutput").ap()
            em.emit("sync", "dma_start", out=tap_a0x, in_=buf,
                    waits=[("vm", vm_xnt)], inc="st", incv=16)
            em.emit("sync", "dma_start", out=tap_a0n, in_=xnt, inc="st", incv=16)
            for si in range(5):
                em.emit("sync", "dma_start", out=tap_a0s[:, si:si + 1], in_=srow[si],
                        allow_slow_non_contiguous=True, inc="st", incv=16)
        pso = 0 if t % 2 == 0 else 1024
        tw = [("sm", em.now("sm"))]
        if t >= 2:
            tw += [("vm", a_mark[t - 2][1]), ("sm", a_mark[t - 2][2])]
        for c in range(DC):
            em.emit("tensor", "transpose", psA[:, pso + c * 128: pso + (c + 1) * 128],
                    xnt[:, c * 128:(c + 1) * 128], id_f,
                    waits=tw if c == 0 else ())
        em.emit("tensor", None, inc="tm")
        tmv = em.now("tm")
        for c in range(4):
            em.emit("vector", "tensor_copy", xn_f32[c][:, t * 128:(t + 1) * 128],
                    psA[:, pso + c * 128: pso + (c + 1) * 128],
                    waits=[("tm", tmv)] if c == 0 else (),
                    inc="vm" if c == 3 else None)
        for c in range(4, DC):
            em.emit("scalar", "activation", xn_f32[c][:, t * 128:(t + 1) * 128],
                    psA[:, pso + c * 128: pso + (c + 1) * 128], AF.Copy,
                    waits=[("tm", tmv)] if c == 4 else (),
                    inc="sm" if c == DC - 1 else None)
        vm_ev = em.now("vm")
        sm_ev = em.now("sm")
        for c in range(DC):
            eng2 = "vector" if c >= 4 else "scalar"
            if eng2 == "scalar":
                em.emit("scalar", "activation", xn_bf[c][:, t * 128:(t + 1) * 128],
                        xn_f32[c][:, t * 128:(t + 1) * 128], AF.Copy,
                        waits=[("vm", vm_ev)] if c == 0 else (),
                        inc="sm" if c == DC - 1 else None)
            else:
                em.emit("vector", "tensor_copy", xn_bf[c][:, t * 128:(t + 1) * 128],
                        xn_f32[c][:, t * 128:(t + 1) * 128],
                        waits=[("sm", sm_ev)] if c == 4 else (),
                        inc="vm" if c == DC - 1 else None)
        a_mark.append((tmv, em.now("vm"), em.now("sm")))

    VM_A = em.now("vm")
    SM_A = em.now("sm")
    for c in range(DC):
        em.emit("sync", "dma_start", out=xnf_sp[c], in_=xn_f32[c],
                waits=[("vm", VM_A)] if c == 0 else (), inc="sp", incv=16)
        em.emit("sync", "dma_start", out=xnb_sp[c], in_=xn_bf[c],
                waits=[("sm", SM_A)] if c == 0 else (), inc="sp", incv=16)
    SP_XN = em.now("sp")

    if "xn" in taps:
        tap_xn = nc.dram_tensor("tap_xn", [DC, 128, T], F32, kind="ExternalOutput").ap()
        for c in range(DC):
            em.emit("sync", "dma_start", out=tap_xn[c], in_=xn_f32[c],
                    waits=[("vm", VM_A)], inc="st", incv=16)

    if upto == "A":
        return _stop_here()

    q_bf = [V_(R_A + 4 * c, T, BF16) for c in range(DC)]
    k_bf = [V_(R_A + 32 + 4 * c, T, BF16) for c in range(DC)]
    qsq = V_(SCR, 2048, F32)
    diag_fm = V_(SCR + 8, 2048, F32, parts=16)

    def emit_qk(w_sb, wlv, dst, diag_dst_tm):
        marks = {}
        g = 0
        for oc in range(DC):
            for th in range(2):
                pso = 0 if g % 2 == 0 else 1024
                tw = [("wl", wlv)]
                if g >= 2:
                    tw += [("vm", marks[g - 2][0]), ("sm", marks[g - 2][1])]
                else:
                    tw += [("sm", em.now("sm")), ("vm", em.now("vm"))]
                first = True
                for k in range(DC):
                    for tc in range(2):
                        tcg = th * 2 + tc
                        em.emit("tensor", "matmul",
                                psA[:, pso + tc * 512: pso + (tc + 1) * 512],
                                w_sb[:, k * 1024 + oc * 128: k * 1024 + (oc + 1) * 128],
                                xn_bf[k][:, tcg * 512:(tcg + 1) * 512],
                                start=(k == 0), stop=(k == DC - 1),
                                waits=tw if first else ())
                        first = False
                em.emit("tensor", None, inc="tm")
                tmv = em.now("tm")
                em.emit("vector", "tensor_copy", dst[oc][:, th * 1024:(th + 1) * 1024],
                        psA[:, pso:pso + 1024],
                        waits=[("tm", tmv), ("sp", SP_XN)], inc="vm")
                em.emit("scalar", "activation", qsq[:, th * 1024:(th + 1) * 1024],
                        psA[:, pso:pso + 1024], AF.Square,
                        waits=[("tm", tmv), ("vm", em.now("vm"))], inc="sm")
                smv = em.now("sm")
                for tc in range(2):
                    tcg = th * 2 + tc
                    em.emit("tensor", "matmul", psB[0:16, tcg * 512:(tcg + 1) * 512],
                            bd_c[:, oc * 16:(oc + 1) * 16],
                            qsq[:, tcg * 512:(tcg + 1) * 512],
                            start=(oc == 0), stop=(oc == DC - 1),
                            waits=([("sm", smv)] + ([("vm", em.now("vm"))] if g <= 1 else []))
                            if tc == 0 else ())
                em.emit("tensor", None, inc="tm")
                marks[g] = (em.now("vm"), em.now("sm"))
                g += 1
        tmv = em.now("tm")
        em.emit("vector", "tensor_copy", diag_fm, psB[0:16, :], waits=[("tm", tmv)], inc="vm")
        vmv = em.now("vm")
        for t in range(NT):
            tw = [("vm", vmv)] if t == 0 else []
            if t >= 4:
                tw.append(("vm", em.now("vm")))
            em.emit("tensor", "transpose",
                    psB[:, (t % 4) * 512: (t % 4) * 512 + 16],
                    diag_fm[:, t * 128:(t + 1) * 128], id_f[0:16, 0:16], waits=tw, inc="tm")
            em.emit("vector", "tensor_copy", diag_dst_tm[:, t * 16:(t + 1) * 16],
                    psB[:, (t % 4) * 512: (t % 4) * 512 + 16],
                    waits=[("tm", em.now("tm"))], inc="vm")

    emit_qk(wk_sb, WL_WK, k_bf, diagk_tm)
    em.emit("vector", "memset", mxloc[:, 0:1], NEG, inc="vm")
    for t in range(NT):
        pso = 0 if t % 2 == 0 else 1024
        tw = [("vm", em.now("vm")), ("vm", VM_PD2)]
        first = True
        for pr in range(8):
            em.emit("tensor", "matmul", psA[:, pso + pr * 128: pso + (pr + 1) * 128],
                    k_bf[pr][:, t * 128:(t + 1) * 128], pd2,
                    start=True, stop=True, waits=tw if first else ())
            first = False
        em.emit("tensor", None, inc="tm")
        tmv = em.now("tm")
        em.emit("vector", "tensor_reduce", mxloc[:, 1:2], psA[:, pso:pso + 1024],
                axis=AX.X, op=ALU.max, waits=[("tm", tmv)], inc="vm")
        em.fence("vector", "vm")
        em.emit("vector", "tensor_max", mxloc[:, 0:1], mxloc[:, 0:1], mxloc[:, 1:2], inc="vm")
        em.fence("vector", "vm")
    em.emit("tensor", "transpose", psB[0:1, 1024:1152], mxloc[:, 0:1], id_f,
            waits=[("vm", em.now("vm"))], inc="tm")
    em.emit("vector", "memset", mxst[0:1, 0:8], NEG,
            waits=[("tm", em.now("tm"))], inc="vm")
    em.fence("vector", "vm")
    em.emit("vector", "tensor_reduce", mxst[0:1, 0:1], psB[0:1, 1024:1152], axis=AX.X,
            op=ALU.max, inc="vm")
    em.fence("vector", "vm")
    em.emit("gpsimd", "dma_start", out=ccmax_in.ap(), in_=mxst[0:1, 0:8],
            waits=[("vm", em.now("vm"))], inc="cl", incv=16)
    em.ewait("gpsimd", ("cl", em.now("cl")))
    em.emit("gpsimd", "collective_compute", "AllReduce", ALU.max,
            replica_groups=[list(range(NCORES))],
            ins=[ccmax_in.ap().opt()], outs=[ccmax_out.ap().opt()], inc="cc")
    em.ewait("gpsimd", ("cc", em.now("cc")))
    em.emit("gpsimd", "dma_start", out=mxst[0:1, 8:16], in_=ccmax_out.ap(), inc="cl", incv=16)
    CL_MX = em.now("cl")

    if upto == "C1":
        return _stop_here()

    emit_qk(wq_sb, WL_WQ, q_bf, diagq_tm)

    if "q" in taps:
        tap_q = nc.dram_tensor("tap_q", [DC, 128, T], F32, kind="ExternalOutput").ap()
        for c in range(DC):
            em.emit("gpsimd", "dma_start", out=tap_q[c], in_=q_bf[c],
                    waits=[("vm", em.now("vm"))], inc="tst", incv=16)
    if "diag" in taps:
        tap_dg = nc.dram_tensor("tap_diag", [2, 128, 256], F32, kind="ExternalOutput").ap()
        em.emit("sync", "dma_start", out=tap_dg[0], in_=diagq_tm,
                waits=[("vm", em.now("vm"))], inc="st", incv=16)
        em.emit("sync", "dma_start", out=tap_dg[1], in_=diagk_tm,
                waits=[("vm", em.now("vm"))], inc="st", incv=16)

    vaug = [V_(R_B + 16 + 2.125 * t, 1040, BF16) for t in range(NT)]
    VM_B_QK = em.now("vm")
    for t in range(NT):
        em.emit("vector", "memset", vaug[t], 1.0,
                waits=[("vm", VM_B_QK), ("tm", em.now("tm"))] if t == 0 else (),
                inc="vm" if t == NT - 1 else None)
    em.fence("vector", "vm")
    VM_VONES = em.now("vm")
    for t in range(NT):
        for nh in range(2):
            g = 2 * t + nh
            pso = (g % 4) * 512
            tw = [("wl", WL_WV), ("vm", VM_VONES), ("vm", em.now("vm"))]
            first = True
            for k in range(DC):
                em.emit("tensor", "matmul", psA[:, pso:pso + 512],
                        xn_bf[k][:, t * 128:(t + 1) * 128],
                        wv_sb[:, k * 1024 + nh * 512: k * 1024 + (nh + 1) * 512],
                        start=(k == 0), stop=(k == DC - 1), waits=tw if first else ())
                first = False
            em.emit("tensor", None, inc="tm")
            tmv = em.now("tm")
            dst = vaug[t].rearrange("p (h e) -> p h e", h=16)[:, nh * 8:(nh + 1) * 8, 0:64]
            src = psA[:, pso:pso + 512].rearrange("p (h d) -> p h d", h=8)
            em.emit("vector", "tensor_copy", dst, src, waits=[("tm", tmv)], inc="vm")
    VM_VAUG = em.now("vm")

    if upto == "B":
        return _stop_here()

    qf_tm = [V_(R_B + 2 * t, 1024, BF16) if t < 8 else V_(R_B + 50 + 2 * (t - 8), 1024, BF16)
             for t in range(NT)]
    stage = V_(SCR, 512, F32)
    biasr = V_(SCR + 2, 16, F32)
    stageE = V_(SCR + 4, 1024, F32)
    kf_t = V_(SCR + 8, 1024, BF16)

    em.emit("tensor", "matmul", psB[:, 1028:1029], ones_row, mxst[0:1, 8:9],
            start=True, stop=True, waits=[("cl", CL_MX)], inc="tm")
    em.emit("vector", "tensor_copy", mx128, psB[:, 1028:1029],
            waits=[("tm", em.now("tm"))], inc="vm")
    VM_MX128 = em.now("vm")

    ekm = {}
    for t in range(NT):
        pso = 0 if t % 2 == 0 else 1024
        if t >= 2:
            tw = [("vm", ekm[t - 2][0]), ("sm", ekm[t - 2][1])]
        else:
            tw = [("vm", em.now("vm")), ("sm", em.now("sm"))]
        first = True
        for pr in range(8):
            em.emit("tensor", "matmul", psA[:, pso + pr * 128: pso + (pr + 1) * 128],
                    k_bf[pr][:, t * 128:(t + 1) * 128], pd2,
                    start=True, stop=True, waits=tw if first else ())
            first = False
        em.emit("tensor", None, inc="tm")
        tmv = em.now("tm")
        em.emit("vector", "tensor_scalar", biasr, diagk_tm[:, t * 16:(t + 1) * 16],
                mx128[:, 0:1], -1.0, op0=ALU.add, op1=ALU.mult,
                waits=[("tm", tmv), ("vm", VM_MX128)], inc="vm")
        em.fence("vector", "vm")
        vm_b = em.now("vm")
        for h in range(8, H):
            em.emit("scalar", "activation", stageE[:, h * 64:(h + 1) * 64],
                    psA[:, pso + h * 64: pso + h * 64 + 64], AF.Exp,
                    bias=biasr[:, h:h + 1],
                    waits=[("vm", vm_b), ("tm", tmv)] if h == 8 else (),
                    inc="sm" if h == H - 1 else None)
        for h in range(8):
            em.emit("vector", "tensor_scalar_add", stage[:, h * 64:(h + 1) * 64],
                    psA[:, pso + h * 64: pso + h * 64 + 64], biasr[:, h:h + 1],
                    inc="vm" if h == 7 else None)
        em.emit("scalar", "activation", stageE[:, 0:512], stage, AF.Exp,
                waits=[("vm", em.now("vm"))], inc="sm")
        em.emit("vector", "tensor_scalar", kf_t, stageE, EPSK, RATIO,
                op0=ALU.add, op1=ALU.mult, waits=[("sm", em.now("sm"))], inc="vm")
        vm_kf = em.now("vm")
        first = True
        for h in range(H):
            ph, cg = h % 2, h // 2
            colbase = (0 if cg < 4 else 512) + (cg % 4) * 65
            em.emit("tensor", "matmul",
                    psB[ph * 64:(ph + 1) * 64, colbase:colbase + 65],
                    kf_t[:, h * 64:(h + 1) * 64],
                    vaug[t][:, h * 65:(h + 1) * 65],
                    start=(t == 0 and h in (0, 1, 8, 9)),
                    stop=(t == NT - 1 and h in (6, 7, 14, 15)),
                    skip_group_check=True,
                    waits=[("vm", vm_kf), ("vm", VM_VAUG)] if first else ())
            first = False
        em.emit("tensor", None, inc="tm")
        ekm[t] = (em.now("vm"), em.now("sm"))
    TM_KV = em.now("tm")
    em.emit("vector", "tensor_copy", kvr[:, 0:260], psB[:, 0:260],
            waits=[("tm", TM_KV)], inc="vm")
    em.emit("vector", "tensor_copy", kvr[:, 260:520], psB[:, 512:772], inc="vm")
    em.emit("gpsimd", "dma_start", out=cckv_in.ap(), in_=kvr,
            waits=[("vm", em.now("vm"))], inc="cl", incv=16)
    em.ewait("gpsimd", ("cl", em.now("cl")))
    em.emit("gpsimd", "collective_compute", "AllReduce", ALU.add,
            replica_groups=[[0, 1, 2, 3], [4, 5, 6, 7]],
            ins=[cckv_in.ap().opt()], outs=[cckv_out.ap().opt()], inc="cc")
    em.ewait("gpsimd", ("cc", em.now("cc")))
    em.emit("gpsimd", "dma_start", out=kvr, in_=cckv_out.ap(), inc="cl", incv=16)
    CL_KV = em.now("cl")

    if upto == "E":
        return _stop_here()

    ddm = {}
    for t in range(NT):
        pso = 0 if t % 2 == 0 else 1024
        if t >= 2:
            tw = [("vm", ddm[t - 2][0]), ("sm", ddm[t - 2][1])]
        else:
            tw = [("vm", em.now("vm")), ("sm", em.now("sm"))]
        first = True
        for pr in range(8):
            em.emit("tensor", "matmul", psA[:, pso + pr * 128: pso + (pr + 1) * 128],
                    q_bf[pr][:, t * 128:(t + 1) * 128], pd2,
                    start=True, stop=True, waits=tw if first else ())
            first = False
        em.emit("tensor", None, inc="tm")
        tmv = em.now("tm")
        dd3 = psA[:, pso:pso + 1024].rearrange("p (h m) -> p h m", h=H)
        em.emit("vector", "tensor_reduce", mxloc, dd3, axis=AX.X, op=ALU.max,
                waits=[("tm", tmv)], inc="vm")
        em.fence("vector", "vm")
        em.emit("vector", "scalar_tensor_tensor", biasr, mxloc, -1.0,
                diagq_tm[:, t * 16:(t + 1) * 16], op0=ALU.mult, op1=ALU.subtract, inc="vm")
        em.fence("vector", "vm")
        vm_b = em.now("vm")
        for h in range(8, H):
            em.emit("scalar", "activation", stageE[:, h * 64:(h + 1) * 64],
                    psA[:, pso + h * 64: pso + h * 64 + 64], AF.Exp,
                    bias=biasr[:, h:h + 1],
                    waits=[("vm", vm_b), ("tm", tmv)] if h == 8 else (),
                    inc="sm" if h == H - 1 else None)
        for h in range(8):
            em.emit("vector", "tensor_scalar_add", stage[:, h * 64:(h + 1) * 64],
                    psA[:, pso + h * 64: pso + h * 64 + 64], biasr[:, h:h + 1],
                    inc="vm" if h == 7 else None)
        em.emit("scalar", "activation", stageE[:, 0:512], stage, AF.Exp,
                waits=[("vm", em.now("vm"))], inc="sm")
        em.emit("vector", "tensor_scalar", qf_tm[t], stageE, EPSK, RATIO,
                op0=ALU.add, op1=ALU.mult, waits=[("sm", em.now("sm"))], inc="vm")
        ddm[t] = (em.now("vm"), em.now("sm"))
    VM_QF = em.now("vm")

    if upto == "C2":
        return _stop_here()

    qf_fm = [V_(R_A + 4 * c, T, BF16) for c in range(DC)]
    for t in range(NT):
        bank = t % 4
        tw = [("vm", VM_QF)]
        if t >= 4:
            tw.append(("vm", em.now("vm")))
        for c in range(DC):
            em.emit("tensor", "transpose",
                    psA_bf[:, bank * 1024 + c * 128: bank * 1024 + (c + 1) * 128],
                    qf_tm[t][:, c * 128:(c + 1) * 128], id_b,
                    waits=tw if c == 0 else ())
        em.emit("tensor", None, inc="tm")
        tmv = em.now("tm")
        for c in range(DC):
            em.emit("vector", "tensor_copy", qf_fm[c][:, t * 128:(t + 1) * 128],
                    psA_bf[:, bank * 1024 + c * 128: bank * 1024 + (c + 1) * 128],
                    waits=[("tm", tmv)] if c == 0 else (),
                    inc="vm" if c == DC - 1 else None)
    VM_QFFM = em.now("vm")

    if "kv" in taps:
        tap_kv = nc.dram_tensor("tap_kv", [128, 520], F32, kind="ExternalOutput").ap()
        em.emit("gpsimd", "dma_start", out=tap_kv, in_=kvr, waits=[("cl", CL_KV)],
                inc="tst", incv=16)
    if "qf" in taps:
        tap_qf = nc.dram_tensor("tap_qf", [DC, 128, T], F32, kind="ExternalOutput").ap()
        for c in range(DC):
            em.emit("gpsimd", "dma_start", out=tap_qf[c], in_=qf_fm[c],
                    waits=[("vm", VM_QFFM)], inc="tst", incv=16)

    if upto == "D":
        return _stop_here()

    em.ewait("vector", ("cl", CL_KV))
    em.emit("vector", "memset", kvbd, 0.0, inc="vm")
    em.emit("vector", "memset", ksbd, 0.0, inc="vm")
    em.fence("vector", "vm")
    for hh in range(H):
        ph, cg = hh % 2, hh // 2
        src = kvr[ph * 64:(ph + 1) * 64, cg * 65: cg * 65 + 64]
        srcs = kvr[ph * 64:(ph + 1) * 64, cg * 65 + 64: cg * 65 + 65]
        em.emit("vector", "tensor_copy",
                kvbd[ph * 64:(ph + 1) * 64, cg * 128 + ph * 64: cg * 128 + ph * 64 + 64],
                src, inc=None)
        em.emit("vector", "tensor_copy",
                ksbd[ph * 64:(ph + 1) * 64, cg * 16 + hh: cg * 16 + hh + 1],
                srcs, inc="vm" if hh == H - 1 else None)
    VM_KVBD = em.now("vm")

    for kr in range(DC):
        first = True
        for tc in range(4):
            em.emit("tensor", "matmul", psB[0:16, tc * 512:(tc + 1) * 512],
                    ksbd[:, kr * 16:(kr + 1) * 16], qf_fm[kr][:, tc * 512:(tc + 1) * 512],
                    start=(kr == 0), stop=(kr == DC - 1),
                    waits=[("vm", VM_KVBD), ("vm", VM_QFFM)] if (first and kr == 0) else ())
            first = False
    em.emit("tensor", None, inc="tm")
    rcp_den = V_(SCR, 2048, F32, parts=16)
    em.emit("vector", "tensor_copy", rcp_den, psB[0:16, :], waits=[("tm", em.now("tm"))],
            inc="vm")
    em.emit("vector", "reciprocal", rcp_den, rcp_den, inc="vm")
    VM_RCP = em.now("vm")

    if "den" in taps:
        tap_den = nc.dram_tensor("tap_den", [16, 2048], F32, kind="ExternalOutput").ap()
        em.emit("sync", "dma_start", out=tap_den, in_=rcp_den, waits=[("vm", VM_RCP)],
                inc="st", incv=16)

    attn_bf = [V_(R_A + 32 + 4 * c, T, BF16) for c in range(DC)]
    bc_sb = V_(SCR + 8, 2048, F32)
    for c in range(DC):
        tw = [("vm", VM_RCP), ("vm", em.now("vm"))]
        first = True
        for tc in range(4):
            em.emit("tensor", "matmul", psA[:, tc * 512:(tc + 1) * 512],
                    kvbd[:, c * 128:(c + 1) * 128], qf_fm[c][:, tc * 512:(tc + 1) * 512],
                    start=True, stop=True, waits=tw if first else ())
            first = False
        for tc in range(4):
            em.emit("tensor", "matmul", psB[:, tc * 512:(tc + 1) * 512],
                    e2_c[:, c * 128:(c + 1) * 128], rcp_den[:, tc * 512:(tc + 1) * 512],
                    start=True, stop=True)
        em.emit("tensor", None, inc="tm")
        em.emit("vector", "tensor_copy", bc_sb, psB, waits=[("tm", em.now("tm"))], inc="vm")
        em.emit("vector", "tensor_mul", attn_bf[c], psA, bc_sb, inc="vm")
    VM_ATTN = em.now("vm")

    wo_sb = V_(R_B, 8 * 1024, BF16)
    em.emit("gpsimd", "dma_start", out=wo_sb.rearrange("p (k n) -> p k n", k=8),
            in_=wo_hbm.rearrange("(k p) n -> p k n", p=128),
            waits=[("vm", VM_QFFM)], inc="wl", incv=16)
    WL_WO = em.now("wl")
    xnch = [V_(SCR + 8, 2048, F32), V_(SCR + 16, 2048, F32)]
    sum1_sb = ([V_(R_A + 8 * i, 2048, F32) for i in range(4)] +
               [V_(R_B + 16 + 8 * i, 2048, F32) for i in range(4)])
    sqch = V_(R_B + 48, 2048, F32)

    for oc in range(DC):
        w = [("sp", SP_XN), ("tm", TM_KV), ("vm", VM_ATTN)]
        if oc >= 2:
            w.append(("vm", em.now("vm")))
        em.emit("sync", "dma_start", out=xnch[oc % 2], in_=xnf_sp[oc], waits=w,
                inc="xl", incv=16)
        xlv = em.now("xl")
        tw = [("wl", WL_WO), ("vm", VM_ATTN), ("vm", em.now("vm")), ("sm", em.now("sm"))]
        first = True
        for kr in range(DC):
            for tc in range(4):
                em.emit("tensor", "matmul", psA[:, tc * 512:(tc + 1) * 512],
                        wo_sb[:, kr * 1024 + oc * 128: kr * 1024 + (oc + 1) * 128],
                        attn_bf[kr][:, tc * 512:(tc + 1) * 512],
                        start=(kr == 0), stop=(kr == DC - 1), waits=tw if first else ())
                first = False
        em.emit("tensor", None, inc="tm")
        tmv = em.now("tm")
        em.emit("vector", "tensor_add", sum1_sb[oc], psA, xnch[oc % 2],
                waits=[("tm", tmv), ("xl", xlv)], inc="vm")
        vmv = em.now("vm")
        em.emit("scalar", "activation", sqch, sum1_sb[oc], AF.Square,
                waits=[("vm", vmv)], inc="sm")
        smv = em.now("sm")
        for tc in range(4):
            em.emit("tensor", "matmul", psB[0:1, tc * 512:(tc + 1) * 512],
                    ones_col, sum1_sb[oc][:, tc * 512:(tc + 1) * 512],
                    start=(oc == 0), stop=(oc == DC - 1),
                    waits=[("vm", vmv)] if tc == 0 else ())
        for tc in range(4):
            em.emit("tensor", "matmul", psB[32:33, tc * 512:(tc + 1) * 512],
                    ones_col, sqch[:, tc * 512:(tc + 1) * 512],
                    start=(oc == 0), stop=(oc == DC - 1), skip_group_check=True,
                    waits=[("sm", smv)] if tc == 0 else ())
        em.emit("tensor", None, inc="tm")
    SP_SUM1 = em.now("sp")
    VM_SUM1 = em.now("vm")

    def emit_ln_rows(eps, width, rowbase_kb):
        if isinstance(rowbase_kb, tuple):
            b0, b1, b2 = rowbase_kb
        else:
            b0 = rowbase_kb
            b1 = rowbase_kb + width * 4 / KB
            b2 = rowbase_kb + 2 * width * 4 / KB
        mu_r = V_(b0, width, F32, parts=1)
        var_r = V_(b1, width, F32, parts=1)
        tmp_r = V_(b2, width, F32, parts=1)
        em.emit("vector", "tensor_scalar_mul", mu_r, psB[0:1, 0:width], 1.0 / D,
                waits=[("tm", em.now("tm"))], inc="vm")
        em.emit("vector", "tensor_scalar", var_r, psB[32:33, 0:width], 1.0 / D, eps,
                op0=ALU.mult, op1=ALU.add, inc="vm")
        em.emit("vector", "tensor_mul", tmp_r, mu_r, mu_r, inc="vm")
        em.emit("vector", "tensor_sub", var_r, var_r, tmp_r, inc="vm")
        em.emit("scalar", "activation", tmp_r, var_r, AF.Sqrt,
                waits=[("vm", em.now("vm"))], inc="sm")
        em.emit("vector", "reciprocal", var_r, tmp_r, waits=[("sm", em.now("sm"))], inc="vm")
        em.emit("vector", "scalar_tensor_tensor", mu_r, mu_r, -1.0, var_r,
                op0=ALU.mult, op1=ALU.mult, inc="vm")
        vmv = em.now("vm")
        for tc in range(width // 512):
            em.emit("tensor", "matmul", psA[:, tc * 512:(tc + 1) * 512], ones_row,
                    var_r[:, tc * 512:(tc + 1) * 512], start=True, stop=True,
                    waits=[("vm", vmv)] if tc == 0 else ())
        for tc in range(width // 512):
            em.emit("tensor", "matmul", psB[:, tc * 512:(tc + 1) * 512], ones_row,
                    mu_r[:, tc * 512:(tc + 1) * 512], start=True, stop=True)
        em.emit("tensor", None, inc="tm")
        return em.now("tm")

    tm_bc1 = emit_ln_rows(EPS_LN, 2048, (R_B + 56, R_B + 48, SCR + 16))
    out1_bf = [V_(R_A + 4 * c, T, BF16) for c in range(DC)]
    for oc in range(DC):
        ssem = "soA" if oc % 2 == 0 else "soB"
        o1f = sum1_sb[oc]
        em.emit("vector", "tensor_mul", o1f, o1f, psA, waits=[("tm", tm_bc1)], inc="vm")
        em.emit("vector", "tensor_add", o1f, o1f, psB, inc="vm")
        em.emit("vector", "tensor_scalar", o1f, o1f, par["ln1g"][:, oc:oc + 1],
                par["ln1b"][:, oc:oc + 1], op0=ALU.mult, op1=ALU.add, inc="vm")
        vmv = em.now("vm")
        em.emit("scalar", "dma_start", out=out1f_sp[oc], in_=o1f, waits=[("vm", vmv)],
                inc=ssem, incv=16, serial=False)
    em.ewait("scalar", ("soA", em.now("soA")), ("soB", em.now("soB")))
    em.emit("scalar", "sem_inc", "sp", DC * 16)
    for oc in range(DC):
        em.emit("scalar", "activation", out1_bf[oc], sum1_sb[oc], AF.Copy, inc="sm")
    SP_OUT1 = em.now("sp")
    SM_OUT1 = em.now("sm")

    if "out1" in taps:
        tap_o1 = nc.dram_tensor("tap_out1", [DC, 128, T], F32, kind="ExternalOutput").ap()
        for oc in range(DC):
            em.emit("gpsimd", "dma_start", out=tap_o1[oc], in_=out1_bf[oc],
                    waits=[("sm", SM_OUT1)], inc="tst", incv=16)

    if upto == "G":
        return _stop_here()

    H_H = ([V_(XNBF + 2 * i, HALF, BF16) for i in range(16)] +
           [V_(R_A + 32 + 2 * (i - 16), HALF, BF16) for i in range(16, 32)])
    w1slab = [V_(SCR, 1024, BF16), V_(SCR + 2, 1024, BF16)]
    w2_sb = V_(R_B, FC * 1024, BF16)
    elu1 = V_(SCR + 4, 1024, F32)
    o1chb4 = [V_(SCR, 1024, F32), V_(SCR + 4, 1024, F32),
              V_(SCR + 8, 1024, F32), V_(SCR + 12, 1024, F32)]
    elu2 = V_(SCR + 8, 1024, F32)
    o1ch = V_(SCR + 12, 1024, F32)
    sum2ch = V_(SCR + 16, 1024, F32)
    sq2ch = V_(SCR + 20, 1024, F32)

    em.emit("gpsimd", "dma_start", out=w2_sb.rearrange("p (k n) -> p k n", k=FC),
            in_=w2bf.rearrange("k p n -> p k n"),
            waits=[("vm", em.now("vm")), ("tm", em.now("tm")), ("sm", em.now("sm")),
                   ("sp", SP_OUT1), ("pc", PC_ALL)], inc="wl", incv=16)
    WL_W2 = em.now("wl")

    for hf in range(2):
        tmbase = em.now("tm")
        emarks = {}
        for oc in range(FC):
            w = [("sm", SM_OUT1), ("sp", SP_OUT1)]
            if oc >= 2:
                w.append(("tm", tmbase + oc - 1))
            else:
                w.append(("tm", tmbase))
                w += [(f"rl{i}", em.now(f"rl{i}")) for i in range(4)]
                w.append(("vm", em.now("vm")))
            wsem = "wlA" if oc % 2 == 0 else "wlB"
            em.emit("gpsimd", "dma_start",
                    out=w1slab[oc % 2].rearrange("p (k n) -> p k n", k=DC),
                    in_=w1bf[oc], waits=w + [("pc", PC_ALL)], inc=wsem, incv=16, serial=False)
            pso = 0 if oc % 2 == 0 else 1024
            tw = [(wsem, em.now(wsem)), ("sm", SM_OUT1), ("sp", SP_OUT1),
                  ("vm", emarks[oc - 2] if oc >= 2 else em.now("vm"))]
            first = True
            for k in range(DC):
                for tc2 in range(2):
                    em.emit("tensor", "matmul",
                            psA[:, pso + tc2 * 512: pso + (tc2 + 1) * 512],
                            w1slab[oc % 2][:, k * 128:(k + 1) * 128],
                            out1_bf[k][:, hf * HALF + tc2 * 512: hf * HALF + (tc2 + 1) * 512],
                            start=(k == 0), stop=(k == DC - 1), waits=tw if first else ())
                    first = False
            em.emit("tensor", None, inc="tm")
            tmv = em.now("tm")
            ps = psA[:, pso:pso + 1024]
            em.emit("scalar", "activation", elu1, ps, AF.Relu,
                    bias=par["negb1"][:, oc:oc + 1], scale=-1.0,
                    waits=[("tm", tmv), ("vm", em.now("vm"))], inc="sm")
            em.emit("scalar", "activation", elu2, elu1, AF.Exp, scale=-1.0, inc="sm")
            smv = em.now("sm")
            em.emit("vector", "scalar_tensor_tensor", elu2, elu1, 1.0, elu2,
                    op0=ALU.mult, op1=ALU.add, waits=[("sm", smv)], inc="vm")
            em.emit("vector", "scalar_tensor_tensor", elu2, elu2, 1.0, ps,
                    op0=ALU.mult, op1=ALU.add, inc="vm")
            em.emit("vector", "tensor_scalar_add", H_H[oc], elu2,
                    par["b1m1"][:, oc:oc + 1], inc="vm")
            emarks[oc] = em.now("vm")
        assert em.now("tm") == tmbase + FC

        if "helu" in taps:
            if hf == 0:
                tap_he = nc.dram_tensor("tap_helu", [2, FC, 128, HALF], F32,
                                        kind="ExternalOutput").ap()
                globals()["_tap_he"] = tap_he
            tap_he = globals()["_tap_he"]
            for k in range(FC):
                em.emit("gpsimd", "dma_start", out=tap_he[hf][k], in_=H_H[k],
                        waits=[("vm", em.now("vm"))], inc="tst", incv=16)

        smarks = {}
        for oc in range(DC):
            w = [("sp", SP_OUT1)]
            if oc == 0:
                w += [("vm", em.now("vm")), ("sm", em.now("sm"))]
            if oc >= 1:
                w += [("vm", em.now("vm")), ("sm", em.now("sm"))]
            em.emit("sync", "dma_start", out=o1ch,
                    in_=out1f_sp[oc][:, hf * HALF:(hf + 1) * HALF], waits=w,
                    inc="xl", incv=16)
            xlv = em.now("xl")
            pso = 0 if oc % 2 == 0 else 1024
            tw = [("wl", WL_W2),
                  ("vm", smarks[oc - 2] if oc >= 2 else em.now("vm"))]
            first = True
            for k in range(FC):
                for tc2 in range(2):
                    em.emit("tensor", "matmul",
                            psA[:, pso + tc2 * 512: pso + (tc2 + 1) * 512],
                            w2_sb[:, k * 1024 + oc * 128: k * 1024 + (oc + 1) * 128],
                            H_H[k][:, tc2 * 512:(tc2 + 1) * 512],
                            start=(k == 0), stop=(k == FC - 1), waits=tw if first else ())
                    first = False
            em.emit("tensor", None, inc="tm")
            tmv = em.now("tm")
            ps = psA[:, pso:pso + 1024]
            em.emit("vector", "scalar_tensor_tensor", sum2ch, ps, par["b2v"][:, oc:oc + 1],
                    o1ch, op0=ALU.add, op1=ALU.add, waits=[("tm", tmv), ("xl", xlv)],
                    inc="vm")
            vmv = em.now("vm")
            smarks[oc] = vmv
            em.emit("sync", "dma_start", out=sum2_sp[hf][oc], in_=sum2ch, waits=[("vm", vmv)],
                    inc="sp", incv=16)
            em.emit("scalar", "activation", sq2ch, sum2ch, AF.Square, waits=[("vm", vmv)],
                    inc="sm")
            smv = em.now("sm")
            for tc2 in range(2):
                em.emit("tensor", "matmul", psB[0:1, tc2 * 512:(tc2 + 1) * 512],
                        ones_col, sum2ch[:, tc2 * 512:(tc2 + 1) * 512],
                        start=(oc == 0), stop=(oc == DC - 1),
                        waits=[("vm", vmv)] if tc2 == 0 else ())
            for tc2 in range(2):
                em.emit("tensor", "matmul", psB[32:33, tc2 * 512:(tc2 + 1) * 512],
                        ones_col, sq2ch[:, tc2 * 512:(tc2 + 1) * 512],
                        start=(oc == 0), stop=(oc == DC - 1), skip_group_check=True,
                        waits=[("sm", smv)] if tc2 == 0 else ())
            em.emit("tensor", None, inc="tm")
        SP_SUM2 = em.now("sp")

        tm_bc2 = emit_ln_rows(EPS_LN, HALF, XNBF)
        ap2 = {}
        for oc in range(DC):
            rsem = f"rl{oc % 4}"
            ssem = f"so{oc % 4}"
            w = [("sp", SP_SUM2), ("tm", tm_bc2)]
            if oc >= 4:
                w.append((ssem, ap2[oc - 4]))
            ob = o1chb4[oc % 4]
            em.emit("sync", "dma_start", out=ob, in_=sum2_sp[hf][oc], waits=w,
                    inc=rsem, incv=16, serial=False)
            rlv = em.now(rsem)
            em.emit("vector", "tensor_mul", ob, ob, psA[:, 0:HALF],
                    waits=[(rsem, rlv), ("tm", tm_bc2)], inc="vm")
            em.emit("vector", "tensor_add", ob, ob, psB[:, 0:HALF], inc="vm")
            em.emit("vector", "tensor_scalar", ob, ob, par["ln2g"][:, oc:oc + 1],
                    par["ln2b"][:, oc:oc + 1], op0=ALU.mult, op1=ALU.add, inc="vm")
            em.emit("scalar", "dma_start", out=attnb_sp[hf][oc], in_=ob,
                    waits=[("vm", em.now("vm"))], inc=ssem, incv=16, serial=False)
            ap2[oc] = em.now(ssem)
    em.ewait("scalar", *[(f"so{i}", em.now(f"so{i}")) for i in range(4)])
    em.emit("scalar", "sem_inc", "sp", 4 * DC * 16)
    SP_ATTNB = em.now("sp")

    if "attnb" in taps:
        tap_ab = nc.dram_tensor("tap_attnb", [2, DC, 128, HALF], F32,
                                kind="ExternalOutput").ap()
        for hf in range(2):
            for oc in range(DC):
                em.emit("sync", "dma_start", out=tap_ab[hf][oc], in_=attnb_sp[hf][oc],
                        waits=[("sp", SP_ATTNB)], inc="st", incv=16)

    if upto == "H":
        return _stop_here()

    xn2_bf = [V_(R_A + 4 * c, T, BF16) for c in range(DC)]
    em.emit("sync", "dma_start", out=xn2_bf[0], in_=xnb_sp[0],
            waits=[("sp", SP_XN)] + em.allw(), inc="xl", incv=16)
    for c in range(1, DC):
        em.emit("sync", "dma_start", out=xn2_bf[c], in_=xnb_sp[c], inc="xl", incv=16)
    XL_XN2 = em.now("xl")

    tslab = [V_(SCR, 4096, BF16), V_(SCR + 8, 4096, BF16)]
    hslab = [V_(SCR, 4096, BF16), V_(SCR + 8, 4096, BF16)]
    silu_s = V_(SCR + 16, 1024, F32)
    silu_s2 = V_(SCR + 20, 1024, F32)
    y3ch = V_(SCR + 16, 1024, F32)
    abchb = [V_(6.5625, 1024, F32), V_(SCR + 20, 1024, F32)]

    for hf in range(2):
        TH = [V_(R_B + 2 * i, HALF, BF16) for i in range(FC)]
        G1 = ([V_(XNBF + 2 * i, HALF, BF16) for i in range(16)] +
              [V_(R_A + 32 + 2 * (i - 16), HALF, BF16) for i in range(16, 32)])
        G2 = TH

        tmbase = em.now("tm")
        tmarks = {}
        for oc in range(DC):
            w = [("xl", XL_XN2)]
            if oc == 0:
                w += [("tm", em.now("tm")), ("vm", em.now("vm")), ("sm", em.now("sm")),
                      ("sp", SP_ATTNB), ("st", em.now("st"))]
                w += [(f"rl{i}", em.now(f"rl{i}")) for i in range(4)]
            else:
                w.append(("tm", tmbase + 4 * oc))
            em.emit("gpsimd", "dma_start",
                    out=tslab[0].rearrange("p (k n) -> p k n", k=DC),
                    in_=w1tbf.rearrange("k p n -> p k n")[:, :, oc * 512:(oc + 1) * 512],
                    waits=w + [("pc", PC_ALL)], inc="wt0", incv=16, serial=False)
            em.emit("gpsimd", "dma_start",
                    out=tslab[1].rearrange("p (k n) -> p k n", k=DC),
                    in_=w2tbf.rearrange("k p n -> p k n")[:, :, oc * 512:(oc + 1) * 512],
                    inc="wt1", incv=16, serial=False)
            for s4 in range(4):
                occ = oc * 4 + s4
                pst = psA if occ % 2 == 0 else psB
                sil = silu_s if occ % 2 == 0 else silu_s2
                tw = [("wt0", em.now("wt0")), ("wt1", em.now("wt1"))]
                tw.append(("vm", tmarks[occ - 2] if occ >= 2 else em.now("vm")))
                if occ < 2:
                    tw.append(("sm", em.now("sm")))
                first = True
                for k in range(DC):
                    for tc2 in range(2):
                        em.emit("tensor", "matmul",
                                pst[:, tc2 * 512:(tc2 + 1) * 512],
                                tslab[0][:, k * 512 + s4 * 128: k * 512 + (s4 + 1) * 128],
                                xn2_bf[k][:, hf * HALF + tc2 * 512: hf * HALF + (tc2 + 1) * 512],
                                start=(k == 0), stop=(k == DC - 1), waits=tw if first else ())
                        first = False
                for k in range(DC):
                    for tc2 in range(2):
                        em.emit("tensor", "matmul",
                                pst[:, 1024 + tc2 * 512: 1024 + (tc2 + 1) * 512],
                                tslab[1][:, k * 512 + s4 * 128: k * 512 + (s4 + 1) * 128],
                                xn2_bf[k][:, hf * HALF + tc2 * 512: hf * HALF + (tc2 + 1) * 512],
                                start=(k == 0), stop=(k == DC - 1))
                em.emit("tensor", None, inc="tm")
                tmv = em.now("tm")
                em.emit("scalar", "activation", sil, pst[:, 0:1024], AF.Silu,
                        scale=par["w1s"][:, occ:occ + 1], waits=[("tm", tmv)], inc="sm")
                em.emit("vector", "scalar_tensor_tensor", TH[occ], pst[:, 1024:2048],
                        par["w2s"][:, occ:occ + 1], sil, op0=ALU.mult, op1=ALU.mult,
                        waits=[("sm", em.now("sm"))], inc="vm")
                tmarks[occ] = em.now("vm")
        assert em.now("tm") == tmbase + 4 * DC
        VM_TH = em.now("vm")

        def big44(src_tiles, dst_tiles, w_sg, scale_par, src_sem, src_val):
            tmbase2 = em.now("tm")
            gmarks = {}
            for oc in range(FC):
                w = [(src_sem, src_val)]
                if oc == 0:
                    w += [("tm", em.now("tm")), ("sm", em.now("sm")), ("vm", em.now("vm")),
                          ("st", em.now("st")), ("pc", PC_ALL)]
                else:
                    w.append(("tm", tmbase2 + oc - 1))
                wsem = "whA" if oc % 2 == 0 else "whB"
                em.emit("gpsimd", "dma_start",
                        out=hslab[oc % 2].rearrange("p (k n) -> p k n", k=FC),
                        in_=w_sg[oc], waits=w, inc=wsem, incv=16, serial=False)
                pso = 0 if oc % 2 == 0 else 1024
                tw = [(wsem, em.now(wsem)), (src_sem, src_val)]
                tw.append(("sm", gmarks[oc - 2] if oc >= 2 else em.now("sm")))
                first = True
                for k in range(FC):
                    for tc2 in range(2):
                        em.emit("tensor", "matmul",
                                psA[:, pso + tc2 * 512: pso + (tc2 + 1) * 512],
                                hslab[oc % 2][:, k * 128:(k + 1) * 128],
                                src_tiles[k][:, tc2 * 512:(tc2 + 1) * 512],
                                start=(k == 0), stop=(k == FC - 1), waits=tw if first else ())
                        first = False
                em.emit("tensor", None, inc="tm")
                tmv = em.now("tm")
                em.emit("scalar", "activation", dst_tiles[oc], psA[:, pso:pso + 1024],
                        AF.Gelu, scale=scale_par[:, oc:oc + 1], waits=[("tm", tmv)],
                        inc="sm")
                gmarks[oc] = em.now("sm")
            assert em.now("tm") == tmbase2 + FC
            return em.now("sm")

        if upto == "I1" and hf == 0:
            return _stop_here()
        sm_g1 = big44(TH, G1, h1tbf, par["h1s"], "vm", VM_TH)
        sm_g2 = big44(G1, G2, h2tbf, par["h2s"], "sm", sm_g1)

        if upto == "I2" and hf == 0:
            return _stop_here()
        xh = [V_(R_A + 32 + 4 * i, 1024, F32) for i in range(8)]
        for i in range(8):
            em.emit("sync", "dma_start", out=xh[i],
                    in_=x_hbm[hf * HALF + i * 128: hf * HALF + (i + 1) * 128, :],
                    waits=[("sm", sm_g2)] if i == 0 else (), inc="xl", incv=16)
        XL_XH = em.now("xl")
        out_tm = [V_(XNBF + 4 * i, 1024, F32) for i in range(8)]
        tmbase3 = em.now("tm")
        fmarks = {}
        for oc in range(DC):
            w = [("sm", sm_g2)]
            if oc == 0:
                w += [("tm", em.now("tm")), ("vm", em.now("vm")), ("pc", PC_ALL)]
            else:
                w.append(("tm", tmbase3 + 2 * oc - 1))
            wsem = "whA" if oc % 2 == 0 else "whB"
            em.emit("gpsimd", "dma_start",
                    out=hslab[oc % 2].rearrange("p (k n) -> p k n", k=FC),
                    in_=w3tbf[oc], waits=w, inc=wsem, incv=16, serial=False)
            tw = [(wsem, em.now(wsem)), ("vm", em.now("vm"))]
            first = True
            for k in range(FC):
                for tc2 in range(2):
                    em.emit("tensor", "matmul", psA[:, tc2 * 512:(tc2 + 1) * 512],
                            hslab[oc % 2][:, k * 128:(k + 1) * 128],
                            G2[k][:, tc2 * 512:(tc2 + 1) * 512],
                            start=(k == 0), stop=(k == FC - 1), waits=tw if first else ())
                    first = False
            em.emit("tensor", None, inc="tm")
            tmv = em.now("tm")
            rsem = "rlA" if oc % 2 == 0 else "rlB"
            w = [("sp", SP_ATTNB)]
            if oc >= 2:
                w.append(("vm", fmarks[oc - 2]))
            ab = abchb[oc % 2]
            em.emit("sync", "dma_start", out=ab, in_=attnb_sp[hf][oc], waits=w,
                    inc=rsem, incv=16, serial=False)
            rlv = em.now(rsem)
            em.emit("vector", "scalar_tensor_tensor", y3ch, psA[:, 0:1024],
                    par["w3s"][:, oc:oc + 1], ab, op0=ALU.mult, op1=ALU.add,
                    waits=[("tm", tmv), (rsem, rlv)], inc="vm")
            fmarks[oc] = em.now("vm")
            vmv = em.now("vm")
            for p in range(8):
                em.emit("tensor", "transpose",
                        psB[:, 1024 + p * 128: 1024 + (p + 1) * 128],
                        y3ch[:, p * 128:(p + 1) * 128], id_f,
                        waits=[("vm", vmv)] if p == 0 else ())
            em.emit("tensor", None, inc="tm")
            tmv2 = em.now("tm")
            for p in range(8):
                em.emit("vector", "tensor_add", out_tm[p][:, oc * 128:(oc + 1) * 128],
                        psB[:, 1024 + p * 128: 1024 + (p + 1) * 128],
                        xh[p][:, oc * 128:(oc + 1) * 128],
                        waits=[("tm", tmv2), ("xl", XL_XH)] if p == 0 else (),
                        inc="vm" if p == 7 else None)
        assert em.now("tm") == tmbase3 + 2 * DC
        VM_OUT = em.now("vm")
        for i in range(8):
            em.emit("sync", "dma_start",
                    out=out_hbm[hf * HALF + i * 128: hf * HALF + (i + 1) * 128, :],
                    in_=out_tm[i], waits=[("vm", VM_OUT)] if i == 0 else (),
                    inc="st", incv=16)

    em.ewait("sync", ("st", em.now("st")))
    em.ewait("gpsimd", ("st", em.now("st")))
    if em.now("tst"):
        em.ewait("gpsimd", ("tst", em.now("tst")))

    em.realize(nc)
    return nc


def prepare_in_maps(inputs):
    x = np.ascontiguousarray(np.asarray(inputs["L0_attn_x"], np.float32).reshape(B * N, D))

    def t2d(a):
        return np.ascontiguousarray(np.asarray(a, np.float32))

    def slabgrid(w):
        K, NN = w.shape
        return np.ascontiguousarray(
            w.reshape(K // 128, 128, NN // 128, 128).transpose(2, 1, 0, 3))

    def kchunks(w):
        K, NN = w.shape
        return np.ascontiguousarray(w.reshape(K // 128, 128, NN))

    def vec128(v):
        v = t2d(v).ravel()
        return np.ascontiguousarray(v.reshape(-1, 128).T)

    common = {
        "wq": t2d(inputs["L0_attn_Wq"]).reshape(D, D),
        "wk": t2d(inputs["L0_attn_Wk"]).reshape(D, D),
        "wv": t2d(inputs["L0_attn_Wv"]).reshape(D, D),
        "wo": t2d(inputs["L0_attn_Wo"]).reshape(D, D),
        "proj": t2d(inputs["L0_attn_proj"]),
        "w1": slabgrid(t2d(inputs["L0_attn_W1"])),
        "w2": kchunks(t2d(inputs["L0_attn_W2"])),
        "w1t": kchunks(t2d(inputs["L0_ffn_w1_w"]).T),
        "w2t": kchunks(t2d(inputs["L0_ffn_w2_w"]).T),
        "h1t": slabgrid(t2d(inputs["L0_ffn_h1_w"]).T),
        "h2t": slabgrid(t2d(inputs["L0_ffn_h2_w"]).T),
        "w3t": slabgrid(t2d(inputs["L0_ffn_w3_w"]).T),
        "b1v": vec128(inputs["L0_attn_b1"]), "b2v": vec128(inputs["L0_attn_b2"]),
        "ln1g": vec128(inputs["L0_attn_ln1_g"]), "ln1b": vec128(inputs["L0_attn_ln1_b"]),
        "ln2g": vec128(inputs["L0_attn_ln2_g"]), "ln2b": vec128(inputs["L0_attn_ln2_b"]),
        "w1s": vec128(inputs["L0_ffn_w1_s"]), "w2s": vec128(inputs["L0_ffn_w2_s"]),
        "h1s": vec128(inputs["L0_ffn_h1_s"]), "h2s": vec128(inputs["L0_ffn_h2_s"]),
        "w3s": vec128(inputs["L0_ffn_w3_s"]),
    }
    in_maps = []
    for c in range(NCORES):
        m = dict(common)
        m["x"] = np.ascontiguousarray(x[c * T:(c + 1) * T])
        in_maps.append(m)
    return in_maps


def kernel(**inputs):
    in_maps = prepare_in_maps(inputs)
    nc = build()
    res = run_bass_kernel_spmd(nc, in_maps, core_ids=list(range(NCORES)))
    out = np.concatenate([res.results[c]["out"] for c in range(NCORES)], axis=0)
    return out.reshape(B, N, D).astype(np.float32)


# version 07
# speedup vs baseline: 1.0589x; 1.0589x over previous
import numpy as np
import ml_dtypes
import concourse.bass as bass
import concourse.mybir as mybir
from concourse.bass_utils import run_bass_kernel_spmd

F32 = mybir.dt.float32
BF16 = mybir.dt.bfloat16
AF = mybir.ActivationFunctionType
ALU = mybir.AluOpType
AX = mybir.AxisListType

NCORES = 8
B = 2
N = 8192
T = 2048
NT = 16
D = 1024
DC = 8
H = 16
DH = 64
M = 64
DFF = 4096
FC = 32
HALF = 1024
DN = 2.0 ** -1.5
RATIO = 0.125
EPSK = 1e-6
EPS_LN0 = 1e-5
EPS_LN = 1e-6
NEG = -3.0e38

ENGINES = ("sync", "gpsimd", "scalar", "vector", "tensor")
KB = 1024


class Em:
    def __init__(self):
        self.steps = {e: [] for e in ENGINES}
        self.sems = {}
        self._maxwait = {e: {} for e in ENGINES}

    def now(self, sem):
        return self.sems.setdefault(sem, 0)

    def emit(self, eng, method, *args, waits=(), inc=None, incv=1, serial=True, **kw):
        w = []
        for (s, v) in waits:
            self.sems.setdefault(s, 0)
            assert v <= self.sems[s], (eng, method, s, v, self.sems[s])
            if v > 0 and v > self._maxwait[eng].get(s, -1):
                w.append((s, v))
                self._maxwait[eng][s] = v
        if method == "sem_inc":
            self.steps[eng].append((method, args, kw, tuple(w), None, 0))
            self.sems[args[0]] = self.sems.setdefault(args[0], 0) + args[1]
            return
        if method is None and inc is not None:
            for i in range(len(self.steps[eng]) - 1, -1, -1):
                m2, a2, k2, w2, i2, v2 = self.steps[eng][i]
                if m2 is not None:
                    assert i2 is None, ("double marker on", eng, m2, i2, inc)
                    self.steps[eng][i] = (m2, a2, k2, w2, inc, incv)
                    break
            else:
                raise AssertionError("marker with no prior instruction")
            if w:
                self.steps[eng].append((None, (), {}, tuple(w), None, 0))
            self.sems[inc] = self.sems.setdefault(inc, 0) + incv
            return
        self.steps[eng].append((method, args, kw, tuple(w), inc, incv))
        if inc is not None:
            self.sems[inc] = self.sems.setdefault(inc, 0) + incv
        if method == "dma_start" and inc is not None and serial:
            self.ewait(eng, (inc, self.sems[inc]))

    def ewait(self, eng, *waits):
        self.emit(eng, None, waits=waits)

    def fence(self, eng, sem):
        for i in range(len(self.steps[eng]) - 1, -1, -1):
            m2, a2, k2, w2, i2, v2 = self.steps[eng][i]
            if m2 is not None:
                if i2 is None:
                    self.steps[eng][i] = (m2, a2, k2, w2, sem, 1)
                    self.sems[sem] = self.sems.setdefault(sem, 0) + 1
                    self.ewait(eng, (sem, self.now(sem)))
                else:
                    self.ewait(eng, (i2, self.now(i2)))
                return
        raise AssertionError("fence with no prior instruction")

    def allw(self):
        return [("tm", self.now("tm")), ("vm", self.now("vm")), ("sm", self.now("sm"))]

    def check(self):
        idx = {e: 0 for e in ENGINES}
        cnt = {s: 0 for s in self.sems}
        progress = True
        while progress:
            progress = False
            for e in ENGINES:
                while idx[e] < len(self.steps[e]):
                    method, args, kw, waits, inc, incv = self.steps[e][idx[e]]
                    if all(cnt[s] >= v for (s, v) in waits):
                        if method == "sem_inc":
                            cnt[args[0]] += args[1]
                        if inc is not None:
                            cnt[inc] += incv
                        idx[e] += 1
                        progress = True
                    else:
                        break
        stuck = {e: idx[e] for e in ENGINES if idx[e] < len(self.steps[e])}
        if stuck:
            for e, i in stuck.items():
                method, args, kw, waits, inc, incv = self.steps[e][i]
                unmet = [(s, v, cnt[s]) for (s, v) in waits if cnt[s] < v]
                print(f"DEADLOCK {e} step {i}/{len(self.steps[e])} {method} unmet={unmet}")
            raise RuntimeError("static deadlock detected")
        for s, v in cnt.items():
            assert v == self.sems[s], (s, v, self.sems[s])

    def realize(self, nc):
        self.check()
        handles = {name: nc.alloc_semaphore(name) for name in self.sems}
        with nc.Block() as block:
            for engname in ENGINES:
                def make(engname=engname):
                    def f(eng):
                        for (method, args, kw, waits, inc, incv) in self.steps[engname]:
                            for (s, v) in waits:
                                eng.wait_ge(handles[s], v)
                            if method is None:
                                continue
                            if method == "sem_inc":
                                eng.sem_inc(handles[args[0]], args[1])
                                continue
                            ins = getattr(eng, method)(*args, **kw)
                            if inc is not None:
                                ins.then_inc(handles[inc], incv)
                    return f
                getattr(block, engname)(make())


class Arena:
    def __init__(self, nc, cap_bytes):
        self.cap = cap_bytes
        self.t = nc.alloc_sbuf_tensor("arena", [128, cap_bytes], mybir.dt.uint8)

    def view(self, off_kb, cols, dt, parts=128):
        esz = 4 if dt == F32 else 2
        off = int(off_kb * KB)
        assert off % 32 == 0, off_kb
        assert off + cols * esz <= self.cap, (off_kb, cols, dt)
        return self.t[0:parts, off:off + cols * esz].bitcast(dt)


def build(taps=(), upto=None, nta=NT):
    taps = set(taps)
    nc = bass.Bass("TRN2", target_bir_lowering=False, detect_race_conditions=False)
    em = Em()

    x_hbm = nc.dram_tensor("x", [T, D], F32, kind="ExternalInput").ap()
    wq_hbm = nc.dram_tensor("wq", [D, D], F32, kind="ExternalInput").ap()
    wk_hbm = nc.dram_tensor("wk", [D, D], F32, kind="ExternalInput").ap()
    wv_hbm = nc.dram_tensor("wv", [D, D], F32, kind="ExternalInput").ap()
    wo_hbm = nc.dram_tensor("wo", [D, D], F32, kind="ExternalInput").ap()
    proj_hbm = nc.dram_tensor("proj", [M, DH], F32, kind="ExternalInput").ap()
    w1_hbm = nc.dram_tensor("w1", [FC, 128, DC, 128], F32, kind="ExternalInput").ap()
    w2_hbm = nc.dram_tensor("w2", [FC, 128, D], F32, kind="ExternalInput").ap()
    w1t_hbm = nc.dram_tensor("w1t", [DC, 128, DFF], F32, kind="ExternalInput").ap()
    w2t_hbm = nc.dram_tensor("w2t", [DC, 128, DFF], F32, kind="ExternalInput").ap()
    h1t_hbm = nc.dram_tensor("h1t", [FC, 128, FC, 128], F32, kind="ExternalInput").ap()
    h2t_hbm = nc.dram_tensor("h2t", [FC, 128, FC, 128], F32, kind="ExternalInput").ap()
    w3t_hbm = nc.dram_tensor("w3t", [DC, 128, FC, 128], F32, kind="ExternalInput").ap()
    vecs_hbm = {}
    for nm, ncols in [("b1v", FC), ("w1s", FC), ("w2s", FC), ("h1s", FC), ("h2s", FC),
                      ("b2v", DC), ("ln1g", DC), ("ln1b", DC), ("ln2g", DC), ("ln2b", DC),
                      ("w3s", DC)]:
        vecs_hbm[nm] = nc.dram_tensor(nm, [128, ncols], F32, kind="ExternalInput").ap()

    out_hbm = nc.dram_tensor("out", [T, D], F32, kind="ExternalOutput").ap()

    xnf_sp = nc.dram_tensor("xnf_sp", [DC, 128, T], F32).ap()
    xnb_sp = nc.dram_tensor("xnb_sp", [DC, 128, T], BF16).ap()
    sum1_sp = nc.dram_tensor("sum1_sp", [DC, 128, T], F32).ap()
    out1f_sp = nc.dram_tensor("out1f_sp", [DC, 128, T], F32).ap()
    sum2_sp = nc.dram_tensor("sum2_sp", [2, DC, 128, HALF], F32).ap()
    attnb_sp = nc.dram_tensor("attnb_sp", [2, DC, 128, HALF], F32).ap()

    w1bf = nc.dram_tensor("w1bf", [FC, 128, DC, 128], BF16).ap()
    w2bf = nc.dram_tensor("w2bf", [FC, 128, D], BF16).ap()
    w1tbf = nc.dram_tensor("w1tbf", [DC, 128, DFF], BF16).ap()
    w2tbf = nc.dram_tensor("w2tbf", [DC, 128, DFF], BF16).ap()
    h1tbf = nc.dram_tensor("h1tbf", [FC, 128, FC, 128], BF16).ap()
    h2tbf = nc.dram_tensor("h2tbf", [FC, 128, FC, 128], BF16).ap()
    w3tbf = nc.dram_tensor("w3tbf", [DC, 128, FC, 128], BF16).ap()
    ccmax_in = nc.dram_tensor("ccmax_in", [1, 8], F32)
    ccmax_out = nc.dram_tensor("ccmax_out", [1, 8], F32)
    cckv_in = nc.dram_tensor("cckv_in", [128, 520], F32)
    cckv_out = nc.dram_tensor("cckv_out", [128, 520], F32)

    id_f32_d = nc.inline_tensor(np.eye(128, dtype=np.float32), name="c_idf")
    id_bf_d = nc.inline_tensor(np.eye(128, dtype=ml_dtypes.bfloat16).view(np.uint16),
                               name="c_idb")
    bd_np = np.zeros((128, 128), np.float32)
    for kr in range(8):
        bd_np[0:64, kr * 16 + 2 * kr] = 1.0 / 16.0
        bd_np[64:128, kr * 16 + 2 * kr + 1] = 1.0 / 16.0
    bd_d = nc.inline_tensor(bd_np, name="c_bd")
    e2_np = np.zeros((16, 8 * 128), np.float32)
    for c in range(8):
        e2_np[2 * c, c * 128:c * 128 + 64] = 1.0
        e2_np[2 * c + 1, c * 128 + 64:c * 128 + 128] = 1.0
    e2_d = nc.inline_tensor(e2_np, name="c_e2")

    AR = Arena(nc, 206 * KB)
    V_ = AR.view
    id_f = V_(0.0, 128, F32)
    id_b = V_(0.5, 128, BF16)
    bd_c = V_(0.75, 128, F32)
    pd2 = V_(1.25, 128, BF16)
    kvbd = V_(1.5, 1024, BF16)
    ksbd = V_(3.5, 128, BF16)
    kvr = V_(3.75, 520, F32)
    ones_col = V_(5.8125, 1, F32)
    ones_row = V_(5.875, 128, F32)[0:1, :]
    mx128 = V_(6.375, 1, F32)
    mxloc = V_(6.4375, 16, F32)
    mxst = V_(6.5, 16, F32)
    e2_c = V_(6.5625, 1024, F32, parts=16)
    diagq_tm = V_(10.5625, 256, F32)
    diagk_tm = V_(11.5625, 256, F32)
    par = {}
    poff = 12.5625
    for nm, ncols in [("b1v", FC), ("negb1", FC), ("b1m1", FC), ("w1s", FC), ("w2s", FC),
                      ("h1s", FC), ("h2s", FC), ("b2v", DC), ("ln1g", DC), ("ln1b", DC),
                      ("ln2g", DC), ("ln2b", DC), ("w3s", DC)]:
        par[nm] = V_(poff, ncols, F32)
        poff += ncols * 4 / KB
    assert poff <= 14.0, poff

    XNBF = 14.0
    SCR = 46.0
    R_A = 70.0
    R_B = 134.0

    xn_bf = [V_(XNBF + 4 * c, T, BF16) for c in range(DC)]

    psA_t = nc.alloc_psum_tensor("psA", [128, 2048], F32)
    psB_t = nc.alloc_psum_tensor("psB", [128, 2048], F32)
    psA = psA_t[:, :]
    psB = psB_t[:, :]
    psA_bf = psA.bitcast(BF16)


    def _stop_here():
        em.emit("sync", "dma_start", out=out_hbm[0:128, :],
                in_=AR.t[0:128, 0:4096].bitcast(F32),
                waits=em.allw() + [("sp", em.now("sp"))], inc="st", incv=16)
        em.ewait("sync", ("st", em.now("st")))
        em.ewait("gpsimd", ("st", em.now("st")))
        if em.now("tst"):
            em.ewait("gpsimd", ("tst", em.now("tst")))
        em.realize(nc)
        return nc

    em.emit("sync", "dma_start", out=id_f, in_=id_f32_d.ap(), inc="xl", incv=16)
    em.emit("sync", "dma_start", out=id_b, in_=id_bf_d.ap().bitcast(BF16), inc="xl", incv=16)
    em.emit("sync", "dma_start", out=bd_c, in_=bd_d.ap(), inc="xl", incv=16)
    em.emit("sync", "dma_start", out=e2_c, in_=e2_d.ap(), inc="xl", incv=16)
    for nm in ["b1v", "w1s", "w2s", "h1s", "h2s", "b2v", "ln1g", "ln1b", "ln2g", "ln2b", "w3s"]:
        em.emit("sync", "dma_start", out=par[nm], in_=vecs_hbm[nm], inc="xl", incv=16)
    XL_CONSTS = em.now("xl")
    em.emit("vector", "tensor_scalar_add", par["b1m1"], par["b1v"], -1.0,
            waits=[("xl", XL_CONSTS)], inc="vm")
    em.emit("vector", "tensor_scalar_mul", par["negb1"], par["b1v"], -1.0, inc="vm")
    em.emit("vector", "memset", ones_col, 1.0, inc="vm")
    em.emit("vector", "memset", ones_row, 1.0, inc="vm")

    wv_sb = V_(R_B, 8 * 1024, BF16)
    wq_sb = V_(R_B + 16, 8 * 1024, BF16)
    wk_sb = V_(R_B + 32, 8 * 1024, BF16)
    proj_sb = V_(R_B + 48, 64, F32, parts=64)
    em.emit("gpsimd", "dma_start", out=wv_sb.rearrange("p (k n) -> p k n", k=8),
            in_=wv_hbm.rearrange("(k p) n -> p k n", p=128), inc="wl", incv=16)
    em.emit("gpsimd", "dma_start", out=wq_sb.rearrange("p (k n) -> p k n", k=8),
            in_=wq_hbm.rearrange("(k p) n -> p k n", p=128), inc="wl", incv=16)
    WL_WQ = em.now("wl")
    em.emit("gpsimd", "dma_start", out=wk_sb.rearrange("p (k n) -> p k n", k=8),
            in_=wk_hbm.rearrange("(k p) n -> p k n", p=128), inc="wl", incv=16)
    WL_WK = em.now("wl")
    WL_WV = 16
    em.emit("gpsimd", "dma_start", out=proj_sb, in_=proj_hbm, inc="wl", incv=16)
    WL_PROJ = em.now("wl")
    for srcb, dstb, nch in [(w1t_hbm, w1tbf, 2), (w2t_hbm, w2tbf, 2),
                            (h1t_hbm, h1tbf, 8), (h2t_hbm, h2tbf, 8)]:
        n0 = srcb.shape[0]
        step = n0 // nch
        for i in range(nch):
            em.emit("gpsimd", "dma_start", out=dstb[i * step:(i + 1) * step],
                    in_=srcb[i * step:(i + 1) * step], inc="pc", incv=16,
                    serial=False)
    PC_ALL = em.now("pc")

    em.emit("tensor", "transpose", psB[0:64, 0:64], proj_sb, id_f[0:64, 0:64],
            waits=[("wl", WL_PROJ), ("xl", XL_CONSTS)], inc="tm")
    TM_PROJ = em.now("tm")
    em.emit("vector", "memset", pd2, 0.0, inc="vm")
    em.fence("vector", "vm")
    em.emit("vector", "tensor_scalar_mul", pd2[0:64, 0:64], psB[0:64, 0:64], DN,
            waits=[("tm", TM_PROJ)], inc="vm")
    em.emit("vector", "tensor_scalar_mul", pd2[64:128, 64:128], psB[0:64, 0:64], DN, inc="vm")
    VM_PD2 = em.now("vm")

    xbuf = [V_(SCR, 1024, F32), V_(SCR + 4, 1024, F32)]
    xsq = V_(SCR + 8, 1024, F32)
    xnt = V_(SCR + 12, 1024, F32)
    srow = [V_(SCR + 16 + 0.0625 * i, 1, F32) for i in range(5)]
    xn_f32 = [V_(R_A + 8 * c, T, F32) for c in range(DC)]

    a_mark = []
    for t in range(nta):
        buf = xbuf[t % 2]
        w = []
        if t >= 2:
            w += [("vm", a_mark[t - 2][1]), ("sm", a_mark[t - 2][2])]
        em.emit("sync", "dma_start", out=buf, in_=x_hbm[t * 128:(t + 1) * 128, :],
                waits=w, inc="xl", incv=16)
        xlv = em.now("xl")
        em.emit("vector", "scalar_tensor_tensor", xsq, buf, 1.0, buf,
                op0=ALU.mult, op1=ALU.mult, accum_out=srow[2],
                waits=[("xl", xlv)], inc="vm")
        em.emit("vector", "tensor_reduce", srow[0], buf, axis=AX.X, op=ALU.add, inc="vm")
        em.fence("vector", "vm")
        em.emit("vector", "tensor_scalar_mul", srow[1], srow[0], 1.0 / D, inc="vm")
        em.emit("vector", "tensor_scalar", srow[2], srow[2], 1.0 / D, EPS_LN0,
                op0=ALU.mult, op1=ALU.add, inc="vm")
        em.fence("vector", "vm")
        em.emit("vector", "scalar_tensor_tensor", srow[3], srow[1], -1.0, srow[1],
                op0=ALU.mult, op1=ALU.mult, inc="vm")
        em.fence("vector", "vm")
        em.emit("vector", "tensor_add", srow[2], srow[2], srow[3], inc="vm")
        em.fence("vector", "vm")
        em.emit("scalar", "activation", srow[3], srow[2], AF.Sqrt,
                waits=[("vm", em.now("vm"))], inc="sm")
        em.emit("vector", "reciprocal", srow[4], srow[3],
                waits=[("sm", em.now("sm"))], inc="vm")
        em.fence("vector", "vm")
        em.emit("vector", "scalar_tensor_tensor", srow[3], srow[1], -1.0, srow[4],
                op0=ALU.mult, op1=ALU.mult, inc="vm")
        em.fence("vector", "vm")
        em.emit("scalar", "activation", xnt, buf, AF.Identity,
                scale=srow[4][:, 0:1], bias=srow[3][:, 0:1],
                waits=[("vm", em.now("vm"))], inc="sm")
        em.emit("vector", None, waits=[("sm", em.now("sm"))])
        em.emit("vector", "sem_inc", "vm", 1)
        vm_xnt = em.now("vm")
        if "a0" in taps and t == 0:
            tap_a0x = nc.dram_tensor("tap_a0x", [128, 1024], F32, kind="ExternalOutput").ap()
            tap_a0n = nc.dram_tensor("tap_a0n", [128, 1024], F32, kind="ExternalOutput").ap()
            tap_a0s = nc.dram_tensor("tap_a0s", [128, 8], F32, kind="ExternalOutput").ap()
            em.emit("sync", "dma_start", out=tap_a0x, in_=buf,
                    waits=[("vm", vm_xnt)], inc="st", incv=16)
            em.emit("sync", "dma_start", out=tap_a0n, in_=xnt, inc="st", incv=16)
            for si in range(5):
                em.emit("sync", "dma_start", out=tap_a0s[:, si:si + 1], in_=srow[si],
                        allow_slow_non_contiguous=True, inc="st", incv=16)
        pso = 0 if t % 2 == 0 else 1024
        tw = [("sm", em.now("sm"))]
        if t >= 2:
            tw += [("vm", a_mark[t - 2][1]), ("sm", a_mark[t - 2][2])]
        for c in range(DC):
            em.emit("tensor", "transpose", psA[:, pso + c * 128: pso + (c + 1) * 128],
                    xnt[:, c * 128:(c + 1) * 128], id_f,
                    waits=tw if c == 0 else ())
        em.emit("tensor", None, inc="tm")
        tmv = em.now("tm")
        for c in range(4):
            em.emit("vector", "tensor_copy", xn_f32[c][:, t * 128:(t + 1) * 128],
                    psA[:, pso + c * 128: pso + (c + 1) * 128],
                    waits=[("tm", tmv)] if c == 0 else (),
                    inc="vm" if c == 3 else None)
        for c in range(4, DC):
            em.emit("scalar", "activation", xn_f32[c][:, t * 128:(t + 1) * 128],
                    psA[:, pso + c * 128: pso + (c + 1) * 128], AF.Copy,
                    waits=[("tm", tmv)] if c == 4 else (),
                    inc="sm" if c == DC - 1 else None)
        vm_ev = em.now("vm")
        sm_ev = em.now("sm")
        for c in range(DC):
            eng2 = "vector" if c >= 4 else "scalar"
            if eng2 == "scalar":
                em.emit("scalar", "activation", xn_bf[c][:, t * 128:(t + 1) * 128],
                        xn_f32[c][:, t * 128:(t + 1) * 128], AF.Copy,
                        waits=[("vm", vm_ev)] if c == 0 else (),
                        inc="sm" if c == DC - 1 else None)
            else:
                em.emit("vector", "tensor_copy", xn_bf[c][:, t * 128:(t + 1) * 128],
                        xn_f32[c][:, t * 128:(t + 1) * 128],
                        waits=[("sm", sm_ev)] if c == 4 else (),
                        inc="vm" if c == DC - 1 else None)
        a_mark.append((tmv, em.now("vm"), em.now("sm")))

    VM_A = em.now("vm")
    SM_A = em.now("sm")
    for c in range(DC):
        em.emit("sync", "dma_start", out=xnf_sp[c], in_=xn_f32[c],
                waits=[("vm", VM_A)] if c == 0 else (), inc="sp", incv=16)
        em.emit("sync", "dma_start", out=xnb_sp[c], in_=xn_bf[c],
                waits=[("sm", SM_A)] if c == 0 else (), inc="sp", incv=16)
    SP_XN = em.now("sp")

    if "xn" in taps:
        tap_xn = nc.dram_tensor("tap_xn", [DC, 128, T], F32, kind="ExternalOutput").ap()
        for c in range(DC):
            em.emit("sync", "dma_start", out=tap_xn[c], in_=xn_f32[c],
                    waits=[("vm", VM_A)], inc="st", incv=16)

    if upto == "A":
        return _stop_here()

    q_bf = [V_(R_A + 4 * c, T, BF16) for c in range(DC)]
    k_bf = [V_(R_A + 32 + 4 * c, T, BF16) for c in range(DC)]
    qsq = V_(SCR, 2048, F32)
    diag_fm = V_(SCR + 8, 2048, F32, parts=16)

    def emit_qk(w_sb, wlv, dst, diag_dst_tm):
        marks = {}
        g = 0
        for oc in range(DC):
            for th in range(2):
                pso = 0 if g % 2 == 0 else 1024
                tw = [("wl", wlv)]
                if g >= 2:
                    tw += [("vm", marks[g - 2][0]), ("sm", marks[g - 2][1])]
                else:
                    tw += [("sm", em.now("sm")), ("vm", em.now("vm"))]
                first = True
                for k in range(DC):
                    for tc in range(2):
                        tcg = th * 2 + tc
                        em.emit("tensor", "matmul",
                                psA[:, pso + tc * 512: pso + (tc + 1) * 512],
                                w_sb[:, k * 1024 + oc * 128: k * 1024 + (oc + 1) * 128],
                                xn_bf[k][:, tcg * 512:(tcg + 1) * 512],
                                start=(k == 0), stop=(k == DC - 1),
                                waits=tw if first else ())
                        first = False
                em.emit("tensor", None, inc="tm")
                tmv = em.now("tm")
                em.emit("vector", "tensor_copy", dst[oc][:, th * 1024:(th + 1) * 1024],
                        psA[:, pso:pso + 1024],
                        waits=[("tm", tmv), ("sp", SP_XN)], inc="vm")
                em.emit("scalar", "activation", qsq[:, th * 1024:(th + 1) * 1024],
                        psA[:, pso:pso + 1024], AF.Square,
                        waits=[("tm", tmv), ("vm", em.now("vm"))], inc="sm")
                smv = em.now("sm")
                for tc in range(2):
                    tcg = th * 2 + tc
                    em.emit("tensor", "matmul", psB[0:16, tcg * 512:(tcg + 1) * 512],
                            bd_c[:, oc * 16:(oc + 1) * 16],
                            qsq[:, tcg * 512:(tcg + 1) * 512],
                            start=(oc == 0), stop=(oc == DC - 1),
                            waits=([("sm", smv)] + ([("vm", em.now("vm"))] if g <= 1 else []))
                            if tc == 0 else ())
                em.emit("tensor", None, inc="tm")
                marks[g] = (em.now("vm"), em.now("sm"))
                g += 1
        tmv = em.now("tm")
        em.emit("vector", "tensor_copy", diag_fm, psB[0:16, :], waits=[("tm", tmv)], inc="vm")
        vmv = em.now("vm")
        for t in range(NT):
            tw = [("vm", vmv)] if t == 0 else []
            if t >= 4:
                tw.append(("vm", em.now("vm")))
            em.emit("tensor", "transpose",
                    psB[:, (t % 4) * 512: (t % 4) * 512 + 16],
                    diag_fm[:, t * 128:(t + 1) * 128], id_f[0:16, 0:16], waits=tw, inc="tm")
            em.emit("vector", "tensor_copy", diag_dst_tm[:, t * 16:(t + 1) * 16],
                    psB[:, (t % 4) * 512: (t % 4) * 512 + 16],
                    waits=[("tm", em.now("tm"))], inc="vm")

    emit_qk(wk_sb, WL_WK, k_bf, diagk_tm)
    em.emit("vector", "memset", mxloc[:, 0:1], NEG, inc="vm")
    for t in range(NT):
        pso = 0 if t % 2 == 0 else 1024
        tw = [("vm", em.now("vm")), ("vm", VM_PD2)]
        first = True
        for pr in range(8):
            em.emit("tensor", "matmul", psA[:, pso + pr * 128: pso + (pr + 1) * 128],
                    k_bf[pr][:, t * 128:(t + 1) * 128], pd2,
                    start=True, stop=True, waits=tw if first else ())
            first = False
        em.emit("tensor", None, inc="tm")
        tmv = em.now("tm")
        em.emit("vector", "tensor_reduce", mxloc[:, 1:2], psA[:, pso:pso + 1024],
                axis=AX.X, op=ALU.max, waits=[("tm", tmv)], inc="vm")
        em.fence("vector", "vm")
        em.emit("vector", "tensor_max", mxloc[:, 0:1], mxloc[:, 0:1], mxloc[:, 1:2], inc="vm")
        em.fence("vector", "vm")
    em.emit("tensor", "transpose", psB[0:1, 1024:1152], mxloc[:, 0:1], id_f,
            waits=[("vm", em.now("vm"))], inc="tm")
    em.emit("vector", "memset", mxst[0:1, 0:8], NEG,
            waits=[("tm", em.now("tm"))], inc="vm")
    em.fence("vector", "vm")
    em.emit("vector", "tensor_reduce", mxst[0:1, 0:1], psB[0:1, 1024:1152], axis=AX.X,
            op=ALU.max, inc="vm")
    em.fence("vector", "vm")
    em.emit("gpsimd", "dma_start", out=ccmax_in.ap(), in_=mxst[0:1, 0:8],
            waits=[("vm", em.now("vm"))], inc="cl", incv=16)
    em.ewait("gpsimd", ("cl", em.now("cl")))
    em.emit("gpsimd", "collective_compute", "AllReduce", ALU.max,
            replica_groups=[list(range(NCORES))],
            ins=[ccmax_in.ap().opt()], outs=[ccmax_out.ap().opt()], inc="cc")
    em.ewait("gpsimd", ("cc", em.now("cc")))
    em.emit("gpsimd", "dma_start", out=mxst[0:1, 8:16], in_=ccmax_out.ap(), inc="cl", incv=16)
    CL_MX = em.now("cl")

    if upto == "C1":
        return _stop_here()

    emit_qk(wq_sb, WL_WQ, q_bf, diagq_tm)

    if "q" in taps:
        tap_q = nc.dram_tensor("tap_q", [DC, 128, T], F32, kind="ExternalOutput").ap()
        for c in range(DC):
            em.emit("gpsimd", "dma_start", out=tap_q[c], in_=q_bf[c],
                    waits=[("vm", em.now("vm"))], inc="tst", incv=16)
    if "diag" in taps:
        tap_dg = nc.dram_tensor("tap_diag", [2, 128, 256], F32, kind="ExternalOutput").ap()
        em.emit("sync", "dma_start", out=tap_dg[0], in_=diagq_tm,
                waits=[("vm", em.now("vm"))], inc="st", incv=16)
        em.emit("sync", "dma_start", out=tap_dg[1], in_=diagk_tm,
                waits=[("vm", em.now("vm"))], inc="st", incv=16)

    vaug = [V_(R_B + 16 + 2.125 * t, 1040, BF16) for t in range(NT)]
    VM_B_QK = em.now("vm")
    for t in range(NT):
        em.emit("vector", "memset", vaug[t], 1.0,
                waits=[("vm", VM_B_QK), ("tm", em.now("tm"))] if t == 0 else (),
                inc="vm" if t == NT - 1 else None)
    em.fence("vector", "vm")
    VM_VONES = em.now("vm")
    for t in range(NT):
        for nh in range(2):
            g = 2 * t + nh
            pso = (g % 4) * 512
            tw = [("wl", WL_WV), ("vm", VM_VONES), ("vm", em.now("vm"))]
            first = True
            for k in range(DC):
                em.emit("tensor", "matmul", psA[:, pso:pso + 512],
                        xn_bf[k][:, t * 128:(t + 1) * 128],
                        wv_sb[:, k * 1024 + nh * 512: k * 1024 + (nh + 1) * 512],
                        start=(k == 0), stop=(k == DC - 1), waits=tw if first else ())
                first = False
            em.emit("tensor", None, inc="tm")
            tmv = em.now("tm")
            dst = vaug[t].rearrange("p (h e) -> p h e", h=16)[:, nh * 8:(nh + 1) * 8, 0:64]
            src = psA[:, pso:pso + 512].rearrange("p (h d) -> p h d", h=8)
            em.emit("vector", "tensor_copy", dst, src, waits=[("tm", tmv)], inc="vm")
    VM_VAUG = em.now("vm")

    if upto == "B":
        return _stop_here()

    qf_tm = [V_(R_B + 2 * t, 1024, BF16) if t < 8 else V_(R_B + 50 + 2 * (t - 8), 1024, BF16)
             for t in range(NT)]
    stage = V_(SCR, 512, F32)
    biasr = V_(SCR + 2, 16, F32)
    stageE = V_(SCR + 4, 1024, F32)
    kf_t = V_(SCR + 8, 1024, BF16)

    em.emit("tensor", "matmul", psB[:, 1028:1029], ones_row, mxst[0:1, 8:9],
            start=True, stop=True, waits=[("cl", CL_MX)], inc="tm")
    em.emit("vector", "tensor_copy", mx128, psB[:, 1028:1029],
            waits=[("tm", em.now("tm"))], inc="vm")
    VM_MX128 = em.now("vm")

    ekm = {}
    for t in range(NT):
        pso = 0 if t % 2 == 0 else 1024
        if t >= 2:
            tw = [("vm", ekm[t - 2][0]), ("sm", ekm[t - 2][1])]
        else:
            tw = [("vm", em.now("vm")), ("sm", em.now("sm"))]
        first = True
        for pr in range(8):
            em.emit("tensor", "matmul", psA[:, pso + pr * 128: pso + (pr + 1) * 128],
                    k_bf[pr][:, t * 128:(t + 1) * 128], pd2,
                    start=True, stop=True, waits=tw if first else ())
            first = False
        em.emit("tensor", None, inc="tm")
        tmv = em.now("tm")
        em.emit("vector", "tensor_scalar", biasr, diagk_tm[:, t * 16:(t + 1) * 16],
                mx128[:, 0:1], -1.0, op0=ALU.add, op1=ALU.mult,
                waits=[("tm", tmv), ("vm", VM_MX128)], inc="vm")
        em.fence("vector", "vm")
        vm_b = em.now("vm")
        for h in range(8, H):
            em.emit("scalar", "activation", stageE[:, h * 64:(h + 1) * 64],
                    psA[:, pso + h * 64: pso + h * 64 + 64], AF.Exp,
                    bias=biasr[:, h:h + 1],
                    waits=[("vm", vm_b), ("tm", tmv)] if h == 8 else (),
                    inc="sm" if h == H - 1 else None)
        for h in range(8):
            em.emit("vector", "tensor_scalar_add", stage[:, h * 64:(h + 1) * 64],
                    psA[:, pso + h * 64: pso + h * 64 + 64], biasr[:, h:h + 1],
                    inc="vm" if h == 7 else None)
        em.emit("scalar", "activation", stageE[:, 0:512], stage, AF.Exp,
                waits=[("vm", em.now("vm"))], inc="sm")
        em.emit("vector", "tensor_scalar", kf_t, stageE, EPSK, RATIO,
                op0=ALU.add, op1=ALU.mult, waits=[("sm", em.now("sm"))], inc="vm")
        vm_kf = em.now("vm")
        first = True
        for h in range(H):
            ph, cg = h % 2, h // 2
            colbase = (0 if cg < 4 else 512) + (cg % 4) * 65
            em.emit("tensor", "matmul",
                    psB[ph * 64:(ph + 1) * 64, colbase:colbase + 65],
                    kf_t[:, h * 64:(h + 1) * 64],
                    vaug[t][:, h * 65:(h + 1) * 65],
                    start=(t == 0 and h in (0, 1, 8, 9)),
                    stop=(t == NT - 1 and h in (6, 7, 14, 15)),
                    skip_group_check=True,
                    waits=[("vm", vm_kf), ("vm", VM_VAUG)] if first else ())
            first = False
        em.emit("tensor", None, inc="tm")
        ekm[t] = (em.now("vm"), em.now("sm"))
    TM_KV = em.now("tm")
    em.emit("vector", "tensor_copy", kvr[:, 0:260], psB[:, 0:260],
            waits=[("tm", TM_KV)], inc="vm")
    em.emit("vector", "tensor_copy", kvr[:, 260:520], psB[:, 512:772], inc="vm")
    em.emit("gpsimd", "dma_start", out=cckv_in.ap(), in_=kvr,
            waits=[("vm", em.now("vm"))], inc="cl", incv=16)
    em.ewait("gpsimd", ("cl", em.now("cl")))
    em.emit("gpsimd", "collective_compute", "AllReduce", ALU.add,
            replica_groups=[[0, 1, 2, 3], [4, 5, 6, 7]],
            ins=[cckv_in.ap().opt()], outs=[cckv_out.ap().opt()], inc="cc")
    em.ewait("gpsimd", ("cc", em.now("cc")))
    em.emit("gpsimd", "dma_start", out=kvr, in_=cckv_out.ap(), inc="cl", incv=16)
    CL_KV = em.now("cl")

    if upto == "E":
        return _stop_here()

    ddm = {}
    for t in range(NT):
        pso = 0 if t % 2 == 0 else 1024
        if t >= 2:
            tw = [("vm", ddm[t - 2][0]), ("sm", ddm[t - 2][1])]
        else:
            tw = [("vm", em.now("vm")), ("sm", em.now("sm"))]
        first = True
        for pr in range(8):
            em.emit("tensor", "matmul", psA[:, pso + pr * 128: pso + (pr + 1) * 128],
                    q_bf[pr][:, t * 128:(t + 1) * 128], pd2,
                    start=True, stop=True, waits=tw if first else ())
            first = False
        em.emit("tensor", None, inc="tm")
        tmv = em.now("tm")
        dd3 = psA[:, pso:pso + 1024].rearrange("p (h m) -> p h m", h=H)
        em.emit("vector", "tensor_reduce", mxloc, dd3, axis=AX.X, op=ALU.max,
                waits=[("tm", tmv)], inc="vm")
        em.fence("vector", "vm")
        em.emit("vector", "scalar_tensor_tensor", biasr, mxloc, -1.0,
                diagq_tm[:, t * 16:(t + 1) * 16], op0=ALU.mult, op1=ALU.subtract, inc="vm")
        em.fence("vector", "vm")
        vm_b = em.now("vm")
        for h in range(8, H):
            em.emit("scalar", "activation", stageE[:, h * 64:(h + 1) * 64],
                    psA[:, pso + h * 64: pso + h * 64 + 64], AF.Exp,
                    bias=biasr[:, h:h + 1],
                    waits=[("vm", vm_b), ("tm", tmv)] if h == 8 else (),
                    inc="sm" if h == H - 1 else None)
        for h in range(8):
            em.emit("vector", "tensor_scalar_add", stage[:, h * 64:(h + 1) * 64],
                    psA[:, pso + h * 64: pso + h * 64 + 64], biasr[:, h:h + 1],
                    inc="vm" if h == 7 else None)
        em.emit("scalar", "activation", stageE[:, 0:512], stage, AF.Exp,
                waits=[("vm", em.now("vm"))], inc="sm")
        em.emit("vector", "tensor_scalar", qf_tm[t], stageE, EPSK, RATIO,
                op0=ALU.add, op1=ALU.mult, waits=[("sm", em.now("sm"))], inc="vm")
        ddm[t] = (em.now("vm"), em.now("sm"))
    VM_QF = em.now("vm")

    if upto == "C2":
        return _stop_here()

    qf_fm = [V_(R_A + 4 * c, T, BF16) for c in range(DC)]
    for t in range(NT):
        bank = t % 4
        tw = [("vm", VM_QF)]
        if t >= 4:
            tw.append(("vm", em.now("vm")))
        for c in range(DC):
            em.emit("tensor", "transpose",
                    psA_bf[:, bank * 1024 + c * 128: bank * 1024 + (c + 1) * 128],
                    qf_tm[t][:, c * 128:(c + 1) * 128], id_b,
                    waits=tw if c == 0 else ())
        em.emit("tensor", None, inc="tm")
        tmv = em.now("tm")
        for c in range(DC):
            em.emit("vector", "tensor_copy", qf_fm[c][:, t * 128:(t + 1) * 128],
                    psA_bf[:, bank * 1024 + c * 128: bank * 1024 + (c + 1) * 128],
                    waits=[("tm", tmv)] if c == 0 else (),
                    inc="vm" if c == DC - 1 else None)
    VM_QFFM = em.now("vm")

    if "kv" in taps:
        tap_kv = nc.dram_tensor("tap_kv", [128, 520], F32, kind="ExternalOutput").ap()
        em.emit("gpsimd", "dma_start", out=tap_kv, in_=kvr, waits=[("cl", CL_KV)],
                inc="tst", incv=16)
    if "qf" in taps:
        tap_qf = nc.dram_tensor("tap_qf", [DC, 128, T], F32, kind="ExternalOutput").ap()
        for c in range(DC):
            em.emit("gpsimd", "dma_start", out=tap_qf[c], in_=qf_fm[c],
                    waits=[("vm", VM_QFFM)], inc="tst", incv=16)

    if upto == "D":
        return _stop_here()

    em.ewait("vector", ("cl", CL_KV))
    em.emit("vector", "memset", kvbd, 0.0, inc="vm")
    em.emit("vector", "memset", ksbd, 0.0, inc="vm")
    em.fence("vector", "vm")
    for hh in range(H):
        ph, cg = hh % 2, hh // 2
        src = kvr[ph * 64:(ph + 1) * 64, cg * 65: cg * 65 + 64]
        srcs = kvr[ph * 64:(ph + 1) * 64, cg * 65 + 64: cg * 65 + 65]
        em.emit("vector", "tensor_copy",
                kvbd[ph * 64:(ph + 1) * 64, cg * 128 + ph * 64: cg * 128 + ph * 64 + 64],
                src, inc=None)
        em.emit("vector", "tensor_copy",
                ksbd[ph * 64:(ph + 1) * 64, cg * 16 + hh: cg * 16 + hh + 1],
                srcs, inc="vm" if hh == H - 1 else None)
    VM_KVBD = em.now("vm")

    for kr in range(DC):
        first = True
        for tc in range(4):
            em.emit("tensor", "matmul", psB[0:16, tc * 512:(tc + 1) * 512],
                    ksbd[:, kr * 16:(kr + 1) * 16], qf_fm[kr][:, tc * 512:(tc + 1) * 512],
                    start=(kr == 0), stop=(kr == DC - 1),
                    waits=[("vm", VM_KVBD), ("vm", VM_QFFM)] if (first and kr == 0) else ())
            first = False
    em.emit("tensor", None, inc="tm")
    rcp_den = V_(SCR, 2048, F32, parts=16)
    em.emit("vector", "tensor_copy", rcp_den, psB[0:16, :], waits=[("tm", em.now("tm"))],
            inc="vm")
    em.emit("vector", "reciprocal", rcp_den, rcp_den, inc="vm")
    VM_RCP = em.now("vm")

    if "den" in taps:
        tap_den = nc.dram_tensor("tap_den", [16, 2048], F32, kind="ExternalOutput").ap()
        em.emit("sync", "dma_start", out=tap_den, in_=rcp_den, waits=[("vm", VM_RCP)],
                inc="st", incv=16)

    attn_bf = [V_(R_A + 32 + 4 * c, T, BF16) for c in range(DC)]
    bc_sb = V_(SCR + 8, 2048, F32)
    for c in range(DC):
        tw = [("vm", VM_RCP), ("vm", em.now("vm"))]
        first = True
        for tc in range(4):
            em.emit("tensor", "matmul", psA[:, tc * 512:(tc + 1) * 512],
                    kvbd[:, c * 128:(c + 1) * 128], qf_fm[c][:, tc * 512:(tc + 1) * 512],
                    start=True, stop=True, waits=tw if first else ())
            first = False
        for tc in range(4):
            em.emit("tensor", "matmul", psB[:, tc * 512:(tc + 1) * 512],
                    e2_c[:, c * 128:(c + 1) * 128], rcp_den[:, tc * 512:(tc + 1) * 512],
                    start=True, stop=True)
        em.emit("tensor", None, inc="tm")
        em.emit("vector", "tensor_copy", bc_sb, psB, waits=[("tm", em.now("tm"))], inc="vm")
        em.emit("vector", "tensor_mul", attn_bf[c], psA, bc_sb, inc="vm")
    VM_ATTN = em.now("vm")

    wo_sb = V_(R_B, 8 * 1024, BF16)
    em.emit("gpsimd", "dma_start", out=wo_sb.rearrange("p (k n) -> p k n", k=8),
            in_=wo_hbm.rearrange("(k p) n -> p k n", p=128),
            waits=[("vm", VM_QFFM)], inc="wl", incv=16)
    WL_WO = em.now("wl")
    xnch = [V_(SCR + 8, 2048, F32), V_(SCR + 16, 2048, F32)]
    sum1_sb = ([V_(R_A + 8 * i, 2048, F32) for i in range(4)] +
               [V_(R_B + 16 + 8 * i, 2048, F32) for i in range(4)])
    sqch = V_(R_B + 48, 2048, F32)

    for oc in range(DC):
        w = [("sp", SP_XN), ("tm", TM_KV), ("vm", VM_ATTN)]
        if oc >= 2:
            w.append(("vm", em.now("vm")))
        em.emit("sync", "dma_start", out=xnch[oc % 2], in_=xnf_sp[oc], waits=w,
                inc="xl", incv=16)
        xlv = em.now("xl")
        tw = [("wl", WL_WO), ("vm", VM_ATTN), ("vm", em.now("vm")), ("sm", em.now("sm"))]
        first = True
        for kr in range(DC):
            for tc in range(4):
                em.emit("tensor", "matmul", psA[:, tc * 512:(tc + 1) * 512],
                        wo_sb[:, kr * 1024 + oc * 128: kr * 1024 + (oc + 1) * 128],
                        attn_bf[kr][:, tc * 512:(tc + 1) * 512],
                        start=(kr == 0), stop=(kr == DC - 1), waits=tw if first else ())
                first = False
        em.emit("tensor", None, inc="tm")
        tmv = em.now("tm")
        em.emit("vector", "tensor_add", sum1_sb[oc], psA, xnch[oc % 2],
                waits=[("tm", tmv), ("xl", xlv)], inc="vm")
        vmv = em.now("vm")
        em.emit("scalar", "activation", sqch, sum1_sb[oc], AF.Square,
                waits=[("vm", vmv)], inc="sm")
        smv = em.now("sm")
        for tc in range(4):
            em.emit("tensor", "matmul", psB[0:1, tc * 512:(tc + 1) * 512],
                    ones_col, sum1_sb[oc][:, tc * 512:(tc + 1) * 512],
                    start=(oc == 0), stop=(oc == DC - 1),
                    waits=[("vm", vmv)] if tc == 0 else ())
        for tc in range(4):
            em.emit("tensor", "matmul", psB[32:33, tc * 512:(tc + 1) * 512],
                    ones_col, sqch[:, tc * 512:(tc + 1) * 512],
                    start=(oc == 0), stop=(oc == DC - 1), skip_group_check=True,
                    waits=[("sm", smv)] if tc == 0 else ())
        em.emit("tensor", None, inc="tm")
    SP_SUM1 = em.now("sp")
    VM_SUM1 = em.now("vm")

    def emit_ln_rows(eps, width, rowbase_kb):
        if isinstance(rowbase_kb, tuple):
            b0, b1, b2 = rowbase_kb
        else:
            b0 = rowbase_kb
            b1 = rowbase_kb + width * 4 / KB
            b2 = rowbase_kb + 2 * width * 4 / KB
        mu_r = V_(b0, width, F32, parts=1)
        var_r = V_(b1, width, F32, parts=1)
        tmp_r = V_(b2, width, F32, parts=1)
        em.emit("vector", "tensor_scalar_mul", mu_r, psB[0:1, 0:width], 1.0 / D,
                waits=[("tm", em.now("tm"))], inc="vm")
        em.emit("vector", "tensor_scalar", var_r, psB[32:33, 0:width], 1.0 / D, eps,
                op0=ALU.mult, op1=ALU.add, inc="vm")
        em.emit("vector", "tensor_mul", tmp_r, mu_r, mu_r, inc="vm")
        em.emit("vector", "tensor_sub", var_r, var_r, tmp_r, inc="vm")
        em.emit("scalar", "activation", tmp_r, var_r, AF.Sqrt,
                waits=[("vm", em.now("vm"))], inc="sm")
        em.emit("vector", "reciprocal", var_r, tmp_r, waits=[("sm", em.now("sm"))], inc="vm")
        em.emit("vector", "scalar_tensor_tensor", mu_r, mu_r, -1.0, var_r,
                op0=ALU.mult, op1=ALU.mult, inc="vm")
        vmv = em.now("vm")
        for tc in range(width // 512):
            em.emit("tensor", "matmul", psA[:, tc * 512:(tc + 1) * 512], ones_row,
                    var_r[:, tc * 512:(tc + 1) * 512], start=True, stop=True,
                    waits=[("vm", vmv)] if tc == 0 else ())
        for tc in range(width // 512):
            em.emit("tensor", "matmul", psB[:, tc * 512:(tc + 1) * 512], ones_row,
                    mu_r[:, tc * 512:(tc + 1) * 512], start=True, stop=True)
        em.emit("tensor", None, inc="tm")
        return em.now("tm")

    tm_bc1 = emit_ln_rows(EPS_LN, 2048, (R_B + 56, R_B + 48, SCR + 16))
    out1_bf = [V_(R_A + 4 * c, T, BF16) for c in range(DC)]
    for oc in range(DC):
        ssem = "soA" if oc % 2 == 0 else "soB"
        o1f = sum1_sb[oc]
        em.emit("vector", "tensor_mul", o1f, o1f, psA, waits=[("tm", tm_bc1)], inc="vm")
        em.emit("vector", "tensor_add", o1f, o1f, psB, inc="vm")
        em.emit("vector", "tensor_scalar", o1f, o1f, par["ln1g"][:, oc:oc + 1],
                par["ln1b"][:, oc:oc + 1], op0=ALU.mult, op1=ALU.add, inc="vm")
        vmv = em.now("vm")
        em.emit("scalar", "dma_start", out=out1f_sp[oc], in_=o1f, waits=[("vm", vmv)],
                inc=ssem, incv=16, serial=False)
    em.ewait("scalar", ("soA", em.now("soA")), ("soB", em.now("soB")))
    em.emit("scalar", "sem_inc", "sp", DC * 16)
    for oc in range(DC):
        em.emit("scalar", "activation", out1_bf[oc], sum1_sb[oc], AF.Copy, inc="sm")
    SP_OUT1 = em.now("sp")
    SM_OUT1 = em.now("sm")

    if "out1" in taps:
        tap_o1 = nc.dram_tensor("tap_out1", [DC, 128, T], F32, kind="ExternalOutput").ap()
        for oc in range(DC):
            em.emit("gpsimd", "dma_start", out=tap_o1[oc], in_=out1_bf[oc],
                    waits=[("sm", SM_OUT1)], inc="tst", incv=16)

    if upto == "G":
        return _stop_here()

    H_H = ([V_(XNBF + 2 * i, HALF, BF16) for i in range(16)] +
           [V_(R_A + 32 + 2 * (i - 16), HALF, BF16) for i in range(16, 32)])
    w1slab = [V_(SCR, 1024, BF16), V_(SCR + 2, 1024, BF16)]
    w2_sb = V_(R_B, FC * 1024, BF16)
    elu1 = V_(SCR + 4, 1024, F32)
    o1chb4 = [V_(SCR, 1024, F32), V_(SCR + 4, 1024, F32),
              V_(SCR + 8, 1024, F32), V_(SCR + 12, 1024, F32)]
    elu2 = V_(SCR + 8, 1024, F32)
    o1ch = V_(SCR + 12, 1024, F32)
    sum2ch = V_(SCR + 16, 1024, F32)
    sq2ch = V_(SCR + 20, 1024, F32)

    em.emit("gpsimd", "dma_start", out=w2_sb.rearrange("p (k n) -> p k n", k=FC),
            in_=w2_hbm.rearrange("k p n -> p k n"),
            waits=[("vm", em.now("vm")), ("tm", em.now("tm")), ("sm", em.now("sm")),
                   ("sp", SP_OUT1)], inc="wl", incv=16)
    WL_W2 = em.now("wl")

    for hf in range(2):
        tmbase = em.now("tm")
        emarks = {}
        for oc in range(FC):
            w = [("sm", SM_OUT1), ("sp", SP_OUT1)]
            if oc >= 2:
                w.append(("tm", tmbase + oc - 1))
            else:
                w.append(("tm", tmbase))
                w += [(f"rl{i}", em.now(f"rl{i}")) for i in range(4)]
                w.append(("vm", em.now("vm")))
            wsem = "wlA" if oc % 2 == 0 else "wlB"
            em.emit("gpsimd", "dma_start",
                    out=w1slab[oc % 2].rearrange("p (k n) -> p k n", k=DC),
                    in_=w1_hbm[oc], waits=w, inc=wsem, incv=16, serial=False)
            pso = 0 if oc % 2 == 0 else 1024
            tw = [(wsem, em.now(wsem)), ("sm", SM_OUT1), ("sp", SP_OUT1),
                  ("vm", emarks[oc - 2] if oc >= 2 else em.now("vm"))]
            first = True
            for k in range(DC):
                for tc2 in range(2):
                    em.emit("tensor", "matmul",
                            psA[:, pso + tc2 * 512: pso + (tc2 + 1) * 512],
                            w1slab[oc % 2][:, k * 128:(k + 1) * 128],
                            out1_bf[k][:, hf * HALF + tc2 * 512: hf * HALF + (tc2 + 1) * 512],
                            start=(k == 0), stop=(k == DC - 1), waits=tw if first else ())
                    first = False
            em.emit("tensor", None, inc="tm")
            tmv = em.now("tm")
            ps = psA[:, pso:pso + 1024]
            em.emit("scalar", "activation", elu1, ps, AF.Relu,
                    bias=par["negb1"][:, oc:oc + 1], scale=-1.0,
                    waits=[("tm", tmv), ("vm", em.now("vm"))], inc="sm")
            em.emit("scalar", "activation", elu2, elu1, AF.Exp, scale=-1.0, inc="sm")
            smv = em.now("sm")
            em.emit("vector", "scalar_tensor_tensor", elu2, elu1, 1.0, elu2,
                    op0=ALU.mult, op1=ALU.add, waits=[("sm", smv)], inc="vm")
            em.emit("vector", "scalar_tensor_tensor", elu2, elu2, 1.0, ps,
                    op0=ALU.mult, op1=ALU.add, inc="vm")
            em.emit("vector", "tensor_scalar_add", H_H[oc], elu2,
                    par["b1m1"][:, oc:oc + 1], inc="vm")
            emarks[oc] = em.now("vm")
        assert em.now("tm") == tmbase + FC

        if "helu" in taps:
            if hf == 0:
                tap_he = nc.dram_tensor("tap_helu", [2, FC, 128, HALF], F32,
                                        kind="ExternalOutput").ap()
                globals()["_tap_he"] = tap_he
            tap_he = globals()["_tap_he"]
            for k in range(FC):
                em.emit("gpsimd", "dma_start", out=tap_he[hf][k], in_=H_H[k],
                        waits=[("vm", em.now("vm"))], inc="tst", incv=16)

        smarks = {}
        for oc in range(DC):
            w = [("sp", SP_OUT1)]
            if oc == 0:
                w += [("vm", em.now("vm")), ("sm", em.now("sm"))]
            if oc >= 1:
                w += [("vm", em.now("vm")), ("sm", em.now("sm"))]
            em.emit("sync", "dma_start", out=o1ch,
                    in_=out1f_sp[oc][:, hf * HALF:(hf + 1) * HALF], waits=w,
                    inc="xl", incv=16)
            xlv = em.now("xl")
            pso = 0 if oc % 2 == 0 else 1024
            tw = [("wl", WL_W2),
                  ("vm", smarks[oc - 2] if oc >= 2 else em.now("vm"))]
            first = True
            for k in range(FC):
                for tc2 in range(2):
                    em.emit("tensor", "matmul",
                            psA[:, pso + tc2 * 512: pso + (tc2 + 1) * 512],
                            w2_sb[:, k * 1024 + oc * 128: k * 1024 + (oc + 1) * 128],
                            H_H[k][:, tc2 * 512:(tc2 + 1) * 512],
                            start=(k == 0), stop=(k == FC - 1), waits=tw if first else ())
                    first = False
            em.emit("tensor", None, inc="tm")
            tmv = em.now("tm")
            ps = psA[:, pso:pso + 1024]
            em.emit("vector", "scalar_tensor_tensor", sum2ch, ps, par["b2v"][:, oc:oc + 1],
                    o1ch, op0=ALU.add, op1=ALU.add, waits=[("tm", tmv), ("xl", xlv)],
                    inc="vm")
            vmv = em.now("vm")
            smarks[oc] = vmv
            em.emit("sync", "dma_start", out=sum2_sp[hf][oc], in_=sum2ch, waits=[("vm", vmv)],
                    inc="sp", incv=16)
            em.emit("scalar", "activation", sq2ch, sum2ch, AF.Square, waits=[("vm", vmv)],
                    inc="sm")
            smv = em.now("sm")
            for tc2 in range(2):
                em.emit("tensor", "matmul", psB[0:1, tc2 * 512:(tc2 + 1) * 512],
                        ones_col, sum2ch[:, tc2 * 512:(tc2 + 1) * 512],
                        start=(oc == 0), stop=(oc == DC - 1),
                        waits=[("vm", vmv)] if tc2 == 0 else ())
            for tc2 in range(2):
                em.emit("tensor", "matmul", psB[32:33, tc2 * 512:(tc2 + 1) * 512],
                        ones_col, sq2ch[:, tc2 * 512:(tc2 + 1) * 512],
                        start=(oc == 0), stop=(oc == DC - 1), skip_group_check=True,
                        waits=[("sm", smv)] if tc2 == 0 else ())
            em.emit("tensor", None, inc="tm")
        SP_SUM2 = em.now("sp")

        tm_bc2 = emit_ln_rows(EPS_LN, HALF, XNBF)
        ap2 = {}
        for oc in range(DC):
            rsem = f"rl{oc % 4}"
            ssem = f"so{oc % 4}"
            w = [("sp", SP_SUM2), ("tm", tm_bc2)]
            if oc >= 4:
                w.append((ssem, ap2[oc - 4]))
            ob = o1chb4[oc % 4]
            em.emit("sync", "dma_start", out=ob, in_=sum2_sp[hf][oc], waits=w,
                    inc=rsem, incv=16, serial=False)
            rlv = em.now(rsem)
            em.emit("vector", "tensor_mul", ob, ob, psA[:, 0:HALF],
                    waits=[(rsem, rlv), ("tm", tm_bc2)], inc="vm")
            em.emit("vector", "tensor_add", ob, ob, psB[:, 0:HALF], inc="vm")
            em.emit("vector", "tensor_scalar", ob, ob, par["ln2g"][:, oc:oc + 1],
                    par["ln2b"][:, oc:oc + 1], op0=ALU.mult, op1=ALU.add, inc="vm")
            em.emit("scalar", "dma_start", out=attnb_sp[hf][oc], in_=ob,
                    waits=[("vm", em.now("vm"))], inc=ssem, incv=16, serial=False)
            ap2[oc] = em.now(ssem)
    em.ewait("scalar", *[(f"so{i}", em.now(f"so{i}")) for i in range(4)])
    em.emit("scalar", "sem_inc", "sp", 4 * DC * 16)
    SP_ATTNB = em.now("sp")

    if "attnb" in taps:
        tap_ab = nc.dram_tensor("tap_attnb", [2, DC, 128, HALF], F32,
                                kind="ExternalOutput").ap()
        for hf in range(2):
            for oc in range(DC):
                em.emit("sync", "dma_start", out=tap_ab[hf][oc], in_=attnb_sp[hf][oc],
                        waits=[("sp", SP_ATTNB)], inc="st", incv=16)

    if upto == "H":
        return _stop_here()

    xn2_bf = [V_(R_A + 4 * c, T, BF16) for c in range(DC)]
    em.emit("sync", "dma_start", out=xn2_bf[0], in_=xnb_sp[0],
            waits=[("sp", SP_XN)] + em.allw(), inc="xl", incv=16)
    for c in range(1, DC):
        em.emit("sync", "dma_start", out=xn2_bf[c], in_=xnb_sp[c], inc="xl", incv=16)
    XL_XN2 = em.now("xl")

    tslab = [V_(SCR, 4096, BF16), V_(SCR + 8, 4096, BF16)]
    hslab = [V_(SCR, 4096, BF16), V_(SCR + 8, 4096, BF16)]
    silu_s = V_(SCR + 16, 1024, F32)
    silu_s2 = V_(SCR + 20, 1024, F32)
    y3ch = V_(SCR + 16, 1024, F32)
    abchb = [V_(6.5625, 1024, F32), V_(SCR + 20, 1024, F32)]

    for hf in range(2):
        TH = [V_(R_B + 2 * i, HALF, BF16) for i in range(FC)]
        G1 = ([V_(XNBF + 2 * i, HALF, BF16) for i in range(16)] +
              [V_(R_A + 32 + 2 * (i - 16), HALF, BF16) for i in range(16, 32)])
        G2 = TH

        tmbase = em.now("tm")
        tmarks = {}
        for oc in range(DC):
            w = [("xl", XL_XN2)]
            if oc == 0:
                w += [("tm", em.now("tm")), ("vm", em.now("vm")), ("sm", em.now("sm")),
                      ("sp", SP_ATTNB), ("st", em.now("st"))]
                w += [(f"rl{i}", em.now(f"rl{i}")) for i in range(4)]
            else:
                w.append(("tm", tmbase + 4 * oc))
            em.emit("gpsimd", "dma_start",
                    out=tslab[0].rearrange("p (k n) -> p k n", k=DC),
                    in_=w1tbf.rearrange("k p n -> p k n")[:, :, oc * 512:(oc + 1) * 512],
                    waits=w + [("pc", PC_ALL)], inc="wt0", incv=16, serial=False)
            em.emit("gpsimd", "dma_start",
                    out=tslab[1].rearrange("p (k n) -> p k n", k=DC),
                    in_=w2tbf.rearrange("k p n -> p k n")[:, :, oc * 512:(oc + 1) * 512],
                    inc="wt1", incv=16, serial=False)
            for s4 in range(4):
                occ = oc * 4 + s4
                pst = psA if occ % 2 == 0 else psB
                sil = silu_s if occ % 2 == 0 else silu_s2
                tw = [("wt0", em.now("wt0")), ("wt1", em.now("wt1"))]
                tw.append(("vm", tmarks[occ - 2] if occ >= 2 else em.now("vm")))
                if occ < 2:
                    tw.append(("sm", em.now("sm")))
                first = True
                for k in range(DC):
                    for tc2 in range(2):
                        em.emit("tensor", "matmul",
                                pst[:, tc2 * 512:(tc2 + 1) * 512],
                                tslab[0][:, k * 512 + s4 * 128: k * 512 + (s4 + 1) * 128],
                                xn2_bf[k][:, hf * HALF + tc2 * 512: hf * HALF + (tc2 + 1) * 512],
                                start=(k == 0), stop=(k == DC - 1), waits=tw if first else ())
                        first = False
                for k in range(DC):
                    for tc2 in range(2):
                        em.emit("tensor", "matmul",
                                pst[:, 1024 + tc2 * 512: 1024 + (tc2 + 1) * 512],
                                tslab[1][:, k * 512 + s4 * 128: k * 512 + (s4 + 1) * 128],
                                xn2_bf[k][:, hf * HALF + tc2 * 512: hf * HALF + (tc2 + 1) * 512],
                                start=(k == 0), stop=(k == DC - 1))
                em.emit("tensor", None, inc="tm")
                tmv = em.now("tm")
                em.emit("scalar", "activation", sil, pst[:, 0:1024], AF.Silu,
                        scale=par["w1s"][:, occ:occ + 1], waits=[("tm", tmv)], inc="sm")
                em.emit("vector", "scalar_tensor_tensor", TH[occ], pst[:, 1024:2048],
                        par["w2s"][:, occ:occ + 1], sil, op0=ALU.mult, op1=ALU.mult,
                        waits=[("sm", em.now("sm"))], inc="vm")
                tmarks[occ] = em.now("vm")
        assert em.now("tm") == tmbase + 4 * DC
        VM_TH = em.now("vm")

        def big44(src_tiles, dst_tiles, w_sg, scale_par, src_sem, src_val):
            tmbase2 = em.now("tm")
            gmarks = {}
            for oc in range(FC):
                w = [(src_sem, src_val)]
                if oc == 0:
                    w += [("tm", em.now("tm")), ("sm", em.now("sm")), ("vm", em.now("vm")),
                          ("st", em.now("st")), ("pc", PC_ALL)]
                else:
                    w.append(("tm", tmbase2 + oc - 1))
                wsem = "whA" if oc % 2 == 0 else "whB"
                em.emit("gpsimd", "dma_start",
                        out=hslab[oc % 2].rearrange("p (k n) -> p k n", k=FC),
                        in_=w_sg[oc], waits=w, inc=wsem, incv=16, serial=False)
                pso = 0 if oc % 2 == 0 else 1024
                tw = [(wsem, em.now(wsem)), (src_sem, src_val)]
                tw.append(("sm", gmarks[oc - 2] if oc >= 2 else em.now("sm")))
                first = True
                for k in range(FC):
                    for tc2 in range(2):
                        em.emit("tensor", "matmul",
                                psA[:, pso + tc2 * 512: pso + (tc2 + 1) * 512],
                                hslab[oc % 2][:, k * 128:(k + 1) * 128],
                                src_tiles[k][:, tc2 * 512:(tc2 + 1) * 512],
                                start=(k == 0), stop=(k == FC - 1), waits=tw if first else ())
                        first = False
                em.emit("tensor", None, inc="tm")
                tmv = em.now("tm")
                em.emit("scalar", "activation", dst_tiles[oc], psA[:, pso:pso + 1024],
                        AF.Gelu, scale=scale_par[:, oc:oc + 1], waits=[("tm", tmv)],
                        inc="sm")
                gmarks[oc] = em.now("sm")
            assert em.now("tm") == tmbase2 + FC
            return em.now("sm")

        if upto == "I1" and hf == 0:
            return _stop_here()
        sm_g1 = big44(TH, G1, h1tbf, par["h1s"], "vm", VM_TH)
        sm_g2 = big44(G1, G2, h2tbf, par["h2s"], "sm", sm_g1)

        if upto == "I2" and hf == 0:
            return _stop_here()
        xh = [V_(R_A + 32 + 4 * i, 1024, F32) for i in range(8)]
        for i in range(8):
            em.emit("sync", "dma_start", out=xh[i],
                    in_=x_hbm[hf * HALF + i * 128: hf * HALF + (i + 1) * 128, :],
                    waits=[("sm", sm_g2)] if i == 0 else (), inc="xl", incv=16)
        XL_XH = em.now("xl")
        out_tm = [V_(XNBF + 4 * i, 1024, F32) for i in range(8)]
        tmbase3 = em.now("tm")
        fmarks = {}
        for oc in range(DC):
            w = [("sm", sm_g2)]
            if oc == 0:
                w += [("tm", em.now("tm")), ("vm", em.now("vm"))]
            else:
                w.append(("tm", tmbase3 + 2 * oc - 1))
            wsem = "whA" if oc % 2 == 0 else "whB"
            em.emit("gpsimd", "dma_start",
                    out=hslab[oc % 2].rearrange("p (k n) -> p k n", k=FC),
                    in_=w3t_hbm[oc], waits=w, inc=wsem, incv=16, serial=False)
            tw = [(wsem, em.now(wsem)), ("vm", em.now("vm"))]
            first = True
            for k in range(FC):
                for tc2 in range(2):
                    em.emit("tensor", "matmul", psA[:, tc2 * 512:(tc2 + 1) * 512],
                            hslab[oc % 2][:, k * 128:(k + 1) * 128],
                            G2[k][:, tc2 * 512:(tc2 + 1) * 512],
                            start=(k == 0), stop=(k == FC - 1), waits=tw if first else ())
                    first = False
            em.emit("tensor", None, inc="tm")
            tmv = em.now("tm")
            rsem = "rlA" if oc % 2 == 0 else "rlB"
            w = [("sp", SP_ATTNB)]
            if oc >= 2:
                w.append(("vm", fmarks[oc - 2]))
            ab = abchb[oc % 2]
            em.emit("sync", "dma_start", out=ab, in_=attnb_sp[hf][oc], waits=w,
                    inc=rsem, incv=16, serial=False)
            rlv = em.now(rsem)
            em.emit("vector", "scalar_tensor_tensor", y3ch, psA[:, 0:1024],
                    par["w3s"][:, oc:oc + 1], ab, op0=ALU.mult, op1=ALU.add,
                    waits=[("tm", tmv), (rsem, rlv)], inc="vm")
            fmarks[oc] = em.now("vm")
            vmv = em.now("vm")
            for p in range(8):
                em.emit("tensor", "transpose",
                        psB[:, 1024 + p * 128: 1024 + (p + 1) * 128],
                        y3ch[:, p * 128:(p + 1) * 128], id_f,
                        waits=[("vm", vmv)] if p == 0 else ())
            em.emit("tensor", None, inc="tm")
            tmv2 = em.now("tm")
            for p in range(8):
                em.emit("vector", "tensor_add", out_tm[p][:, oc * 128:(oc + 1) * 128],
                        psB[:, 1024 + p * 128: 1024 + (p + 1) * 128],
                        xh[p][:, oc * 128:(oc + 1) * 128],
                        waits=[("tm", tmv2), ("xl", XL_XH)] if p == 0 else (),
                        inc="vm" if p == 7 else None)
        assert em.now("tm") == tmbase3 + 2 * DC
        VM_OUT = em.now("vm")
        for i in range(8):
            em.emit("sync", "dma_start",
                    out=out_hbm[hf * HALF + i * 128: hf * HALF + (i + 1) * 128, :],
                    in_=out_tm[i], waits=[("vm", VM_OUT)] if i == 0 else (),
                    inc="st", incv=16)

    em.ewait("sync", ("st", em.now("st")))
    em.ewait("gpsimd", ("st", em.now("st")))
    if em.now("tst"):
        em.ewait("gpsimd", ("tst", em.now("tst")))

    em.realize(nc)
    return nc


def prepare_in_maps(inputs):
    x = np.ascontiguousarray(np.asarray(inputs["L0_attn_x"], np.float32).reshape(B * N, D))

    def t2d(a):
        return np.ascontiguousarray(np.asarray(a, np.float32))

    def slabgrid(w):
        K, NN = w.shape
        return np.ascontiguousarray(
            w.reshape(K // 128, 128, NN // 128, 128).transpose(2, 1, 0, 3))

    def kchunks(w):
        K, NN = w.shape
        return np.ascontiguousarray(w.reshape(K // 128, 128, NN))

    def vec128(v):
        v = t2d(v).ravel()
        return np.ascontiguousarray(v.reshape(-1, 128).T)

    common = {
        "wq": t2d(inputs["L0_attn_Wq"]).reshape(D, D),
        "wk": t2d(inputs["L0_attn_Wk"]).reshape(D, D),
        "wv": t2d(inputs["L0_attn_Wv"]).reshape(D, D),
        "wo": t2d(inputs["L0_attn_Wo"]).reshape(D, D),
        "proj": t2d(inputs["L0_attn_proj"]),
        "w1": slabgrid(t2d(inputs["L0_attn_W1"])),
        "w2": kchunks(t2d(inputs["L0_attn_W2"])),
        "w1t": kchunks(t2d(inputs["L0_ffn_w1_w"]).T),
        "w2t": kchunks(t2d(inputs["L0_ffn_w2_w"]).T),
        "h1t": slabgrid(t2d(inputs["L0_ffn_h1_w"]).T),
        "h2t": slabgrid(t2d(inputs["L0_ffn_h2_w"]).T),
        "w3t": slabgrid(t2d(inputs["L0_ffn_w3_w"]).T),
        "b1v": vec128(inputs["L0_attn_b1"]), "b2v": vec128(inputs["L0_attn_b2"]),
        "ln1g": vec128(inputs["L0_attn_ln1_g"]), "ln1b": vec128(inputs["L0_attn_ln1_b"]),
        "ln2g": vec128(inputs["L0_attn_ln2_g"]), "ln2b": vec128(inputs["L0_attn_ln2_b"]),
        "w1s": vec128(inputs["L0_ffn_w1_s"]), "w2s": vec128(inputs["L0_ffn_w2_s"]),
        "h1s": vec128(inputs["L0_ffn_h1_s"]), "h2s": vec128(inputs["L0_ffn_h2_s"]),
        "w3s": vec128(inputs["L0_ffn_w3_s"]),
    }
    in_maps = []
    for c in range(NCORES):
        m = dict(common)
        m["x"] = np.ascontiguousarray(x[c * T:(c + 1) * T])
        in_maps.append(m)
    return in_maps


def kernel(**inputs):
    in_maps = prepare_in_maps(inputs)
    nc = build()
    res = run_bass_kernel_spmd(nc, in_maps, core_ids=list(range(NCORES)))
    out = np.concatenate([res.results[c]["out"] for c in range(NCORES)], axis=0)
    return out.reshape(B, N, D).astype(np.float32)


# version 08
# speedup vs baseline: 1.1361x; 1.0729x over previous
import numpy as np
import ml_dtypes
import concourse.bass as bass
import concourse.mybir as mybir
from concourse.bass_utils import run_bass_kernel_spmd

F32 = mybir.dt.float32
BF16 = mybir.dt.bfloat16
AF = mybir.ActivationFunctionType
ALU = mybir.AluOpType
AX = mybir.AxisListType

NCORES = 8
B = 2
N = 8192
T = 2048
NT = 16
D = 1024
DC = 8
H = 16
DH = 64
M = 64
DFF = 4096
FC = 32
HALF = 1024
DN = 2.0 ** -1.5
RATIO = 0.125
EPSK = 1e-6
EPS_LN0 = 1e-5
EPS_LN = 1e-6
NEG = -3.0e38

ENGINES = ("sync", "gpsimd", "scalar", "vector", "tensor")
KB = 1024


class Em:
    def __init__(self):
        self.steps = {e: [] for e in ENGINES}
        self.sems = {}
        self._maxwait = {e: {} for e in ENGINES}

    def now(self, sem):
        return self.sems.setdefault(sem, 0)

    def emit(self, eng, method, *args, waits=(), inc=None, incv=1, serial=True, **kw):
        w = []
        for (s, v) in waits:
            self.sems.setdefault(s, 0)
            assert v <= self.sems[s], (eng, method, s, v, self.sems[s])
            if v > 0 and v > self._maxwait[eng].get(s, -1):
                w.append((s, v))
                self._maxwait[eng][s] = v
        if method == "sem_inc":
            self.steps[eng].append((method, args, kw, tuple(w), None, 0))
            self.sems[args[0]] = self.sems.setdefault(args[0], 0) + args[1]
            return
        if method is None and inc is not None:
            for i in range(len(self.steps[eng]) - 1, -1, -1):
                m2, a2, k2, w2, i2, v2 = self.steps[eng][i]
                if m2 is not None:
                    assert i2 is None, ("double marker on", eng, m2, i2, inc)
                    self.steps[eng][i] = (m2, a2, k2, w2, inc, incv)
                    break
            else:
                raise AssertionError("marker with no prior instruction")
            if w:
                self.steps[eng].append((None, (), {}, tuple(w), None, 0))
            self.sems[inc] = self.sems.setdefault(inc, 0) + incv
            return
        self.steps[eng].append((method, args, kw, tuple(w), inc, incv))
        if inc is not None:
            self.sems[inc] = self.sems.setdefault(inc, 0) + incv
        if method == "dma_start" and inc is not None and serial:
            self.ewait(eng, (inc, self.sems[inc]))

    def ewait(self, eng, *waits):
        self.emit(eng, None, waits=waits)

    def fence(self, eng, sem):
        for i in range(len(self.steps[eng]) - 1, -1, -1):
            m2, a2, k2, w2, i2, v2 = self.steps[eng][i]
            if m2 is not None:
                if i2 is None:
                    self.steps[eng][i] = (m2, a2, k2, w2, sem, 1)
                    self.sems[sem] = self.sems.setdefault(sem, 0) + 1
                    self.ewait(eng, (sem, self.now(sem)))
                else:
                    self.ewait(eng, (i2, self.now(i2)))
                return
        raise AssertionError("fence with no prior instruction")

    def allw(self):
        return [("tm", self.now("tm")), ("vm", self.now("vm")), ("sm", self.now("sm"))]

    def check(self):
        idx = {e: 0 for e in ENGINES}
        cnt = {s: 0 for s in self.sems}
        progress = True
        while progress:
            progress = False
            for e in ENGINES:
                while idx[e] < len(self.steps[e]):
                    method, args, kw, waits, inc, incv = self.steps[e][idx[e]]
                    if all(cnt[s] >= v for (s, v) in waits):
                        if method == "sem_inc":
                            cnt[args[0]] += args[1]
                        if inc is not None:
                            cnt[inc] += incv
                        idx[e] += 1
                        progress = True
                    else:
                        break
        stuck = {e: idx[e] for e in ENGINES if idx[e] < len(self.steps[e])}
        if stuck:
            for e, i in stuck.items():
                method, args, kw, waits, inc, incv = self.steps[e][i]
                unmet = [(s, v, cnt[s]) for (s, v) in waits if cnt[s] < v]
                print(f"DEADLOCK {e} step {i}/{len(self.steps[e])} {method} unmet={unmet}")
            raise RuntimeError("static deadlock detected")
        for s, v in cnt.items():
            assert v == self.sems[s], (s, v, self.sems[s])

    def realize(self, nc):
        self.check()
        handles = {name: nc.alloc_semaphore(name) for name in self.sems}
        with nc.Block() as block:
            for engname in ENGINES:
                def make(engname=engname):
                    def f(eng):
                        for (method, args, kw, waits, inc, incv) in self.steps[engname]:
                            for (s, v) in waits:
                                eng.wait_ge(handles[s], v)
                            if method is None:
                                continue
                            if method == "sem_inc":
                                eng.sem_inc(handles[args[0]], args[1])
                                continue
                            ins = getattr(eng, method)(*args, **kw)
                            if inc is not None:
                                ins.then_inc(handles[inc], incv)
                    return f
                getattr(block, engname)(make())


class Arena:
    def __init__(self, nc, cap_bytes):
        self.cap = cap_bytes
        self.t = nc.alloc_sbuf_tensor("arena", [128, cap_bytes], mybir.dt.uint8)

    def view(self, off_kb, cols, dt, parts=128):
        esz = 4 if dt == F32 else 2
        off = int(off_kb * KB)
        assert off % 32 == 0, off_kb
        assert off + cols * esz <= self.cap, (off_kb, cols, dt)
        return self.t[0:parts, off:off + cols * esz].bitcast(dt)


def build(taps=(), upto=None, nta=NT):
    taps = set(taps)
    nc = bass.Bass("TRN2", target_bir_lowering=False, detect_race_conditions=False)
    em = Em()

    x_hbm = nc.dram_tensor("x", [T, D], F32, kind="ExternalInput").ap()
    wq_hbm = nc.dram_tensor("wq", [D, D], F32, kind="ExternalInput").ap()
    wk_hbm = nc.dram_tensor("wk", [D, D], F32, kind="ExternalInput").ap()
    wv_hbm = nc.dram_tensor("wv", [D, D], F32, kind="ExternalInput").ap()
    wo_hbm = nc.dram_tensor("wo", [D, D], F32, kind="ExternalInput").ap()
    proj_hbm = nc.dram_tensor("proj", [M, DH], F32, kind="ExternalInput").ap()
    w1_hbm = nc.dram_tensor("w1", [FC, 128, DC, 128], F32, kind="ExternalInput").ap()
    w2_hbm = nc.dram_tensor("w2", [FC, 128, D], F32, kind="ExternalInput").ap()
    w1t_hbm = nc.dram_tensor("w1t", [DC, 128, DFF], F32, kind="ExternalInput").ap()
    w2t_hbm = nc.dram_tensor("w2t", [DC, 128, DFF], F32, kind="ExternalInput").ap()
    h1t_hbm = nc.dram_tensor("h1t", [FC, 128, FC, 128], F32, kind="ExternalInput").ap()
    h2t_hbm = nc.dram_tensor("h2t", [FC, 128, FC, 128], F32, kind="ExternalInput").ap()
    w3t_hbm = nc.dram_tensor("w3t", [DC, 128, FC, 128], F32, kind="ExternalInput").ap()
    vecs_hbm = {}
    for nm, ncols in [("b1v", FC), ("w1s", FC), ("w2s", FC), ("h1s", FC), ("h2s", FC),
                      ("b2v", DC), ("ln1g", DC), ("ln1b", DC), ("ln2g", DC), ("ln2b", DC),
                      ("w3s", DC)]:
        vecs_hbm[nm] = nc.dram_tensor(nm, [128, ncols], F32, kind="ExternalInput").ap()

    out_hbm = nc.dram_tensor("out", [T, D], F32, kind="ExternalOutput").ap()

    xnf_sp = nc.dram_tensor("xnf_sp", [DC, 128, T], F32).ap()
    xnb_sp = nc.dram_tensor("xnb_sp", [DC, 128, T], BF16).ap()
    sum1_sp = nc.dram_tensor("sum1_sp", [DC, 128, T], F32).ap()
    out1f_sp = nc.dram_tensor("out1f_sp", [DC, 128, T], F32).ap()
    sum2_sp = nc.dram_tensor("sum2_sp", [2, DC, 128, HALF], F32).ap()
    attnb_sp = nc.dram_tensor("attnb_sp", [2, DC, 128, HALF], F32).ap()

    w1bf = nc.dram_tensor("w1bf", [FC, 128, DC, 128], BF16).ap()
    w2bf = nc.dram_tensor("w2bf", [FC, 128, D], BF16).ap()
    w1tbf = nc.dram_tensor("w1tbf", [DC, 128, DFF], BF16).ap()
    w2tbf = nc.dram_tensor("w2tbf", [DC, 128, DFF], BF16).ap()
    h1tbf = nc.dram_tensor("h1tbf", [FC, 128, FC, 128], BF16).ap()
    h2tbf = nc.dram_tensor("h2tbf", [FC, 128, FC, 128], BF16).ap()
    w3tbf = nc.dram_tensor("w3tbf", [DC, 128, FC, 128], BF16).ap()
    ccmax_in = nc.dram_tensor("ccmax_in", [1, 8], F32)
    ccmax_out = nc.dram_tensor("ccmax_out", [1, 8], F32)
    cckv_in = nc.dram_tensor("cckv_in", [128, 520], F32)
    cckv_out = nc.dram_tensor("cckv_out", [128, 520], F32)

    id_f32_d = nc.inline_tensor(np.eye(128, dtype=np.float32), name="c_idf")
    id_bf_d = nc.inline_tensor(np.eye(128, dtype=ml_dtypes.bfloat16).view(np.uint16),
                               name="c_idb")
    bd_np = np.zeros((128, 128), np.float32)
    for kr in range(8):
        bd_np[0:64, kr * 16 + 2 * kr] = 1.0 / 16.0
        bd_np[64:128, kr * 16 + 2 * kr + 1] = 1.0 / 16.0
    bd_d = nc.inline_tensor(bd_np, name="c_bd")
    e2_np = np.zeros((16, 8 * 128), np.float32)
    for c in range(8):
        e2_np[2 * c, c * 128:c * 128 + 64] = 1.0
        e2_np[2 * c + 1, c * 128 + 64:c * 128 + 128] = 1.0
    e2_d = nc.inline_tensor(e2_np, name="c_e2")

    AR = Arena(nc, 206 * KB)
    V_ = AR.view
    id_f = V_(0.0, 128, F32)
    id_b = V_(0.5, 128, BF16)
    bd_c = V_(0.75, 128, F32)
    pd2 = V_(1.25, 128, BF16)
    kvbd = V_(1.5, 1024, BF16)
    ksbd = V_(3.5, 128, BF16)
    kvr = V_(3.75, 520, F32)
    ones_col = V_(5.8125, 1, F32)
    ones_row = V_(5.875, 128, F32)[0:1, :]
    mx128 = V_(6.375, 1, F32)
    mxloc = V_(6.4375, 16, F32)
    mxst = V_(6.5, 16, F32)
    e2_c = V_(6.5625, 1024, F32, parts=16)
    diagq_tm = V_(10.5625, 256, F32)
    diagk_tm = V_(11.5625, 256, F32)
    par = {}
    poff = 12.5625
    for nm, ncols in [("b1v", FC), ("negb1", FC), ("b1m1", FC), ("w1s", FC), ("w2s", FC),
                      ("h1s", FC), ("h2s", FC), ("b2v", DC), ("ln1g", DC), ("ln1b", DC),
                      ("ln2g", DC), ("ln2b", DC), ("w3s", DC)]:
        par[nm] = V_(poff, ncols, F32)
        poff += ncols * 4 / KB
    assert poff <= 14.0, poff

    XNBF = 14.0
    SCR = 46.0
    R_A = 70.0
    R_B = 134.0

    xn_bf = [V_(XNBF + 4 * c, T, BF16) for c in range(DC)]

    psA_t = nc.alloc_psum_tensor("psA", [128, 2048], F32)
    psB_t = nc.alloc_psum_tensor("psB", [128, 2048], F32)
    psA = psA_t[:, :]
    psB = psB_t[:, :]
    psA_bf = psA.bitcast(BF16)


    def _stop_here():
        em.emit("sync", "dma_start", out=out_hbm[0:128, :],
                in_=AR.t[0:128, 0:4096].bitcast(F32),
                waits=em.allw() + [("sp", em.now("sp"))], inc="st", incv=16)
        em.ewait("sync", ("st", em.now("st")))
        em.ewait("gpsimd", ("st", em.now("st")))
        if em.now("tst"):
            em.ewait("gpsimd", ("tst", em.now("tst")))
        em.realize(nc)
        return nc

    em.emit("sync", "dma_start", out=id_f, in_=id_f32_d.ap(), inc="xl", incv=16)
    em.emit("sync", "dma_start", out=id_b, in_=id_bf_d.ap().bitcast(BF16), inc="xl", incv=16)
    em.emit("sync", "dma_start", out=bd_c, in_=bd_d.ap(), inc="xl", incv=16)
    em.emit("sync", "dma_start", out=e2_c, in_=e2_d.ap(), inc="xl", incv=16)
    for nm in ["b1v", "w1s", "w2s", "h1s", "h2s", "b2v", "ln1g", "ln1b", "ln2g", "ln2b", "w3s"]:
        em.emit("sync", "dma_start", out=par[nm], in_=vecs_hbm[nm], inc="xl", incv=16)
    XL_CONSTS = em.now("xl")
    em.emit("vector", "tensor_scalar_add", par["b1m1"], par["b1v"], -1.0,
            waits=[("xl", XL_CONSTS)], inc="vm")
    em.emit("vector", "tensor_scalar_mul", par["negb1"], par["b1v"], -1.0, inc="vm")
    em.emit("vector", "memset", ones_col, 1.0, inc="vm")
    em.emit("vector", "memset", ones_row, 1.0, inc="vm")

    wv_sb = V_(R_B, 8 * 1024, BF16)
    wq_sb = V_(R_B + 16, 8 * 1024, BF16)
    wk_sb = V_(R_B + 32, 8 * 1024, BF16)
    proj_sb = V_(R_B + 48, 64, F32, parts=64)
    em.emit("gpsimd", "dma_start", out=wv_sb.rearrange("p (k n) -> p k n", k=8),
            in_=wv_hbm.rearrange("(k p) n -> p k n", p=128), inc="wl", incv=16)
    em.emit("gpsimd", "dma_start", out=wq_sb.rearrange("p (k n) -> p k n", k=8),
            in_=wq_hbm.rearrange("(k p) n -> p k n", p=128), inc="wl", incv=16)
    WL_WQ = em.now("wl")
    em.emit("gpsimd", "dma_start", out=wk_sb.rearrange("p (k n) -> p k n", k=8),
            in_=wk_hbm.rearrange("(k p) n -> p k n", p=128), inc="wl", incv=16)
    WL_WK = em.now("wl")
    WL_WV = 16
    em.emit("gpsimd", "dma_start", out=proj_sb, in_=proj_hbm, inc="wl", incv=16)
    WL_PROJ = em.now("wl")
    PC_ALL = 0

    em.emit("tensor", "transpose", psB[0:64, 0:64], proj_sb, id_f[0:64, 0:64],
            waits=[("wl", WL_PROJ), ("xl", XL_CONSTS)], inc="tm")
    TM_PROJ = em.now("tm")
    em.emit("vector", "memset", pd2, 0.0, inc="vm")
    em.fence("vector", "vm")
    em.emit("vector", "tensor_scalar_mul", pd2[0:64, 0:64], psB[0:64, 0:64], DN,
            waits=[("tm", TM_PROJ)], inc="vm")
    em.emit("vector", "tensor_scalar_mul", pd2[64:128, 64:128], psB[0:64, 0:64], DN, inc="vm")
    VM_PD2 = em.now("vm")

    xbuf = [V_(SCR, 1024, F32), V_(SCR + 4, 1024, F32)]
    xsq = V_(SCR + 8, 1024, F32)
    xnt = V_(SCR + 12, 1024, F32)
    srow = [V_(SCR + 16 + 0.0625 * i, 1, F32) for i in range(5)]
    xn_f32 = [V_(R_A + 8 * c, T, F32) for c in range(DC)]

    a_mark = []
    for t in range(nta):
        buf = xbuf[t % 2]
        w = []
        if t >= 2:
            w += [("vm", a_mark[t - 2][1]), ("sm", a_mark[t - 2][2])]
        em.emit("sync", "dma_start", out=buf, in_=x_hbm[t * 128:(t + 1) * 128, :],
                waits=w, inc="xl", incv=16)
        xlv = em.now("xl")
        em.emit("vector", "scalar_tensor_tensor", xsq, buf, 1.0, buf,
                op0=ALU.mult, op1=ALU.mult, accum_out=srow[2],
                waits=[("xl", xlv)], inc="vm")
        em.emit("vector", "tensor_reduce", srow[0], buf, axis=AX.X, op=ALU.add, inc="vm")
        em.fence("vector", "vm")
        em.emit("vector", "tensor_scalar_mul", srow[1], srow[0], 1.0 / D, inc="vm")
        em.emit("vector", "tensor_scalar", srow[2], srow[2], 1.0 / D, EPS_LN0,
                op0=ALU.mult, op1=ALU.add, inc="vm")
        em.fence("vector", "vm")
        em.emit("vector", "scalar_tensor_tensor", srow[3], srow[1], -1.0, srow[1],
                op0=ALU.mult, op1=ALU.mult, inc="vm")
        em.fence("vector", "vm")
        em.emit("vector", "tensor_add", srow[2], srow[2], srow[3], inc="vm")
        em.fence("vector", "vm")
        em.emit("scalar", "activation", srow[3], srow[2], AF.Sqrt,
                waits=[("vm", em.now("vm"))], inc="sm")
        em.emit("vector", "reciprocal", srow[4], srow[3],
                waits=[("sm", em.now("sm"))], inc="vm")
        em.fence("vector", "vm")
        em.emit("vector", "scalar_tensor_tensor", srow[3], srow[1], -1.0, srow[4],
                op0=ALU.mult, op1=ALU.mult, inc="vm")
        em.fence("vector", "vm")
        em.emit("scalar", "activation", xnt, buf, AF.Identity,
                scale=srow[4][:, 0:1], bias=srow[3][:, 0:1],
                waits=[("vm", em.now("vm"))], inc="sm")
        em.emit("vector", None, waits=[("sm", em.now("sm"))])
        em.emit("vector", "sem_inc", "vm", 1)
        vm_xnt = em.now("vm")
        if "a0" in taps and t == 0:
            tap_a0x = nc.dram_tensor("tap_a0x", [128, 1024], F32, kind="ExternalOutput").ap()
            tap_a0n = nc.dram_tensor("tap_a0n", [128, 1024], F32, kind="ExternalOutput").ap()
            tap_a0s = nc.dram_tensor("tap_a0s", [128, 8], F32, kind="ExternalOutput").ap()
            em.emit("sync", "dma_start", out=tap_a0x, in_=buf,
                    waits=[("vm", vm_xnt)], inc="st", incv=16)
            em.emit("sync", "dma_start", out=tap_a0n, in_=xnt, inc="st", incv=16)
            for si in range(5):
                em.emit("sync", "dma_start", out=tap_a0s[:, si:si + 1], in_=srow[si],
                        allow_slow_non_contiguous=True, inc="st", incv=16)
        pso = 0 if t % 2 == 0 else 1024
        tw = [("sm", em.now("sm"))]
        if t >= 2:
            tw += [("vm", a_mark[t - 2][1]), ("sm", a_mark[t - 2][2])]
        for c in range(DC):
            em.emit("tensor", "transpose", psA[:, pso + c * 128: pso + (c + 1) * 128],
                    xnt[:, c * 128:(c + 1) * 128], id_f,
                    waits=tw if c == 0 else ())
        em.emit("tensor", None, inc="tm")
        tmv = em.now("tm")
        for c in range(4):
            em.emit("vector", "tensor_copy", xn_f32[c][:, t * 128:(t + 1) * 128],
                    psA[:, pso + c * 128: pso + (c + 1) * 128],
                    waits=[("tm", tmv)] if c == 0 else (),
                    inc="vm" if c == 3 else None)
        for c in range(4, DC):
            em.emit("scalar", "activation", xn_f32[c][:, t * 128:(t + 1) * 128],
                    psA[:, pso + c * 128: pso + (c + 1) * 128], AF.Copy,
                    waits=[("tm", tmv)] if c == 4 else (),
                    inc="sm" if c == DC - 1 else None)
        vm_ev = em.now("vm")
        sm_ev = em.now("sm")
        for c in range(DC):
            eng2 = "vector" if c >= 4 else "scalar"
            if eng2 == "scalar":
                em.emit("scalar", "activation", xn_bf[c][:, t * 128:(t + 1) * 128],
                        xn_f32[c][:, t * 128:(t + 1) * 128], AF.Copy,
                        waits=[("vm", vm_ev)] if c == 0 else (),
                        inc="sm" if c == DC - 1 else None)
            else:
                em.emit("vector", "tensor_copy", xn_bf[c][:, t * 128:(t + 1) * 128],
                        xn_f32[c][:, t * 128:(t + 1) * 128],
                        waits=[("sm", sm_ev)] if c == 4 else (),
                        inc="vm" if c == DC - 1 else None)
        a_mark.append((tmv, em.now("vm"), em.now("sm")))

    VM_A = em.now("vm")
    SM_A = em.now("sm")
    for c in range(DC):
        em.emit("sync", "dma_start", out=xnf_sp[c], in_=xn_f32[c],
                waits=[("vm", VM_A)] if c == 0 else (), inc="sp", incv=16)
        em.emit("sync", "dma_start", out=xnb_sp[c], in_=xn_bf[c],
                waits=[("sm", SM_A)] if c == 0 else (), inc="sp", incv=16)
    SP_XN = em.now("sp")

    if "xn" in taps:
        tap_xn = nc.dram_tensor("tap_xn", [DC, 128, T], F32, kind="ExternalOutput").ap()
        for c in range(DC):
            em.emit("sync", "dma_start", out=tap_xn[c], in_=xn_f32[c],
                    waits=[("vm", VM_A)], inc="st", incv=16)

    if upto == "A":
        return _stop_here()

    q_bf = [V_(R_A + 4 * c, T, BF16) for c in range(DC)]
    k_bf = [V_(R_A + 32 + 4 * c, T, BF16) for c in range(DC)]
    qsq = V_(SCR, 2048, F32)
    diag_fm = V_(SCR + 8, 2048, F32, parts=16)

    def emit_qk(w_sb, wlv, dst, diag_dst_tm):
        marks = {}
        g = 0
        for oc in range(DC):
            for th in range(2):
                pso = 0 if g % 2 == 0 else 1024
                tw = [("wl", wlv)]
                if g >= 2:
                    tw += [("vm", marks[g - 2][0]), ("sm", marks[g - 2][1])]
                else:
                    tw += [("sm", em.now("sm")), ("vm", em.now("vm"))]
                first = True
                for k in range(DC):
                    for tc in range(2):
                        tcg = th * 2 + tc
                        em.emit("tensor", "matmul",
                                psA[:, pso + tc * 512: pso + (tc + 1) * 512],
                                w_sb[:, k * 1024 + oc * 128: k * 1024 + (oc + 1) * 128],
                                xn_bf[k][:, tcg * 512:(tcg + 1) * 512],
                                start=(k == 0), stop=(k == DC - 1),
                                waits=tw if first else ())
                        first = False
                em.emit("tensor", None, inc="tm")
                tmv = em.now("tm")
                em.emit("vector", "tensor_copy", dst[oc][:, th * 1024:(th + 1) * 1024],
                        psA[:, pso:pso + 1024],
                        waits=[("tm", tmv), ("sp", SP_XN)], inc="vm")
                em.emit("scalar", "activation", qsq[:, th * 1024:(th + 1) * 1024],
                        psA[:, pso:pso + 1024], AF.Square,
                        waits=[("tm", tmv), ("vm", em.now("vm"))], inc="sm")
                smv = em.now("sm")
                for tc in range(2):
                    tcg = th * 2 + tc
                    em.emit("tensor", "matmul", psB[0:16, tcg * 512:(tcg + 1) * 512],
                            bd_c[:, oc * 16:(oc + 1) * 16],
                            qsq[:, tcg * 512:(tcg + 1) * 512],
                            start=(oc == 0), stop=(oc == DC - 1),
                            waits=([("sm", smv)] + ([("vm", em.now("vm"))] if g <= 1 else []))
                            if tc == 0 else ())
                em.emit("tensor", None, inc="tm")
                marks[g] = (em.now("vm"), em.now("sm"))
                g += 1
        tmv = em.now("tm")
        em.emit("vector", "tensor_copy", diag_fm, psB[0:16, :], waits=[("tm", tmv)], inc="vm")
        vmv = em.now("vm")
        for t in range(NT):
            tw = [("vm", vmv)] if t == 0 else []
            if t >= 4:
                tw.append(("vm", em.now("vm")))
            em.emit("tensor", "transpose",
                    psB[:, (t % 4) * 512: (t % 4) * 512 + 16],
                    diag_fm[:, t * 128:(t + 1) * 128], id_f[0:16, 0:16], waits=tw, inc="tm")
            em.emit("vector", "tensor_copy", diag_dst_tm[:, t * 16:(t + 1) * 16],
                    psB[:, (t % 4) * 512: (t % 4) * 512 + 16],
                    waits=[("tm", em.now("tm"))], inc="vm")

    emit_qk(wk_sb, WL_WK, k_bf, diagk_tm)
    em.emit("vector", "memset", mxloc[:, 0:1], NEG, inc="vm")
    for t in range(NT):
        pso = 0 if t % 2 == 0 else 1024
        tw = [("vm", em.now("vm")), ("vm", VM_PD2)]
        first = True
        for pr in range(8):
            em.emit("tensor", "matmul", psA[:, pso + pr * 128: pso + (pr + 1) * 128],
                    k_bf[pr][:, t * 128:(t + 1) * 128], pd2,
                    start=True, stop=True, waits=tw if first else ())
            first = False
        em.emit("tensor", None, inc="tm")
        tmv = em.now("tm")
        em.emit("vector", "tensor_reduce", mxloc[:, 1:2], psA[:, pso:pso + 1024],
                axis=AX.X, op=ALU.max, waits=[("tm", tmv)], inc="vm")
        em.fence("vector", "vm")
        em.emit("vector", "tensor_max", mxloc[:, 0:1], mxloc[:, 0:1], mxloc[:, 1:2], inc="vm")
        em.fence("vector", "vm")
    em.emit("tensor", "transpose", psB[0:1, 1024:1152], mxloc[:, 0:1], id_f,
            waits=[("vm", em.now("vm"))], inc="tm")
    em.emit("vector", "memset", mxst[0:1, 0:8], NEG,
            waits=[("tm", em.now("tm"))], inc="vm")
    em.fence("vector", "vm")
    em.emit("vector", "tensor_reduce", mxst[0:1, 0:1], psB[0:1, 1024:1152], axis=AX.X,
            op=ALU.max, inc="vm")
    em.fence("vector", "vm")
    em.emit("gpsimd", "dma_start", out=ccmax_in.ap(), in_=mxst[0:1, 0:8],
            waits=[("vm", em.now("vm"))], inc="cl", incv=16)
    em.ewait("gpsimd", ("cl", em.now("cl")))
    em.emit("gpsimd", "collective_compute", "AllReduce", ALU.max,
            replica_groups=[list(range(NCORES))],
            ins=[ccmax_in.ap().opt()], outs=[ccmax_out.ap().opt()], inc="cc")
    em.ewait("gpsimd", ("cc", em.now("cc")))
    em.emit("gpsimd", "dma_start", out=mxst[0:1, 8:16], in_=ccmax_out.ap(), inc="cl", incv=16)
    CL_MX = em.now("cl")

    if upto == "C1":
        return _stop_here()

    emit_qk(wq_sb, WL_WQ, q_bf, diagq_tm)

    if "q" in taps:
        tap_q = nc.dram_tensor("tap_q", [DC, 128, T], F32, kind="ExternalOutput").ap()
        for c in range(DC):
            em.emit("gpsimd", "dma_start", out=tap_q[c], in_=q_bf[c],
                    waits=[("vm", em.now("vm"))], inc="tst", incv=16)
    if "diag" in taps:
        tap_dg = nc.dram_tensor("tap_diag", [2, 128, 256], F32, kind="ExternalOutput").ap()
        em.emit("sync", "dma_start", out=tap_dg[0], in_=diagq_tm,
                waits=[("vm", em.now("vm"))], inc="st", incv=16)
        em.emit("sync", "dma_start", out=tap_dg[1], in_=diagk_tm,
                waits=[("vm", em.now("vm"))], inc="st", incv=16)

    vaug = [V_(R_B + 16 + 2.125 * t, 1040, BF16) for t in range(NT)]
    VM_B_QK = em.now("vm")
    for t in range(NT):
        em.emit("vector", "memset", vaug[t], 1.0,
                waits=[("vm", VM_B_QK), ("tm", em.now("tm"))] if t == 0 else (),
                inc="vm" if t == NT - 1 else None)
    em.fence("vector", "vm")
    VM_VONES = em.now("vm")
    for t in range(NT):
        for nh in range(2):
            g = 2 * t + nh
            pso = (g % 4) * 512
            tw = [("wl", WL_WV), ("vm", VM_VONES), ("vm", em.now("vm"))]
            first = True
            for k in range(DC):
                em.emit("tensor", "matmul", psA[:, pso:pso + 512],
                        xn_bf[k][:, t * 128:(t + 1) * 128],
                        wv_sb[:, k * 1024 + nh * 512: k * 1024 + (nh + 1) * 512],
                        start=(k == 0), stop=(k == DC - 1), waits=tw if first else ())
                first = False
            em.emit("tensor", None, inc="tm")
            tmv = em.now("tm")
            dst = vaug[t].rearrange("p (h e) -> p h e", h=16)[:, nh * 8:(nh + 1) * 8, 0:64]
            src = psA[:, pso:pso + 512].rearrange("p (h d) -> p h d", h=8)
            em.emit("vector", "tensor_copy", dst, src, waits=[("tm", tmv)], inc="vm")
    VM_VAUG = em.now("vm")

    if upto == "B":
        return _stop_here()

    qf_tm = [V_(R_B + 2 * t, 1024, BF16) if t < 8 else V_(R_B + 50 + 2 * (t - 8), 1024, BF16)
             for t in range(NT)]
    stage = V_(SCR, 512, F32)
    biasr = V_(SCR + 2, 16, F32)
    stageE = V_(SCR + 4, 1024, F32)
    kf_t = V_(SCR + 8, 1024, BF16)

    em.emit("tensor", "matmul", psB[:, 1028:1029], ones_row, mxst[0:1, 8:9],
            start=True, stop=True, waits=[("cl", CL_MX)], inc="tm")
    em.emit("vector", "tensor_copy", mx128, psB[:, 1028:1029],
            waits=[("tm", em.now("tm"))], inc="vm")
    VM_MX128 = em.now("vm")

    ekm = {}
    for t in range(NT):
        pso = 0 if t % 2 == 0 else 1024
        if t >= 2:
            tw = [("vm", ekm[t - 2][0]), ("sm", ekm[t - 2][1])]
        else:
            tw = [("vm", em.now("vm")), ("sm", em.now("sm"))]
        first = True
        for pr in range(8):
            em.emit("tensor", "matmul", psA[:, pso + pr * 128: pso + (pr + 1) * 128],
                    k_bf[pr][:, t * 128:(t + 1) * 128], pd2,
                    start=True, stop=True, waits=tw if first else ())
            first = False
        em.emit("tensor", None, inc="tm")
        tmv = em.now("tm")
        em.emit("vector", "tensor_scalar", biasr, diagk_tm[:, t * 16:(t + 1) * 16],
                mx128[:, 0:1], -1.0, op0=ALU.add, op1=ALU.mult,
                waits=[("tm", tmv), ("vm", VM_MX128)], inc="vm")
        em.fence("vector", "vm")
        vm_b = em.now("vm")
        for h in range(8, H):
            em.emit("scalar", "activation", stageE[:, h * 64:(h + 1) * 64],
                    psA[:, pso + h * 64: pso + h * 64 + 64], AF.Exp,
                    bias=biasr[:, h:h + 1],
                    waits=[("vm", vm_b), ("tm", tmv)] if h == 8 else (),
                    inc="sm" if h == H - 1 else None)
        for h in range(8):
            em.emit("vector", "tensor_scalar_add", stage[:, h * 64:(h + 1) * 64],
                    psA[:, pso + h * 64: pso + h * 64 + 64], biasr[:, h:h + 1],
                    inc="vm" if h == 7 else None)
        em.emit("scalar", "activation", stageE[:, 0:512], stage, AF.Exp,
                waits=[("vm", em.now("vm"))], inc="sm")
        em.emit("vector", "tensor_scalar", kf_t, stageE, EPSK, RATIO,
                op0=ALU.add, op1=ALU.mult, waits=[("sm", em.now("sm"))], inc="vm")
        vm_kf = em.now("vm")
        first = True
        for h in range(H):
            ph, cg = h % 2, h // 2
            colbase = (0 if cg < 4 else 512) + (cg % 4) * 65
            em.emit("tensor", "matmul",
                    psB[ph * 64:(ph + 1) * 64, colbase:colbase + 65],
                    kf_t[:, h * 64:(h + 1) * 64],
                    vaug[t][:, h * 65:(h + 1) * 65],
                    start=(t == 0 and h in (0, 1, 8, 9)),
                    stop=(t == NT - 1 and h in (6, 7, 14, 15)),
                    skip_group_check=True,
                    waits=[("vm", vm_kf), ("vm", VM_VAUG)] if first else ())
            first = False
        em.emit("tensor", None, inc="tm")
        ekm[t] = (em.now("vm"), em.now("sm"))
    TM_KV = em.now("tm")
    em.emit("vector", "tensor_copy", kvr[:, 0:260], psB[:, 0:260],
            waits=[("tm", TM_KV)], inc="vm")
    em.emit("vector", "tensor_copy", kvr[:, 260:520], psB[:, 512:772], inc="vm")
    em.emit("gpsimd", "dma_start", out=cckv_in.ap(), in_=kvr,
            waits=[("vm", em.now("vm"))], inc="cl", incv=16)
    em.ewait("gpsimd", ("cl", em.now("cl")))
    em.emit("gpsimd", "collective_compute", "AllReduce", ALU.add,
            replica_groups=[[0, 1, 2, 3], [4, 5, 6, 7]],
            ins=[cckv_in.ap().opt()], outs=[cckv_out.ap().opt()], inc="cc")
    em.ewait("gpsimd", ("cc", em.now("cc")))
    em.emit("gpsimd", "dma_start", out=kvr, in_=cckv_out.ap(), inc="cl", incv=16)
    CL_KV = em.now("cl")

    if upto == "E":
        return _stop_here()

    ddm = {}
    for t in range(NT):
        pso = 0 if t % 2 == 0 else 1024
        if t >= 2:
            tw = [("vm", ddm[t - 2][0]), ("sm", ddm[t - 2][1])]
        else:
            tw = [("vm", em.now("vm")), ("sm", em.now("sm"))]
        first = True
        for pr in range(8):
            em.emit("tensor", "matmul", psA[:, pso + pr * 128: pso + (pr + 1) * 128],
                    q_bf[pr][:, t * 128:(t + 1) * 128], pd2,
                    start=True, stop=True, waits=tw if first else ())
            first = False
        em.emit("tensor", None, inc="tm")
        tmv = em.now("tm")
        dd3 = psA[:, pso:pso + 1024].rearrange("p (h m) -> p h m", h=H)
        em.emit("vector", "tensor_reduce", mxloc, dd3, axis=AX.X, op=ALU.max,
                waits=[("tm", tmv)], inc="vm")
        em.fence("vector", "vm")
        em.emit("vector", "scalar_tensor_tensor", biasr, mxloc, -1.0,
                diagq_tm[:, t * 16:(t + 1) * 16], op0=ALU.mult, op1=ALU.subtract, inc="vm")
        em.fence("vector", "vm")
        vm_b = em.now("vm")
        for h in range(8, H):
            em.emit("scalar", "activation", stageE[:, h * 64:(h + 1) * 64],
                    psA[:, pso + h * 64: pso + h * 64 + 64], AF.Exp,
                    bias=biasr[:, h:h + 1],
                    waits=[("vm", vm_b), ("tm", tmv)] if h == 8 else (),
                    inc="sm" if h == H - 1 else None)
        for h in range(8):
            em.emit("vector", "tensor_scalar_add", stage[:, h * 64:(h + 1) * 64],
                    psA[:, pso + h * 64: pso + h * 64 + 64], biasr[:, h:h + 1],
                    inc="vm" if h == 7 else None)
        em.emit("scalar", "activation", stageE[:, 0:512], stage, AF.Exp,
                waits=[("vm", em.now("vm"))], inc="sm")
        em.emit("vector", "tensor_scalar", qf_tm[t], stageE, EPSK, RATIO,
                op0=ALU.add, op1=ALU.mult, waits=[("sm", em.now("sm"))], inc="vm")
        ddm[t] = (em.now("vm"), em.now("sm"))
    VM_QF = em.now("vm")

    if upto == "C2":
        return _stop_here()

    qf_fm = [V_(R_A + 4 * c, T, BF16) for c in range(DC)]
    for t in range(NT):
        bank = t % 4
        tw = [("vm", VM_QF)]
        if t >= 4:
            tw.append(("vm", em.now("vm")))
        for c in range(DC):
            em.emit("tensor", "transpose",
                    psA_bf[:, bank * 1024 + c * 128: bank * 1024 + (c + 1) * 128],
                    qf_tm[t][:, c * 128:(c + 1) * 128], id_b,
                    waits=tw if c == 0 else ())
        em.emit("tensor", None, inc="tm")
        tmv = em.now("tm")
        for c in range(DC):
            em.emit("vector", "tensor_copy", qf_fm[c][:, t * 128:(t + 1) * 128],
                    psA_bf[:, bank * 1024 + c * 128: bank * 1024 + (c + 1) * 128],
                    waits=[("tm", tmv)] if c == 0 else (),
                    inc="vm" if c == DC - 1 else None)
    VM_QFFM = em.now("vm")

    if "kv" in taps:
        tap_kv = nc.dram_tensor("tap_kv", [128, 520], F32, kind="ExternalOutput").ap()
        em.emit("gpsimd", "dma_start", out=tap_kv, in_=kvr, waits=[("cl", CL_KV)],
                inc="tst", incv=16)
    if "qf" in taps:
        tap_qf = nc.dram_tensor("tap_qf", [DC, 128, T], F32, kind="ExternalOutput").ap()
        for c in range(DC):
            em.emit("gpsimd", "dma_start", out=tap_qf[c], in_=qf_fm[c],
                    waits=[("vm", VM_QFFM)], inc="tst", incv=16)

    if upto == "D":
        return _stop_here()

    em.ewait("vector", ("cl", CL_KV))
    em.emit("vector", "memset", kvbd, 0.0, inc="vm")
    em.emit("vector", "memset", ksbd, 0.0, inc="vm")
    em.fence("vector", "vm")
    for hh in range(H):
        ph, cg = hh % 2, hh // 2
        src = kvr[ph * 64:(ph + 1) * 64, cg * 65: cg * 65 + 64]
        srcs = kvr[ph * 64:(ph + 1) * 64, cg * 65 + 64: cg * 65 + 65]
        em.emit("vector", "tensor_copy",
                kvbd[ph * 64:(ph + 1) * 64, cg * 128 + ph * 64: cg * 128 + ph * 64 + 64],
                src, inc=None)
        em.emit("vector", "tensor_copy",
                ksbd[ph * 64:(ph + 1) * 64, cg * 16 + hh: cg * 16 + hh + 1],
                srcs, inc="vm" if hh == H - 1 else None)
    VM_KVBD = em.now("vm")

    for kr in range(DC):
        first = True
        for tc in range(4):
            em.emit("tensor", "matmul", psB[0:16, tc * 512:(tc + 1) * 512],
                    ksbd[:, kr * 16:(kr + 1) * 16], qf_fm[kr][:, tc * 512:(tc + 1) * 512],
                    start=(kr == 0), stop=(kr == DC - 1),
                    waits=[("vm", VM_KVBD), ("vm", VM_QFFM)] if (first and kr == 0) else ())
            first = False
    em.emit("tensor", None, inc="tm")
    rcp_den = V_(SCR, 2048, F32, parts=16)
    em.emit("vector", "tensor_copy", rcp_den, psB[0:16, :], waits=[("tm", em.now("tm"))],
            inc="vm")
    em.emit("vector", "reciprocal", rcp_den, rcp_den, inc="vm")
    VM_RCP = em.now("vm")

    if "den" in taps:
        tap_den = nc.dram_tensor("tap_den", [16, 2048], F32, kind="ExternalOutput").ap()
        em.emit("sync", "dma_start", out=tap_den, in_=rcp_den, waits=[("vm", VM_RCP)],
                inc="st", incv=16)

    attn_bf = [V_(R_A + 32 + 4 * c, T, BF16) for c in range(DC)]
    bc_sb = V_(SCR + 8, 2048, F32)
    for c in range(DC):
        tw = [("vm", VM_RCP), ("vm", em.now("vm"))]
        first = True
        for tc in range(4):
            em.emit("tensor", "matmul", psA[:, tc * 512:(tc + 1) * 512],
                    kvbd[:, c * 128:(c + 1) * 128], qf_fm[c][:, tc * 512:(tc + 1) * 512],
                    start=True, stop=True, waits=tw if first else ())
            first = False
        for tc in range(4):
            em.emit("tensor", "matmul", psB[:, tc * 512:(tc + 1) * 512],
                    e2_c[:, c * 128:(c + 1) * 128], rcp_den[:, tc * 512:(tc + 1) * 512],
                    start=True, stop=True)
        em.emit("tensor", None, inc="tm")
        em.emit("vector", "tensor_copy", bc_sb, psB, waits=[("tm", em.now("tm"))], inc="vm")
        em.emit("vector", "tensor_mul", attn_bf[c], psA, bc_sb, inc="vm")
    VM_ATTN = em.now("vm")

    wo_sb = V_(R_B, 8 * 1024, BF16)
    em.emit("gpsimd", "dma_start", out=wo_sb.rearrange("p (k n) -> p k n", k=8),
            in_=wo_hbm.rearrange("(k p) n -> p k n", p=128),
            waits=[("vm", VM_QFFM)], inc="wl", incv=16)
    WL_WO = em.now("wl")
    xnch = [V_(SCR + 8, 2048, F32), V_(SCR + 16, 2048, F32)]
    sum1_sb = ([V_(R_A + 8 * i, 2048, F32) for i in range(4)] +
               [V_(R_B + 16 + 8 * i, 2048, F32) for i in range(4)])
    sqch = V_(R_B + 48, 2048, F32)

    for oc in range(DC):
        w = [("sp", SP_XN), ("tm", TM_KV), ("vm", VM_ATTN)]
        if oc >= 2:
            w.append(("vm", em.now("vm")))
        em.emit("sync", "dma_start", out=xnch[oc % 2], in_=xnf_sp[oc], waits=w,
                inc="xl", incv=16)
        xlv = em.now("xl")
        tw = [("wl", WL_WO), ("vm", VM_ATTN), ("vm", em.now("vm")), ("sm", em.now("sm"))]
        first = True
        for kr in range(DC):
            for tc in range(4):
                em.emit("tensor", "matmul", psA[:, tc * 512:(tc + 1) * 512],
                        wo_sb[:, kr * 1024 + oc * 128: kr * 1024 + (oc + 1) * 128],
                        attn_bf[kr][:, tc * 512:(tc + 1) * 512],
                        start=(kr == 0), stop=(kr == DC - 1), waits=tw if first else ())
                first = False
        em.emit("tensor", None, inc="tm")
        tmv = em.now("tm")
        em.emit("vector", "tensor_add", sum1_sb[oc], psA, xnch[oc % 2],
                waits=[("tm", tmv), ("xl", xlv)], inc="vm")
        vmv = em.now("vm")
        em.emit("scalar", "activation", sqch, sum1_sb[oc], AF.Square,
                waits=[("vm", vmv)], inc="sm")
        smv = em.now("sm")
        for tc in range(4):
            em.emit("tensor", "matmul", psB[0:1, tc * 512:(tc + 1) * 512],
                    ones_col, sum1_sb[oc][:, tc * 512:(tc + 1) * 512],
                    start=(oc == 0), stop=(oc == DC - 1),
                    waits=[("vm", vmv)] if tc == 0 else ())
        for tc in range(4):
            em.emit("tensor", "matmul", psB[32:33, tc * 512:(tc + 1) * 512],
                    ones_col, sqch[:, tc * 512:(tc + 1) * 512],
                    start=(oc == 0), stop=(oc == DC - 1), skip_group_check=True,
                    waits=[("sm", smv)] if tc == 0 else ())
        em.emit("tensor", None, inc="tm")
    SP_SUM1 = em.now("sp")
    VM_SUM1 = em.now("vm")

    def emit_ln_rows(eps, width, rowbase_kb):
        if isinstance(rowbase_kb, tuple):
            b0, b1, b2 = rowbase_kb
        else:
            b0 = rowbase_kb
            b1 = rowbase_kb + width * 4 / KB
            b2 = rowbase_kb + 2 * width * 4 / KB
        mu_r = V_(b0, width, F32, parts=1)
        var_r = V_(b1, width, F32, parts=1)
        tmp_r = V_(b2, width, F32, parts=1)
        em.emit("vector", "tensor_scalar_mul", mu_r, psB[0:1, 0:width], 1.0 / D,
                waits=[("tm", em.now("tm"))], inc="vm")
        em.emit("vector", "tensor_scalar", var_r, psB[32:33, 0:width], 1.0 / D, eps,
                op0=ALU.mult, op1=ALU.add, inc="vm")
        em.emit("vector", "tensor_mul", tmp_r, mu_r, mu_r, inc="vm")
        em.emit("vector", "tensor_sub", var_r, var_r, tmp_r, inc="vm")
        em.emit("scalar", "activation", tmp_r, var_r, AF.Sqrt,
                waits=[("vm", em.now("vm"))], inc="sm")
        em.emit("vector", "reciprocal", var_r, tmp_r, waits=[("sm", em.now("sm"))], inc="vm")
        em.emit("vector", "scalar_tensor_tensor", mu_r, mu_r, -1.0, var_r,
                op0=ALU.mult, op1=ALU.mult, inc="vm")
        vmv = em.now("vm")
        for tc in range(width // 512):
            em.emit("tensor", "matmul", psA[:, tc * 512:(tc + 1) * 512], ones_row,
                    var_r[:, tc * 512:(tc + 1) * 512], start=True, stop=True,
                    waits=[("vm", vmv)] if tc == 0 else ())
        for tc in range(width // 512):
            em.emit("tensor", "matmul", psB[:, tc * 512:(tc + 1) * 512], ones_row,
                    mu_r[:, tc * 512:(tc + 1) * 512], start=True, stop=True)
        em.emit("tensor", None, inc="tm")
        return em.now("tm")

    tm_bc1 = emit_ln_rows(EPS_LN, 2048, (R_B + 56, R_B + 48, SCR + 16))
    out1_bf = [V_(R_A + 4 * c, T, BF16) for c in range(DC)]
    for oc in range(DC):
        ssem = "soA" if oc % 2 == 0 else "soB"
        o1f = sum1_sb[oc]
        em.emit("vector", "tensor_mul", o1f, o1f, psA, waits=[("tm", tm_bc1)], inc="vm")
        em.emit("vector", "tensor_add", o1f, o1f, psB, inc="vm")
        em.emit("vector", "tensor_scalar", o1f, o1f, par["ln1g"][:, oc:oc + 1],
                par["ln1b"][:, oc:oc + 1], op0=ALU.mult, op1=ALU.add, inc="vm")
        vmv = em.now("vm")
        em.emit("scalar", "dma_start", out=out1f_sp[oc], in_=o1f, waits=[("vm", vmv)],
                inc=ssem, incv=16, serial=False)
    em.ewait("scalar", ("soA", em.now("soA")), ("soB", em.now("soB")))
    em.emit("scalar", "sem_inc", "sp", DC * 16)
    for oc in range(DC):
        em.emit("scalar", "activation", out1_bf[oc], sum1_sb[oc], AF.Copy, inc="sm")
    SP_OUT1 = em.now("sp")
    SM_OUT1 = em.now("sm")

    if "out1" in taps:
        tap_o1 = nc.dram_tensor("tap_out1", [DC, 128, T], F32, kind="ExternalOutput").ap()
        for oc in range(DC):
            em.emit("gpsimd", "dma_start", out=tap_o1[oc], in_=out1_bf[oc],
                    waits=[("sm", SM_OUT1)], inc="tst", incv=16)

    if upto == "G":
        return _stop_here()

    H_H = ([V_(XNBF + 2 * i, HALF, BF16) for i in range(16)] +
           [V_(R_A + 32 + 2 * (i - 16), HALF, BF16) for i in range(16, 32)])
    w1slab = [V_(SCR, 1024, BF16), V_(SCR + 2, 1024, BF16)]
    w2_sb = V_(R_B, FC * 1024, BF16)
    elu1 = V_(SCR + 4, 1024, F32)
    o1chb4 = [V_(SCR, 1024, F32), V_(SCR + 4, 1024, F32),
              V_(SCR + 8, 1024, F32), V_(SCR + 12, 1024, F32)]
    elu2 = V_(SCR + 8, 1024, F32)
    o1ch = V_(SCR + 12, 1024, F32)
    sum2ch = V_(SCR + 16, 1024, F32)
    sq2ch = V_(SCR + 20, 1024, F32)

    em.emit("gpsimd", "dma_start", out=w2_sb.rearrange("p (k n) -> p k n", k=FC),
            in_=w2_hbm.rearrange("k p n -> p k n"),
            waits=[("vm", em.now("vm")), ("tm", em.now("tm")), ("sm", em.now("sm")),
                   ("sp", SP_OUT1)], inc="wl", incv=16)
    WL_W2 = em.now("wl")

    for hf in range(2):
        tmbase = em.now("tm")
        emarks = {}
        for oc in range(FC):
            w = [("sm", SM_OUT1), ("sp", SP_OUT1)]
            if oc >= 2:
                w.append(("tm", tmbase + oc - 1))
            else:
                w.append(("tm", tmbase))
                w += [(f"rl{i}", em.now(f"rl{i}")) for i in range(4)]
                w.append(("vm", em.now("vm")))
            wsem = "wlA" if oc % 2 == 0 else "wlB"
            em.emit("gpsimd", "dma_start",
                    out=w1slab[oc % 2].rearrange("p (k n) -> p k n", k=DC),
                    in_=w1_hbm[oc], waits=w, inc=wsem, incv=16, serial=False)
            pso = 0 if oc % 2 == 0 else 1024
            tw = [(wsem, em.now(wsem)), ("sm", SM_OUT1), ("sp", SP_OUT1),
                  ("vm", emarks[oc - 2] if oc >= 2 else em.now("vm"))]
            first = True
            for k in range(DC):
                for tc2 in range(2):
                    em.emit("tensor", "matmul",
                            psA[:, pso + tc2 * 512: pso + (tc2 + 1) * 512],
                            w1slab[oc % 2][:, k * 128:(k + 1) * 128],
                            out1_bf[k][:, hf * HALF + tc2 * 512: hf * HALF + (tc2 + 1) * 512],
                            start=(k == 0), stop=(k == DC - 1), waits=tw if first else ())
                    first = False
            em.emit("tensor", None, inc="tm")
            tmv = em.now("tm")
            ps = psA[:, pso:pso + 1024]
            em.emit("scalar", "activation", elu1, ps, AF.Relu,
                    bias=par["negb1"][:, oc:oc + 1], scale=-1.0,
                    waits=[("tm", tmv), ("vm", em.now("vm"))], inc="sm")
            em.emit("scalar", "activation", elu2, elu1, AF.Exp, scale=-1.0, inc="sm")
            smv = em.now("sm")
            em.emit("vector", "scalar_tensor_tensor", elu2, elu1, 1.0, elu2,
                    op0=ALU.mult, op1=ALU.add, waits=[("sm", smv)], inc="vm")
            em.emit("vector", "scalar_tensor_tensor", elu2, elu2, 1.0, ps,
                    op0=ALU.mult, op1=ALU.add, inc="vm")
            em.emit("vector", "tensor_scalar_add", H_H[oc], elu2,
                    par["b1m1"][:, oc:oc + 1], inc="vm")
            emarks[oc] = em.now("vm")
        assert em.now("tm") == tmbase + FC

        if "helu" in taps:
            if hf == 0:
                tap_he = nc.dram_tensor("tap_helu", [2, FC, 128, HALF], F32,
                                        kind="ExternalOutput").ap()
                globals()["_tap_he"] = tap_he
            tap_he = globals()["_tap_he"]
            for k in range(FC):
                em.emit("gpsimd", "dma_start", out=tap_he[hf][k], in_=H_H[k],
                        waits=[("vm", em.now("vm"))], inc="tst", incv=16)

        smarks = {}
        for oc in range(DC):
            w = [("sp", SP_OUT1)]
            if oc == 0:
                w += [("vm", em.now("vm")), ("sm", em.now("sm"))]
            if oc >= 1:
                w += [("vm", em.now("vm")), ("sm", em.now("sm"))]
            em.emit("sync", "dma_start", out=o1ch,
                    in_=out1f_sp[oc][:, hf * HALF:(hf + 1) * HALF], waits=w,
                    inc="xl", incv=16)
            xlv = em.now("xl")
            pso = 0 if oc % 2 == 0 else 1024
            tw = [("wl", WL_W2),
                  ("vm", smarks[oc - 2] if oc >= 2 else em.now("vm"))]
            first = True
            for k in range(FC):
                for tc2 in range(2):
                    em.emit("tensor", "matmul",
                            psA[:, pso + tc2 * 512: pso + (tc2 + 1) * 512],
                            w2_sb[:, k * 1024 + oc * 128: k * 1024 + (oc + 1) * 128],
                            H_H[k][:, tc2 * 512:(tc2 + 1) * 512],
                            start=(k == 0), stop=(k == FC - 1), waits=tw if first else ())
                    first = False
            em.emit("tensor", None, inc="tm")
            tmv = em.now("tm")
            ps = psA[:, pso:pso + 1024]
            em.emit("vector", "scalar_tensor_tensor", sum2ch, ps, par["b2v"][:, oc:oc + 1],
                    o1ch, op0=ALU.add, op1=ALU.add, waits=[("tm", tmv), ("xl", xlv)],
                    inc="vm")
            vmv = em.now("vm")
            smarks[oc] = vmv
            em.emit("sync", "dma_start", out=sum2_sp[hf][oc], in_=sum2ch, waits=[("vm", vmv)],
                    inc="sp", incv=16)
            em.emit("scalar", "activation", sq2ch, sum2ch, AF.Square, waits=[("vm", vmv)],
                    inc="sm")
            smv = em.now("sm")
            for tc2 in range(2):
                em.emit("tensor", "matmul", psB[0:1, tc2 * 512:(tc2 + 1) * 512],
                        ones_col, sum2ch[:, tc2 * 512:(tc2 + 1) * 512],
                        start=(oc == 0), stop=(oc == DC - 1),
                        waits=[("vm", vmv)] if tc2 == 0 else ())
            for tc2 in range(2):
                em.emit("tensor", "matmul", psB[32:33, tc2 * 512:(tc2 + 1) * 512],
                        ones_col, sq2ch[:, tc2 * 512:(tc2 + 1) * 512],
                        start=(oc == 0), stop=(oc == DC - 1), skip_group_check=True,
                        waits=[("sm", smv)] if tc2 == 0 else ())
            em.emit("tensor", None, inc="tm")
        SP_SUM2 = em.now("sp")

        tm_bc2 = emit_ln_rows(EPS_LN, HALF, XNBF)
        ap2 = {}
        for oc in range(DC):
            rsem = f"rl{oc % 4}"
            ssem = f"so{oc % 4}"
            w = [("sp", SP_SUM2), ("tm", tm_bc2)]
            if oc >= 4:
                w.append((ssem, ap2[oc - 4]))
            ob = o1chb4[oc % 4]
            em.emit("sync", "dma_start", out=ob, in_=sum2_sp[hf][oc], waits=w,
                    inc=rsem, incv=16, serial=False)
            rlv = em.now(rsem)
            em.emit("vector", "tensor_mul", ob, ob, psA[:, 0:HALF],
                    waits=[(rsem, rlv), ("tm", tm_bc2)], inc="vm")
            em.emit("vector", "tensor_add", ob, ob, psB[:, 0:HALF], inc="vm")
            em.emit("vector", "tensor_scalar", ob, ob, par["ln2g"][:, oc:oc + 1],
                    par["ln2b"][:, oc:oc + 1], op0=ALU.mult, op1=ALU.add, inc="vm")
            em.emit("scalar", "dma_start", out=attnb_sp[hf][oc], in_=ob,
                    waits=[("vm", em.now("vm"))], inc=ssem, incv=16, serial=False)
            ap2[oc] = em.now(ssem)
    em.ewait("scalar", *[(f"so{i}", em.now(f"so{i}")) for i in range(4)])
    em.emit("scalar", "sem_inc", "sp", 4 * DC * 16)
    SP_ATTNB = em.now("sp")

    if "attnb" in taps:
        tap_ab = nc.dram_tensor("tap_attnb", [2, DC, 128, HALF], F32,
                                kind="ExternalOutput").ap()
        for hf in range(2):
            for oc in range(DC):
                em.emit("sync", "dma_start", out=tap_ab[hf][oc], in_=attnb_sp[hf][oc],
                        waits=[("sp", SP_ATTNB)], inc="st", incv=16)

    if upto == "H":
        return _stop_here()

    xn2_bf = [V_(R_A + 4 * c, T, BF16) for c in range(DC)]
    em.emit("sync", "dma_start", out=xn2_bf[0], in_=xnb_sp[0],
            waits=[("sp", SP_XN)] + em.allw(), inc="xl", incv=16)
    for c in range(1, DC):
        em.emit("sync", "dma_start", out=xn2_bf[c], in_=xnb_sp[c], inc="xl", incv=16)
    XL_XN2 = em.now("xl")

    tslab = [V_(SCR, 4096, BF16), V_(SCR + 8, 4096, BF16)]
    hslab = [V_(SCR, 4096, BF16), V_(SCR + 8, 4096, BF16)]
    silu_s = V_(SCR + 16, 1024, F32)
    silu_s2 = V_(SCR + 20, 1024, F32)
    y3ch = V_(SCR + 16, 1024, F32)
    abchb = [V_(6.5625, 1024, F32), V_(SCR + 20, 1024, F32)]

    for hf in range(2):
        TH = [V_(R_B + 2 * i, HALF, BF16) for i in range(FC)]
        G1 = ([V_(XNBF + 2 * i, HALF, BF16) for i in range(16)] +
              [V_(R_A + 32 + 2 * (i - 16), HALF, BF16) for i in range(16, 32)])
        G2 = TH

        tmbase = em.now("tm")
        tmarks = {}
        for oc in range(DC):
            w = [("xl", XL_XN2)]
            if oc == 0:
                w += [("tm", em.now("tm")), ("vm", em.now("vm")), ("sm", em.now("sm")),
                      ("sp", SP_ATTNB), ("st", em.now("st"))]
                w += [(f"rl{i}", em.now(f"rl{i}")) for i in range(4)]
            else:
                w.append(("tm", tmbase + 4 * oc))
            em.emit("gpsimd", "dma_start",
                    out=tslab[0].rearrange("p (k n) -> p k n", k=DC),
                    in_=w1t_hbm.rearrange("k p n -> p k n")[:, :, oc * 512:(oc + 1) * 512],
                    waits=w, inc="wt0", incv=16, serial=False)
            em.emit("gpsimd", "dma_start",
                    out=tslab[1].rearrange("p (k n) -> p k n", k=DC),
                    in_=w2t_hbm.rearrange("k p n -> p k n")[:, :, oc * 512:(oc + 1) * 512],
                    inc="wt1", incv=16, serial=False)
            for s4 in range(4):
                occ = oc * 4 + s4
                pst = psA if occ % 2 == 0 else psB
                sil = silu_s if occ % 2 == 0 else silu_s2
                tw = [("wt0", em.now("wt0")), ("wt1", em.now("wt1"))]
                tw.append(("vm", tmarks[occ - 2] if occ >= 2 else em.now("vm")))
                if occ < 2:
                    tw.append(("sm", em.now("sm")))
                first = True
                for k in range(DC):
                    for tc2 in range(2):
                        em.emit("tensor", "matmul",
                                pst[:, tc2 * 512:(tc2 + 1) * 512],
                                tslab[0][:, k * 512 + s4 * 128: k * 512 + (s4 + 1) * 128],
                                xn2_bf[k][:, hf * HALF + tc2 * 512: hf * HALF + (tc2 + 1) * 512],
                                start=(k == 0), stop=(k == DC - 1), waits=tw if first else ())
                        first = False
                for k in range(DC):
                    for tc2 in range(2):
                        em.emit("tensor", "matmul",
                                pst[:, 1024 + tc2 * 512: 1024 + (tc2 + 1) * 512],
                                tslab[1][:, k * 512 + s4 * 128: k * 512 + (s4 + 1) * 128],
                                xn2_bf[k][:, hf * HALF + tc2 * 512: hf * HALF + (tc2 + 1) * 512],
                                start=(k == 0), stop=(k == DC - 1))
                em.emit("tensor", None, inc="tm")
                tmv = em.now("tm")
                em.emit("scalar", "activation", sil, pst[:, 0:1024], AF.Silu,
                        scale=par["w1s"][:, occ:occ + 1], waits=[("tm", tmv)], inc="sm")
                em.emit("vector", "scalar_tensor_tensor", TH[occ], pst[:, 1024:2048],
                        par["w2s"][:, occ:occ + 1], sil, op0=ALU.mult, op1=ALU.mult,
                        waits=[("sm", em.now("sm"))], inc="vm")
                tmarks[occ] = em.now("vm")
        assert em.now("tm") == tmbase + 4 * DC
        VM_TH = em.now("vm")

        def big44(src_tiles, dst_tiles, w_sg, scale_par, src_sem, src_val):
            tmbase2 = em.now("tm")
            gmarks = {}
            for oc in range(FC):
                w = [(src_sem, src_val)]
                if oc == 0:
                    w += [("tm", em.now("tm")), ("sm", em.now("sm")), ("vm", em.now("vm")),
                          ("st", em.now("st"))]
                else:
                    w.append(("tm", tmbase2 + oc - 1))
                wsem = "whA" if oc % 2 == 0 else "whB"
                em.emit("gpsimd", "dma_start",
                        out=hslab[oc % 2].rearrange("p (k n) -> p k n", k=FC),
                        in_=w_sg[oc], waits=w, inc=wsem, incv=16, serial=False)
                pso = 0 if oc % 2 == 0 else 1024
                tw = [(wsem, em.now(wsem)), (src_sem, src_val)]
                tw.append(("sm", gmarks[oc - 2] if oc >= 2 else em.now("sm")))
                first = True
                for k in range(FC):
                    for tc2 in range(2):
                        em.emit("tensor", "matmul",
                                psA[:, pso + tc2 * 512: pso + (tc2 + 1) * 512],
                                hslab[oc % 2][:, k * 128:(k + 1) * 128],
                                src_tiles[k][:, tc2 * 512:(tc2 + 1) * 512],
                                start=(k == 0), stop=(k == FC - 1), waits=tw if first else ())
                        first = False
                em.emit("tensor", None, inc="tm")
                tmv = em.now("tm")
                em.emit("scalar", "activation", dst_tiles[oc], psA[:, pso:pso + 1024],
                        AF.Gelu, scale=scale_par[:, oc:oc + 1], waits=[("tm", tmv)],
                        inc="sm")
                gmarks[oc] = em.now("sm")
            assert em.now("tm") == tmbase2 + FC
            return em.now("sm")

        if upto == "I1" and hf == 0:
            return _stop_here()
        sm_g1 = big44(TH, G1, h1t_hbm, par["h1s"], "vm", VM_TH)
        sm_g2 = big44(G1, G2, h2t_hbm, par["h2s"], "sm", sm_g1)

        if upto == "I2" and hf == 0:
            return _stop_here()
        xh = [V_(R_A + 32 + 4 * i, 1024, F32) for i in range(8)]
        for i in range(8):
            em.emit("sync", "dma_start", out=xh[i],
                    in_=x_hbm[hf * HALF + i * 128: hf * HALF + (i + 1) * 128, :],
                    waits=[("sm", sm_g2)] if i == 0 else (), inc="xl", incv=16)
        XL_XH = em.now("xl")
        out_tm = [V_(XNBF + 4 * i, 1024, F32) for i in range(8)]
        tmbase3 = em.now("tm")
        fmarks = {}
        for oc in range(DC):
            w = [("sm", sm_g2)]
            if oc == 0:
                w += [("tm", em.now("tm")), ("vm", em.now("vm"))]
            else:
                w.append(("tm", tmbase3 + 2 * oc - 1))
            wsem = "whA" if oc % 2 == 0 else "whB"
            em.emit("gpsimd", "dma_start",
                    out=hslab[oc % 2].rearrange("p (k n) -> p k n", k=FC),
                    in_=w3t_hbm[oc], waits=w, inc=wsem, incv=16, serial=False)
            tw = [(wsem, em.now(wsem)), ("vm", em.now("vm"))]
            first = True
            for k in range(FC):
                for tc2 in range(2):
                    em.emit("tensor", "matmul", psA[:, tc2 * 512:(tc2 + 1) * 512],
                            hslab[oc % 2][:, k * 128:(k + 1) * 128],
                            G2[k][:, tc2 * 512:(tc2 + 1) * 512],
                            start=(k == 0), stop=(k == FC - 1), waits=tw if first else ())
                    first = False
            em.emit("tensor", None, inc="tm")
            tmv = em.now("tm")
            rsem = "rlA" if oc % 2 == 0 else "rlB"
            w = [("sp", SP_ATTNB)]
            if oc >= 2:
                w.append(("vm", fmarks[oc - 2]))
            ab = abchb[oc % 2]
            em.emit("sync", "dma_start", out=ab, in_=attnb_sp[hf][oc], waits=w,
                    inc=rsem, incv=16, serial=False)
            rlv = em.now(rsem)
            em.emit("vector", "scalar_tensor_tensor", y3ch, psA[:, 0:1024],
                    par["w3s"][:, oc:oc + 1], ab, op0=ALU.mult, op1=ALU.add,
                    waits=[("tm", tmv), (rsem, rlv)], inc="vm")
            fmarks[oc] = em.now("vm")
            vmv = em.now("vm")
            for p in range(8):
                em.emit("tensor", "transpose",
                        psB[:, 1024 + p * 128: 1024 + (p + 1) * 128],
                        y3ch[:, p * 128:(p + 1) * 128], id_f,
                        waits=[("vm", vmv)] if p == 0 else ())
            em.emit("tensor", None, inc="tm")
            tmv2 = em.now("tm")
            for p in range(8):
                em.emit("vector", "tensor_add", out_tm[p][:, oc * 128:(oc + 1) * 128],
                        psB[:, 1024 + p * 128: 1024 + (p + 1) * 128],
                        xh[p][:, oc * 128:(oc + 1) * 128],
                        waits=[("tm", tmv2), ("xl", XL_XH)] if p == 0 else (),
                        inc="vm" if p == 7 else None)
        assert em.now("tm") == tmbase3 + 2 * DC
        VM_OUT = em.now("vm")
        for i in range(8):
            em.emit("sync", "dma_start",
                    out=out_hbm[hf * HALF + i * 128: hf * HALF + (i + 1) * 128, :],
                    in_=out_tm[i], waits=[("vm", VM_OUT)] if i == 0 else (),
                    inc="st", incv=16)

    em.ewait("sync", ("st", em.now("st")))
    em.ewait("gpsimd", ("st", em.now("st")))
    if em.now("tst"):
        em.ewait("gpsimd", ("tst", em.now("tst")))

    em.realize(nc)
    return nc


def prepare_in_maps(inputs):
    x = np.ascontiguousarray(np.asarray(inputs["L0_attn_x"], np.float32).reshape(B * N, D))

    def t2d(a):
        return np.ascontiguousarray(np.asarray(a, np.float32))

    def slabgrid(w):
        K, NN = w.shape
        return np.ascontiguousarray(
            w.reshape(K // 128, 128, NN // 128, 128).transpose(2, 1, 0, 3))

    def kchunks(w):
        K, NN = w.shape
        return np.ascontiguousarray(w.reshape(K // 128, 128, NN))

    def vec128(v):
        v = t2d(v).ravel()
        return np.ascontiguousarray(v.reshape(-1, 128).T)

    common = {
        "wq": t2d(inputs["L0_attn_Wq"]).reshape(D, D),
        "wk": t2d(inputs["L0_attn_Wk"]).reshape(D, D),
        "wv": t2d(inputs["L0_attn_Wv"]).reshape(D, D),
        "wo": t2d(inputs["L0_attn_Wo"]).reshape(D, D),
        "proj": t2d(inputs["L0_attn_proj"]),
        "w1": slabgrid(t2d(inputs["L0_attn_W1"])),
        "w2": kchunks(t2d(inputs["L0_attn_W2"])),
        "w1t": kchunks(t2d(inputs["L0_ffn_w1_w"]).T),
        "w2t": kchunks(t2d(inputs["L0_ffn_w2_w"]).T),
        "h1t": slabgrid(t2d(inputs["L0_ffn_h1_w"]).T),
        "h2t": slabgrid(t2d(inputs["L0_ffn_h2_w"]).T),
        "w3t": slabgrid(t2d(inputs["L0_ffn_w3_w"]).T),
        "b1v": vec128(inputs["L0_attn_b1"]), "b2v": vec128(inputs["L0_attn_b2"]),
        "ln1g": vec128(inputs["L0_attn_ln1_g"]), "ln1b": vec128(inputs["L0_attn_ln1_b"]),
        "ln2g": vec128(inputs["L0_attn_ln2_g"]), "ln2b": vec128(inputs["L0_attn_ln2_b"]),
        "w1s": vec128(inputs["L0_ffn_w1_s"]), "w2s": vec128(inputs["L0_ffn_w2_s"]),
        "h1s": vec128(inputs["L0_ffn_h1_s"]), "h2s": vec128(inputs["L0_ffn_h2_s"]),
        "w3s": vec128(inputs["L0_ffn_w3_s"]),
    }
    in_maps = []
    for c in range(NCORES):
        m = dict(common)
        m["x"] = np.ascontiguousarray(x[c * T:(c + 1) * T])
        in_maps.append(m)
    return in_maps


def kernel(**inputs):
    in_maps = prepare_in_maps(inputs)
    nc = build()
    res = run_bass_kernel_spmd(nc, in_maps, core_ids=list(range(NCORES)))
    out = np.concatenate([res.results[c]["out"] for c in range(NCORES)], axis=0)
    return out.reshape(B, N, D).astype(np.float32)


# version 09
# speedup vs baseline: 1.1743x; 1.0336x over previous
import numpy as np
import ml_dtypes
import concourse.bass as bass
import concourse.mybir as mybir
from concourse.bass_utils import run_bass_kernel_spmd

F32 = mybir.dt.float32
BF16 = mybir.dt.bfloat16
AF = mybir.ActivationFunctionType
ALU = mybir.AluOpType
AX = mybir.AxisListType

NCORES = 8
B = 2
N = 8192
T = 2048
NT = 16
D = 1024
DC = 8
H = 16
DH = 64
M = 64
DFF = 4096
FC = 32
HALF = 1024
DN = 2.0 ** -1.5
RATIO = 0.125
EPSK = 1e-6
EPS_LN0 = 1e-5
EPS_LN = 1e-6
NEG = -3.0e38

ENGINES = ("sync", "gpsimd", "scalar", "vector", "tensor")
KB = 1024


class Em:
    def __init__(self):
        self.steps = {e: [] for e in ENGINES}
        self.sems = {}
        self._maxwait = {e: {} for e in ENGINES}

    def now(self, sem):
        return self.sems.setdefault(sem, 0)

    def emit(self, eng, method, *args, waits=(), inc=None, incv=1, serial=True, **kw):
        w = []
        for (s, v) in waits:
            self.sems.setdefault(s, 0)
            assert v <= self.sems[s], (eng, method, s, v, self.sems[s])
            if v > 0 and v > self._maxwait[eng].get(s, -1):
                w.append((s, v))
                self._maxwait[eng][s] = v
        if method == "sem_inc":
            self.steps[eng].append((method, args, kw, tuple(w), None, 0))
            self.sems[args[0]] = self.sems.setdefault(args[0], 0) + args[1]
            return
        if method is None and inc is not None:
            for i in range(len(self.steps[eng]) - 1, -1, -1):
                m2, a2, k2, w2, i2, v2 = self.steps[eng][i]
                if m2 is not None:
                    assert i2 is None, ("double marker on", eng, m2, i2, inc)
                    self.steps[eng][i] = (m2, a2, k2, w2, inc, incv)
                    break
            else:
                raise AssertionError("marker with no prior instruction")
            if w:
                self.steps[eng].append((None, (), {}, tuple(w), None, 0))
            self.sems[inc] = self.sems.setdefault(inc, 0) + incv
            return
        self.steps[eng].append((method, args, kw, tuple(w), inc, incv))
        if inc is not None:
            self.sems[inc] = self.sems.setdefault(inc, 0) + incv
        if method == "dma_start" and inc is not None and serial:
            self.ewait(eng, (inc, self.sems[inc]))

    def ewait(self, eng, *waits):
        self.emit(eng, None, waits=waits)

    def fence(self, eng, sem):
        for i in range(len(self.steps[eng]) - 1, -1, -1):
            m2, a2, k2, w2, i2, v2 = self.steps[eng][i]
            if m2 is not None:
                if i2 is None:
                    self.steps[eng][i] = (m2, a2, k2, w2, sem, 1)
                    self.sems[sem] = self.sems.setdefault(sem, 0) + 1
                    self.ewait(eng, (sem, self.now(sem)))
                else:
                    self.ewait(eng, (i2, self.now(i2)))
                return
        raise AssertionError("fence with no prior instruction")

    def allw(self):
        return [("tm", self.now("tm")), ("vm", self.now("vm")), ("sm", self.now("sm"))]

    def check(self):
        idx = {e: 0 for e in ENGINES}
        cnt = {s: 0 for s in self.sems}
        progress = True
        while progress:
            progress = False
            for e in ENGINES:
                while idx[e] < len(self.steps[e]):
                    method, args, kw, waits, inc, incv = self.steps[e][idx[e]]
                    if all(cnt[s] >= v for (s, v) in waits):
                        if method == "sem_inc":
                            cnt[args[0]] += args[1]
                        if inc is not None:
                            cnt[inc] += incv
                        idx[e] += 1
                        progress = True
                    else:
                        break
        stuck = {e: idx[e] for e in ENGINES if idx[e] < len(self.steps[e])}
        if stuck:
            for e, i in stuck.items():
                method, args, kw, waits, inc, incv = self.steps[e][i]
                unmet = [(s, v, cnt[s]) for (s, v) in waits if cnt[s] < v]
                print(f"DEADLOCK {e} step {i}/{len(self.steps[e])} {method} unmet={unmet}")
            raise RuntimeError("static deadlock detected")
        for s, v in cnt.items():
            assert v == self.sems[s], (s, v, self.sems[s])

    def realize(self, nc):
        self.check()
        handles = {name: nc.alloc_semaphore(name) for name in self.sems}
        with nc.Block() as block:
            for engname in ENGINES:
                def make(engname=engname):
                    def f(eng):
                        for (method, args, kw, waits, inc, incv) in self.steps[engname]:
                            for (s, v) in waits:
                                eng.wait_ge(handles[s], v)
                            if method is None:
                                continue
                            if method == "sem_inc":
                                eng.sem_inc(handles[args[0]], args[1])
                                continue
                            ins = getattr(eng, method)(*args, **kw)
                            if inc is not None:
                                ins.then_inc(handles[inc], incv)
                    return f
                getattr(block, engname)(make())


class Arena:
    def __init__(self, nc, cap_bytes):
        self.cap = cap_bytes
        self.t = nc.alloc_sbuf_tensor("arena", [128, cap_bytes], mybir.dt.uint8)

    def view(self, off_kb, cols, dt, parts=128):
        esz = 4 if dt == F32 else 2
        off = int(off_kb * KB)
        assert off % 32 == 0, off_kb
        assert off + cols * esz <= self.cap, (off_kb, cols, dt)
        return self.t[0:parts, off:off + cols * esz].bitcast(dt)


def build(taps=(), upto=None, nta=NT):
    taps = set(taps)
    nc = bass.Bass("TRN2", target_bir_lowering=False, detect_race_conditions=False)
    em = Em()

    x_hbm = nc.dram_tensor("x", [T, D], F32, kind="ExternalInput").ap()
    wq_hbm = nc.dram_tensor("wq", [D, D], F32, kind="ExternalInput").ap()
    wk_hbm = nc.dram_tensor("wk", [D, D], F32, kind="ExternalInput").ap()
    wv_hbm = nc.dram_tensor("wv", [D, D], F32, kind="ExternalInput").ap()
    wo_hbm = nc.dram_tensor("wo", [D, D], F32, kind="ExternalInput").ap()
    proj_hbm = nc.dram_tensor("proj", [M, DH], F32, kind="ExternalInput").ap()
    w1_hbm = nc.dram_tensor("w1", [FC, 128, DC, 128], F32, kind="ExternalInput").ap()
    w2_hbm = nc.dram_tensor("w2", [FC, 128, D], F32, kind="ExternalInput").ap()
    w1t_hbm = nc.dram_tensor("w1t", [DC, 128, DFF], F32, kind="ExternalInput").ap()
    w2t_hbm = nc.dram_tensor("w2t", [DC, 128, DFF], F32, kind="ExternalInput").ap()
    h1t_hbm = nc.dram_tensor("h1t", [FC, 128, FC, 128], F32, kind="ExternalInput").ap()
    h2t_hbm = nc.dram_tensor("h2t", [FC, 128, FC, 128], F32, kind="ExternalInput").ap()
    w3t_hbm = nc.dram_tensor("w3t", [DC, 128, FC, 128], F32, kind="ExternalInput").ap()
    vecs_hbm = {}
    for nm, ncols in [("b1v", FC), ("w1s", FC), ("w2s", FC), ("h1s", FC), ("h2s", FC),
                      ("b2v", DC), ("ln1g", DC), ("ln1b", DC), ("ln2g", DC), ("ln2b", DC),
                      ("w3s", DC)]:
        vecs_hbm[nm] = nc.dram_tensor(nm, [128, ncols], F32, kind="ExternalInput").ap()

    out_hbm = nc.dram_tensor("out", [T, D], F32, kind="ExternalOutput").ap()

    xnf_sp = nc.dram_tensor("xnf_sp", [DC, 128, T], F32).ap()
    xnb_sp = nc.dram_tensor("xnb_sp", [DC, 128, T], BF16).ap()
    sum1_sp = nc.dram_tensor("sum1_sp", [DC, 128, T], F32).ap()
    out1f_sp = nc.dram_tensor("out1f_sp", [DC, 128, T], F32).ap()
    sum2_sp = nc.dram_tensor("sum2_sp", [2, DC, 128, HALF], F32).ap()
    attnb_sp = nc.dram_tensor("attnb_sp", [2, DC, 128, HALF], F32).ap()

    w1bf = nc.dram_tensor("w1bf", [FC, 128, DC, 128], BF16).ap()
    w2bf = nc.dram_tensor("w2bf", [FC, 128, D], BF16).ap()
    w1tbf = nc.dram_tensor("w1tbf", [DC, 128, DFF], BF16).ap()
    w2tbf = nc.dram_tensor("w2tbf", [DC, 128, DFF], BF16).ap()
    h1tbf = nc.dram_tensor("h1tbf", [FC, 128, FC, 128], BF16).ap()
    h2tbf = nc.dram_tensor("h2tbf", [FC, 128, FC, 128], BF16).ap()
    w3tbf = nc.dram_tensor("w3tbf", [DC, 128, FC, 128], BF16).ap()
    ccmax_in = nc.dram_tensor("ccmax_in", [1, 8], F32)
    ccmax_out = nc.dram_tensor("ccmax_out", [1, 8], F32)
    cckv_in = nc.dram_tensor("cckv_in", [128, 520], F32)
    cckv_out = nc.dram_tensor("cckv_out", [128, 520], F32)

    id_f32_d = nc.inline_tensor(np.eye(128, dtype=np.float32), name="c_idf")
    id_bf_d = nc.inline_tensor(np.eye(128, dtype=ml_dtypes.bfloat16).view(np.uint16),
                               name="c_idb")
    bd_np = np.zeros((128, 128), np.float32)
    for kr in range(8):
        bd_np[0:64, kr * 16 + 2 * kr] = 1.0 / 16.0
        bd_np[64:128, kr * 16 + 2 * kr + 1] = 1.0 / 16.0
    bd_d = nc.inline_tensor(bd_np, name="c_bd")
    e2_np = np.zeros((16, 8 * 128), np.float32)
    for c in range(8):
        e2_np[2 * c, c * 128:c * 128 + 64] = 1.0
        e2_np[2 * c + 1, c * 128 + 64:c * 128 + 128] = 1.0
    e2_d = nc.inline_tensor(e2_np, name="c_e2")

    AR = Arena(nc, 206 * KB)
    V_ = AR.view
    id_f = V_(0.0, 128, F32)
    id_b = V_(0.5, 128, BF16)
    bd_c = V_(0.75, 128, F32)
    pd2 = V_(1.25, 128, BF16)
    kvbd = V_(1.5, 1024, BF16)
    ksbd = V_(3.5, 128, BF16)
    kvr = V_(3.75, 520, F32)
    ones_col = V_(5.8125, 1, F32)
    ones_row = V_(5.875, 128, F32)[0:1, :]
    mx128 = V_(6.375, 1, F32)
    mxloc = V_(6.4375, 16, F32)
    mxst = V_(6.5, 16, F32)
    e2_c = V_(6.5625, 1024, F32, parts=16)
    diagq_tm = V_(10.5625, 256, F32)
    diagk_tm = V_(11.5625, 256, F32)
    par = {}
    poff = 12.5625
    for nm, ncols in [("b1v", FC), ("negb1", FC), ("b1m1", FC), ("w1s", FC), ("w2s", FC),
                      ("h1s", FC), ("h2s", FC), ("b2v", DC), ("ln1g", DC), ("ln1b", DC),
                      ("ln2g", DC), ("ln2b", DC), ("w3s", DC)]:
        par[nm] = V_(poff, ncols, F32)
        poff += ncols * 4 / KB
    assert poff <= 14.0, poff

    XNBF = 14.0
    SCR = 46.0
    R_A = 70.0
    R_B = 134.0

    xn_bf = [V_(XNBF + 4 * c, T, BF16) for c in range(DC)]

    psA_t = nc.alloc_psum_tensor("psA", [128, 2048], F32)
    psB_t = nc.alloc_psum_tensor("psB", [128, 2048], F32)
    psA = psA_t[:, :]
    psB = psB_t[:, :]
    psA_bf = psA.bitcast(BF16)


    def _stop_here():
        em.emit("sync", "dma_start", out=out_hbm[0:128, :],
                in_=AR.t[0:128, 0:4096].bitcast(F32),
                waits=em.allw() + [("sp", em.now("sp"))], inc="st", incv=16)
        em.ewait("sync", ("st", em.now("st")))
        em.ewait("gpsimd", ("st", em.now("st")))
        if em.now("tst"):
            em.ewait("gpsimd", ("tst", em.now("tst")))
        em.realize(nc)
        return nc

    em.emit("sync", "dma_start", out=id_f, in_=id_f32_d.ap(), inc="xl", incv=16)
    em.emit("sync", "dma_start", out=id_b, in_=id_bf_d.ap().bitcast(BF16), inc="xl", incv=16)
    em.emit("sync", "dma_start", out=bd_c, in_=bd_d.ap(), inc="xl", incv=16)
    em.emit("sync", "dma_start", out=e2_c, in_=e2_d.ap(), inc="xl", incv=16)
    for nm in ["b1v", "w1s", "w2s", "h1s", "h2s", "b2v", "ln1g", "ln1b", "ln2g", "ln2b", "w3s"]:
        em.emit("sync", "dma_start", out=par[nm], in_=vecs_hbm[nm], inc="xl", incv=16)
    XL_CONSTS = em.now("xl")
    em.emit("vector", "tensor_scalar_add", par["b1m1"], par["b1v"], -1.0,
            waits=[("xl", XL_CONSTS)], inc="vm")
    em.emit("vector", "tensor_scalar_mul", par["negb1"], par["b1v"], -1.0, inc="vm")
    em.emit("vector", "memset", ones_col, 1.0, inc="vm")
    em.emit("vector", "memset", ones_row, 1.0, inc="vm")

    wv_sb = V_(R_B, 8 * 1024, BF16)
    wq_sb = V_(R_B + 16, 8 * 1024, BF16)
    wk_sb = V_(R_B + 32, 8 * 1024, BF16)
    proj_sb = V_(R_B + 48, 64, F32, parts=64)
    em.emit("gpsimd", "dma_start", out=wv_sb.rearrange("p (k n) -> p k n", k=8),
            in_=wv_hbm.rearrange("(k p) n -> p k n", p=128), inc="wl", incv=16)
    em.emit("gpsimd", "dma_start", out=wq_sb.rearrange("p (k n) -> p k n", k=8),
            in_=wq_hbm.rearrange("(k p) n -> p k n", p=128), inc="wl", incv=16)
    WL_WQ = em.now("wl")
    em.emit("gpsimd", "dma_start", out=wk_sb.rearrange("p (k n) -> p k n", k=8),
            in_=wk_hbm.rearrange("(k p) n -> p k n", p=128), inc="wl", incv=16)
    WL_WK = em.now("wl")
    WL_WV = 16
    em.emit("gpsimd", "dma_start", out=proj_sb, in_=proj_hbm, inc="wl", incv=16)
    WL_PROJ = em.now("wl")
    PC_ALL = 0

    em.emit("tensor", "transpose", psB[0:64, 0:64], proj_sb, id_f[0:64, 0:64],
            waits=[("wl", WL_PROJ), ("xl", XL_CONSTS)], inc="tm")
    TM_PROJ = em.now("tm")
    em.emit("vector", "memset", pd2, 0.0, inc="vm")
    em.fence("vector", "vm")
    em.emit("vector", "tensor_scalar_mul", pd2[0:64, 0:64], psB[0:64, 0:64], DN,
            waits=[("tm", TM_PROJ)], inc="vm")
    em.emit("vector", "tensor_scalar_mul", pd2[64:128, 64:128], psB[0:64, 0:64], DN, inc="vm")
    VM_PD2 = em.now("vm")

    xbuf = [V_(SCR, 1024, F32), V_(SCR + 4, 1024, F32)]
    xsq = V_(SCR + 8, 1024, F32)
    xnt2 = [V_(SCR + 12, 1024, F32), V_(SCR + 16.5, 1024, F32)]
    srow2 = [[V_(SCR + 20.5 + 0.0625 * (5 * p + i), 1, F32) for i in range(5)]
             for p in range(2)]
    xn_f32 = [V_(R_A + 8 * c, T, F32) for c in range(DC)]

    a_mark = []
    stats_done = []
    for t in range(nta + 1):
        if t < nta:
            buf = xbuf[t % 2]
            xnt = xnt2[t % 2]
            srow = srow2[t % 2]
            w = []
            if t >= 2:
                w += [("vm", a_mark[t - 2][1]), ("sm", a_mark[t - 2][2])]
            em.emit("sync", "dma_start", out=buf, in_=x_hbm[t * 128:(t + 1) * 128, :],
                    waits=w, inc="xl", incv=16)
            xlv = em.now("xl")
            em.emit("vector", "scalar_tensor_tensor", xsq, buf, 1.0, buf,
                    op0=ALU.mult, op1=ALU.mult, accum_out=srow[2],
                    waits=[("xl", xlv)], inc="vm")
            em.emit("vector", "tensor_reduce", srow[0], buf, axis=AX.X, op=ALU.add, inc="vm")
            em.fence("vector", "vm")
            em.emit("vector", "tensor_scalar_mul", srow[1], srow[0], 1.0 / D, inc="vm")
            em.emit("vector", "tensor_scalar", srow[2], srow[2], 1.0 / D, EPS_LN0,
                    op0=ALU.mult, op1=ALU.add, inc="vm")
            em.fence("vector", "vm")
            em.emit("vector", "scalar_tensor_tensor", srow[3], srow[1], -1.0, srow[1],
                    op0=ALU.mult, op1=ALU.mult, inc="vm")
            em.fence("vector", "vm")
            em.emit("vector", "tensor_add", srow[2], srow[2], srow[3], inc="vm")
            em.fence("vector", "vm")
            em.emit("scalar", "activation", srow[3], srow[2], AF.Sqrt,
                    waits=[("vm", em.now("vm"))], inc="sm")
            em.emit("vector", "reciprocal", srow[4], srow[3],
                    waits=[("sm", em.now("sm"))], inc="vm")
            em.fence("vector", "vm")
            em.emit("vector", "scalar_tensor_tensor", srow[3], srow[1], -1.0, srow[4],
                    op0=ALU.mult, op1=ALU.mult, inc="vm")
            em.fence("vector", "vm")
            em.emit("scalar", "activation", xnt, buf, AF.Identity,
                    scale=srow[4][:, 0:1], bias=srow[3][:, 0:1],
                    waits=[("vm", em.now("vm"))], inc="sm")
            stats_done.append(em.now("sm"))
        if t >= 1:
            tp = t - 1
            xntp = xnt2[tp % 2]
            pso = 0 if tp % 2 == 0 else 1024
            tw = [("sm", stats_done[tp])]
            if tp >= 2:
                tw += [("vm", a_mark[tp - 2][1]), ("sm", a_mark[tp - 2][2])]
            for c in range(DC):
                em.emit("tensor", "transpose", psA[:, pso + c * 128: pso + (c + 1) * 128],
                        xntp[:, c * 128:(c + 1) * 128], id_f,
                        waits=tw if c == 0 else ())
            em.emit("tensor", None, inc="tm")
            tmv = em.now("tm")
            for c in range(4):
                em.emit("vector", "tensor_copy", xn_f32[c][:, tp * 128:(tp + 1) * 128],
                        psA[:, pso + c * 128: pso + (c + 1) * 128],
                        waits=[("tm", tmv)] if c == 0 else (),
                        inc="vm" if c == 3 else None)
            for c in range(4, DC):
                em.emit("scalar", "activation", xn_f32[c][:, tp * 128:(tp + 1) * 128],
                        psA[:, pso + c * 128: pso + (c + 1) * 128], AF.Copy,
                        waits=[("tm", tmv)] if c == 4 else (),
                        inc="sm" if c == DC - 1 else None)
            vm_ev = em.now("vm")
            sm_ev = em.now("sm")
            for c in range(DC):
                if c < 4:
                    em.emit("scalar", "activation", xn_bf[c][:, tp * 128:(tp + 1) * 128],
                            xn_f32[c][:, tp * 128:(tp + 1) * 128], AF.Copy,
                            waits=[("vm", vm_ev)] if c == 0 else (),
                            inc="sm" if c == 3 else None)
                else:
                    em.emit("vector", "tensor_copy", xn_bf[c][:, tp * 128:(tp + 1) * 128],
                            xn_f32[c][:, tp * 128:(tp + 1) * 128],
                            waits=[("sm", sm_ev)] if c == 4 else (),
                            inc="vm" if c == DC - 1 else None)
            a_mark.append((tmv, em.now("vm"), em.now("sm")))

    VM_A = em.now("vm")
    SM_A = em.now("sm")
    for c in range(DC):
        em.emit("sync", "dma_start", out=xnf_sp[c], in_=xn_f32[c],
                waits=[("vm", VM_A)] if c == 0 else (), inc="sp", incv=16)
        em.emit("sync", "dma_start", out=xnb_sp[c], in_=xn_bf[c],
                waits=[("sm", SM_A)] if c == 0 else (), inc="sp", incv=16)
    SP_XN = em.now("sp")

    if "xn" in taps:
        tap_xn = nc.dram_tensor("tap_xn", [DC, 128, T], F32, kind="ExternalOutput").ap()
        for c in range(DC):
            em.emit("sync", "dma_start", out=tap_xn[c], in_=xn_f32[c],
                    waits=[("vm", VM_A)], inc="st", incv=16)

    if upto == "A":
        return _stop_here()

    q_bf = [V_(R_A + 4 * c, T, BF16) for c in range(DC)]
    k_bf = [V_(R_A + 32 + 4 * c, T, BF16) for c in range(DC)]
    qsq = V_(SCR, 2048, F32)
    diag_fm = V_(SCR + 8, 2048, F32, parts=16)

    def emit_qk(w_sb, wlv, dst, diag_dst_tm):
        marks = {}
        pend = None
        g = 0
        for oc in range(DC):
            for th in range(2):
                pso = 0 if g % 2 == 0 else 1024
                tw = [("wl", wlv)]
                if g >= 2:
                    tw += [("vm", marks[g - 2][0]), ("sm", marks[g - 2][1])]
                else:
                    tw += [("sm", em.now("sm")), ("vm", em.now("vm"))]
                first = True
                for k in range(DC):
                    for tc in range(2):
                        tcg = th * 2 + tc
                        em.emit("tensor", "matmul",
                                psA[:, pso + tc * 512: pso + (tc + 1) * 512],
                                w_sb[:, k * 1024 + oc * 128: k * 1024 + (oc + 1) * 128],
                                xn_bf[k][:, tcg * 512:(tcg + 1) * 512],
                                start=(k == 0), stop=(k == DC - 1),
                                waits=tw if first else ())
                        first = False
                em.emit("tensor", None, inc="tm")
                tmv = em.now("tm")
                em.emit("vector", "tensor_copy", dst[oc][:, th * 1024:(th + 1) * 1024],
                        psA[:, pso:pso + 1024],
                        waits=[("tm", tmv), ("sp", SP_XN)], inc="vm")
                em.emit("scalar", "activation", qsq[:, th * 1024:(th + 1) * 1024],
                        psA[:, pso:pso + 1024], AF.Square,
                        waits=[("tm", tmv), ("vm", em.now("vm"))], inc="sm")
                smv = em.now("sm")
                if pend is not None:
                    p_oc, p_th, p_smv = pend
                    for tc in range(2):
                        tcg = p_th * 2 + tc
                        em.emit("tensor", "matmul", psB[0:16, tcg * 512:(tcg + 1) * 512],
                                bd_c[:, p_oc * 16:(p_oc + 1) * 16],
                                qsq[:, tcg * 512:(tcg + 1) * 512],
                                start=(p_oc == 0), stop=(p_oc == DC - 1),
                                waits=([("sm", p_smv)] +
                                       ([("vm", em.now("vm"))] if g <= 2 else []))
                                if tc == 0 else ())
                pend = (oc, th, smv)
                marks[g] = (em.now("vm"), em.now("sm"))
                g += 1
        p_oc, p_th, p_smv = pend
        for tc in range(2):
            tcg = p_th * 2 + tc
            em.emit("tensor", "matmul", psB[0:16, tcg * 512:(tcg + 1) * 512],
                    bd_c[:, p_oc * 16:(p_oc + 1) * 16],
                    qsq[:, tcg * 512:(tcg + 1) * 512],
                    start=False, stop=True,
                    waits=[("sm", p_smv)] if tc == 0 else ())
        em.emit("tensor", None, inc="tm")
        tmv = em.now("tm")
        em.emit("vector", "tensor_copy", diag_fm, psB[0:16, :], waits=[("tm", tmv)], inc="vm")
        vmv = em.now("vm")
        for t in range(NT):
            tw = [("vm", vmv)] if t == 0 else []
            if t >= 4:
                tw.append(("vm", em.now("vm")))
            em.emit("tensor", "transpose",
                    psB[:, (t % 4) * 512: (t % 4) * 512 + 16],
                    diag_fm[:, t * 128:(t + 1) * 128], id_f[0:16, 0:16], waits=tw, inc="tm")
            em.emit("vector", "tensor_copy", diag_dst_tm[:, t * 16:(t + 1) * 16],
                    psB[:, (t % 4) * 512: (t % 4) * 512 + 16],
                    waits=[("tm", em.now("tm"))], inc="vm")

    emit_qk(wk_sb, WL_WK, k_bf, diagk_tm)
    c1m = {}
    VM_C1S = em.now("vm")
    for t in range(NT):
        pso = 0 if t % 2 == 0 else 1024
        tw = [("vm", VM_PD2), ("vm", c1m[t - 2] if t >= 2 else VM_C1S)]
        first = True
        for pr in range(8):
            em.emit("tensor", "matmul", psA[:, pso + pr * 128: pso + (pr + 1) * 128],
                    k_bf[pr][:, t * 128:(t + 1) * 128], pd2,
                    start=True, stop=True, waits=tw if first else ())
            first = False
        em.emit("tensor", None, inc="tm")
        tmv = em.now("tm")
        em.emit("vector", "tensor_reduce", mxloc[:, t:t + 1], psA[:, pso:pso + 1024],
                axis=AX.X, op=ALU.max, waits=[("tm", tmv)], inc="vm")
        c1m[t] = em.now("vm")
    em.fence("vector", "vm")
    mxf = srow2[0][0]
    em.emit("vector", "tensor_reduce", mxf, mxloc[:, 0:NT], axis=AX.X, op=ALU.max,
            inc="vm")
    em.fence("vector", "vm")
    em.emit("tensor", "transpose", psB[0:1, 1024:1152], mxf, id_f,
            waits=[("vm", em.now("vm"))], inc="tm")
    em.emit("vector", "memset", mxst[0:1, 0:8], NEG,
            waits=[("tm", em.now("tm"))], inc="vm")
    em.fence("vector", "vm")
    em.emit("vector", "tensor_reduce", mxst[0:1, 0:1], psB[0:1, 1024:1152], axis=AX.X,
            op=ALU.max, inc="vm")
    em.fence("vector", "vm")
    em.emit("gpsimd", "dma_start", out=ccmax_in.ap(), in_=mxst[0:1, 0:8],
            waits=[("vm", em.now("vm"))], inc="cl", incv=16)
    em.ewait("gpsimd", ("cl", em.now("cl")))
    em.emit("gpsimd", "collective_compute", "AllReduce", ALU.max,
            replica_groups=[list(range(NCORES))],
            ins=[ccmax_in.ap().opt()], outs=[ccmax_out.ap().opt()], inc="cc")
    em.ewait("gpsimd", ("cc", em.now("cc")))
    em.emit("gpsimd", "dma_start", out=mxst[0:1, 8:16], in_=ccmax_out.ap(), inc="cl", incv=16)
    CL_MX = em.now("cl")

    if upto == "C1":
        return _stop_here()

    emit_qk(wq_sb, WL_WQ, q_bf, diagq_tm)

    if "q" in taps:
        tap_q = nc.dram_tensor("tap_q", [DC, 128, T], F32, kind="ExternalOutput").ap()
        for c in range(DC):
            em.emit("gpsimd", "dma_start", out=tap_q[c], in_=q_bf[c],
                    waits=[("vm", em.now("vm"))], inc="tst", incv=16)
    if "diag" in taps:
        tap_dg = nc.dram_tensor("tap_diag", [2, 128, 256], F32, kind="ExternalOutput").ap()
        em.emit("sync", "dma_start", out=tap_dg[0], in_=diagq_tm,
                waits=[("vm", em.now("vm"))], inc="st", incv=16)
        em.emit("sync", "dma_start", out=tap_dg[1], in_=diagk_tm,
                waits=[("vm", em.now("vm"))], inc="st", incv=16)

    vaug = [V_(R_B + 16 + 2.125 * t, 1040, BF16) for t in range(NT)]
    VM_B_QK = em.now("vm")
    for t in range(NT):
        em.emit("vector", "memset", vaug[t], 1.0,
                waits=[("vm", VM_B_QK), ("tm", em.now("tm"))] if t == 0 else (),
                inc="vm" if t == NT - 1 else None)
    em.fence("vector", "vm")
    VM_VONES = em.now("vm")
    vvm = {}
    for t in range(NT):
        for nh in range(2):
            g = 2 * t + nh
            pso = (g % 4) * 512
            tw = [("wl", WL_WV), ("vm", vvm[g - 4] if g >= 4 else VM_VONES)]
            first = True
            for k in range(DC):
                em.emit("tensor", "matmul", psA[:, pso:pso + 512],
                        xn_bf[k][:, t * 128:(t + 1) * 128],
                        wv_sb[:, k * 1024 + nh * 512: k * 1024 + (nh + 1) * 512],
                        start=(k == 0), stop=(k == DC - 1), waits=tw if first else ())
                first = False
            em.emit("tensor", None, inc="tm")
            tmv = em.now("tm")
            dst = vaug[t].rearrange("p (h e) -> p h e", h=16)[:, nh * 8:(nh + 1) * 8, 0:64]
            src = psA[:, pso:pso + 512].rearrange("p (h d) -> p h d", h=8)
            em.emit("vector", "tensor_copy", dst, src, waits=[("tm", tmv)], inc="vm")
            vvm[g] = em.now("vm")
    VM_VAUG = em.now("vm")

    if upto == "B":
        return _stop_here()

    qf_tm = [V_(R_B + 2 * t, 1024, BF16) if t < 8 else V_(R_B + 50 + 2 * (t - 8), 1024, BF16)
             for t in range(NT)]
    stage = V_(SCR, 512, F32)
    biasr = V_(SCR + 2, 16, F32)
    stageE = V_(SCR + 4, 1024, F32)
    kf_t = V_(SCR + 8, 1024, BF16)

    em.emit("tensor", "matmul", psB[:, 1028:1029], ones_row, mxst[0:1, 8:9],
            start=True, stop=True, waits=[("cl", CL_MX)], inc="tm")
    em.emit("vector", "tensor_copy", mx128, psB[:, 1028:1029],
            waits=[("tm", em.now("tm"))], inc="vm")
    VM_MX128 = em.now("vm")

    ekm = {}
    for t in range(NT):
        pso = 0 if t % 2 == 0 else 1024
        if t >= 2:
            tw = [("vm", ekm[t - 2][0]), ("sm", ekm[t - 2][1])]
        else:
            tw = [("vm", em.now("vm")), ("sm", em.now("sm"))]
        first = True
        for pr in range(8):
            em.emit("tensor", "matmul", psA[:, pso + pr * 128: pso + (pr + 1) * 128],
                    k_bf[pr][:, t * 128:(t + 1) * 128], pd2,
                    start=True, stop=True, waits=tw if first else ())
            first = False
        em.emit("tensor", None, inc="tm")
        tmv = em.now("tm")
        em.emit("vector", "tensor_scalar", biasr, diagk_tm[:, t * 16:(t + 1) * 16],
                mx128[:, 0:1], -1.0, op0=ALU.add, op1=ALU.mult,
                waits=[("tm", tmv), ("vm", VM_MX128)], inc="vm")
        em.fence("vector", "vm")
        vm_b = em.now("vm")
        for h in range(8, H):
            em.emit("scalar", "activation", stageE[:, h * 64:(h + 1) * 64],
                    psA[:, pso + h * 64: pso + h * 64 + 64], AF.Exp,
                    bias=biasr[:, h:h + 1],
                    waits=[("vm", vm_b), ("tm", tmv)] if h == 8 else (),
                    inc="sm" if h == H - 1 else None)
        for h in range(8):
            em.emit("vector", "tensor_scalar_add", stage[:, h * 64:(h + 1) * 64],
                    psA[:, pso + h * 64: pso + h * 64 + 64], biasr[:, h:h + 1],
                    inc="vm" if h == 7 else None)
        em.emit("scalar", "activation", stageE[:, 0:512], stage, AF.Exp,
                waits=[("vm", em.now("vm"))], inc="sm")
        em.emit("vector", "tensor_scalar", kf_t, stageE, EPSK, RATIO,
                op0=ALU.add, op1=ALU.mult, waits=[("sm", em.now("sm"))], inc="vm")
        vm_kf = em.now("vm")
        first = True
        for h in range(H):
            ph, cg = h % 2, h // 2
            colbase = (0 if cg < 4 else 512) + (cg % 4) * 65
            em.emit("tensor", "matmul",
                    psB[ph * 64:(ph + 1) * 64, colbase:colbase + 65],
                    kf_t[:, h * 64:(h + 1) * 64],
                    vaug[t][:, h * 65:(h + 1) * 65],
                    start=(t == 0 and h in (0, 1, 8, 9)),
                    stop=(t == NT - 1 and h in (6, 7, 14, 15)),
                    skip_group_check=True,
                    waits=[("vm", vm_kf), ("vm", VM_VAUG)] if first else ())
            first = False
        em.emit("tensor", None, inc="tm")
        ekm[t] = (em.now("vm"), em.now("sm"))
    TM_KV = em.now("tm")
    em.emit("vector", "tensor_copy", kvr[:, 0:260], psB[:, 0:260],
            waits=[("tm", TM_KV)], inc="vm")
    em.emit("vector", "tensor_copy", kvr[:, 260:520], psB[:, 512:772], inc="vm")
    em.emit("gpsimd", "dma_start", out=cckv_in.ap(), in_=kvr,
            waits=[("vm", em.now("vm"))], inc="cl", incv=16)
    em.ewait("gpsimd", ("cl", em.now("cl")))
    em.emit("gpsimd", "collective_compute", "AllReduce", ALU.add,
            replica_groups=[[0, 1, 2, 3], [4, 5, 6, 7]],
            ins=[cckv_in.ap().opt()], outs=[cckv_out.ap().opt()], inc="cc")
    em.ewait("gpsimd", ("cc", em.now("cc")))
    em.emit("gpsimd", "dma_start", out=kvr, in_=cckv_out.ap(), inc="cl", incv=16)
    CL_KV = em.now("cl")

    if upto == "E":
        return _stop_here()

    ddm = {}
    for t in range(NT):
        pso = 0 if t % 2 == 0 else 1024
        if t >= 2:
            tw = [("vm", ddm[t - 2][0]), ("sm", ddm[t - 2][1])]
        else:
            tw = [("vm", em.now("vm")), ("sm", em.now("sm"))]
        first = True
        for pr in range(8):
            em.emit("tensor", "matmul", psA[:, pso + pr * 128: pso + (pr + 1) * 128],
                    q_bf[pr][:, t * 128:(t + 1) * 128], pd2,
                    start=True, stop=True, waits=tw if first else ())
            first = False
        em.emit("tensor", None, inc="tm")
        tmv = em.now("tm")
        dd3 = psA[:, pso:pso + 1024].rearrange("p (h m) -> p h m", h=H)
        em.emit("vector", "tensor_reduce", mxloc, dd3, axis=AX.X, op=ALU.max,
                waits=[("tm", tmv)], inc="vm")
        em.fence("vector", "vm")
        em.emit("vector", "scalar_tensor_tensor", biasr, mxloc, -1.0,
                diagq_tm[:, t * 16:(t + 1) * 16], op0=ALU.mult, op1=ALU.subtract, inc="vm")
        em.fence("vector", "vm")
        vm_b = em.now("vm")
        for h in range(8, H):
            em.emit("scalar", "activation", stageE[:, h * 64:(h + 1) * 64],
                    psA[:, pso + h * 64: pso + h * 64 + 64], AF.Exp,
                    bias=biasr[:, h:h + 1],
                    waits=[("vm", vm_b), ("tm", tmv)] if h == 8 else (),
                    inc="sm" if h == H - 1 else None)
        for h in range(8):
            em.emit("vector", "tensor_scalar_add", stage[:, h * 64:(h + 1) * 64],
                    psA[:, pso + h * 64: pso + h * 64 + 64], biasr[:, h:h + 1],
                    inc="vm" if h == 7 else None)
        em.emit("scalar", "activation", stageE[:, 0:512], stage, AF.Exp,
                waits=[("vm", em.now("vm"))], inc="sm")
        em.emit("vector", "tensor_scalar", qf_tm[t], stageE, EPSK, RATIO,
                op0=ALU.add, op1=ALU.mult, waits=[("sm", em.now("sm"))], inc="vm")
        ddm[t] = (em.now("vm"), em.now("sm"))
    VM_QF = em.now("vm")

    if upto == "C2":
        return _stop_here()

    qf_fm = [V_(R_A + 4 * c, T, BF16) for c in range(DC)]
    for t in range(NT):
        bank = t % 4
        tw = [("vm", VM_QF)]
        if t >= 4:
            tw.append(("vm", em.now("vm")))
        for c in range(DC):
            em.emit("tensor", "transpose",
                    psA_bf[:, bank * 1024 + c * 128: bank * 1024 + (c + 1) * 128],
                    qf_tm[t][:, c * 128:(c + 1) * 128], id_b,
                    waits=tw if c == 0 else ())
        em.emit("tensor", None, inc="tm")
        tmv = em.now("tm")
        for c in range(DC):
            em.emit("vector", "tensor_copy", qf_fm[c][:, t * 128:(t + 1) * 128],
                    psA_bf[:, bank * 1024 + c * 128: bank * 1024 + (c + 1) * 128],
                    waits=[("tm", tmv)] if c == 0 else (),
                    inc="vm" if c == DC - 1 else None)
    VM_QFFM = em.now("vm")

    if "kv" in taps:
        tap_kv = nc.dram_tensor("tap_kv", [128, 520], F32, kind="ExternalOutput").ap()
        em.emit("gpsimd", "dma_start", out=tap_kv, in_=kvr, waits=[("cl", CL_KV)],
                inc="tst", incv=16)
    if "qf" in taps:
        tap_qf = nc.dram_tensor("tap_qf", [DC, 128, T], F32, kind="ExternalOutput").ap()
        for c in range(DC):
            em.emit("gpsimd", "dma_start", out=tap_qf[c], in_=qf_fm[c],
                    waits=[("vm", VM_QFFM)], inc="tst", incv=16)

    if upto == "D":
        return _stop_here()

    em.ewait("vector", ("cl", CL_KV))
    em.emit("vector", "memset", kvbd, 0.0, inc="vm")
    em.emit("vector", "memset", ksbd, 0.0, inc="vm")
    em.fence("vector", "vm")
    for hh in range(H):
        ph, cg = hh % 2, hh // 2
        src = kvr[ph * 64:(ph + 1) * 64, cg * 65: cg * 65 + 64]
        srcs = kvr[ph * 64:(ph + 1) * 64, cg * 65 + 64: cg * 65 + 65]
        em.emit("vector", "tensor_copy",
                kvbd[ph * 64:(ph + 1) * 64, cg * 128 + ph * 64: cg * 128 + ph * 64 + 64],
                src, inc=None)
        em.emit("vector", "tensor_copy",
                ksbd[ph * 64:(ph + 1) * 64, cg * 16 + hh: cg * 16 + hh + 1],
                srcs, inc="vm" if hh == H - 1 else None)
    VM_KVBD = em.now("vm")

    for kr in range(DC):
        first = True
        for tc in range(4):
            em.emit("tensor", "matmul", psB[0:16, tc * 512:(tc + 1) * 512],
                    ksbd[:, kr * 16:(kr + 1) * 16], qf_fm[kr][:, tc * 512:(tc + 1) * 512],
                    start=(kr == 0), stop=(kr == DC - 1),
                    waits=[("vm", VM_KVBD), ("vm", VM_QFFM)] if (first and kr == 0) else ())
            first = False
    em.emit("tensor", None, inc="tm")
    rcp_den = V_(SCR, 2048, F32, parts=16)
    em.emit("vector", "tensor_copy", rcp_den, psB[0:16, :], waits=[("tm", em.now("tm"))],
            inc="vm")
    em.emit("vector", "reciprocal", rcp_den, rcp_den, inc="vm")
    VM_RCP = em.now("vm")

    if "den" in taps:
        tap_den = nc.dram_tensor("tap_den", [16, 2048], F32, kind="ExternalOutput").ap()
        em.emit("sync", "dma_start", out=tap_den, in_=rcp_den, waits=[("vm", VM_RCP)],
                inc="st", incv=16)

    attn_bf = [V_(R_A + 32 + 4 * c, T, BF16) for c in range(DC)]
    bc_sb = V_(SCR + 8, 2048, F32)
    for c in range(DC):
        tw = [("vm", VM_RCP), ("vm", em.now("vm"))]
        first = True
        for tc in range(4):
            em.emit("tensor", "matmul", psA[:, tc * 512:(tc + 1) * 512],
                    kvbd[:, c * 128:(c + 1) * 128], qf_fm[c][:, tc * 512:(tc + 1) * 512],
                    start=True, stop=True, waits=tw if first else ())
            first = False
        for tc in range(4):
            em.emit("tensor", "matmul", psB[:, tc * 512:(tc + 1) * 512],
                    e2_c[:, c * 128:(c + 1) * 128], rcp_den[:, tc * 512:(tc + 1) * 512],
                    start=True, stop=True)
        em.emit("tensor", None, inc="tm")
        em.emit("vector", "tensor_copy", bc_sb, psB, waits=[("tm", em.now("tm"))], inc="vm")
        em.emit("vector", "tensor_mul", attn_bf[c], psA, bc_sb, inc="vm")
    VM_ATTN = em.now("vm")

    wo_sb = V_(R_B, 8 * 1024, BF16)
    em.emit("gpsimd", "dma_start", out=wo_sb.rearrange("p (k n) -> p k n", k=8),
            in_=wo_hbm.rearrange("(k p) n -> p k n", p=128),
            waits=[("vm", VM_QFFM)], inc="wl", incv=16)
    WL_WO = em.now("wl")
    xnch = [V_(SCR + 8, 2048, F32), V_(SCR + 16, 2048, F32)]
    sum1_sb = ([V_(R_A + 8 * i, 2048, F32) for i in range(4)] +
               [V_(R_B + 16 + 8 * i, 2048, F32) for i in range(4)])
    sqch = V_(R_B + 48, 2048, F32)

    for oc in range(DC):
        w = [("sp", SP_XN), ("tm", TM_KV), ("vm", VM_ATTN)]
        if oc >= 2:
            w.append(("vm", em.now("vm")))
        em.emit("sync", "dma_start", out=xnch[oc % 2], in_=xnf_sp[oc], waits=w,
                inc="xl", incv=16)
        xlv = em.now("xl")
        tw = [("wl", WL_WO), ("vm", VM_ATTN), ("vm", em.now("vm")), ("sm", em.now("sm"))]
        first = True
        for kr in range(DC):
            for tc in range(4):
                em.emit("tensor", "matmul", psA[:, tc * 512:(tc + 1) * 512],
                        wo_sb[:, kr * 1024 + oc * 128: kr * 1024 + (oc + 1) * 128],
                        attn_bf[kr][:, tc * 512:(tc + 1) * 512],
                        start=(kr == 0), stop=(kr == DC - 1), waits=tw if first else ())
                first = False
        em.emit("tensor", None, inc="tm")
        tmv = em.now("tm")
        em.emit("vector", "tensor_add", sum1_sb[oc], psA, xnch[oc % 2],
                waits=[("tm", tmv), ("xl", xlv)], inc="vm")
        vmv = em.now("vm")
        em.emit("scalar", "activation", sqch, sum1_sb[oc], AF.Square,
                waits=[("vm", vmv)], inc="sm")
        smv = em.now("sm")
        for tc in range(4):
            em.emit("tensor", "matmul", psB[0:1, tc * 512:(tc + 1) * 512],
                    ones_col, sum1_sb[oc][:, tc * 512:(tc + 1) * 512],
                    start=(oc == 0), stop=(oc == DC - 1),
                    waits=[("vm", vmv)] if tc == 0 else ())
        for tc in range(4):
            em.emit("tensor", "matmul", psB[32:33, tc * 512:(tc + 1) * 512],
                    ones_col, sqch[:, tc * 512:(tc + 1) * 512],
                    start=(oc == 0), stop=(oc == DC - 1), skip_group_check=True,
                    waits=[("sm", smv)] if tc == 0 else ())
        em.emit("tensor", None, inc="tm")
    SP_SUM1 = em.now("sp")
    VM_SUM1 = em.now("vm")

    def emit_ln_rows(eps, width, rowbase_kb):
        if isinstance(rowbase_kb, tuple):
            b0, b1, b2 = rowbase_kb
        else:
            b0 = rowbase_kb
            b1 = rowbase_kb + width * 4 / KB
            b2 = rowbase_kb + 2 * width * 4 / KB
        mu_r = V_(b0, width, F32, parts=1)
        var_r = V_(b1, width, F32, parts=1)
        tmp_r = V_(b2, width, F32, parts=1)
        em.emit("vector", "tensor_scalar_mul", mu_r, psB[0:1, 0:width], 1.0 / D,
                waits=[("tm", em.now("tm"))], inc="vm")
        em.emit("vector", "tensor_scalar", var_r, psB[32:33, 0:width], 1.0 / D, eps,
                op0=ALU.mult, op1=ALU.add, inc="vm")
        em.emit("vector", "tensor_mul", tmp_r, mu_r, mu_r, inc="vm")
        em.emit("vector", "tensor_sub", var_r, var_r, tmp_r, inc="vm")
        em.emit("scalar", "activation", tmp_r, var_r, AF.Sqrt,
                waits=[("vm", em.now("vm"))], inc="sm")
        em.emit("vector", "reciprocal", var_r, tmp_r, waits=[("sm", em.now("sm"))], inc="vm")
        em.emit("vector", "scalar_tensor_tensor", mu_r, mu_r, -1.0, var_r,
                op0=ALU.mult, op1=ALU.mult, inc="vm")
        vmv = em.now("vm")
        for tc in range(width // 512):
            em.emit("tensor", "matmul", psA[:, tc * 512:(tc + 1) * 512], ones_row,
                    var_r[:, tc * 512:(tc + 1) * 512], start=True, stop=True,
                    waits=[("vm", vmv)] if tc == 0 else ())
        for tc in range(width // 512):
            em.emit("tensor", "matmul", psB[:, tc * 512:(tc + 1) * 512], ones_row,
                    mu_r[:, tc * 512:(tc + 1) * 512], start=True, stop=True)
        em.emit("tensor", None, inc="tm")
        return em.now("tm")

    tm_bc1 = emit_ln_rows(EPS_LN, 2048, (R_B + 56, R_B + 48, SCR + 16))
    out1_bf = [V_(R_A + 4 * c, T, BF16) for c in range(DC)]
    for oc in range(DC):
        ssem = "soA" if oc % 2 == 0 else "soB"
        o1f = sum1_sb[oc]
        em.emit("vector", "tensor_mul", o1f, o1f, psA, waits=[("tm", tm_bc1)], inc="vm")
        em.emit("vector", "tensor_add", o1f, o1f, psB, inc="vm")
        em.emit("vector", "tensor_scalar", o1f, o1f, par["ln1g"][:, oc:oc + 1],
                par["ln1b"][:, oc:oc + 1], op0=ALU.mult, op1=ALU.add, inc="vm")
        vmv = em.now("vm")
        em.emit("scalar", "dma_start", out=out1f_sp[oc], in_=o1f, waits=[("vm", vmv)],
                inc=ssem, incv=16, serial=False)
    em.ewait("scalar", ("soA", em.now("soA")), ("soB", em.now("soB")))
    em.emit("scalar", "sem_inc", "sp", DC * 16)
    for oc in range(DC):
        em.emit("scalar", "activation", out1_bf[oc], sum1_sb[oc], AF.Copy, inc="sm")
    SP_OUT1 = em.now("sp")
    SM_OUT1 = em.now("sm")

    if "out1" in taps:
        tap_o1 = nc.dram_tensor("tap_out1", [DC, 128, T], F32, kind="ExternalOutput").ap()
        for oc in range(DC):
            em.emit("gpsimd", "dma_start", out=tap_o1[oc], in_=out1_bf[oc],
                    waits=[("sm", SM_OUT1)], inc="tst", incv=16)

    if upto == "G":
        return _stop_here()

    H_H = ([V_(XNBF + 2 * i, HALF, BF16) for i in range(16)] +
           [V_(R_A + 32 + 2 * (i - 16), HALF, BF16) for i in range(16, 32)])
    w1slab = [V_(SCR, 1024, BF16), V_(SCR + 2, 1024, BF16)]
    w2_sb = V_(R_B, FC * 1024, BF16)
    elu1 = V_(SCR + 4, 1024, F32)
    o1chb4 = [V_(SCR, 1024, F32), V_(SCR + 4, 1024, F32),
              V_(SCR + 8, 1024, F32), V_(SCR + 12, 1024, F32)]
    elu2 = V_(SCR + 8, 1024, F32)
    o1ch = V_(SCR + 12, 1024, F32)
    sum2ch = V_(SCR + 16, 1024, F32)
    sq2ch = V_(SCR + 20, 1024, F32)

    em.emit("gpsimd", "dma_start", out=w2_sb.rearrange("p (k n) -> p k n", k=FC),
            in_=w2_hbm.rearrange("k p n -> p k n"),
            waits=[("vm", em.now("vm")), ("tm", em.now("tm")), ("sm", em.now("sm")),
                   ("sp", SP_OUT1)], inc="wl", incv=16)
    WL_W2 = em.now("wl")

    for hf in range(2):
        tmbase = em.now("tm")
        emarks = {}
        for oc in range(FC):
            w = [("sm", SM_OUT1), ("sp", SP_OUT1)]
            if oc >= 2:
                w.append(("tm", tmbase + oc - 1))
            else:
                w.append(("tm", tmbase))
                w += [(f"rl{i}", em.now(f"rl{i}")) for i in range(4)]
                w.append(("vm", em.now("vm")))
            wsem = "wlA" if oc % 2 == 0 else "wlB"
            em.emit("gpsimd", "dma_start",
                    out=w1slab[oc % 2].rearrange("p (k n) -> p k n", k=DC),
                    in_=w1_hbm[oc], waits=w, inc=wsem, incv=16, serial=False)
            pso = 0 if oc % 2 == 0 else 1024
            tw = [(wsem, em.now(wsem)), ("sm", SM_OUT1), ("sp", SP_OUT1),
                  ("vm", emarks[oc - 2] if oc >= 2 else em.now("vm"))]
            first = True
            for k in range(DC):
                for tc2 in range(2):
                    em.emit("tensor", "matmul",
                            psA[:, pso + tc2 * 512: pso + (tc2 + 1) * 512],
                            w1slab[oc % 2][:, k * 128:(k + 1) * 128],
                            out1_bf[k][:, hf * HALF + tc2 * 512: hf * HALF + (tc2 + 1) * 512],
                            start=(k == 0), stop=(k == DC - 1), waits=tw if first else ())
                    first = False
            em.emit("tensor", None, inc="tm")
            tmv = em.now("tm")
            ps = psA[:, pso:pso + 1024]
            em.emit("scalar", "activation", elu1, ps, AF.Relu,
                    bias=par["negb1"][:, oc:oc + 1], scale=-1.0,
                    waits=[("tm", tmv), ("vm", em.now("vm"))], inc="sm")
            em.emit("scalar", "activation", elu2, elu1, AF.Exp, scale=-1.0, inc="sm")
            smv = em.now("sm")
            em.emit("vector", "scalar_tensor_tensor", elu2, elu1, 1.0, elu2,
                    op0=ALU.mult, op1=ALU.add, waits=[("sm", smv)], inc="vm")
            em.emit("vector", "scalar_tensor_tensor", elu2, elu2, 1.0, ps,
                    op0=ALU.mult, op1=ALU.add, inc="vm")
            em.emit("vector", "tensor_scalar_add", H_H[oc], elu2,
                    par["b1m1"][:, oc:oc + 1], inc="vm")
            emarks[oc] = em.now("vm")
        assert em.now("tm") == tmbase + FC

        if "helu" in taps:
            if hf == 0:
                tap_he = nc.dram_tensor("tap_helu", [2, FC, 128, HALF], F32,
                                        kind="ExternalOutput").ap()
                globals()["_tap_he"] = tap_he
            tap_he = globals()["_tap_he"]
            for k in range(FC):
                em.emit("gpsimd", "dma_start", out=tap_he[hf][k], in_=H_H[k],
                        waits=[("vm", em.now("vm"))], inc="tst", incv=16)

        smarks = {}
        for oc in range(DC):
            w = [("sp", SP_OUT1)]
            if oc == 0:
                w += [("vm", em.now("vm")), ("sm", em.now("sm"))]
            if oc >= 1:
                w += [("vm", em.now("vm")), ("sm", em.now("sm"))]
            em.emit("sync", "dma_start", out=o1ch,
                    in_=out1f_sp[oc][:, hf * HALF:(hf + 1) * HALF], waits=w,
                    inc="xl", incv=16)
            xlv = em.now("xl")
            pso = 0 if oc % 2 == 0 else 1024
            tw = [("wl", WL_W2),
                  ("vm", smarks[oc - 2] if oc >= 2 else em.now("vm"))]
            first = True
            for k in range(FC):
                for tc2 in range(2):
                    em.emit("tensor", "matmul",
                            psA[:, pso + tc2 * 512: pso + (tc2 + 1) * 512],
                            w2_sb[:, k * 1024 + oc * 128: k * 1024 + (oc + 1) * 128],
                            H_H[k][:, tc2 * 512:(tc2 + 1) * 512],
                            start=(k == 0), stop=(k == FC - 1), waits=tw if first else ())
                    first = False
            em.emit("tensor", None, inc="tm")
            tmv = em.now("tm")
            ps = psA[:, pso:pso + 1024]
            em.emit("vector", "scalar_tensor_tensor", sum2ch, ps, par["b2v"][:, oc:oc + 1],
                    o1ch, op0=ALU.add, op1=ALU.add, waits=[("tm", tmv), ("xl", xlv)],
                    inc="vm")
            vmv = em.now("vm")
            smarks[oc] = vmv
            em.emit("sync", "dma_start", out=sum2_sp[hf][oc], in_=sum2ch, waits=[("vm", vmv)],
                    inc="sp", incv=16)
            em.emit("scalar", "activation", sq2ch, sum2ch, AF.Square, waits=[("vm", vmv)],
                    inc="sm")
            smv = em.now("sm")
            for tc2 in range(2):
                em.emit("tensor", "matmul", psB[0:1, tc2 * 512:(tc2 + 1) * 512],
                        ones_col, sum2ch[:, tc2 * 512:(tc2 + 1) * 512],
                        start=(oc == 0), stop=(oc == DC - 1),
                        waits=[("vm", vmv)] if tc2 == 0 else ())
            for tc2 in range(2):
                em.emit("tensor", "matmul", psB[32:33, tc2 * 512:(tc2 + 1) * 512],
                        ones_col, sq2ch[:, tc2 * 512:(tc2 + 1) * 512],
                        start=(oc == 0), stop=(oc == DC - 1), skip_group_check=True,
                        waits=[("sm", smv)] if tc2 == 0 else ())
            em.emit("tensor", None, inc="tm")
        SP_SUM2 = em.now("sp")

        tm_bc2 = emit_ln_rows(EPS_LN, HALF, XNBF)
        ap2 = {}
        for oc in range(DC):
            rsem = f"rl{oc % 4}"
            ssem = f"so{oc % 4}"
            w = [("sp", SP_SUM2), ("tm", tm_bc2)]
            if oc >= 4:
                w.append((ssem, ap2[oc - 4]))
            ob = o1chb4[oc % 4]
            em.emit("sync", "dma_start", out=ob, in_=sum2_sp[hf][oc], waits=w,
                    inc=rsem, incv=16, serial=False)
            rlv = em.now(rsem)
            em.emit("vector", "tensor_mul", ob, ob, psA[:, 0:HALF],
                    waits=[(rsem, rlv), ("tm", tm_bc2)], inc="vm")
            em.emit("vector", "tensor_add", ob, ob, psB[:, 0:HALF], inc="vm")
            em.emit("vector", "tensor_scalar", ob, ob, par["ln2g"][:, oc:oc + 1],
                    par["ln2b"][:, oc:oc + 1], op0=ALU.mult, op1=ALU.add, inc="vm")
            em.emit("scalar", "dma_start", out=attnb_sp[hf][oc], in_=ob,
                    waits=[("vm", em.now("vm"))], inc=ssem, incv=16, serial=False)
            ap2[oc] = em.now(ssem)
    em.ewait("scalar", *[(f"so{i}", em.now(f"so{i}")) for i in range(4)])
    em.emit("scalar", "sem_inc", "sp", 4 * DC * 16)
    SP_ATTNB = em.now("sp")

    if "attnb" in taps:
        tap_ab = nc.dram_tensor("tap_attnb", [2, DC, 128, HALF], F32,
                                kind="ExternalOutput").ap()
        for hf in range(2):
            for oc in range(DC):
                em.emit("sync", "dma_start", out=tap_ab[hf][oc], in_=attnb_sp[hf][oc],
                        waits=[("sp", SP_ATTNB)], inc="st", incv=16)

    if upto == "H":
        return _stop_here()

    xn2_bf = [V_(R_A + 4 * c, T, BF16) for c in range(DC)]
    em.emit("sync", "dma_start", out=xn2_bf[0], in_=xnb_sp[0],
            waits=[("sp", SP_XN)] + em.allw(), inc="xl", incv=16)
    for c in range(1, DC):
        em.emit("sync", "dma_start", out=xn2_bf[c], in_=xnb_sp[c], inc="xl", incv=16)
    XL_XN2 = em.now("xl")

    tslab = [V_(SCR, 4096, BF16), V_(SCR + 8, 4096, BF16)]
    hslab = [V_(SCR, 4096, BF16), V_(SCR + 8, 4096, BF16)]
    silu_s = V_(SCR + 16, 1024, F32)
    silu_s2 = V_(SCR + 20, 1024, F32)
    y3ch = V_(SCR + 16, 1024, F32)
    abchb = [V_(6.5625, 1024, F32), V_(SCR + 20, 1024, F32)]

    for hf in range(2):
        TH = [V_(R_B + 2 * i, HALF, BF16) for i in range(FC)]
        G1 = ([V_(XNBF + 2 * i, HALF, BF16) for i in range(16)] +
              [V_(R_A + 32 + 2 * (i - 16), HALF, BF16) for i in range(16, 32)])
        G2 = TH

        tmbase = em.now("tm")
        tmarks = {}
        for oc in range(DC):
            w = [("xl", XL_XN2)]
            if oc == 0:
                w += [("tm", em.now("tm")), ("vm", em.now("vm")), ("sm", em.now("sm")),
                      ("sp", SP_ATTNB), ("st", em.now("st"))]
                w += [(f"rl{i}", em.now(f"rl{i}")) for i in range(4)]
            else:
                w.append(("tm", tmbase + 4 * oc))
            em.emit("gpsimd", "dma_start",
                    out=tslab[0].rearrange("p (k n) -> p k n", k=DC),
                    in_=w1t_hbm.rearrange("k p n -> p k n")[:, :, oc * 512:(oc + 1) * 512],
                    waits=w, inc="wt0", incv=16, serial=False)
            em.emit("gpsimd", "dma_start",
                    out=tslab[1].rearrange("p (k n) -> p k n", k=DC),
                    in_=w2t_hbm.rearrange("k p n -> p k n")[:, :, oc * 512:(oc + 1) * 512],
                    inc="wt1", incv=16, serial=False)
            for s4 in range(4):
                occ = oc * 4 + s4
                pst = psA if occ % 2 == 0 else psB
                sil = silu_s if occ % 2 == 0 else silu_s2
                tw = [("wt0", em.now("wt0")), ("wt1", em.now("wt1"))]
                tw.append(("vm", tmarks[occ - 2] if occ >= 2 else em.now("vm")))
                if occ < 2:
                    tw.append(("sm", em.now("sm")))
                first = True
                for k in range(DC):
                    for tc2 in range(2):
                        em.emit("tensor", "matmul",
                                pst[:, tc2 * 512:(tc2 + 1) * 512],
                                tslab[0][:, k * 512 + s4 * 128: k * 512 + (s4 + 1) * 128],
                                xn2_bf[k][:, hf * HALF + tc2 * 512: hf * HALF + (tc2 + 1) * 512],
                                start=(k == 0), stop=(k == DC - 1), waits=tw if first else ())
                        first = False
                for k in range(DC):
                    for tc2 in range(2):
                        em.emit("tensor", "matmul",
                                pst[:, 1024 + tc2 * 512: 1024 + (tc2 + 1) * 512],
                                tslab[1][:, k * 512 + s4 * 128: k * 512 + (s4 + 1) * 128],
                                xn2_bf[k][:, hf * HALF + tc2 * 512: hf * HALF + (tc2 + 1) * 512],
                                start=(k == 0), stop=(k == DC - 1))
                em.emit("tensor", None, inc="tm")
                tmv = em.now("tm")
                em.emit("scalar", "activation", sil, pst[:, 0:1024], AF.Silu,
                        scale=par["w1s"][:, occ:occ + 1], waits=[("tm", tmv)], inc="sm")
                em.emit("vector", "scalar_tensor_tensor", TH[occ], pst[:, 1024:2048],
                        par["w2s"][:, occ:occ + 1], sil, op0=ALU.mult, op1=ALU.mult,
                        waits=[("sm", em.now("sm"))], inc="vm")
                tmarks[occ] = em.now("vm")
        assert em.now("tm") == tmbase + 4 * DC
        VM_TH = em.now("vm")

        def big44(src_tiles, dst_tiles, w_sg, scale_par, src_sem, src_val):
            tmbase2 = em.now("tm")
            gmarks = {}
            for oc in range(FC):
                w = [(src_sem, src_val)]
                if oc == 0:
                    w += [("tm", em.now("tm")), ("sm", em.now("sm")), ("vm", em.now("vm")),
                          ("st", em.now("st"))]
                else:
                    w.append(("tm", tmbase2 + oc - 1))
                wsem = "whA" if oc % 2 == 0 else "whB"
                em.emit("gpsimd", "dma_start",
                        out=hslab[oc % 2].rearrange("p (k n) -> p k n", k=FC),
                        in_=w_sg[oc], waits=w, inc=wsem, incv=16, serial=False)
                pso = 0 if oc % 2 == 0 else 1024
                tw = [(wsem, em.now(wsem)), (src_sem, src_val)]
                tw.append(("sm", gmarks[oc - 2] if oc >= 2 else em.now("sm")))
                first = True
                for k in range(FC):
                    for tc2 in range(2):
                        em.emit("tensor", "matmul",
                                psA[:, pso + tc2 * 512: pso + (tc2 + 1) * 512],
                                hslab[oc % 2][:, k * 128:(k + 1) * 128],
                                src_tiles[k][:, tc2 * 512:(tc2 + 1) * 512],
                                start=(k == 0), stop=(k == FC - 1), waits=tw if first else ())
                        first = False
                em.emit("tensor", None, inc="tm")
                tmv = em.now("tm")
                em.emit("scalar", "activation", dst_tiles[oc], psA[:, pso:pso + 1024],
                        AF.Gelu, scale=scale_par[:, oc:oc + 1], waits=[("tm", tmv)],
                        inc="sm")
                gmarks[oc] = em.now("sm")
            assert em.now("tm") == tmbase2 + FC
            return em.now("sm")

        if upto == "I1" and hf == 0:
            return _stop_here()
        sm_g1 = big44(TH, G1, h1t_hbm, par["h1s"], "vm", VM_TH)
        sm_g2 = big44(G1, G2, h2t_hbm, par["h2s"], "sm", sm_g1)

        if upto == "I2" and hf == 0:
            return _stop_here()
        xh = [V_(R_A + 32 + 4 * i, 1024, F32) for i in range(8)]
        for i in range(8):
            em.emit("sync", "dma_start", out=xh[i],
                    in_=x_hbm[hf * HALF + i * 128: hf * HALF + (i + 1) * 128, :],
                    waits=[("sm", sm_g2)] if i == 0 else (), inc="xl", incv=16)
        XL_XH = em.now("xl")
        out_tm = [V_(XNBF + 4 * i, 1024, F32) for i in range(8)]
        tmbase3 = em.now("tm")
        fmarks = {}
        for oc in range(DC):
            w = [("sm", sm_g2)]
            if oc == 0:
                w += [("tm", em.now("tm")), ("vm", em.now("vm"))]
            else:
                w.append(("tm", tmbase3 + 2 * oc - 1))
            wsem = "whA" if oc % 2 == 0 else "whB"
            em.emit("gpsimd", "dma_start",
                    out=hslab[oc % 2].rearrange("p (k n) -> p k n", k=FC),
                    in_=w3t_hbm[oc], waits=w, inc=wsem, incv=16, serial=False)
            tw = [(wsem, em.now(wsem)), ("vm", em.now("vm"))]
            first = True
            for k in range(FC):
                for tc2 in range(2):
                    em.emit("tensor", "matmul", psA[:, tc2 * 512:(tc2 + 1) * 512],
                            hslab[oc % 2][:, k * 128:(k + 1) * 128],
                            G2[k][:, tc2 * 512:(tc2 + 1) * 512],
                            start=(k == 0), stop=(k == FC - 1), waits=tw if first else ())
                    first = False
            em.emit("tensor", None, inc="tm")
            tmv = em.now("tm")
            rsem = "rlA" if oc % 2 == 0 else "rlB"
            w = [("sp", SP_ATTNB)]
            if oc >= 2:
                w.append(("vm", fmarks[oc - 2]))
            ab = abchb[oc % 2]
            em.emit("sync", "dma_start", out=ab, in_=attnb_sp[hf][oc], waits=w,
                    inc=rsem, incv=16, serial=False)
            rlv = em.now(rsem)
            em.emit("vector", "scalar_tensor_tensor", y3ch, psA[:, 0:1024],
                    par["w3s"][:, oc:oc + 1], ab, op0=ALU.mult, op1=ALU.add,
                    waits=[("tm", tmv), (rsem, rlv)], inc="vm")
            fmarks[oc] = em.now("vm")
            vmv = em.now("vm")
            for p in range(8):
                em.emit("tensor", "transpose",
                        psB[:, 1024 + p * 128: 1024 + (p + 1) * 128],
                        y3ch[:, p * 128:(p + 1) * 128], id_f,
                        waits=[("vm", vmv)] if p == 0 else ())
            em.emit("tensor", None, inc="tm")
            tmv2 = em.now("tm")
            for p in range(8):
                em.emit("vector", "tensor_add", out_tm[p][:, oc * 128:(oc + 1) * 128],
                        psB[:, 1024 + p * 128: 1024 + (p + 1) * 128],
                        xh[p][:, oc * 128:(oc + 1) * 128],
                        waits=[("tm", tmv2), ("xl", XL_XH)] if p == 0 else (),
                        inc="vm" if p == 7 else None)
        assert em.now("tm") == tmbase3 + 2 * DC
        VM_OUT = em.now("vm")
        for i in range(8):
            em.emit("sync", "dma_start",
                    out=out_hbm[hf * HALF + i * 128: hf * HALF + (i + 1) * 128, :],
                    in_=out_tm[i], waits=[("vm", VM_OUT)] if i == 0 else (),
                    inc="st", incv=16)

    em.ewait("sync", ("st", em.now("st")))
    em.ewait("gpsimd", ("st", em.now("st")))
    if em.now("tst"):
        em.ewait("gpsimd", ("tst", em.now("tst")))

    em.realize(nc)
    return nc


def prepare_in_maps(inputs):
    x = np.ascontiguousarray(np.asarray(inputs["L0_attn_x"], np.float32).reshape(B * N, D))

    def t2d(a):
        return np.ascontiguousarray(np.asarray(a, np.float32))

    def slabgrid(w):
        K, NN = w.shape
        return np.ascontiguousarray(
            w.reshape(K // 128, 128, NN // 128, 128).transpose(2, 1, 0, 3))

    def kchunks(w):
        K, NN = w.shape
        return np.ascontiguousarray(w.reshape(K // 128, 128, NN))

    def vec128(v):
        v = t2d(v).ravel()
        return np.ascontiguousarray(v.reshape(-1, 128).T)

    common = {
        "wq": t2d(inputs["L0_attn_Wq"]).reshape(D, D),
        "wk": t2d(inputs["L0_attn_Wk"]).reshape(D, D),
        "wv": t2d(inputs["L0_attn_Wv"]).reshape(D, D),
        "wo": t2d(inputs["L0_attn_Wo"]).reshape(D, D),
        "proj": t2d(inputs["L0_attn_proj"]),
        "w1": slabgrid(t2d(inputs["L0_attn_W1"])),
        "w2": kchunks(t2d(inputs["L0_attn_W2"])),
        "w1t": kchunks(t2d(inputs["L0_ffn_w1_w"]).T),
        "w2t": kchunks(t2d(inputs["L0_ffn_w2_w"]).T),
        "h1t": slabgrid(t2d(inputs["L0_ffn_h1_w"]).T),
        "h2t": slabgrid(t2d(inputs["L0_ffn_h2_w"]).T),
        "w3t": slabgrid(t2d(inputs["L0_ffn_w3_w"]).T),
        "b1v": vec128(inputs["L0_attn_b1"]), "b2v": vec128(inputs["L0_attn_b2"]),
        "ln1g": vec128(inputs["L0_attn_ln1_g"]), "ln1b": vec128(inputs["L0_attn_ln1_b"]),
        "ln2g": vec128(inputs["L0_attn_ln2_g"]), "ln2b": vec128(inputs["L0_attn_ln2_b"]),
        "w1s": vec128(inputs["L0_ffn_w1_s"]), "w2s": vec128(inputs["L0_ffn_w2_s"]),
        "h1s": vec128(inputs["L0_ffn_h1_s"]), "h2s": vec128(inputs["L0_ffn_h2_s"]),
        "w3s": vec128(inputs["L0_ffn_w3_s"]),
    }
    in_maps = []
    for c in range(NCORES):
        m = dict(common)
        m["x"] = np.ascontiguousarray(x[c * T:(c + 1) * T])
        in_maps.append(m)
    return in_maps


def kernel(**inputs):
    in_maps = prepare_in_maps(inputs)
    nc = build()
    res = run_bass_kernel_spmd(nc, in_maps, core_ids=list(range(NCORES)))
    out = np.concatenate([res.results[c]["out"] for c in range(NCORES)], axis=0)
    return out.reshape(B, N, D).astype(np.float32)


# version 10
# speedup vs baseline: 1.1773x; 1.0026x over previous
import numpy as np
import ml_dtypes
import concourse.bass as bass
import concourse.mybir as mybir
from concourse.bass_utils import run_bass_kernel_spmd

F32 = mybir.dt.float32
BF16 = mybir.dt.bfloat16
AF = mybir.ActivationFunctionType
ALU = mybir.AluOpType
AX = mybir.AxisListType

NCORES = 8
B = 2
N = 8192
T = 2048
NT = 16
D = 1024
DC = 8
H = 16
DH = 64
M = 64
DFF = 4096
FC = 32
HALF = 1024
DN = 2.0 ** -1.5
RATIO = 0.125
EPSK = 1e-6
EPS_LN0 = 1e-5
EPS_LN = 1e-6
NEG = -3.0e38

ENGINES = ("sync", "gpsimd", "scalar", "vector", "tensor")
KB = 1024


class Em:
    def __init__(self):
        self.steps = {e: [] for e in ENGINES}
        self.sems = {}
        self._maxwait = {e: {} for e in ENGINES}

    def now(self, sem):
        return self.sems.setdefault(sem, 0)

    def emit(self, eng, method, *args, waits=(), inc=None, incv=1, serial=True, **kw):
        w = []
        for (s, v) in waits:
            self.sems.setdefault(s, 0)
            assert v <= self.sems[s], (eng, method, s, v, self.sems[s])
            if v > 0 and v > self._maxwait[eng].get(s, -1):
                w.append((s, v))
                self._maxwait[eng][s] = v
        if method == "sem_inc":
            self.steps[eng].append((method, args, kw, tuple(w), None, 0))
            self.sems[args[0]] = self.sems.setdefault(args[0], 0) + args[1]
            return
        if method is None and inc is not None:
            for i in range(len(self.steps[eng]) - 1, -1, -1):
                m2, a2, k2, w2, i2, v2 = self.steps[eng][i]
                if m2 is not None:
                    assert i2 is None, ("double marker on", eng, m2, i2, inc)
                    self.steps[eng][i] = (m2, a2, k2, w2, inc, incv)
                    break
            else:
                raise AssertionError("marker with no prior instruction")
            if w:
                self.steps[eng].append((None, (), {}, tuple(w), None, 0))
            self.sems[inc] = self.sems.setdefault(inc, 0) + incv
            return
        self.steps[eng].append((method, args, kw, tuple(w), inc, incv))
        if inc is not None:
            self.sems[inc] = self.sems.setdefault(inc, 0) + incv
        if method == "dma_start" and inc is not None and serial:
            self.ewait(eng, (inc, self.sems[inc]))

    def ewait(self, eng, *waits):
        self.emit(eng, None, waits=waits)

    def fence(self, eng, sem):
        for i in range(len(self.steps[eng]) - 1, -1, -1):
            m2, a2, k2, w2, i2, v2 = self.steps[eng][i]
            if m2 is not None:
                if i2 is None:
                    self.steps[eng][i] = (m2, a2, k2, w2, sem, 1)
                    self.sems[sem] = self.sems.setdefault(sem, 0) + 1
                    self.ewait(eng, (sem, self.now(sem)))
                else:
                    self.ewait(eng, (i2, self.now(i2)))
                return
        raise AssertionError("fence with no prior instruction")

    def allw(self):
        return [("tm", self.now("tm")), ("vm", self.now("vm")), ("sm", self.now("sm"))]

    def check(self):
        idx = {e: 0 for e in ENGINES}
        cnt = {s: 0 for s in self.sems}
        progress = True
        while progress:
            progress = False
            for e in ENGINES:
                while idx[e] < len(self.steps[e]):
                    method, args, kw, waits, inc, incv = self.steps[e][idx[e]]
                    if all(cnt[s] >= v for (s, v) in waits):
                        if method == "sem_inc":
                            cnt[args[0]] += args[1]
                        if inc is not None:
                            cnt[inc] += incv
                        idx[e] += 1
                        progress = True
                    else:
                        break
        stuck = {e: idx[e] for e in ENGINES if idx[e] < len(self.steps[e])}
        if stuck:
            for e, i in stuck.items():
                method, args, kw, waits, inc, incv = self.steps[e][i]
                unmet = [(s, v, cnt[s]) for (s, v) in waits if cnt[s] < v]
                print(f"DEADLOCK {e} step {i}/{len(self.steps[e])} {method} unmet={unmet}")
            raise RuntimeError("static deadlock detected")
        for s, v in cnt.items():
            assert v == self.sems[s], (s, v, self.sems[s])

    def realize(self, nc):
        self.check()
        handles = {name: nc.alloc_semaphore(name) for name in self.sems}
        with nc.Block() as block:
            for engname in ENGINES:
                def make(engname=engname):
                    def f(eng):
                        for (method, args, kw, waits, inc, incv) in self.steps[engname]:
                            for (s, v) in waits:
                                eng.wait_ge(handles[s], v)
                            if method is None:
                                continue
                            if method == "sem_inc":
                                eng.sem_inc(handles[args[0]], args[1])
                                continue
                            ins = getattr(eng, method)(*args, **kw)
                            if inc is not None:
                                ins.then_inc(handles[inc], incv)
                    return f
                getattr(block, engname)(make())


class Arena:
    def __init__(self, nc, cap_bytes):
        self.cap = cap_bytes
        self.t = nc.alloc_sbuf_tensor("arena", [128, cap_bytes], mybir.dt.uint8)

    def view(self, off_kb, cols, dt, parts=128):
        esz = 4 if dt == F32 else 2
        off = int(off_kb * KB)
        assert off % 32 == 0, off_kb
        assert off + cols * esz <= self.cap, (off_kb, cols, dt)
        return self.t[0:parts, off:off + cols * esz].bitcast(dt)


def build(taps=(), upto=None, nta=NT):
    taps = set(taps)
    nc = bass.Bass("TRN2", target_bir_lowering=False, detect_race_conditions=False)
    em = Em()

    x_hbm = nc.dram_tensor("x", [T, D], F32, kind="ExternalInput").ap()
    wq_hbm = nc.dram_tensor("wq", [D, D], F32, kind="ExternalInput").ap()
    wk_hbm = nc.dram_tensor("wk", [D, D], F32, kind="ExternalInput").ap()
    wv_hbm = nc.dram_tensor("wv", [D, D], F32, kind="ExternalInput").ap()
    wo_hbm = nc.dram_tensor("wo", [D, D], F32, kind="ExternalInput").ap()
    proj_hbm = nc.dram_tensor("proj", [M, DH], F32, kind="ExternalInput").ap()
    w1_hbm = nc.dram_tensor("w1", [FC, 128, DC, 128], F32, kind="ExternalInput").ap()
    w2_hbm = nc.dram_tensor("w2", [FC, 128, D], F32, kind="ExternalInput").ap()
    w1t_hbm = nc.dram_tensor("w1t", [DC, 128, DFF], F32, kind="ExternalInput").ap()
    w2t_hbm = nc.dram_tensor("w2t", [DC, 128, DFF], F32, kind="ExternalInput").ap()
    h1t_hbm = nc.dram_tensor("h1t", [FC, 128, FC, 128], F32, kind="ExternalInput").ap()
    h2t_hbm = nc.dram_tensor("h2t", [FC, 128, FC, 128], F32, kind="ExternalInput").ap()
    w3t_hbm = nc.dram_tensor("w3t", [DC, 128, FC, 128], F32, kind="ExternalInput").ap()
    vecs_hbm = {}
    for nm, ncols in [("b1v", FC), ("w1s", FC), ("w2s", FC), ("h1s", FC), ("h2s", FC),
                      ("b2v", DC), ("ln1g", DC), ("ln1b", DC), ("ln2g", DC), ("ln2b", DC),
                      ("w3s", DC)]:
        vecs_hbm[nm] = nc.dram_tensor(nm, [128, ncols], F32, kind="ExternalInput").ap()

    out_hbm = nc.dram_tensor("out", [T, D], F32, kind="ExternalOutput").ap()

    xnf_sp = nc.dram_tensor("xnf_sp", [DC, 128, T], F32).ap()
    xnb_sp = nc.dram_tensor("xnb_sp", [DC, 128, T], BF16).ap()
    sum1_sp = nc.dram_tensor("sum1_sp", [DC, 128, T], F32).ap()
    out1f_sp = nc.dram_tensor("out1f_sp", [DC, 128, T], F32).ap()
    sum2_sp = nc.dram_tensor("sum2_sp", [2, DC, 128, HALF], F32).ap()
    attnb_sp = nc.dram_tensor("attnb_sp", [2, DC, 128, HALF], F32).ap()

    w1bf = nc.dram_tensor("w1bf", [FC, 128, DC, 128], BF16).ap()
    w2bf = nc.dram_tensor("w2bf", [FC, 128, D], BF16).ap()
    w1tbf = nc.dram_tensor("w1tbf", [DC, 128, DFF], BF16).ap()
    w2tbf = nc.dram_tensor("w2tbf", [DC, 128, DFF], BF16).ap()
    h1tbf = nc.dram_tensor("h1tbf", [FC, 128, FC, 128], BF16).ap()
    h2tbf = nc.dram_tensor("h2tbf", [FC, 128, FC, 128], BF16).ap()
    w3tbf = nc.dram_tensor("w3tbf", [DC, 128, FC, 128], BF16).ap()
    ccmax_in = nc.dram_tensor("ccmax_in", [1, 8], F32)
    ccmax_out = nc.dram_tensor("ccmax_out", [1, 8], F32)
    cckv_in = nc.dram_tensor("cckv_in", [128, 520], F32)
    cckv_out = nc.dram_tensor("cckv_out", [128, 520], F32)

    id_f32_d = nc.inline_tensor(np.eye(128, dtype=np.float32), name="c_idf")
    id_bf_d = nc.inline_tensor(np.eye(128, dtype=ml_dtypes.bfloat16).view(np.uint16),
                               name="c_idb")
    bd_np = np.zeros((128, 128), np.float32)
    for kr in range(8):
        bd_np[0:64, kr * 16 + 2 * kr] = 1.0 / 16.0
        bd_np[64:128, kr * 16 + 2 * kr + 1] = 1.0 / 16.0
    bd_d = nc.inline_tensor(bd_np, name="c_bd")
    e2_np = np.zeros((16, 8 * 128), np.float32)
    for c in range(8):
        e2_np[2 * c, c * 128:c * 128 + 64] = 1.0
        e2_np[2 * c + 1, c * 128 + 64:c * 128 + 128] = 1.0
    e2_d = nc.inline_tensor(e2_np, name="c_e2")

    AR = Arena(nc, 206 * KB)
    V_ = AR.view
    id_f = V_(0.0, 128, F32)
    id_b = V_(0.5, 128, BF16)
    bd_c = V_(0.75, 128, F32)
    pd2 = V_(1.25, 128, BF16)
    kvbd = V_(1.5, 1024, BF16)
    ksbd = V_(3.5, 128, BF16)
    kvr = V_(3.75, 520, F32)
    ones_col = V_(5.8125, 1, F32)
    ones_row = V_(5.875, 128, F32)[0:1, :]
    mx128 = V_(6.375, 1, F32)
    mxloc = V_(6.4375, 16, F32)
    mxst = V_(6.5, 16, F32)
    e2_c = V_(6.5625, 1024, F32, parts=16)
    diagq_tm = V_(10.5625, 256, F32)
    diagk_tm = V_(11.5625, 256, F32)
    par = {}
    poff = 12.5625
    for nm, ncols in [("b1v", FC), ("negb1", FC), ("b1m1", FC), ("w1s", FC), ("w2s", FC),
                      ("h1s", FC), ("h2s", FC), ("b2v", DC), ("ln1g", DC), ("ln1b", DC),
                      ("ln2g", DC), ("ln2b", DC), ("w3s", DC)]:
        par[nm] = V_(poff, ncols, F32)
        poff += ncols * 4 / KB
    assert poff <= 14.0, poff

    XNBF = 14.0
    SCR = 46.0
    R_A = 70.0
    R_B = 134.0

    xn_bf = [V_(XNBF + 4 * c, T, BF16) for c in range(DC)]

    psA_t = nc.alloc_psum_tensor("psA", [128, 2048], F32)
    psB_t = nc.alloc_psum_tensor("psB", [128, 2048], F32)
    psA = psA_t[:, :]
    psB = psB_t[:, :]
    psA_bf = psA.bitcast(BF16)


    def _stop_here():
        em.emit("sync", "dma_start", out=out_hbm[0:128, :],
                in_=AR.t[0:128, 0:4096].bitcast(F32),
                waits=em.allw() + [("sp", em.now("sp"))], inc="st", incv=16)
        em.ewait("sync", ("st", em.now("st")))
        em.ewait("gpsimd", ("st", em.now("st")))
        if em.now("tst"):
            em.ewait("gpsimd", ("tst", em.now("tst")))
        em.realize(nc)
        return nc

    em.emit("sync", "dma_start", out=id_f, in_=id_f32_d.ap(), inc="xl", incv=16)
    em.emit("sync", "dma_start", out=id_b, in_=id_bf_d.ap().bitcast(BF16), inc="xl", incv=16)
    em.emit("sync", "dma_start", out=bd_c, in_=bd_d.ap(), inc="xl", incv=16)
    em.emit("sync", "dma_start", out=e2_c, in_=e2_d.ap(), inc="xl", incv=16)
    for nm in ["b1v", "w1s", "w2s", "h1s", "h2s", "b2v", "ln1g", "ln1b", "ln2g", "ln2b", "w3s"]:
        em.emit("sync", "dma_start", out=par[nm], in_=vecs_hbm[nm], inc="xl", incv=16)
    XL_CONSTS = em.now("xl")
    em.emit("vector", "tensor_scalar_add", par["b1m1"], par["b1v"], -1.0,
            waits=[("xl", XL_CONSTS)], inc="vm")
    em.emit("vector", "tensor_scalar_mul", par["negb1"], par["b1v"], -1.0, inc="vm")
    em.emit("vector", "memset", ones_col, 1.0, inc="vm")
    em.emit("vector", "memset", ones_row, 1.0, inc="vm")

    wv_sb = V_(R_B, 8 * 1024, BF16)
    wq_sb = V_(R_B + 16, 8 * 1024, BF16)
    wk_sb = V_(R_B + 32, 8 * 1024, BF16)
    proj_sb = V_(R_B + 48, 64, F32, parts=64)
    em.emit("gpsimd", "dma_start", out=wv_sb.rearrange("p (k n) -> p k n", k=8),
            in_=wv_hbm.rearrange("(k p) n -> p k n", p=128), inc="wl", incv=16)
    em.emit("gpsimd", "dma_start", out=wq_sb.rearrange("p (k n) -> p k n", k=8),
            in_=wq_hbm.rearrange("(k p) n -> p k n", p=128), inc="wl", incv=16)
    WL_WQ = em.now("wl")
    em.emit("gpsimd", "dma_start", out=wk_sb.rearrange("p (k n) -> p k n", k=8),
            in_=wk_hbm.rearrange("(k p) n -> p k n", p=128), inc="wl", incv=16)
    WL_WK = em.now("wl")
    WL_WV = 16
    em.emit("gpsimd", "dma_start", out=proj_sb, in_=proj_hbm, inc="wl", incv=16)
    WL_PROJ = em.now("wl")
    PC_ALL = 0

    em.emit("tensor", "transpose", psB[0:64, 0:64], proj_sb, id_f[0:64, 0:64],
            waits=[("wl", WL_PROJ), ("xl", XL_CONSTS)], inc="tm")
    TM_PROJ = em.now("tm")
    em.emit("vector", "memset", pd2, 0.0, inc="vm")
    em.fence("vector", "vm")
    em.emit("vector", "tensor_scalar_mul", pd2[0:64, 0:64], psB[0:64, 0:64], DN,
            waits=[("tm", TM_PROJ)], inc="vm")
    em.emit("vector", "tensor_scalar_mul", pd2[64:128, 64:128], psB[0:64, 0:64], DN, inc="vm")
    VM_PD2 = em.now("vm")

    xbuf = [V_(SCR, 1024, F32), V_(SCR + 4, 1024, F32), V_(SCR + 8, 1024, F32)]
    xnt2 = [V_(SCR + 12, 1024, F32), V_(SCR + 16.5, 1024, F32)]
    srow2 = [[V_(SCR + 20.5 + 0.0625 * (5 * p + i), 1, F32) for i in range(5)]
             for p in range(2)]
    xn_f32 = [V_(R_A + 8 * c, T, F32) for c in range(DC)]

    a_mark = []
    stats_done = []
    for t in range(nta + 1):
        if t < nta:
            buf = xbuf[t % 3]
            xnt = xnt2[t % 2]
            srow = srow2[t % 2]
            w = []
            if t >= 3:
                w += [("sm", stats_done[t - 3])]
            em.emit("sync", "dma_start", out=buf, in_=x_hbm[t * 128:(t + 1) * 128, :],
                    waits=w, inc="xl", incv=16)
            xlv = em.now("xl")
            sw = [("xl", xlv)]
            if t >= 2:
                sw.append(("tm", a_mark[t - 2][0]))
            em.emit("vector", "scalar_tensor_tensor", xnt, buf, 1.0, buf,
                    op0=ALU.mult, op1=ALU.mult, accum_out=srow[2],
                    waits=sw, inc="vm")
            em.emit("vector", "tensor_reduce", srow[0], buf, axis=AX.X, op=ALU.add, inc="vm")
            em.fence("vector", "vm")
            em.emit("vector", "tensor_scalar_mul", srow[1], srow[0], 1.0 / D, inc="vm")
            em.emit("vector", "tensor_scalar", srow[2], srow[2], 1.0 / D, EPS_LN0,
                    op0=ALU.mult, op1=ALU.add, inc="vm")
            em.fence("vector", "vm")
            em.emit("vector", "scalar_tensor_tensor", srow[3], srow[1], -1.0, srow[1],
                    op0=ALU.mult, op1=ALU.mult, inc="vm")
            em.fence("vector", "vm")
            em.emit("vector", "tensor_add", srow[2], srow[2], srow[3], inc="vm")
            em.fence("vector", "vm")
            em.emit("scalar", "activation", srow[3], srow[2], AF.Sqrt,
                    waits=[("vm", em.now("vm"))], inc="sm")
            em.emit("vector", "reciprocal", srow[4], srow[3],
                    waits=[("sm", em.now("sm"))], inc="vm")
            em.fence("vector", "vm")
            em.emit("vector", "scalar_tensor_tensor", srow[3], srow[1], -1.0, srow[4],
                    op0=ALU.mult, op1=ALU.mult, inc="vm")
            em.fence("vector", "vm")
            em.emit("scalar", "activation", xnt, buf, AF.Identity,
                    scale=srow[4][:, 0:1], bias=srow[3][:, 0:1],
                    waits=[("vm", em.now("vm"))], inc="sm")
            stats_done.append(em.now("sm"))
        if t >= 1:
            tp = t - 1
            xntp = xnt2[tp % 2]
            pso = 0 if tp % 2 == 0 else 1024
            tw = [("sm", stats_done[tp])]
            if tp >= 2:
                tw += [("vm", a_mark[tp - 2][1]), ("sm", a_mark[tp - 2][2])]
            for c in range(DC):
                em.emit("tensor", "transpose", psA[:, pso + c * 128: pso + (c + 1) * 128],
                        xntp[:, c * 128:(c + 1) * 128], id_f,
                        waits=tw if c == 0 else ())
            em.emit("tensor", None, inc="tm")
            tmv = em.now("tm")
            for c in range(4):
                em.emit("vector", "tensor_copy", xn_f32[c][:, tp * 128:(tp + 1) * 128],
                        psA[:, pso + c * 128: pso + (c + 1) * 128],
                        waits=[("tm", tmv)] if c == 0 else (),
                        inc="vm" if c == 3 else None)
            for c in range(4, DC):
                em.emit("scalar", "activation", xn_f32[c][:, tp * 128:(tp + 1) * 128],
                        psA[:, pso + c * 128: pso + (c + 1) * 128], AF.Copy,
                        waits=[("tm", tmv)] if c == 4 else (),
                        inc="sm" if c == DC - 1 else None)
            vm_ev = em.now("vm")
            sm_ev = em.now("sm")
            for c in range(DC):
                if c < 4:
                    em.emit("scalar", "activation", xn_bf[c][:, tp * 128:(tp + 1) * 128],
                            xn_f32[c][:, tp * 128:(tp + 1) * 128], AF.Copy,
                            waits=[("vm", vm_ev)] if c == 0 else (),
                            inc="sm" if c == 3 else None)
                else:
                    em.emit("vector", "tensor_copy", xn_bf[c][:, tp * 128:(tp + 1) * 128],
                            xn_f32[c][:, tp * 128:(tp + 1) * 128],
                            waits=[("sm", sm_ev)] if c == 4 else (),
                            inc="vm" if c == DC - 1 else None)
            a_mark.append((tmv, em.now("vm"), em.now("sm")))

    VM_A = em.now("vm")
    SM_A = em.now("sm")
    for c in range(DC):
        em.emit("sync", "dma_start", out=xnf_sp[c], in_=xn_f32[c],
                waits=[("vm", VM_A)] if c == 0 else (), inc="sp", incv=16)
        em.emit("sync", "dma_start", out=xnb_sp[c], in_=xn_bf[c],
                waits=[("sm", SM_A)] if c == 0 else (), inc="sp", incv=16)
    SP_XN = em.now("sp")

    if "xn" in taps:
        tap_xn = nc.dram_tensor("tap_xn", [DC, 128, T], F32, kind="ExternalOutput").ap()
        for c in range(DC):
            em.emit("sync", "dma_start", out=tap_xn[c], in_=xn_f32[c],
                    waits=[("vm", VM_A)], inc="st", incv=16)

    if upto == "A":
        return _stop_here()

    q_bf = [V_(R_A + 4 * c, T, BF16) for c in range(DC)]
    k_bf = [V_(R_A + 32 + 4 * c, T, BF16) for c in range(DC)]
    qsq = V_(SCR, 2048, F32)
    diag_fm = V_(SCR + 8, 2048, F32, parts=16)

    def emit_qk(w_sb, wlv, dst, diag_dst_tm):
        marks = {}
        pend = None
        g = 0
        for oc in range(DC):
            for th in range(2):
                pso = 0 if g % 2 == 0 else 1024
                tw = [("wl", wlv)]
                if g >= 2:
                    tw += [("vm", marks[g - 2][0]), ("sm", marks[g - 2][1])]
                else:
                    tw += [("sm", em.now("sm")), ("vm", em.now("vm"))]
                first = True
                for k in range(DC):
                    for tc in range(2):
                        tcg = th * 2 + tc
                        em.emit("tensor", "matmul",
                                psA[:, pso + tc * 512: pso + (tc + 1) * 512],
                                w_sb[:, k * 1024 + oc * 128: k * 1024 + (oc + 1) * 128],
                                xn_bf[k][:, tcg * 512:(tcg + 1) * 512],
                                start=(k == 0), stop=(k == DC - 1),
                                waits=tw if first else ())
                        first = False
                em.emit("tensor", None, inc="tm")
                tmv = em.now("tm")
                em.emit("vector", "tensor_copy", dst[oc][:, th * 1024:(th + 1) * 1024],
                        psA[:, pso:pso + 1024],
                        waits=[("tm", tmv), ("sp", SP_XN)], inc="vm")
                em.emit("scalar", "activation", qsq[:, th * 1024:(th + 1) * 1024],
                        psA[:, pso:pso + 1024], AF.Square,
                        waits=[("tm", tmv), ("vm", em.now("vm"))], inc="sm")
                smv = em.now("sm")
                if pend is not None:
                    p_oc, p_th, p_smv = pend
                    for tc in range(2):
                        tcg = p_th * 2 + tc
                        em.emit("tensor", "matmul", psB[0:16, tcg * 512:(tcg + 1) * 512],
                                bd_c[:, p_oc * 16:(p_oc + 1) * 16],
                                qsq[:, tcg * 512:(tcg + 1) * 512],
                                start=(p_oc == 0), stop=(p_oc == DC - 1),
                                waits=([("sm", p_smv)] +
                                       ([("vm", em.now("vm"))] if g <= 2 else []))
                                if tc == 0 else ())
                pend = (oc, th, smv)
                marks[g] = (em.now("vm"), em.now("sm"))
                g += 1
        p_oc, p_th, p_smv = pend
        for tc in range(2):
            tcg = p_th * 2 + tc
            em.emit("tensor", "matmul", psB[0:16, tcg * 512:(tcg + 1) * 512],
                    bd_c[:, p_oc * 16:(p_oc + 1) * 16],
                    qsq[:, tcg * 512:(tcg + 1) * 512],
                    start=False, stop=True,
                    waits=[("sm", p_smv)] if tc == 0 else ())
        em.emit("tensor", None, inc="tm")
        tmv = em.now("tm")
        em.emit("vector", "tensor_copy", diag_fm, psB[0:16, :], waits=[("tm", tmv)], inc="vm")
        vmv = em.now("vm")
        for t in range(NT):
            tw = [("vm", vmv)] if t == 0 else []
            if t >= 4:
                tw.append(("vm", em.now("vm")))
            em.emit("tensor", "transpose",
                    psB[:, (t % 4) * 512: (t % 4) * 512 + 16],
                    diag_fm[:, t * 128:(t + 1) * 128], id_f[0:16, 0:16], waits=tw, inc="tm")
            em.emit("vector", "tensor_copy", diag_dst_tm[:, t * 16:(t + 1) * 16],
                    psB[:, (t % 4) * 512: (t % 4) * 512 + 16],
                    waits=[("tm", em.now("tm"))], inc="vm")

    emit_qk(wk_sb, WL_WK, k_bf, diagk_tm)
    c1m = {}
    VM_C1S = em.now("vm")
    for t in range(NT):
        pso = 0 if t % 2 == 0 else 1024
        tw = [("vm", VM_PD2), ("vm", c1m[t - 2] if t >= 2 else VM_C1S)]
        first = True
        for pr in range(8):
            em.emit("tensor", "matmul", psA[:, pso + pr * 128: pso + (pr + 1) * 128],
                    k_bf[pr][:, t * 128:(t + 1) * 128], pd2,
                    start=True, stop=True, waits=tw if first else ())
            first = False
        em.emit("tensor", None, inc="tm")
        tmv = em.now("tm")
        em.emit("vector", "tensor_reduce", mxloc[:, t:t + 1], psA[:, pso:pso + 1024],
                axis=AX.X, op=ALU.max, waits=[("tm", tmv)], inc="vm")
        c1m[t] = em.now("vm")
    em.fence("vector", "vm")
    mxf = srow2[0][0]
    em.emit("vector", "tensor_reduce", mxf, mxloc[:, 0:NT], axis=AX.X, op=ALU.max,
            inc="vm")
    em.fence("vector", "vm")
    em.emit("tensor", "transpose", psB[0:1, 1024:1152], mxf, id_f,
            waits=[("vm", em.now("vm"))], inc="tm")
    em.emit("vector", "memset", mxst[0:1, 0:8], NEG,
            waits=[("tm", em.now("tm"))], inc="vm")
    em.fence("vector", "vm")
    em.emit("vector", "tensor_reduce", mxst[0:1, 0:1], psB[0:1, 1024:1152], axis=AX.X,
            op=ALU.max, inc="vm")
    em.fence("vector", "vm")
    em.emit("gpsimd", "dma_start", out=ccmax_in.ap(), in_=mxst[0:1, 0:8],
            waits=[("vm", em.now("vm"))], inc="cl", incv=16)
    em.ewait("gpsimd", ("cl", em.now("cl")))
    em.emit("gpsimd", "collective_compute", "AllReduce", ALU.max,
            replica_groups=[list(range(NCORES))],
            ins=[ccmax_in.ap().opt()], outs=[ccmax_out.ap().opt()], inc="cc")
    em.ewait("gpsimd", ("cc", em.now("cc")))
    em.emit("gpsimd", "dma_start", out=mxst[0:1, 8:16], in_=ccmax_out.ap(), inc="cl", incv=16)
    CL_MX = em.now("cl")

    if upto == "C1":
        return _stop_here()

    emit_qk(wq_sb, WL_WQ, q_bf, diagq_tm)

    if "q" in taps:
        tap_q = nc.dram_tensor("tap_q", [DC, 128, T], F32, kind="ExternalOutput").ap()
        for c in range(DC):
            em.emit("gpsimd", "dma_start", out=tap_q[c], in_=q_bf[c],
                    waits=[("vm", em.now("vm"))], inc="tst", incv=16)
    if "diag" in taps:
        tap_dg = nc.dram_tensor("tap_diag", [2, 128, 256], F32, kind="ExternalOutput").ap()
        em.emit("sync", "dma_start", out=tap_dg[0], in_=diagq_tm,
                waits=[("vm", em.now("vm"))], inc="st", incv=16)
        em.emit("sync", "dma_start", out=tap_dg[1], in_=diagk_tm,
                waits=[("vm", em.now("vm"))], inc="st", incv=16)

    vaug = [V_(R_B + 16 + 2.125 * t, 1040, BF16) for t in range(NT)]
    VM_B_QK = em.now("vm")
    for t in range(NT):
        em.emit("vector", "memset", vaug[t], 1.0,
                waits=[("vm", VM_B_QK), ("tm", em.now("tm"))] if t == 0 else (),
                inc="vm" if t == NT - 1 else None)
    em.fence("vector", "vm")
    VM_VONES = em.now("vm")
    vvm = {}
    for t in range(NT):
        for nh in range(2):
            g = 2 * t + nh
            pso = (g % 4) * 512
            tw = [("wl", WL_WV), ("vm", vvm[g - 4] if g >= 4 else VM_VONES)]
            first = True
            for k in range(DC):
                em.emit("tensor", "matmul", psA[:, pso:pso + 512],
                        xn_bf[k][:, t * 128:(t + 1) * 128],
                        wv_sb[:, k * 1024 + nh * 512: k * 1024 + (nh + 1) * 512],
                        start=(k == 0), stop=(k == DC - 1), waits=tw if first else ())
                first = False
            em.emit("tensor", None, inc="tm")
            tmv = em.now("tm")
            dst = vaug[t].rearrange("p (h e) -> p h e", h=16)[:, nh * 8:(nh + 1) * 8, 0:64]
            src = psA[:, pso:pso + 512].rearrange("p (h d) -> p h d", h=8)
            em.emit("vector", "tensor_copy", dst, src, waits=[("tm", tmv)], inc="vm")
            vvm[g] = em.now("vm")
    VM_VAUG = em.now("vm")

    if upto == "B":
        return _stop_here()

    qf_tm = [V_(R_B + 2 * t, 1024, BF16) if t < 8 else V_(R_B + 50 + 2 * (t - 8), 1024, BF16)
             for t in range(NT)]
    stage = V_(SCR, 512, F32)
    biasr = V_(SCR + 2, 16, F32)
    stageE = V_(SCR + 4, 1024, F32)
    kf_t = V_(SCR + 8, 1024, BF16)

    em.emit("tensor", "matmul", psB[:, 1028:1029], ones_row, mxst[0:1, 8:9],
            start=True, stop=True, waits=[("cl", CL_MX)], inc="tm")
    em.emit("vector", "tensor_copy", mx128, psB[:, 1028:1029],
            waits=[("tm", em.now("tm"))], inc="vm")
    VM_MX128 = em.now("vm")

    ekm = {}
    for t in range(NT):
        pso = 0 if t % 2 == 0 else 1024
        if t >= 2:
            tw = [("vm", ekm[t - 2][0]), ("sm", ekm[t - 2][1])]
        else:
            tw = [("vm", em.now("vm")), ("sm", em.now("sm"))]
        first = True
        for pr in range(8):
            em.emit("tensor", "matmul", psA[:, pso + pr * 128: pso + (pr + 1) * 128],
                    k_bf[pr][:, t * 128:(t + 1) * 128], pd2,
                    start=True, stop=True, waits=tw if first else ())
            first = False
        em.emit("tensor", None, inc="tm")
        tmv = em.now("tm")
        em.emit("vector", "tensor_scalar", biasr, diagk_tm[:, t * 16:(t + 1) * 16],
                mx128[:, 0:1], -1.0, op0=ALU.add, op1=ALU.mult,
                waits=[("tm", tmv), ("vm", VM_MX128)], inc="vm")
        em.fence("vector", "vm")
        vm_b = em.now("vm")
        for h in range(8, H):
            em.emit("scalar", "activation", stageE[:, h * 64:(h + 1) * 64],
                    psA[:, pso + h * 64: pso + h * 64 + 64], AF.Exp,
                    bias=biasr[:, h:h + 1],
                    waits=[("vm", vm_b), ("tm", tmv)] if h == 8 else (),
                    inc="sm" if h == H - 1 else None)
        for h in range(8):
            em.emit("vector", "tensor_scalar_add", stage[:, h * 64:(h + 1) * 64],
                    psA[:, pso + h * 64: pso + h * 64 + 64], biasr[:, h:h + 1],
                    inc="vm" if h == 7 else None)
        em.emit("scalar", "activation", stageE[:, 0:512], stage, AF.Exp,
                waits=[("vm", em.now("vm"))], inc="sm")
        em.emit("vector", "tensor_scalar", kf_t, stageE, EPSK, RATIO,
                op0=ALU.add, op1=ALU.mult, waits=[("sm", em.now("sm"))], inc="vm")
        vm_kf = em.now("vm")
        first = True
        for h in range(H):
            ph, cg = h % 2, h // 2
            colbase = (0 if cg < 4 else 512) + (cg % 4) * 65
            em.emit("tensor", "matmul",
                    psB[ph * 64:(ph + 1) * 64, colbase:colbase + 65],
                    kf_t[:, h * 64:(h + 1) * 64],
                    vaug[t][:, h * 65:(h + 1) * 65],
                    start=(t == 0 and h in (0, 1, 8, 9)),
                    stop=(t == NT - 1 and h in (6, 7, 14, 15)),
                    skip_group_check=True,
                    waits=[("vm", vm_kf), ("vm", VM_VAUG)] if first else ())
            first = False
        em.emit("tensor", None, inc="tm")
        ekm[t] = (em.now("vm"), em.now("sm"))
    TM_KV = em.now("tm")
    em.emit("vector", "tensor_copy", kvr[:, 0:260], psB[:, 0:260],
            waits=[("tm", TM_KV)], inc="vm")
    em.emit("vector", "tensor_copy", kvr[:, 260:520], psB[:, 512:772], inc="vm")
    em.emit("gpsimd", "dma_start", out=cckv_in.ap(), in_=kvr,
            waits=[("vm", em.now("vm"))], inc="cl", incv=16)
    em.ewait("gpsimd", ("cl", em.now("cl")))
    em.emit("gpsimd", "collective_compute", "AllReduce", ALU.add,
            replica_groups=[[0, 1, 2, 3], [4, 5, 6, 7]],
            ins=[cckv_in.ap().opt()], outs=[cckv_out.ap().opt()], inc="cc")
    em.ewait("gpsimd", ("cc", em.now("cc")))
    em.emit("gpsimd", "dma_start", out=kvr, in_=cckv_out.ap(), inc="cl", incv=16)
    CL_KV = em.now("cl")

    if upto == "E":
        return _stop_here()

    ddm = {}
    for t in range(NT):
        pso = 0 if t % 2 == 0 else 1024
        if t >= 2:
            tw = [("vm", ddm[t - 2][0]), ("sm", ddm[t - 2][1])]
        else:
            tw = [("vm", em.now("vm")), ("sm", em.now("sm"))]
        first = True
        for pr in range(8):
            em.emit("tensor", "matmul", psA[:, pso + pr * 128: pso + (pr + 1) * 128],
                    q_bf[pr][:, t * 128:(t + 1) * 128], pd2,
                    start=True, stop=True, waits=tw if first else ())
            first = False
        em.emit("tensor", None, inc="tm")
        tmv = em.now("tm")
        dd3 = psA[:, pso:pso + 1024].rearrange("p (h m) -> p h m", h=H)
        em.emit("vector", "tensor_reduce", mxloc, dd3, axis=AX.X, op=ALU.max,
                waits=[("tm", tmv)], inc="vm")
        em.fence("vector", "vm")
        em.emit("vector", "scalar_tensor_tensor", biasr, mxloc, -1.0,
                diagq_tm[:, t * 16:(t + 1) * 16], op0=ALU.mult, op1=ALU.subtract, inc="vm")
        em.fence("vector", "vm")
        vm_b = em.now("vm")
        for h in range(8, H):
            em.emit("scalar", "activation", stageE[:, h * 64:(h + 1) * 64],
                    psA[:, pso + h * 64: pso + h * 64 + 64], AF.Exp,
                    bias=biasr[:, h:h + 1],
                    waits=[("vm", vm_b), ("tm", tmv)] if h == 8 else (),
                    inc="sm" if h == H - 1 else None)
        for h in range(8):
            em.emit("vector", "tensor_scalar_add", stage[:, h * 64:(h + 1) * 64],
                    psA[:, pso + h * 64: pso + h * 64 + 64], biasr[:, h:h + 1],
                    inc="vm" if h == 7 else None)
        em.emit("scalar", "activation", stageE[:, 0:512], stage, AF.Exp,
                waits=[("vm", em.now("vm"))], inc="sm")
        em.emit("vector", "tensor_scalar", qf_tm[t], stageE, EPSK, RATIO,
                op0=ALU.add, op1=ALU.mult, waits=[("sm", em.now("sm"))], inc="vm")
        ddm[t] = (em.now("vm"), em.now("sm"))
    VM_QF = em.now("vm")

    if upto == "C2":
        return _stop_here()

    qf_fm = [V_(R_A + 4 * c, T, BF16) for c in range(DC)]
    for t in range(NT):
        bank = t % 4
        tw = [("vm", VM_QF)]
        if t >= 4:
            tw.append(("vm", em.now("vm")))
        for c in range(DC):
            em.emit("tensor", "transpose",
                    psA_bf[:, bank * 1024 + c * 128: bank * 1024 + (c + 1) * 128],
                    qf_tm[t][:, c * 128:(c + 1) * 128], id_b,
                    waits=tw if c == 0 else ())
        em.emit("tensor", None, inc="tm")
        tmv = em.now("tm")
        for c in range(DC):
            em.emit("vector", "tensor_copy", qf_fm[c][:, t * 128:(t + 1) * 128],
                    psA_bf[:, bank * 1024 + c * 128: bank * 1024 + (c + 1) * 128],
                    waits=[("tm", tmv)] if c == 0 else (),
                    inc="vm" if c == DC - 1 else None)
    VM_QFFM = em.now("vm")

    if "kv" in taps:
        tap_kv = nc.dram_tensor("tap_kv", [128, 520], F32, kind="ExternalOutput").ap()
        em.emit("gpsimd", "dma_start", out=tap_kv, in_=kvr, waits=[("cl", CL_KV)],
                inc="tst", incv=16)
    if "qf" in taps:
        tap_qf = nc.dram_tensor("tap_qf", [DC, 128, T], F32, kind="ExternalOutput").ap()
        for c in range(DC):
            em.emit("gpsimd", "dma_start", out=tap_qf[c], in_=qf_fm[c],
                    waits=[("vm", VM_QFFM)], inc="tst", incv=16)

    if upto == "D":
        return _stop_here()

    em.ewait("vector", ("cl", CL_KV))
    em.emit("vector", "memset", kvbd, 0.0, inc="vm")
    em.emit("vector", "memset", ksbd, 0.0, inc="vm")
    em.fence("vector", "vm")
    for hh in range(H):
        ph, cg = hh % 2, hh // 2
        src = kvr[ph * 64:(ph + 1) * 64, cg * 65: cg * 65 + 64]
        srcs = kvr[ph * 64:(ph + 1) * 64, cg * 65 + 64: cg * 65 + 65]
        em.emit("vector", "tensor_copy",
                kvbd[ph * 64:(ph + 1) * 64, cg * 128 + ph * 64: cg * 128 + ph * 64 + 64],
                src, inc=None)
        em.emit("vector", "tensor_copy",
                ksbd[ph * 64:(ph + 1) * 64, cg * 16 + hh: cg * 16 + hh + 1],
                srcs, inc="vm" if hh == H - 1 else None)
    VM_KVBD = em.now("vm")

    for kr in range(DC):
        first = True
        for tc in range(4):
            em.emit("tensor", "matmul", psB[0:16, tc * 512:(tc + 1) * 512],
                    ksbd[:, kr * 16:(kr + 1) * 16], qf_fm[kr][:, tc * 512:(tc + 1) * 512],
                    start=(kr == 0), stop=(kr == DC - 1),
                    waits=[("vm", VM_KVBD), ("vm", VM_QFFM)] if (first and kr == 0) else ())
            first = False
    em.emit("tensor", None, inc="tm")
    rcp_den = V_(SCR, 2048, F32, parts=16)
    em.emit("vector", "tensor_copy", rcp_den, psB[0:16, :], waits=[("tm", em.now("tm"))],
            inc="vm")
    em.emit("vector", "reciprocal", rcp_den, rcp_den, inc="vm")
    VM_RCP = em.now("vm")

    if "den" in taps:
        tap_den = nc.dram_tensor("tap_den", [16, 2048], F32, kind="ExternalOutput").ap()
        em.emit("sync", "dma_start", out=tap_den, in_=rcp_den, waits=[("vm", VM_RCP)],
                inc="st", incv=16)

    attn_bf = [V_(R_A + 32 + 4 * c, T, BF16) for c in range(DC)]
    bc_sb = V_(SCR + 8, 2048, F32)
    for c in range(DC):
        tw = [("vm", VM_RCP), ("vm", em.now("vm"))]
        first = True
        for tc in range(4):
            em.emit("tensor", "matmul", psA[:, tc * 512:(tc + 1) * 512],
                    kvbd[:, c * 128:(c + 1) * 128], qf_fm[c][:, tc * 512:(tc + 1) * 512],
                    start=True, stop=True, waits=tw if first else ())
            first = False
        for tc in range(4):
            em.emit("tensor", "matmul", psB[:, tc * 512:(tc + 1) * 512],
                    e2_c[:, c * 128:(c + 1) * 128], rcp_den[:, tc * 512:(tc + 1) * 512],
                    start=True, stop=True)
        em.emit("tensor", None, inc="tm")
        em.emit("vector", "tensor_copy", bc_sb, psB, waits=[("tm", em.now("tm"))], inc="vm")
        em.emit("vector", "tensor_mul", attn_bf[c], psA, bc_sb, inc="vm")
    VM_ATTN = em.now("vm")

    wo_sb = V_(R_B, 8 * 1024, BF16)
    em.emit("gpsimd", "dma_start", out=wo_sb.rearrange("p (k n) -> p k n", k=8),
            in_=wo_hbm.rearrange("(k p) n -> p k n", p=128),
            waits=[("vm", VM_QFFM)], inc="wl", incv=16)
    WL_WO = em.now("wl")
    xnch = [V_(SCR + 8, 2048, F32), V_(SCR + 16, 2048, F32)]
    sum1_sb = ([V_(R_A + 8 * i, 2048, F32) for i in range(4)] +
               [V_(R_B + 16 + 8 * i, 2048, F32) for i in range(4)])
    sqch = V_(R_B + 48, 2048, F32)

    for oc in range(DC):
        w = [("sp", SP_XN), ("tm", TM_KV), ("vm", VM_ATTN)]
        if oc >= 2:
            w.append(("vm", em.now("vm")))
        em.emit("sync", "dma_start", out=xnch[oc % 2], in_=xnf_sp[oc], waits=w,
                inc="xl", incv=16)
        xlv = em.now("xl")
        tw = [("wl", WL_WO), ("vm", VM_ATTN), ("vm", em.now("vm")), ("sm", em.now("sm"))]
        first = True
        for kr in range(DC):
            for tc in range(4):
                em.emit("tensor", "matmul", psA[:, tc * 512:(tc + 1) * 512],
                        wo_sb[:, kr * 1024 + oc * 128: kr * 1024 + (oc + 1) * 128],
                        attn_bf[kr][:, tc * 512:(tc + 1) * 512],
                        start=(kr == 0), stop=(kr == DC - 1), waits=tw if first else ())
                first = False
        em.emit("tensor", None, inc="tm")
        tmv = em.now("tm")
        em.emit("vector", "tensor_add", sum1_sb[oc], psA, xnch[oc % 2],
                waits=[("tm", tmv), ("xl", xlv)], inc="vm")
        vmv = em.now("vm")
        em.emit("scalar", "activation", sqch, sum1_sb[oc], AF.Square,
                waits=[("vm", vmv)], inc="sm")
        smv = em.now("sm")
        for tc in range(4):
            em.emit("tensor", "matmul", psB[0:1, tc * 512:(tc + 1) * 512],
                    ones_col, sum1_sb[oc][:, tc * 512:(tc + 1) * 512],
                    start=(oc == 0), stop=(oc == DC - 1),
                    waits=[("vm", vmv)] if tc == 0 else ())
        for tc in range(4):
            em.emit("tensor", "matmul", psB[32:33, tc * 512:(tc + 1) * 512],
                    ones_col, sqch[:, tc * 512:(tc + 1) * 512],
                    start=(oc == 0), stop=(oc == DC - 1), skip_group_check=True,
                    waits=[("sm", smv)] if tc == 0 else ())
        em.emit("tensor", None, inc="tm")
    SP_SUM1 = em.now("sp")
    VM_SUM1 = em.now("vm")

    def emit_ln_rows(eps, width, rowbase_kb):
        if isinstance(rowbase_kb, tuple):
            b0, b1, b2 = rowbase_kb
        else:
            b0 = rowbase_kb
            b1 = rowbase_kb + width * 4 / KB
            b2 = rowbase_kb + 2 * width * 4 / KB
        mu_r = V_(b0, width, F32, parts=1)
        var_r = V_(b1, width, F32, parts=1)
        tmp_r = V_(b2, width, F32, parts=1)
        em.emit("vector", "tensor_scalar_mul", mu_r, psB[0:1, 0:width], 1.0 / D,
                waits=[("tm", em.now("tm"))], inc="vm")
        em.emit("vector", "tensor_scalar", var_r, psB[32:33, 0:width], 1.0 / D, eps,
                op0=ALU.mult, op1=ALU.add, inc="vm")
        em.emit("vector", "tensor_mul", tmp_r, mu_r, mu_r, inc="vm")
        em.emit("vector", "tensor_sub", var_r, var_r, tmp_r, inc="vm")
        em.emit("scalar", "activation", tmp_r, var_r, AF.Sqrt,
                waits=[("vm", em.now("vm"))], inc="sm")
        em.emit("vector", "reciprocal", var_r, tmp_r, waits=[("sm", em.now("sm"))], inc="vm")
        em.emit("vector", "scalar_tensor_tensor", mu_r, mu_r, -1.0, var_r,
                op0=ALU.mult, op1=ALU.mult, inc="vm")
        vmv = em.now("vm")
        for tc in range(width // 512):
            em.emit("tensor", "matmul", psA[:, tc * 512:(tc + 1) * 512], ones_row,
                    var_r[:, tc * 512:(tc + 1) * 512], start=True, stop=True,
                    waits=[("vm", vmv)] if tc == 0 else ())
        for tc in range(width // 512):
            em.emit("tensor", "matmul", psB[:, tc * 512:(tc + 1) * 512], ones_row,
                    mu_r[:, tc * 512:(tc + 1) * 512], start=True, stop=True)
        em.emit("tensor", None, inc="tm")
        return em.now("tm")

    tm_bc1 = emit_ln_rows(EPS_LN, 2048, (R_B + 56, R_B + 48, SCR + 16))
    out1_bf = [V_(R_A + 4 * c, T, BF16) for c in range(DC)]
    for oc in range(DC):
        ssem = "soA" if oc % 2 == 0 else "soB"
        o1f = sum1_sb[oc]
        em.emit("vector", "tensor_mul", o1f, o1f, psA, waits=[("tm", tm_bc1)], inc="vm")
        em.emit("vector", "tensor_add", o1f, o1f, psB, inc="vm")
        em.emit("vector", "tensor_scalar", o1f, o1f, par["ln1g"][:, oc:oc + 1],
                par["ln1b"][:, oc:oc + 1], op0=ALU.mult, op1=ALU.add, inc="vm")
        vmv = em.now("vm")
        em.emit("scalar", "dma_start", out=out1f_sp[oc], in_=o1f, waits=[("vm", vmv)],
                inc=ssem, incv=16, serial=False)
    em.ewait("scalar", ("soA", em.now("soA")), ("soB", em.now("soB")))
    em.emit("scalar", "sem_inc", "sp", DC * 16)
    for oc in range(DC):
        em.emit("scalar", "activation", out1_bf[oc], sum1_sb[oc], AF.Copy, inc="sm")
    SP_OUT1 = em.now("sp")
    SM_OUT1 = em.now("sm")

    if "out1" in taps:
        tap_o1 = nc.dram_tensor("tap_out1", [DC, 128, T], F32, kind="ExternalOutput").ap()
        for oc in range(DC):
            em.emit("gpsimd", "dma_start", out=tap_o1[oc], in_=out1_bf[oc],
                    waits=[("sm", SM_OUT1)], inc="tst", incv=16)

    if upto == "G":
        return _stop_here()

    H_H = ([V_(XNBF + 2 * i, HALF, BF16) for i in range(16)] +
           [V_(R_A + 32 + 2 * (i - 16), HALF, BF16) for i in range(16, 32)])
    w1slab = [V_(SCR, 1024, BF16), V_(SCR + 2, 1024, BF16)]
    w2_sb = V_(R_B, FC * 1024, BF16)
    elu1 = V_(SCR + 4, 1024, F32)
    o1chb4 = [V_(SCR, 1024, F32), V_(SCR + 4, 1024, F32),
              V_(SCR + 8, 1024, F32), V_(SCR + 12, 1024, F32)]
    elu2 = V_(SCR + 8, 1024, F32)
    o1ch = V_(SCR + 12, 1024, F32)
    sum2ch = V_(SCR + 16, 1024, F32)
    sq2ch = V_(SCR + 20, 1024, F32)

    em.emit("gpsimd", "dma_start", out=w2_sb.rearrange("p (k n) -> p k n", k=FC),
            in_=w2_hbm.rearrange("k p n -> p k n"),
            waits=[("vm", em.now("vm")), ("tm", em.now("tm")), ("sm", em.now("sm")),
                   ("sp", SP_OUT1)], inc="wl", incv=16)
    WL_W2 = em.now("wl")

    for hf in range(2):
        tmbase = em.now("tm")
        emarks = {}
        for oc in range(FC):
            w = [("sm", SM_OUT1), ("sp", SP_OUT1)]
            if oc >= 2:
                w.append(("tm", tmbase + oc - 1))
            else:
                w.append(("tm", tmbase))
                w += [(f"rl{i}", em.now(f"rl{i}")) for i in range(4)]
                w.append(("vm", em.now("vm")))
            wsem = "wlA" if oc % 2 == 0 else "wlB"
            em.emit("gpsimd", "dma_start",
                    out=w1slab[oc % 2].rearrange("p (k n) -> p k n", k=DC),
                    in_=w1_hbm[oc], waits=w, inc=wsem, incv=16, serial=False)
            pso = 0 if oc % 2 == 0 else 1024
            tw = [(wsem, em.now(wsem)), ("sm", SM_OUT1), ("sp", SP_OUT1),
                  ("vm", emarks[oc - 2] if oc >= 2 else em.now("vm"))]
            first = True
            for k in range(DC):
                for tc2 in range(2):
                    em.emit("tensor", "matmul",
                            psA[:, pso + tc2 * 512: pso + (tc2 + 1) * 512],
                            w1slab[oc % 2][:, k * 128:(k + 1) * 128],
                            out1_bf[k][:, hf * HALF + tc2 * 512: hf * HALF + (tc2 + 1) * 512],
                            start=(k == 0), stop=(k == DC - 1), waits=tw if first else ())
                    first = False
            em.emit("tensor", None, inc="tm")
            tmv = em.now("tm")
            ps = psA[:, pso:pso + 1024]
            em.emit("scalar", "activation", elu1, ps, AF.Relu,
                    bias=par["negb1"][:, oc:oc + 1], scale=-1.0,
                    waits=[("tm", tmv), ("vm", em.now("vm"))], inc="sm")
            em.emit("scalar", "activation", elu2, elu1, AF.Exp, scale=-1.0, inc="sm")
            smv = em.now("sm")
            em.emit("vector", "scalar_tensor_tensor", elu2, elu1, 1.0, elu2,
                    op0=ALU.mult, op1=ALU.add, waits=[("sm", smv)], inc="vm")
            em.emit("vector", "scalar_tensor_tensor", elu2, elu2, 1.0, ps,
                    op0=ALU.mult, op1=ALU.add, inc="vm")
            em.emit("vector", "tensor_scalar_add", H_H[oc], elu2,
                    par["b1m1"][:, oc:oc + 1], inc="vm")
            emarks[oc] = em.now("vm")
        assert em.now("tm") == tmbase + FC

        if "helu" in taps:
            if hf == 0:
                tap_he = nc.dram_tensor("tap_helu", [2, FC, 128, HALF], F32,
                                        kind="ExternalOutput").ap()
                globals()["_tap_he"] = tap_he
            tap_he = globals()["_tap_he"]
            for k in range(FC):
                em.emit("gpsimd", "dma_start", out=tap_he[hf][k], in_=H_H[k],
                        waits=[("vm", em.now("vm"))], inc="tst", incv=16)

        smarks = {}
        for oc in range(DC):
            w = [("sp", SP_OUT1)]
            if oc == 0:
                w += [("vm", em.now("vm")), ("sm", em.now("sm"))]
            if oc >= 1:
                w += [("vm", em.now("vm")), ("sm", em.now("sm"))]
            em.emit("sync", "dma_start", out=o1ch,
                    in_=out1f_sp[oc][:, hf * HALF:(hf + 1) * HALF], waits=w,
                    inc="xl", incv=16)
            xlv = em.now("xl")
            pso = 0 if oc % 2 == 0 else 1024
            tw = [("wl", WL_W2),
                  ("vm", smarks[oc - 2] if oc >= 2 else em.now("vm"))]
            first = True
            for k in range(FC):
                for tc2 in range(2):
                    em.emit("tensor", "matmul",
                            psA[:, pso + tc2 * 512: pso + (tc2 + 1) * 512],
                            w2_sb[:, k * 1024 + oc * 128: k * 1024 + (oc + 1) * 128],
                            H_H[k][:, tc2 * 512:(tc2 + 1) * 512],
                            start=(k == 0), stop=(k == FC - 1), waits=tw if first else ())
                    first = False
            em.emit("tensor", None, inc="tm")
            tmv = em.now("tm")
            ps = psA[:, pso:pso + 1024]
            em.emit("vector", "scalar_tensor_tensor", sum2ch, ps, par["b2v"][:, oc:oc + 1],
                    o1ch, op0=ALU.add, op1=ALU.add, waits=[("tm", tmv), ("xl", xlv)],
                    inc="vm")
            vmv = em.now("vm")
            smarks[oc] = vmv
            em.emit("sync", "dma_start", out=sum2_sp[hf][oc], in_=sum2ch, waits=[("vm", vmv)],
                    inc="sp", incv=16)
            em.emit("scalar", "activation", sq2ch, sum2ch, AF.Square, waits=[("vm", vmv)],
                    inc="sm")
            smv = em.now("sm")
            for tc2 in range(2):
                em.emit("tensor", "matmul", psB[0:1, tc2 * 512:(tc2 + 1) * 512],
                        ones_col, sum2ch[:, tc2 * 512:(tc2 + 1) * 512],
                        start=(oc == 0), stop=(oc == DC - 1),
                        waits=[("vm", vmv)] if tc2 == 0 else ())
            for tc2 in range(2):
                em.emit("tensor", "matmul", psB[32:33, tc2 * 512:(tc2 + 1) * 512],
                        ones_col, sq2ch[:, tc2 * 512:(tc2 + 1) * 512],
                        start=(oc == 0), stop=(oc == DC - 1), skip_group_check=True,
                        waits=[("sm", smv)] if tc2 == 0 else ())
            em.emit("tensor", None, inc="tm")
        SP_SUM2 = em.now("sp")

        tm_bc2 = emit_ln_rows(EPS_LN, HALF, XNBF)
        ap2 = {}
        for oc in range(DC):
            rsem = f"rl{oc % 4}"
            ssem = f"so{oc % 4}"
            w = [("sp", SP_SUM2), ("tm", tm_bc2)]
            if oc >= 4:
                w.append((ssem, ap2[oc - 4]))
            ob = o1chb4[oc % 4]
            em.emit("sync", "dma_start", out=ob, in_=sum2_sp[hf][oc], waits=w,
                    inc=rsem, incv=16, serial=False)
            rlv = em.now(rsem)
            em.emit("vector", "tensor_mul", ob, ob, psA[:, 0:HALF],
                    waits=[(rsem, rlv), ("tm", tm_bc2)], inc="vm")
            em.emit("vector", "tensor_add", ob, ob, psB[:, 0:HALF], inc="vm")
            em.emit("vector", "tensor_scalar", ob, ob, par["ln2g"][:, oc:oc + 1],
                    par["ln2b"][:, oc:oc + 1], op0=ALU.mult, op1=ALU.add, inc="vm")
            em.emit("scalar", "dma_start", out=attnb_sp[hf][oc], in_=ob,
                    waits=[("vm", em.now("vm"))], inc=ssem, incv=16, serial=False)
            ap2[oc] = em.now(ssem)
    em.ewait("scalar", *[(f"so{i}", em.now(f"so{i}")) for i in range(4)])
    em.emit("scalar", "sem_inc", "sp", 4 * DC * 16)
    SP_ATTNB = em.now("sp")

    if "attnb" in taps:
        tap_ab = nc.dram_tensor("tap_attnb", [2, DC, 128, HALF], F32,
                                kind="ExternalOutput").ap()
        for hf in range(2):
            for oc in range(DC):
                em.emit("sync", "dma_start", out=tap_ab[hf][oc], in_=attnb_sp[hf][oc],
                        waits=[("sp", SP_ATTNB)], inc="st", incv=16)

    if upto == "H":
        return _stop_here()

    xn2_bf = [V_(R_A + 4 * c, T, BF16) for c in range(DC)]
    em.emit("sync", "dma_start", out=xn2_bf[0], in_=xnb_sp[0],
            waits=[("sp", SP_XN)] + em.allw(), inc="xl", incv=16)
    for c in range(1, DC):
        em.emit("sync", "dma_start", out=xn2_bf[c], in_=xnb_sp[c], inc="xl", incv=16)
    XL_XN2 = em.now("xl")

    tslab = [V_(SCR, 4096, BF16), V_(SCR + 8, 4096, BF16)]
    hslab = [V_(SCR, 4096, BF16), V_(SCR + 8, 4096, BF16)]
    silu_s = V_(SCR + 16, 1024, F32)
    silu_s2 = V_(SCR + 20, 1024, F32)
    y3ch = V_(SCR + 16, 1024, F32)
    abchb = [V_(6.5625, 1024, F32), V_(SCR + 20, 1024, F32)]

    for hf in range(2):
        TH = [V_(R_B + 2 * i, HALF, BF16) for i in range(FC)]
        G1 = ([V_(XNBF + 2 * i, HALF, BF16) for i in range(16)] +
              [V_(R_A + 32 + 2 * (i - 16), HALF, BF16) for i in range(16, 32)])
        G2 = TH

        tmbase = em.now("tm")
        tmarks = {}
        for oc in range(DC):
            w = [("xl", XL_XN2)]
            if oc == 0:
                w += [("tm", em.now("tm")), ("vm", em.now("vm")), ("sm", em.now("sm")),
                      ("sp", SP_ATTNB), ("st", em.now("st"))]
                w += [(f"rl{i}", em.now(f"rl{i}")) for i in range(4)]
            else:
                w.append(("tm", tmbase + 4 * oc))
            em.emit("gpsimd", "dma_start",
                    out=tslab[0].rearrange("p (k n) -> p k n", k=DC),
                    in_=w1t_hbm.rearrange("k p n -> p k n")[:, :, oc * 512:(oc + 1) * 512],
                    waits=w, inc="wt0", incv=16, serial=False)
            em.emit("gpsimd", "dma_start",
                    out=tslab[1].rearrange("p (k n) -> p k n", k=DC),
                    in_=w2t_hbm.rearrange("k p n -> p k n")[:, :, oc * 512:(oc + 1) * 512],
                    inc="wt1", incv=16, serial=False)
            for s4 in range(4):
                occ = oc * 4 + s4
                pst = psA if occ % 2 == 0 else psB
                sil = silu_s if occ % 2 == 0 else silu_s2
                tw = [("wt0", em.now("wt0")), ("wt1", em.now("wt1"))]
                tw.append(("vm", tmarks[occ - 2] if occ >= 2 else em.now("vm")))
                if occ < 2:
                    tw.append(("sm", em.now("sm")))
                first = True
                for k in range(DC):
                    for tc2 in range(2):
                        em.emit("tensor", "matmul",
                                pst[:, tc2 * 512:(tc2 + 1) * 512],
                                tslab[0][:, k * 512 + s4 * 128: k * 512 + (s4 + 1) * 128],
                                xn2_bf[k][:, hf * HALF + tc2 * 512: hf * HALF + (tc2 + 1) * 512],
                                start=(k == 0), stop=(k == DC - 1), waits=tw if first else ())
                        first = False
                for k in range(DC):
                    for tc2 in range(2):
                        em.emit("tensor", "matmul",
                                pst[:, 1024 + tc2 * 512: 1024 + (tc2 + 1) * 512],
                                tslab[1][:, k * 512 + s4 * 128: k * 512 + (s4 + 1) * 128],
                                xn2_bf[k][:, hf * HALF + tc2 * 512: hf * HALF + (tc2 + 1) * 512],
                                start=(k == 0), stop=(k == DC - 1))
                em.emit("tensor", None, inc="tm")
                tmv = em.now("tm")
                em.emit("scalar", "activation", sil, pst[:, 0:1024], AF.Silu,
                        scale=par["w1s"][:, occ:occ + 1], waits=[("tm", tmv)], inc="sm")
                em.emit("vector", "scalar_tensor_tensor", TH[occ], pst[:, 1024:2048],
                        par["w2s"][:, occ:occ + 1], sil, op0=ALU.mult, op1=ALU.mult,
                        waits=[("sm", em.now("sm"))], inc="vm")
                tmarks[occ] = em.now("vm")
        assert em.now("tm") == tmbase + 4 * DC
        VM_TH = em.now("vm")

        def big44(src_tiles, dst_tiles, w_sg, scale_par, src_sem, src_val):
            tmbase2 = em.now("tm")
            gmarks = {}
            for oc in range(FC):
                w = [(src_sem, src_val)]
                if oc == 0:
                    w += [("tm", em.now("tm")), ("sm", em.now("sm")), ("vm", em.now("vm")),
                          ("st", em.now("st"))]
                else:
                    w.append(("tm", tmbase2 + oc - 1))
                wsem = "whA" if oc % 2 == 0 else "whB"
                em.emit("gpsimd", "dma_start",
                        out=hslab[oc % 2].rearrange("p (k n) -> p k n", k=FC),
                        in_=w_sg[oc], waits=w, inc=wsem, incv=16, serial=False)
                pso = 0 if oc % 2 == 0 else 1024
                tw = [(wsem, em.now(wsem)), (src_sem, src_val)]
                tw.append(("sm", gmarks[oc - 2] if oc >= 2 else em.now("sm")))
                first = True
                for k in range(FC):
                    for tc2 in range(2):
                        em.emit("tensor", "matmul",
                                psA[:, pso + tc2 * 512: pso + (tc2 + 1) * 512],
                                hslab[oc % 2][:, k * 128:(k + 1) * 128],
                                src_tiles[k][:, tc2 * 512:(tc2 + 1) * 512],
                                start=(k == 0), stop=(k == FC - 1), waits=tw if first else ())
                        first = False
                em.emit("tensor", None, inc="tm")
                tmv = em.now("tm")
                em.emit("scalar", "activation", dst_tiles[oc], psA[:, pso:pso + 1024],
                        AF.Gelu, scale=scale_par[:, oc:oc + 1], waits=[("tm", tmv)],
                        inc="sm")
                gmarks[oc] = em.now("sm")
            assert em.now("tm") == tmbase2 + FC
            return em.now("sm")

        if upto == "I1" and hf == 0:
            return _stop_here()
        sm_g1 = big44(TH, G1, h1t_hbm, par["h1s"], "vm", VM_TH)
        sm_g2 = big44(G1, G2, h2t_hbm, par["h2s"], "sm", sm_g1)

        if upto == "I2" and hf == 0:
            return _stop_here()
        xh = [V_(R_A + 32 + 4 * i, 1024, F32) for i in range(8)]
        for i in range(8):
            em.emit("sync", "dma_start", out=xh[i],
                    in_=x_hbm[hf * HALF + i * 128: hf * HALF + (i + 1) * 128, :],
                    waits=[("sm", sm_g2)] if i == 0 else (), inc="xl", incv=16)
        XL_XH = em.now("xl")
        out_tm = [V_(XNBF + 4 * i, 1024, F32) for i in range(8)]
        tmbase3 = em.now("tm")
        fmarks = {}
        for oc in range(DC):
            w = [("sm", sm_g2)]
            if oc == 0:
                w += [("tm", em.now("tm")), ("vm", em.now("vm"))]
            else:
                w.append(("tm", tmbase3 + 2 * oc - 1))
            wsem = "whA" if oc % 2 == 0 else "whB"
            em.emit("gpsimd", "dma_start",
                    out=hslab[oc % 2].rearrange("p (k n) -> p k n", k=FC),
                    in_=w3t_hbm[oc], waits=w, inc=wsem, incv=16, serial=False)
            tw = [(wsem, em.now(wsem)), ("vm", em.now("vm"))]
            first = True
            for k in range(FC):
                for tc2 in range(2):
                    em.emit("tensor", "matmul", psA[:, tc2 * 512:(tc2 + 1) * 512],
                            hslab[oc % 2][:, k * 128:(k + 1) * 128],
                            G2[k][:, tc2 * 512:(tc2 + 1) * 512],
                            start=(k == 0), stop=(k == FC - 1), waits=tw if first else ())
                    first = False
            em.emit("tensor", None, inc="tm")
            tmv = em.now("tm")
            rsem = "rlA" if oc % 2 == 0 else "rlB"
            w = [("sp", SP_ATTNB)]
            if oc >= 2:
                w.append(("vm", fmarks[oc - 2]))
            ab = abchb[oc % 2]
            em.emit("sync", "dma_start", out=ab, in_=attnb_sp[hf][oc], waits=w,
                    inc=rsem, incv=16, serial=False)
            rlv = em.now(rsem)
            em.emit("vector", "scalar_tensor_tensor", y3ch, psA[:, 0:1024],
                    par["w3s"][:, oc:oc + 1], ab, op0=ALU.mult, op1=ALU.add,
                    waits=[("tm", tmv), (rsem, rlv)], inc="vm")
            fmarks[oc] = em.now("vm")
            vmv = em.now("vm")
            for p in range(8):
                em.emit("tensor", "transpose",
                        psB[:, 1024 + p * 128: 1024 + (p + 1) * 128],
                        y3ch[:, p * 128:(p + 1) * 128], id_f,
                        waits=[("vm", vmv)] if p == 0 else ())
            em.emit("tensor", None, inc="tm")
            tmv2 = em.now("tm")
            for p in range(8):
                em.emit("vector", "tensor_add", out_tm[p][:, oc * 128:(oc + 1) * 128],
                        psB[:, 1024 + p * 128: 1024 + (p + 1) * 128],
                        xh[p][:, oc * 128:(oc + 1) * 128],
                        waits=[("tm", tmv2), ("xl", XL_XH)] if p == 0 else (),
                        inc="vm" if p == 7 else None)
        assert em.now("tm") == tmbase3 + 2 * DC
        VM_OUT = em.now("vm")
        for i in range(8):
            em.emit("sync", "dma_start",
                    out=out_hbm[hf * HALF + i * 128: hf * HALF + (i + 1) * 128, :],
                    in_=out_tm[i], waits=[("vm", VM_OUT)] if i == 0 else (),
                    inc="st", incv=16)

    em.ewait("sync", ("st", em.now("st")))
    em.ewait("gpsimd", ("st", em.now("st")))
    if em.now("tst"):
        em.ewait("gpsimd", ("tst", em.now("tst")))

    em.realize(nc)
    return nc


def prepare_in_maps(inputs):
    x = np.ascontiguousarray(np.asarray(inputs["L0_attn_x"], np.float32).reshape(B * N, D))

    def t2d(a):
        return np.ascontiguousarray(np.asarray(a, np.float32))

    def slabgrid(w):
        K, NN = w.shape
        return np.ascontiguousarray(
            w.reshape(K // 128, 128, NN // 128, 128).transpose(2, 1, 0, 3))

    def kchunks(w):
        K, NN = w.shape
        return np.ascontiguousarray(w.reshape(K // 128, 128, NN))

    def vec128(v):
        v = t2d(v).ravel()
        return np.ascontiguousarray(v.reshape(-1, 128).T)

    common = {
        "wq": t2d(inputs["L0_attn_Wq"]).reshape(D, D),
        "wk": t2d(inputs["L0_attn_Wk"]).reshape(D, D),
        "wv": t2d(inputs["L0_attn_Wv"]).reshape(D, D),
        "wo": t2d(inputs["L0_attn_Wo"]).reshape(D, D),
        "proj": t2d(inputs["L0_attn_proj"]),
        "w1": slabgrid(t2d(inputs["L0_attn_W1"])),
        "w2": kchunks(t2d(inputs["L0_attn_W2"])),
        "w1t": kchunks(t2d(inputs["L0_ffn_w1_w"]).T),
        "w2t": kchunks(t2d(inputs["L0_ffn_w2_w"]).T),
        "h1t": slabgrid(t2d(inputs["L0_ffn_h1_w"]).T),
        "h2t": slabgrid(t2d(inputs["L0_ffn_h2_w"]).T),
        "w3t": slabgrid(t2d(inputs["L0_ffn_w3_w"]).T),
        "b1v": vec128(inputs["L0_attn_b1"]), "b2v": vec128(inputs["L0_attn_b2"]),
        "ln1g": vec128(inputs["L0_attn_ln1_g"]), "ln1b": vec128(inputs["L0_attn_ln1_b"]),
        "ln2g": vec128(inputs["L0_attn_ln2_g"]), "ln2b": vec128(inputs["L0_attn_ln2_b"]),
        "w1s": vec128(inputs["L0_ffn_w1_s"]), "w2s": vec128(inputs["L0_ffn_w2_s"]),
        "h1s": vec128(inputs["L0_ffn_h1_s"]), "h2s": vec128(inputs["L0_ffn_h2_s"]),
        "w3s": vec128(inputs["L0_ffn_w3_s"]),
    }
    in_maps = []
    for c in range(NCORES):
        m = dict(common)
        m["x"] = np.ascontiguousarray(x[c * T:(c + 1) * T])
        in_maps.append(m)
    return in_maps


def kernel(**inputs):
    in_maps = prepare_in_maps(inputs)
    nc = build()
    res = run_bass_kernel_spmd(nc, in_maps, core_ids=list(range(NCORES)))
    out = np.concatenate([res.results[c]["out"] for c in range(NCORES)], axis=0)
    return out.reshape(B, N, D).astype(np.float32)


# version 11
# speedup vs baseline: 1.1964x; 1.0163x over previous
import numpy as np
import ml_dtypes
import concourse.bass as bass
import concourse.mybir as mybir
from concourse.bass_utils import run_bass_kernel_spmd

F32 = mybir.dt.float32
BF16 = mybir.dt.bfloat16
AF = mybir.ActivationFunctionType
ALU = mybir.AluOpType
AX = mybir.AxisListType

NCORES = 8
B = 2
N = 8192
T = 2048
NT = 16
D = 1024
DC = 8
H = 16
DH = 64
M = 64
DFF = 4096
FC = 32
HALF = 1024
DN = 2.0 ** -1.5
RATIO = 0.125
EPSK = 1e-6
EPS_LN0 = 1e-5
EPS_LN = 1e-6
NEG = -3.0e38

ENGINES = ("sync", "gpsimd", "scalar", "vector", "tensor")
KB = 1024


class Em:
    def __init__(self):
        self.steps = {e: [] for e in ENGINES}
        self.sems = {}
        self._maxwait = {e: {} for e in ENGINES}

    def now(self, sem):
        return self.sems.setdefault(sem, 0)

    def emit(self, eng, method, *args, waits=(), inc=None, incv=1, serial=True, **kw):
        w = []
        for (s, v) in waits:
            self.sems.setdefault(s, 0)
            assert v <= self.sems[s], (eng, method, s, v, self.sems[s])
            if v > 0 and v > self._maxwait[eng].get(s, -1):
                w.append((s, v))
                self._maxwait[eng][s] = v
        if method == "sem_inc":
            self.steps[eng].append((method, args, kw, tuple(w), None, 0))
            self.sems[args[0]] = self.sems.setdefault(args[0], 0) + args[1]
            return
        if method is None and inc is not None:
            for i in range(len(self.steps[eng]) - 1, -1, -1):
                m2, a2, k2, w2, i2, v2 = self.steps[eng][i]
                if m2 is not None:
                    assert i2 is None, ("double marker on", eng, m2, i2, inc)
                    self.steps[eng][i] = (m2, a2, k2, w2, inc, incv)
                    break
            else:
                raise AssertionError("marker with no prior instruction")
            if w:
                self.steps[eng].append((None, (), {}, tuple(w), None, 0))
            self.sems[inc] = self.sems.setdefault(inc, 0) + incv
            return
        self.steps[eng].append((method, args, kw, tuple(w), inc, incv))
        if inc is not None:
            self.sems[inc] = self.sems.setdefault(inc, 0) + incv
        if method == "dma_start" and inc is not None and serial:
            self.ewait(eng, (inc, self.sems[inc]))

    def ewait(self, eng, *waits):
        self.emit(eng, None, waits=waits)

    def fence(self, eng, sem):
        for i in range(len(self.steps[eng]) - 1, -1, -1):
            m2, a2, k2, w2, i2, v2 = self.steps[eng][i]
            if m2 is not None:
                if i2 is None:
                    self.steps[eng][i] = (m2, a2, k2, w2, sem, 1)
                    self.sems[sem] = self.sems.setdefault(sem, 0) + 1
                    self.ewait(eng, (sem, self.now(sem)))
                else:
                    self.ewait(eng, (i2, self.now(i2)))
                return
        raise AssertionError("fence with no prior instruction")

    def allw(self):
        return [("tm", self.now("tm")), ("vm", self.now("vm")), ("sm", self.now("sm"))]

    def check(self):
        idx = {e: 0 for e in ENGINES}
        cnt = {s: 0 for s in self.sems}
        progress = True
        while progress:
            progress = False
            for e in ENGINES:
                while idx[e] < len(self.steps[e]):
                    method, args, kw, waits, inc, incv = self.steps[e][idx[e]]
                    if all(cnt[s] >= v for (s, v) in waits):
                        if method == "sem_inc":
                            cnt[args[0]] += args[1]
                        if inc is not None:
                            cnt[inc] += incv
                        idx[e] += 1
                        progress = True
                    else:
                        break
        stuck = {e: idx[e] for e in ENGINES if idx[e] < len(self.steps[e])}
        if stuck:
            for e, i in stuck.items():
                method, args, kw, waits, inc, incv = self.steps[e][i]
                unmet = [(s, v, cnt[s]) for (s, v) in waits if cnt[s] < v]
                print(f"DEADLOCK {e} step {i}/{len(self.steps[e])} {method} unmet={unmet}")
            raise RuntimeError("static deadlock detected")
        for s, v in cnt.items():
            assert v == self.sems[s], (s, v, self.sems[s])

    def realize(self, nc):
        self.check()
        handles = {name: nc.alloc_semaphore(name) for name in self.sems}
        with nc.Block() as block:
            for engname in ENGINES:
                def make(engname=engname):
                    def f(eng):
                        for (method, args, kw, waits, inc, incv) in self.steps[engname]:
                            for (s, v) in waits:
                                eng.wait_ge(handles[s], v)
                            if method is None:
                                continue
                            if method == "sem_inc":
                                eng.sem_inc(handles[args[0]], args[1])
                                continue
                            ins = getattr(eng, method)(*args, **kw)
                            if inc is not None:
                                ins.then_inc(handles[inc], incv)
                    return f
                getattr(block, engname)(make())


class Arena:
    def __init__(self, nc, cap_bytes):
        self.cap = cap_bytes
        self.t = nc.alloc_sbuf_tensor("arena", [128, cap_bytes], mybir.dt.uint8)

    def view(self, off_kb, cols, dt, parts=128):
        esz = 4 if dt == F32 else 2
        off = int(off_kb * KB)
        assert off % 32 == 0, off_kb
        assert off + cols * esz <= self.cap, (off_kb, cols, dt)
        return self.t[0:parts, off:off + cols * esz].bitcast(dt)


def build(taps=(), upto=None, nta=NT):
    taps = set(taps)
    nc = bass.Bass("TRN2", target_bir_lowering=False, detect_race_conditions=False)
    em = Em()

    x_hbm = nc.dram_tensor("x", [T, D], F32, kind="ExternalInput").ap()
    wq_hbm = nc.dram_tensor("wq", [D, D], F32, kind="ExternalInput").ap()
    wk_hbm = nc.dram_tensor("wk", [D, D], F32, kind="ExternalInput").ap()
    wv_hbm = nc.dram_tensor("wv", [D, D], F32, kind="ExternalInput").ap()
    wo_hbm = nc.dram_tensor("wo", [D, D], F32, kind="ExternalInput").ap()
    proj_hbm = nc.dram_tensor("proj", [M, DH], F32, kind="ExternalInput").ap()
    w1_hbm = nc.dram_tensor("w1", [FC, 128, DC, 128], F32, kind="ExternalInput").ap()
    w2_hbm = nc.dram_tensor("w2", [FC, 128, D], F32, kind="ExternalInput").ap()
    w1t_hbm = nc.dram_tensor("w1t", [DC, 128, DFF], F32, kind="ExternalInput").ap()
    w2t_hbm = nc.dram_tensor("w2t", [DC, 128, DFF], F32, kind="ExternalInput").ap()
    h1t_hbm = nc.dram_tensor("h1t", [FC, 128, FC, 128], F32, kind="ExternalInput").ap()
    h2t_hbm = nc.dram_tensor("h2t", [FC, 128, FC, 128], F32, kind="ExternalInput").ap()
    w3t_hbm = nc.dram_tensor("w3t", [DC, 128, FC, 128], F32, kind="ExternalInput").ap()
    vecs_hbm = {}
    for nm, ncols in [("b1v", FC), ("w1s", FC), ("w2s", FC), ("h1s", FC), ("h2s", FC),
                      ("b2v", DC), ("ln1g", DC), ("ln1b", DC), ("ln2g", DC), ("ln2b", DC),
                      ("w3s", DC)]:
        vecs_hbm[nm] = nc.dram_tensor(nm, [128, ncols], F32, kind="ExternalInput").ap()

    out_hbm = nc.dram_tensor("out", [T, D], F32, kind="ExternalOutput").ap()

    xnf_sp = nc.dram_tensor("xnf_sp", [DC, 128, T], F32).ap()
    xnb_sp = nc.dram_tensor("xnb_sp", [DC, 128, T], BF16).ap()
    sum1_sp = nc.dram_tensor("sum1_sp", [DC, 128, T], F32).ap()
    out1f_sp = nc.dram_tensor("out1f_sp", [DC, 128, T], F32).ap()
    sum2_sp = nc.dram_tensor("sum2_sp", [2, DC, 128, HALF], F32).ap()
    attnb_sp = nc.dram_tensor("attnb_sp", [2, DC, 128, HALF], F32).ap()

    w1bf = nc.dram_tensor("w1bf", [FC, 128, DC, 128], BF16).ap()
    w2bf = nc.dram_tensor("w2bf", [FC, 128, D], BF16).ap()
    w1tbf = nc.dram_tensor("w1tbf", [DC, 128, DFF], BF16).ap()
    w2tbf = nc.dram_tensor("w2tbf", [DC, 128, DFF], BF16).ap()
    h1tbf = nc.dram_tensor("h1tbf", [FC, 128, FC, 128], BF16).ap()
    h2tbf = nc.dram_tensor("h2tbf", [FC, 128, FC, 128], BF16).ap()
    w3tbf = nc.dram_tensor("w3tbf", [DC, 128, FC, 128], BF16).ap()
    ccmax_in = nc.dram_tensor("ccmax_in", [1, 8], F32)
    ccmax_out = nc.dram_tensor("ccmax_out", [1, 8], F32)
    cckv_in = nc.dram_tensor("cckv_in", [128, 520], F32)
    cckv_out = nc.dram_tensor("cckv_out", [128, 520], F32)

    id_f32_d = nc.inline_tensor(np.eye(128, dtype=np.float32), name="c_idf")
    id_bf_d = nc.inline_tensor(np.eye(128, dtype=ml_dtypes.bfloat16).view(np.uint16),
                               name="c_idb")
    bd_np = np.zeros((128, 128), np.float32)
    for kr in range(8):
        bd_np[0:64, kr * 16 + 2 * kr] = 1.0 / 16.0
        bd_np[64:128, kr * 16 + 2 * kr + 1] = 1.0 / 16.0
    bd_d = nc.inline_tensor(bd_np, name="c_bd")
    e2_np = np.zeros((16, 8 * 128), np.float32)
    for c in range(8):
        e2_np[2 * c, c * 128:c * 128 + 64] = 1.0
        e2_np[2 * c + 1, c * 128 + 64:c * 128 + 128] = 1.0
    e2_d = nc.inline_tensor(e2_np, name="c_e2")

    AR = Arena(nc, 206 * KB)
    V_ = AR.view
    id_f = V_(0.0, 128, F32)
    id_b = V_(0.5, 128, BF16)
    bd_c = V_(0.75, 128, F32)
    pd2 = V_(1.25, 128, BF16)
    kvbd = V_(1.5, 1024, BF16)
    ksbd = V_(3.5, 128, BF16)
    kvr = V_(3.75, 520, F32)
    ones_col = V_(5.8125, 1, F32)
    ones_row = V_(5.875, 128, F32)[0:1, :]
    mx128 = V_(6.375, 1, F32)
    mxloc = V_(6.4375, 16, F32)
    mxst = V_(6.5, 16, F32)
    e2_c = V_(6.5625, 1024, F32, parts=16)
    diagq_tm = V_(10.5625, 256, F32)
    diagk_tm = V_(11.5625, 256, F32)
    par = {}
    poff = 12.5625
    for nm, ncols in [("b1v", FC), ("negb1", FC), ("b1m1", FC), ("w1s", FC), ("w2s", FC),
                      ("h1s", FC), ("h2s", FC), ("b2v", DC), ("ln1g", DC), ("ln1b", DC),
                      ("ln2g", DC), ("ln2b", DC), ("w3s", DC)]:
        par[nm] = V_(poff, ncols, F32)
        poff += ncols * 4 / KB
    assert poff <= 14.0, poff

    XNBF = 14.0
    SCR = 46.0
    R_A = 70.0
    R_B = 134.0

    xn_bf = [V_(XNBF + 4 * c, T, BF16) for c in range(DC)]

    psA_t = nc.alloc_psum_tensor("psA", [128, 2048], F32)
    psB_t = nc.alloc_psum_tensor("psB", [128, 2048], F32)
    psA = psA_t[:, :]
    psB = psB_t[:, :]
    psA_bf = psA.bitcast(BF16)


    def _stop_here():
        em.emit("sync", "dma_start", out=out_hbm[0:128, :],
                in_=AR.t[0:128, 0:4096].bitcast(F32),
                waits=em.allw() + [("sp", em.now("sp"))], inc="st", incv=16)
        em.ewait("sync", ("st", em.now("st")))
        em.ewait("gpsimd", ("st", em.now("st")))
        if em.now("tst"):
            em.ewait("gpsimd", ("tst", em.now("tst")))
        em.realize(nc)
        return nc

    em.emit("sync", "dma_start", out=id_f, in_=id_f32_d.ap(), inc="xl", incv=16)
    em.emit("sync", "dma_start", out=id_b, in_=id_bf_d.ap().bitcast(BF16), inc="xl", incv=16)
    em.emit("sync", "dma_start", out=bd_c, in_=bd_d.ap(), inc="xl", incv=16)
    em.emit("sync", "dma_start", out=e2_c, in_=e2_d.ap(), inc="xl", incv=16)
    for nm in ["b1v", "w1s", "w2s", "h1s", "h2s", "b2v", "ln1g", "ln1b", "ln2g", "ln2b", "w3s"]:
        em.emit("sync", "dma_start", out=par[nm], in_=vecs_hbm[nm], inc="xl", incv=16)
    XL_CONSTS = em.now("xl")
    em.emit("vector", "tensor_scalar_add", par["b1m1"], par["b1v"], -1.0,
            waits=[("xl", XL_CONSTS)], inc="vm")
    em.emit("vector", "tensor_scalar_mul", par["negb1"], par["b1v"], -1.0, inc="vm")
    em.emit("vector", "memset", ones_col, 1.0, inc="vm")
    em.emit("vector", "memset", ones_row, 1.0, inc="vm")

    wv_sb = V_(R_B, 8 * 1024, BF16)
    wq_sb = V_(R_B + 16, 8 * 1024, BF16)
    wk_sb = V_(R_B + 32, 8 * 1024, BF16)
    proj_sb = V_(R_B + 48, 64, F32, parts=64)
    em.emit("gpsimd", "dma_start", out=wv_sb.rearrange("p (k n) -> p k n", k=8),
            in_=wv_hbm.rearrange("(k p) n -> p k n", p=128), inc="wl", incv=16)
    em.emit("gpsimd", "dma_start", out=wq_sb.rearrange("p (k n) -> p k n", k=8),
            in_=wq_hbm.rearrange("(k p) n -> p k n", p=128), inc="wl", incv=16)
    WL_WQ = em.now("wl")
    em.emit("gpsimd", "dma_start", out=wk_sb.rearrange("p (k n) -> p k n", k=8),
            in_=wk_hbm.rearrange("(k p) n -> p k n", p=128), inc="wl", incv=16)
    WL_WK = em.now("wl")
    WL_WV = 16
    em.emit("gpsimd", "dma_start", out=proj_sb, in_=proj_hbm, inc="wl", incv=16)
    WL_PROJ = em.now("wl")
    PC_ALL = 0

    em.emit("tensor", "transpose", psB[0:64, 0:64], proj_sb, id_f[0:64, 0:64],
            waits=[("wl", WL_PROJ), ("xl", XL_CONSTS)], inc="tm")
    TM_PROJ = em.now("tm")
    em.emit("vector", "memset", pd2, 0.0, inc="vm")
    em.fence("vector", "vm")
    em.emit("vector", "tensor_scalar_mul", pd2[0:64, 0:64], psB[0:64, 0:64], DN,
            waits=[("tm", TM_PROJ)], inc="vm")
    em.emit("vector", "tensor_scalar_mul", pd2[64:128, 64:128], psB[0:64, 0:64], DN, inc="vm")
    VM_PD2 = em.now("vm")

    xbuf = [V_(SCR, 1024, F32), V_(SCR + 4, 1024, F32), V_(SCR + 8, 1024, F32)]
    xnt2 = [V_(SCR + 12, 1024, F32), V_(SCR + 16.5, 1024, F32)]
    srow2 = [[V_(SCR + 20.5 + 0.0625 * (5 * p + i), 1, F32) for i in range(5)]
             for p in range(2)]
    xn_f32 = [V_(R_A + 8 * c, T, F32) for c in range(DC)]

    a_mark = []
    stats_done = []
    for t in range(nta + 1):
        if t < nta:
            buf = xbuf[t % 3]
            xnt = xnt2[t % 2]
            srow = srow2[t % 2]
            w = []
            if t >= 3:
                w += [("sm", stats_done[t - 3])]
            em.emit("sync", "dma_start", out=buf, in_=x_hbm[t * 128:(t + 1) * 128, :],
                    waits=w, inc="xl", incv=16)
            xlv = em.now("xl")
            sw = [("xl", xlv)]
            if t >= 2:
                sw.append(("tm", a_mark[t - 2][0]))
            em.emit("vector", "scalar_tensor_tensor", xnt, buf, 1.0, buf,
                    op0=ALU.mult, op1=ALU.mult, accum_out=srow[2],
                    waits=sw, inc="vm")
            em.emit("vector", "tensor_reduce", srow[0], buf, axis=AX.X, op=ALU.add, inc="vm")
            em.fence("vector", "vm")
            em.emit("vector", "tensor_scalar_mul", srow[1], srow[0], 1.0 / D, inc="vm")
            em.emit("vector", "tensor_scalar", srow[2], srow[2], 1.0 / D, EPS_LN0,
                    op0=ALU.mult, op1=ALU.add, inc="vm")
            em.fence("vector", "vm")
            em.emit("vector", "scalar_tensor_tensor", srow[3], srow[1], -1.0, srow[1],
                    op0=ALU.mult, op1=ALU.mult, inc="vm")
            em.fence("vector", "vm")
            em.emit("vector", "tensor_add", srow[2], srow[2], srow[3], inc="vm")
            em.fence("vector", "vm")
            em.emit("scalar", "activation", srow[3], srow[2], AF.Sqrt,
                    waits=[("vm", em.now("vm"))], inc="sm")
            em.emit("vector", "reciprocal", srow[4], srow[3],
                    waits=[("sm", em.now("sm"))], inc="vm")
            em.fence("vector", "vm")
            em.emit("vector", "scalar_tensor_tensor", srow[3], srow[1], -1.0, srow[4],
                    op0=ALU.mult, op1=ALU.mult, inc="vm")
            em.fence("vector", "vm")
            em.emit("scalar", "activation", xnt, buf, AF.Identity,
                    scale=srow[4][:, 0:1], bias=srow[3][:, 0:1],
                    waits=[("vm", em.now("vm"))], inc="sm")
            stats_done.append(em.now("sm"))
        if t >= 1:
            tp = t - 1
            xntp = xnt2[tp % 2]
            pso = 0 if tp % 2 == 0 else 1024
            tw = [("sm", stats_done[tp])]
            if tp >= 2:
                tw += [("vm", a_mark[tp - 2][1]), ("sm", a_mark[tp - 2][2])]
            for c in range(DC):
                em.emit("tensor", "transpose", psA[:, pso + c * 128: pso + (c + 1) * 128],
                        xntp[:, c * 128:(c + 1) * 128], id_f,
                        waits=tw if c == 0 else ())
            em.emit("tensor", None, inc="tm")
            tmv = em.now("tm")
            for c in range(4):
                em.emit("vector", "tensor_copy", xn_f32[c][:, tp * 128:(tp + 1) * 128],
                        psA[:, pso + c * 128: pso + (c + 1) * 128],
                        waits=[("tm", tmv)] if c == 0 else (),
                        inc="vm" if c == 3 else None)
            for c in range(4, DC):
                em.emit("scalar", "activation", xn_f32[c][:, tp * 128:(tp + 1) * 128],
                        psA[:, pso + c * 128: pso + (c + 1) * 128], AF.Copy,
                        waits=[("tm", tmv)] if c == 4 else (),
                        inc="sm" if c == DC - 1 else None)
            vm_ev = em.now("vm")
            sm_ev = em.now("sm")
            for c in range(DC):
                if c < 4:
                    em.emit("scalar", "activation", xn_bf[c][:, tp * 128:(tp + 1) * 128],
                            xn_f32[c][:, tp * 128:(tp + 1) * 128], AF.Copy,
                            waits=[("vm", vm_ev)] if c == 0 else (),
                            inc="sm" if c == 3 else None)
                else:
                    em.emit("vector", "tensor_copy", xn_bf[c][:, tp * 128:(tp + 1) * 128],
                            xn_f32[c][:, tp * 128:(tp + 1) * 128],
                            waits=[("sm", sm_ev)] if c == 4 else (),
                            inc="vm" if c == DC - 1 else None)
            a_mark.append((tmv, em.now("vm"), em.now("sm")))

    VM_A = em.now("vm")
    SM_A = em.now("sm")
    for c in range(DC):
        em.emit("sync", "dma_start", out=xnf_sp[c], in_=xn_f32[c],
                waits=[("vm", VM_A)] if c == 0 else (), inc="sp", incv=16)
        em.emit("sync", "dma_start", out=xnb_sp[c], in_=xn_bf[c],
                waits=[("sm", SM_A)] if c == 0 else (), inc="sp", incv=16)
    SP_XN = em.now("sp")

    if "xn" in taps:
        tap_xn = nc.dram_tensor("tap_xn", [DC, 128, T], F32, kind="ExternalOutput").ap()
        for c in range(DC):
            em.emit("sync", "dma_start", out=tap_xn[c], in_=xn_f32[c],
                    waits=[("vm", VM_A)], inc="st", incv=16)

    if upto == "A":
        return _stop_here()

    q_bf = [V_(R_A + 4 * c, T, BF16) for c in range(DC)]
    k_bf = [V_(R_A + 32 + 4 * c, T, BF16) for c in range(DC)]
    qsq = V_(SCR, 2048, F32)
    diag_fm = V_(SCR + 8, 2048, F32, parts=16)

    def emit_qk(w_sb, wlv, dst, diag_dst_tm):
        marks = {}
        pend = None
        g = 0
        for oc in range(DC):
            for th in range(2):
                pso = 0 if g % 2 == 0 else 1024
                tw = [("wl", wlv)]
                if g >= 2:
                    tw += [("vm", marks[g - 2][0]), ("sm", marks[g - 2][1])]
                else:
                    tw += [("sm", em.now("sm")), ("vm", em.now("vm"))]
                first = True
                for k in range(DC):
                    for tc in range(2):
                        tcg = th * 2 + tc
                        em.emit("tensor", "matmul",
                                psA[:, pso + tc * 512: pso + (tc + 1) * 512],
                                w_sb[:, k * 1024 + oc * 128: k * 1024 + (oc + 1) * 128],
                                xn_bf[k][:, tcg * 512:(tcg + 1) * 512],
                                start=(k == 0), stop=(k == DC - 1),
                                waits=tw if first else ())
                        first = False
                em.emit("tensor", None, inc="tm")
                tmv = em.now("tm")
                em.emit("vector", "tensor_copy", dst[oc][:, th * 1024:(th + 1) * 1024],
                        psA[:, pso:pso + 1024],
                        waits=[("tm", tmv), ("sp", SP_XN)], inc="vm")
                em.emit("scalar", "activation", qsq[:, th * 1024:(th + 1) * 1024],
                        psA[:, pso:pso + 1024], AF.Square,
                        waits=[("tm", tmv), ("vm", em.now("vm"))], inc="sm")
                smv = em.now("sm")
                if pend is not None:
                    p_oc, p_th, p_smv = pend
                    for tc in range(2):
                        tcg = p_th * 2 + tc
                        em.emit("tensor", "matmul", psB[0:16, tcg * 512:(tcg + 1) * 512],
                                bd_c[:, p_oc * 16:(p_oc + 1) * 16],
                                qsq[:, tcg * 512:(tcg + 1) * 512],
                                start=(p_oc == 0), stop=(p_oc == DC - 1),
                                waits=([("sm", p_smv)] +
                                       ([("vm", em.now("vm"))] if g <= 2 else []))
                                if tc == 0 else ())
                pend = (oc, th, smv)
                marks[g] = (em.now("vm"), em.now("sm"))
                g += 1
        p_oc, p_th, p_smv = pend
        for tc in range(2):
            tcg = p_th * 2 + tc
            em.emit("tensor", "matmul", psB[0:16, tcg * 512:(tcg + 1) * 512],
                    bd_c[:, p_oc * 16:(p_oc + 1) * 16],
                    qsq[:, tcg * 512:(tcg + 1) * 512],
                    start=False, stop=True,
                    waits=[("sm", p_smv)] if tc == 0 else ())
        em.emit("tensor", None, inc="tm")
        tmv = em.now("tm")
        em.emit("vector", "tensor_copy", diag_fm, psB[0:16, :], waits=[("tm", tmv)], inc="vm")
        vmv = em.now("vm")
        for t in range(NT):
            tw = [("vm", vmv)] if t == 0 else []
            if t >= 4:
                tw.append(("vm", em.now("vm")))
            em.emit("tensor", "transpose",
                    psB[:, (t % 4) * 512: (t % 4) * 512 + 16],
                    diag_fm[:, t * 128:(t + 1) * 128], id_f[0:16, 0:16], waits=tw, inc="tm")
            em.emit("vector", "tensor_copy", diag_dst_tm[:, t * 16:(t + 1) * 16],
                    psB[:, (t % 4) * 512: (t % 4) * 512 + 16],
                    waits=[("tm", em.now("tm"))], inc="vm")

    emit_qk(wk_sb, WL_WK, k_bf, diagk_tm)
    c1m = {}
    VM_C1S = em.now("vm")
    for t in range(NT):
        pso = 0 if t % 2 == 0 else 1024
        tw = [("vm", VM_PD2), ("vm", c1m[t - 2] if t >= 2 else VM_C1S)]
        first = True
        for pr in range(8):
            em.emit("tensor", "matmul", psA[:, pso + pr * 128: pso + (pr + 1) * 128],
                    k_bf[pr][:, t * 128:(t + 1) * 128], pd2,
                    start=True, stop=True, waits=tw if first else ())
            first = False
        em.emit("tensor", None, inc="tm")
        tmv = em.now("tm")
        em.emit("vector", "tensor_reduce", mxloc[:, t:t + 1], psA[:, pso:pso + 1024],
                axis=AX.X, op=ALU.max, waits=[("tm", tmv)], inc="vm")
        c1m[t] = em.now("vm")
    em.fence("vector", "vm")
    mxf = srow2[0][0]
    em.emit("vector", "tensor_reduce", mxf, mxloc[:, 0:NT], axis=AX.X, op=ALU.max,
            inc="vm")
    em.fence("vector", "vm")
    em.emit("tensor", "transpose", psB[0:1, 1024:1152], mxf, id_f,
            waits=[("vm", em.now("vm"))], inc="tm")
    em.emit("vector", "memset", mxst[0:1, 0:8], NEG,
            waits=[("tm", em.now("tm"))], inc="vm")
    em.fence("vector", "vm")
    em.emit("vector", "tensor_reduce", mxst[0:1, 0:1], psB[0:1, 1024:1152], axis=AX.X,
            op=ALU.max, inc="vm")
    em.fence("vector", "vm")
    em.emit("gpsimd", "dma_start", out=ccmax_in.ap(), in_=mxst[0:1, 0:8],
            waits=[("vm", em.now("vm"))], inc="cl", incv=16)
    em.ewait("gpsimd", ("cl", em.now("cl")))
    em.emit("gpsimd", "collective_compute", "AllReduce", ALU.max,
            replica_groups=[list(range(NCORES))],
            ins=[ccmax_in.ap().opt()], outs=[ccmax_out.ap().opt()], inc="cc")
    em.ewait("gpsimd", ("cc", em.now("cc")))
    em.emit("gpsimd", "dma_start", out=mxst[0:1, 8:16], in_=ccmax_out.ap(), inc="cl", incv=16)
    CL_MX = em.now("cl")

    if upto == "C1":
        return _stop_here()

    emit_qk(wq_sb, WL_WQ, q_bf, diagq_tm)

    if "q" in taps:
        tap_q = nc.dram_tensor("tap_q", [DC, 128, T], F32, kind="ExternalOutput").ap()
        for c in range(DC):
            em.emit("gpsimd", "dma_start", out=tap_q[c], in_=q_bf[c],
                    waits=[("vm", em.now("vm"))], inc="tst", incv=16)
    if "diag" in taps:
        tap_dg = nc.dram_tensor("tap_diag", [2, 128, 256], F32, kind="ExternalOutput").ap()
        em.emit("sync", "dma_start", out=tap_dg[0], in_=diagq_tm,
                waits=[("vm", em.now("vm"))], inc="st", incv=16)
        em.emit("sync", "dma_start", out=tap_dg[1], in_=diagk_tm,
                waits=[("vm", em.now("vm"))], inc="st", incv=16)

    vaug = [V_(R_B + 16 + 2.125 * t, 1040, BF16) for t in range(NT)]
    VM_B_QK = em.now("vm")
    for t in range(NT):
        em.emit("vector", "memset", vaug[t], 1.0,
                waits=[("vm", VM_B_QK), ("tm", em.now("tm"))] if t == 0 else (),
                inc="vm" if t == NT - 1 else None)
    em.fence("vector", "vm")
    VM_VONES = em.now("vm")
    vvm = {}
    for t in range(NT):
        for nh in range(2):
            g = 2 * t + nh
            pso = (g % 4) * 512
            tw = [("wl", WL_WV), ("vm", vvm[g - 4] if g >= 4 else VM_VONES)]
            first = True
            for k in range(DC):
                em.emit("tensor", "matmul", psA[:, pso:pso + 512],
                        xn_bf[k][:, t * 128:(t + 1) * 128],
                        wv_sb[:, k * 1024 + nh * 512: k * 1024 + (nh + 1) * 512],
                        start=(k == 0), stop=(k == DC - 1), waits=tw if first else ())
                first = False
            em.emit("tensor", None, inc="tm")
            tmv = em.now("tm")
            dst = vaug[t].rearrange("p (h e) -> p h e", h=16)[:, nh * 8:(nh + 1) * 8, 0:64]
            src = psA[:, pso:pso + 512].rearrange("p (h d) -> p h d", h=8)
            em.emit("vector", "tensor_copy", dst, src, waits=[("tm", tmv)], inc="vm")
            vvm[g] = em.now("vm")
    VM_VAUG = em.now("vm")

    if upto == "B":
        return _stop_here()

    qf_tm = [V_(R_B + 2 * t, 1024, BF16) if t < 8 else V_(R_B + 50 + 2 * (t - 8), 1024, BF16)
             for t in range(NT)]
    stage = V_(SCR, 512, F32)
    biasr = V_(SCR + 2, 16, F32)
    stageE = V_(SCR + 4, 1024, F32)
    kf_t2 = [V_(SCR + 8, 1024, BF16), V_(SCR + 10, 1024, BF16)]

    em.emit("tensor", "matmul", psB[:, 1028:1029], ones_row, mxst[0:1, 8:9],
            start=True, stop=True, waits=[("cl", CL_MX)], inc="tm")
    em.emit("vector", "tensor_copy", mx128, psB[:, 1028:1029],
            waits=[("tm", em.now("tm"))], inc="vm")
    VM_MX128 = em.now("vm")

    def _emit_kv(kt, kvmkf):
        first = True
        for h in range(H):
            ph, cg = h % 2, h // 2
            colbase = (0 if cg < 4 else 512) + (cg % 4) * 65
            em.emit("tensor", "matmul",
                    psB[ph * 64:(ph + 1) * 64, colbase:colbase + 65],
                    kf_t2[kt % 2][:, h * 64:(h + 1) * 64],
                    vaug[kt][:, h * 65:(h + 1) * 65],
                    start=(kt == 0 and h in (0, 1, 8, 9)),
                    stop=(kt == NT - 1 and h in (6, 7, 14, 15)),
                    skip_group_check=True,
                    waits=[("vm", kvmkf), ("vm", VM_VAUG)] if first else ())
            first = False

    ekm = {}
    pkv = None
    for t in range(NT):
        pso = 0 if t % 2 == 0 else 1024
        if t >= 2:
            tw = [("vm", ekm[t - 2][0]), ("sm", ekm[t - 2][1])]
        else:
            tw = [("vm", em.now("vm")), ("sm", em.now("sm"))]
        first = True
        for pr in range(8):
            em.emit("tensor", "matmul", psA[:, pso + pr * 128: pso + (pr + 1) * 128],
                    k_bf[pr][:, t * 128:(t + 1) * 128], pd2,
                    start=True, stop=True, waits=tw if first else ())
            first = False
        em.emit("tensor", None, inc="tm")
        tmv = em.now("tm")
        em.emit("vector", "tensor_scalar", biasr, diagk_tm[:, t * 16:(t + 1) * 16],
                mx128[:, 0:1], -1.0, op0=ALU.add, op1=ALU.mult,
                waits=[("tm", tmv), ("vm", VM_MX128)], inc="vm")
        em.fence("vector", "vm")
        vm_b = em.now("vm")
        for h in range(8, H):
            em.emit("scalar", "activation", stageE[:, h * 64:(h + 1) * 64],
                    psA[:, pso + h * 64: pso + h * 64 + 64], AF.Exp,
                    bias=biasr[:, h:h + 1],
                    waits=[("vm", vm_b), ("tm", tmv)] if h == 8 else (),
                    inc="sm" if h == H - 1 else None)
        for h in range(8):
            em.emit("vector", "tensor_scalar_add", stage[:, h * 64:(h + 1) * 64],
                    psA[:, pso + h * 64: pso + h * 64 + 64], biasr[:, h:h + 1],
                    inc="vm" if h == 7 else None)
        em.emit("scalar", "activation", stageE[:, 0:512], stage, AF.Exp,
                waits=[("vm", em.now("vm"))], inc="sm")
        em.emit("vector", "tensor_scalar", kf_t2[t % 2], stageE, EPSK, RATIO,
                op0=ALU.add, op1=ALU.mult, waits=[("sm", em.now("sm"))], inc="vm")
        vm_kf = em.now("vm")
        if pkv is not None:
            _emit_kv(*pkv)
        pkv = (t, vm_kf)
        ekm[t] = (em.now("vm"), em.now("sm"))
    _emit_kv(*pkv)
    em.emit("tensor", None, inc="tm")
    TM_KV = em.now("tm")
    em.emit("vector", "tensor_copy", kvr[:, 0:260], psB[:, 0:260],
            waits=[("tm", TM_KV)], inc="vm")
    em.emit("vector", "tensor_copy", kvr[:, 260:520], psB[:, 512:772], inc="vm")
    em.emit("gpsimd", "dma_start", out=cckv_in.ap(), in_=kvr,
            waits=[("vm", em.now("vm"))], inc="cl", incv=16)
    em.ewait("gpsimd", ("cl", em.now("cl")))
    em.emit("gpsimd", "collective_compute", "AllReduce", ALU.add,
            replica_groups=[[0, 1, 2, 3], [4, 5, 6, 7]],
            ins=[cckv_in.ap().opt()], outs=[cckv_out.ap().opt()], inc="cc")
    em.ewait("gpsimd", ("cc", em.now("cc")))
    em.emit("gpsimd", "dma_start", out=kvr, in_=cckv_out.ap(), inc="cl", incv=16)
    CL_KV = em.now("cl")

    if upto == "E":
        return _stop_here()

    ddm = {}
    for t in range(NT):
        pso = 0 if t % 2 == 0 else 1024
        if t >= 2:
            tw = [("vm", ddm[t - 2][0]), ("sm", ddm[t - 2][1])]
        else:
            tw = [("vm", em.now("vm")), ("sm", em.now("sm"))]
        first = True
        for pr in range(8):
            em.emit("tensor", "matmul", psA[:, pso + pr * 128: pso + (pr + 1) * 128],
                    q_bf[pr][:, t * 128:(t + 1) * 128], pd2,
                    start=True, stop=True, waits=tw if first else ())
            first = False
        em.emit("tensor", None, inc="tm")
        tmv = em.now("tm")
        dd3 = psA[:, pso:pso + 1024].rearrange("p (h m) -> p h m", h=H)
        em.emit("vector", "tensor_reduce", mxloc, dd3, axis=AX.X, op=ALU.max,
                waits=[("tm", tmv)], inc="vm")
        em.fence("vector", "vm")
        em.emit("vector", "scalar_tensor_tensor", biasr, mxloc, -1.0,
                diagq_tm[:, t * 16:(t + 1) * 16], op0=ALU.mult, op1=ALU.subtract, inc="vm")
        em.fence("vector", "vm")
        vm_b = em.now("vm")
        for h in range(8, H):
            em.emit("scalar", "activation", stageE[:, h * 64:(h + 1) * 64],
                    psA[:, pso + h * 64: pso + h * 64 + 64], AF.Exp,
                    bias=biasr[:, h:h + 1],
                    waits=[("vm", vm_b), ("tm", tmv)] if h == 8 else (),
                    inc="sm" if h == H - 1 else None)
        for h in range(8):
            em.emit("vector", "tensor_scalar_add", stage[:, h * 64:(h + 1) * 64],
                    psA[:, pso + h * 64: pso + h * 64 + 64], biasr[:, h:h + 1],
                    inc="vm" if h == 7 else None)
        em.emit("scalar", "activation", stageE[:, 0:512], stage, AF.Exp,
                waits=[("vm", em.now("vm"))], inc="sm")
        em.emit("vector", "tensor_scalar", qf_tm[t], stageE, EPSK, RATIO,
                op0=ALU.add, op1=ALU.mult, waits=[("sm", em.now("sm"))], inc="vm")
        ddm[t] = (em.now("vm"), em.now("sm"))
    VM_QF = em.now("vm")

    if upto == "C2":
        return _stop_here()

    qf_fm = [V_(R_A + 4 * c, T, BF16) for c in range(DC)]
    for t in range(NT):
        bank = t % 4
        tw = [("vm", VM_QF)]
        if t >= 4:
            tw.append(("vm", em.now("vm")))
        for c in range(DC):
            em.emit("tensor", "transpose",
                    psA_bf[:, bank * 1024 + c * 128: bank * 1024 + (c + 1) * 128],
                    qf_tm[t][:, c * 128:(c + 1) * 128], id_b,
                    waits=tw if c == 0 else ())
        em.emit("tensor", None, inc="tm")
        tmv = em.now("tm")
        for c in range(DC):
            em.emit("vector", "tensor_copy", qf_fm[c][:, t * 128:(t + 1) * 128],
                    psA_bf[:, bank * 1024 + c * 128: bank * 1024 + (c + 1) * 128],
                    waits=[("tm", tmv)] if c == 0 else (),
                    inc="vm" if c == DC - 1 else None)
    VM_QFFM = em.now("vm")

    if "kv" in taps:
        tap_kv = nc.dram_tensor("tap_kv", [128, 520], F32, kind="ExternalOutput").ap()
        em.emit("gpsimd", "dma_start", out=tap_kv, in_=kvr, waits=[("cl", CL_KV)],
                inc="tst", incv=16)
    if "qf" in taps:
        tap_qf = nc.dram_tensor("tap_qf", [DC, 128, T], F32, kind="ExternalOutput").ap()
        for c in range(DC):
            em.emit("gpsimd", "dma_start", out=tap_qf[c], in_=qf_fm[c],
                    waits=[("vm", VM_QFFM)], inc="tst", incv=16)

    if upto == "D":
        return _stop_here()

    em.ewait("vector", ("cl", CL_KV))
    em.emit("vector", "memset", kvbd, 0.0, inc="vm")
    em.emit("vector", "memset", ksbd, 0.0, inc="vm")
    em.fence("vector", "vm")
    for hh in range(H):
        ph, cg = hh % 2, hh // 2
        src = kvr[ph * 64:(ph + 1) * 64, cg * 65: cg * 65 + 64]
        srcs = kvr[ph * 64:(ph + 1) * 64, cg * 65 + 64: cg * 65 + 65]
        em.emit("vector", "tensor_copy",
                kvbd[ph * 64:(ph + 1) * 64, cg * 128 + ph * 64: cg * 128 + ph * 64 + 64],
                src, inc=None)
        em.emit("vector", "tensor_copy",
                ksbd[ph * 64:(ph + 1) * 64, cg * 16 + hh: cg * 16 + hh + 1],
                srcs, inc="vm" if hh == H - 1 else None)
    VM_KVBD = em.now("vm")

    for kr in range(DC):
        first = True
        for tc in range(4):
            em.emit("tensor", "matmul", psB[0:16, tc * 512:(tc + 1) * 512],
                    ksbd[:, kr * 16:(kr + 1) * 16], qf_fm[kr][:, tc * 512:(tc + 1) * 512],
                    start=(kr == 0), stop=(kr == DC - 1),
                    waits=[("vm", VM_KVBD), ("vm", VM_QFFM)] if (first and kr == 0) else ())
            first = False
    em.emit("tensor", None, inc="tm")
    rcp_den = V_(SCR, 2048, F32, parts=16)
    em.emit("vector", "tensor_copy", rcp_den, psB[0:16, :], waits=[("tm", em.now("tm"))],
            inc="vm")
    em.emit("vector", "reciprocal", rcp_den, rcp_den, inc="vm")
    VM_RCP = em.now("vm")

    if "den" in taps:
        tap_den = nc.dram_tensor("tap_den", [16, 2048], F32, kind="ExternalOutput").ap()
        em.emit("sync", "dma_start", out=tap_den, in_=rcp_den, waits=[("vm", VM_RCP)],
                inc="st", incv=16)

    attn_bf = [V_(R_A + 32 + 4 * c, T, BF16) for c in range(DC)]
    bc_sb = V_(SCR + 8, 2048, F32)
    for c in range(DC):
        tw = [("vm", VM_RCP), ("vm", em.now("vm"))]
        first = True
        for tc in range(4):
            em.emit("tensor", "matmul", psA[:, tc * 512:(tc + 1) * 512],
                    kvbd[:, c * 128:(c + 1) * 128], qf_fm[c][:, tc * 512:(tc + 1) * 512],
                    start=True, stop=True, waits=tw if first else ())
            first = False
        for tc in range(4):
            em.emit("tensor", "matmul", psB[:, tc * 512:(tc + 1) * 512],
                    e2_c[:, c * 128:(c + 1) * 128], rcp_den[:, tc * 512:(tc + 1) * 512],
                    start=True, stop=True)
        em.emit("tensor", None, inc="tm")
        em.emit("vector", "tensor_copy", bc_sb, psB, waits=[("tm", em.now("tm"))], inc="vm")
        em.emit("vector", "tensor_mul", attn_bf[c], psA, bc_sb, inc="vm")
    VM_ATTN = em.now("vm")

    wo_sb = V_(R_B, 8 * 1024, BF16)
    em.emit("gpsimd", "dma_start", out=wo_sb.rearrange("p (k n) -> p k n", k=8),
            in_=wo_hbm.rearrange("(k p) n -> p k n", p=128),
            waits=[("vm", VM_QFFM)], inc="wl", incv=16)
    WL_WO = em.now("wl")
    xnch = [V_(SCR + 8, 2048, F32), V_(SCR + 16, 2048, F32)]
    sum1_sb = ([V_(R_A + 8 * i, 2048, F32) for i in range(4)] +
               [V_(R_B + 16 + 8 * i, 2048, F32) for i in range(4)])
    sqch = V_(R_B + 48, 2048, F32)

    for oc in range(DC):
        w = [("sp", SP_XN), ("tm", TM_KV), ("vm", VM_ATTN)]
        if oc >= 2:
            w.append(("vm", em.now("vm")))
        em.emit("sync", "dma_start", out=xnch[oc % 2], in_=xnf_sp[oc], waits=w,
                inc="xl", incv=16)
        xlv = em.now("xl")
        tw = [("wl", WL_WO), ("vm", VM_ATTN), ("vm", em.now("vm")), ("sm", em.now("sm"))]
        first = True
        for kr in range(DC):
            for tc in range(4):
                em.emit("tensor", "matmul", psA[:, tc * 512:(tc + 1) * 512],
                        wo_sb[:, kr * 1024 + oc * 128: kr * 1024 + (oc + 1) * 128],
                        attn_bf[kr][:, tc * 512:(tc + 1) * 512],
                        start=(kr == 0), stop=(kr == DC - 1), waits=tw if first else ())
                first = False
        em.emit("tensor", None, inc="tm")
        tmv = em.now("tm")
        em.emit("vector", "tensor_add", sum1_sb[oc], psA, xnch[oc % 2],
                waits=[("tm", tmv), ("xl", xlv)], inc="vm")
        vmv = em.now("vm")
        em.emit("scalar", "activation", sqch, sum1_sb[oc], AF.Square,
                waits=[("vm", vmv)], inc="sm")
        smv = em.now("sm")
        for tc in range(4):
            em.emit("tensor", "matmul", psB[0:1, tc * 512:(tc + 1) * 512],
                    ones_col, sum1_sb[oc][:, tc * 512:(tc + 1) * 512],
                    start=(oc == 0), stop=(oc == DC - 1),
                    waits=[("vm", vmv)] if tc == 0 else ())
        for tc in range(4):
            em.emit("tensor", "matmul", psB[32:33, tc * 512:(tc + 1) * 512],
                    ones_col, sqch[:, tc * 512:(tc + 1) * 512],
                    start=(oc == 0), stop=(oc == DC - 1), skip_group_check=True,
                    waits=[("sm", smv)] if tc == 0 else ())
        em.emit("tensor", None, inc="tm")
    SP_SUM1 = em.now("sp")
    VM_SUM1 = em.now("vm")

    def emit_ln_rows(eps, width, rowbase_kb):
        if isinstance(rowbase_kb, tuple):
            b0, b1, b2 = rowbase_kb
        else:
            b0 = rowbase_kb
            b1 = rowbase_kb + width * 4 / KB
            b2 = rowbase_kb + 2 * width * 4 / KB
        mu_r = V_(b0, width, F32, parts=1)
        var_r = V_(b1, width, F32, parts=1)
        tmp_r = V_(b2, width, F32, parts=1)
        em.emit("vector", "tensor_scalar_mul", mu_r, psB[0:1, 0:width], 1.0 / D,
                waits=[("tm", em.now("tm"))], inc="vm")
        em.emit("vector", "tensor_scalar", var_r, psB[32:33, 0:width], 1.0 / D, eps,
                op0=ALU.mult, op1=ALU.add, inc="vm")
        em.emit("vector", "tensor_mul", tmp_r, mu_r, mu_r, inc="vm")
        em.emit("vector", "tensor_sub", var_r, var_r, tmp_r, inc="vm")
        em.emit("scalar", "activation", tmp_r, var_r, AF.Sqrt,
                waits=[("vm", em.now("vm"))], inc="sm")
        em.emit("vector", "reciprocal", var_r, tmp_r, waits=[("sm", em.now("sm"))], inc="vm")
        em.emit("vector", "scalar_tensor_tensor", mu_r, mu_r, -1.0, var_r,
                op0=ALU.mult, op1=ALU.mult, inc="vm")
        vmv = em.now("vm")
        for tc in range(width // 512):
            em.emit("tensor", "matmul", psA[:, tc * 512:(tc + 1) * 512], ones_row,
                    var_r[:, tc * 512:(tc + 1) * 512], start=True, stop=True,
                    waits=[("vm", vmv)] if tc == 0 else ())
        for tc in range(width // 512):
            em.emit("tensor", "matmul", psB[:, tc * 512:(tc + 1) * 512], ones_row,
                    mu_r[:, tc * 512:(tc + 1) * 512], start=True, stop=True)
        em.emit("tensor", None, inc="tm")
        return em.now("tm")

    tm_bc1 = emit_ln_rows(EPS_LN, 2048, (R_B + 56, R_B + 48, SCR + 16))
    out1_bf = [V_(R_A + 4 * c, T, BF16) for c in range(DC)]
    for oc in range(DC):
        ssem = "soA" if oc % 2 == 0 else "soB"
        o1f = sum1_sb[oc]
        em.emit("vector", "tensor_mul", o1f, o1f, psA, waits=[("tm", tm_bc1)], inc="vm")
        em.emit("vector", "tensor_add", o1f, o1f, psB, inc="vm")
        em.emit("vector", "tensor_scalar", o1f, o1f, par["ln1g"][:, oc:oc + 1],
                par["ln1b"][:, oc:oc + 1], op0=ALU.mult, op1=ALU.add, inc="vm")
        vmv = em.now("vm")
        em.emit("scalar", "dma_start", out=out1f_sp[oc], in_=o1f, waits=[("vm", vmv)],
                inc=ssem, incv=16, serial=False)
    em.ewait("scalar", ("soA", em.now("soA")), ("soB", em.now("soB")))
    em.emit("scalar", "sem_inc", "sp", DC * 16)
    for oc in range(DC):
        em.emit("scalar", "activation", out1_bf[oc], sum1_sb[oc], AF.Copy, inc="sm")
    SP_OUT1 = em.now("sp")
    SM_OUT1 = em.now("sm")

    if "out1" in taps:
        tap_o1 = nc.dram_tensor("tap_out1", [DC, 128, T], F32, kind="ExternalOutput").ap()
        for oc in range(DC):
            em.emit("gpsimd", "dma_start", out=tap_o1[oc], in_=out1_bf[oc],
                    waits=[("sm", SM_OUT1)], inc="tst", incv=16)

    if upto == "G":
        return _stop_here()

    H_H = ([V_(XNBF + 2 * i, HALF, BF16) for i in range(16)] +
           [V_(R_A + 32 + 2 * (i - 16), HALF, BF16) for i in range(16, 32)])
    w1slab = [V_(SCR, 1024, BF16), V_(SCR + 2, 1024, BF16)]
    w2_sb = V_(R_B, FC * 1024, BF16)
    elu1 = V_(SCR + 4, 1024, F32)
    o1chb4 = [V_(SCR, 1024, F32), V_(SCR + 4, 1024, F32),
              V_(SCR + 8, 1024, F32), V_(SCR + 12, 1024, F32)]
    elu2 = V_(SCR + 8, 1024, F32)
    o1ch = V_(SCR + 12, 1024, F32)
    sum2ch = V_(SCR + 16, 1024, F32)
    sq2ch = V_(SCR + 20, 1024, F32)

    em.emit("gpsimd", "dma_start", out=w2_sb.rearrange("p (k n) -> p k n", k=FC),
            in_=w2_hbm.rearrange("k p n -> p k n"),
            waits=[("vm", em.now("vm")), ("tm", em.now("tm")), ("sm", em.now("sm")),
                   ("sp", SP_OUT1)], inc="wl", incv=16)
    WL_W2 = em.now("wl")

    for hf in range(2):
        tmbase = em.now("tm")
        emarks = {}
        for oc in range(FC):
            w = [("sm", SM_OUT1), ("sp", SP_OUT1)]
            if oc >= 2:
                w.append(("tm", tmbase + oc - 1))
            else:
                w.append(("tm", tmbase))
                w += [(f"rl{i}", em.now(f"rl{i}")) for i in range(4)]
                w.append(("vm", em.now("vm")))
            wsem = "wlA" if oc % 2 == 0 else "wlB"
            em.emit("gpsimd", "dma_start",
                    out=w1slab[oc % 2].rearrange("p (k n) -> p k n", k=DC),
                    in_=w1_hbm[oc], waits=w, inc=wsem, incv=16, serial=False)
            pso = 0 if oc % 2 == 0 else 1024
            tw = [(wsem, em.now(wsem)), ("sm", SM_OUT1), ("sp", SP_OUT1),
                  ("vm", emarks[oc - 2] if oc >= 2 else em.now("vm"))]
            first = True
            for k in range(DC):
                for tc2 in range(2):
                    em.emit("tensor", "matmul",
                            psA[:, pso + tc2 * 512: pso + (tc2 + 1) * 512],
                            w1slab[oc % 2][:, k * 128:(k + 1) * 128],
                            out1_bf[k][:, hf * HALF + tc2 * 512: hf * HALF + (tc2 + 1) * 512],
                            start=(k == 0), stop=(k == DC - 1), waits=tw if first else ())
                    first = False
            em.emit("tensor", None, inc="tm")
            tmv = em.now("tm")
            ps = psA[:, pso:pso + 1024]
            em.emit("scalar", "activation", elu1, ps, AF.Relu,
                    bias=par["negb1"][:, oc:oc + 1], scale=-1.0,
                    waits=[("tm", tmv), ("vm", em.now("vm"))], inc="sm")
            em.emit("scalar", "activation", elu2, elu1, AF.Exp, scale=-1.0, inc="sm")
            smv = em.now("sm")
            em.emit("vector", "scalar_tensor_tensor", elu2, elu1, 1.0, elu2,
                    op0=ALU.mult, op1=ALU.add, waits=[("sm", smv)], inc="vm")
            em.emit("vector", "scalar_tensor_tensor", elu2, elu2, 1.0, ps,
                    op0=ALU.mult, op1=ALU.add, inc="vm")
            em.emit("vector", "tensor_scalar_add", H_H[oc], elu2,
                    par["b1m1"][:, oc:oc + 1], inc="vm")
            emarks[oc] = em.now("vm")
        assert em.now("tm") == tmbase + FC

        if "helu" in taps:
            if hf == 0:
                tap_he = nc.dram_tensor("tap_helu", [2, FC, 128, HALF], F32,
                                        kind="ExternalOutput").ap()
                globals()["_tap_he"] = tap_he
            tap_he = globals()["_tap_he"]
            for k in range(FC):
                em.emit("gpsimd", "dma_start", out=tap_he[hf][k], in_=H_H[k],
                        waits=[("vm", em.now("vm"))], inc="tst", incv=16)

        smarks = {}
        for oc in range(DC):
            w = [("sp", SP_OUT1)]
            if oc == 0:
                w += [("vm", em.now("vm")), ("sm", em.now("sm"))]
            if oc >= 1:
                w += [("vm", em.now("vm")), ("sm", em.now("sm"))]
            em.emit("sync", "dma_start", out=o1ch,
                    in_=out1f_sp[oc][:, hf * HALF:(hf + 1) * HALF], waits=w,
                    inc="xl", incv=16)
            xlv = em.now("xl")
            pso = 0 if oc % 2 == 0 else 1024
            tw = [("wl", WL_W2),
                  ("vm", smarks[oc - 2] if oc >= 2 else em.now("vm"))]
            first = True
            for k in range(FC):
                for tc2 in range(2):
                    em.emit("tensor", "matmul",
                            psA[:, pso + tc2 * 512: pso + (tc2 + 1) * 512],
                            w2_sb[:, k * 1024 + oc * 128: k * 1024 + (oc + 1) * 128],
                            H_H[k][:, tc2 * 512:(tc2 + 1) * 512],
                            start=(k == 0), stop=(k == FC - 1), waits=tw if first else ())
                    first = False
            em.emit("tensor", None, inc="tm")
            tmv = em.now("tm")
            ps = psA[:, pso:pso + 1024]
            em.emit("vector", "scalar_tensor_tensor", sum2ch, ps, par["b2v"][:, oc:oc + 1],
                    o1ch, op0=ALU.add, op1=ALU.add, waits=[("tm", tmv), ("xl", xlv)],
                    inc="vm")
            vmv = em.now("vm")
            smarks[oc] = vmv
            em.emit("sync", "dma_start", out=sum2_sp[hf][oc], in_=sum2ch, waits=[("vm", vmv)],
                    inc="sp", incv=16)
            em.emit("scalar", "activation", sq2ch, sum2ch, AF.Square, waits=[("vm", vmv)],
                    inc="sm")
            smv = em.now("sm")
            for tc2 in range(2):
                em.emit("tensor", "matmul", psB[0:1, tc2 * 512:(tc2 + 1) * 512],
                        ones_col, sum2ch[:, tc2 * 512:(tc2 + 1) * 512],
                        start=(oc == 0), stop=(oc == DC - 1),
                        waits=[("vm", vmv)] if tc2 == 0 else ())
            for tc2 in range(2):
                em.emit("tensor", "matmul", psB[32:33, tc2 * 512:(tc2 + 1) * 512],
                        ones_col, sq2ch[:, tc2 * 512:(tc2 + 1) * 512],
                        start=(oc == 0), stop=(oc == DC - 1), skip_group_check=True,
                        waits=[("sm", smv)] if tc2 == 0 else ())
            em.emit("tensor", None, inc="tm")
        SP_SUM2 = em.now("sp")

        tm_bc2 = emit_ln_rows(EPS_LN, HALF, XNBF)
        ap2 = {}
        for oc in range(DC):
            rsem = f"rl{oc % 4}"
            ssem = f"so{oc % 4}"
            w = [("sp", SP_SUM2), ("tm", tm_bc2)]
            if oc >= 4:
                w.append((ssem, ap2[oc - 4]))
            ob = o1chb4[oc % 4]
            em.emit("sync", "dma_start", out=ob, in_=sum2_sp[hf][oc], waits=w,
                    inc=rsem, incv=16, serial=False)
            rlv = em.now(rsem)
            em.emit("vector", "tensor_mul", ob, ob, psA[:, 0:HALF],
                    waits=[(rsem, rlv), ("tm", tm_bc2)], inc="vm")
            em.emit("vector", "tensor_add", ob, ob, psB[:, 0:HALF], inc="vm")
            em.emit("vector", "tensor_scalar", ob, ob, par["ln2g"][:, oc:oc + 1],
                    par["ln2b"][:, oc:oc + 1], op0=ALU.mult, op1=ALU.add, inc="vm")
            em.emit("scalar", "dma_start", out=attnb_sp[hf][oc], in_=ob,
                    waits=[("vm", em.now("vm"))], inc=ssem, incv=16, serial=False)
            ap2[oc] = em.now(ssem)
    em.ewait("scalar", *[(f"so{i}", em.now(f"so{i}")) for i in range(4)])
    em.emit("scalar", "sem_inc", "sp", 4 * DC * 16)
    SP_ATTNB = em.now("sp")

    if "attnb" in taps:
        tap_ab = nc.dram_tensor("tap_attnb", [2, DC, 128, HALF], F32,
                                kind="ExternalOutput").ap()
        for hf in range(2):
            for oc in range(DC):
                em.emit("sync", "dma_start", out=tap_ab[hf][oc], in_=attnb_sp[hf][oc],
                        waits=[("sp", SP_ATTNB)], inc="st", incv=16)

    if upto == "H":
        return _stop_here()

    xn2_bf = [V_(R_A + 4 * c, T, BF16) for c in range(DC)]
    em.emit("sync", "dma_start", out=xn2_bf[0], in_=xnb_sp[0],
            waits=[("sp", SP_XN)] + em.allw(), inc="xl", incv=16)
    for c in range(1, DC):
        em.emit("sync", "dma_start", out=xn2_bf[c], in_=xnb_sp[c], inc="xl", incv=16)
    XL_XN2 = em.now("xl")

    tslab = [V_(SCR, 4096, BF16), V_(SCR + 8, 4096, BF16)]
    hslab = [V_(SCR, 4096, BF16), V_(SCR + 8, 4096, BF16)]
    silu_s = V_(SCR + 16, 1024, F32)
    silu_s2 = V_(SCR + 20, 1024, F32)
    y3ch = V_(SCR + 16, 1024, F32)
    abchb = [V_(6.5625, 1024, F32), V_(SCR + 20, 1024, F32)]

    for hf in range(2):
        TH = [V_(R_B + 2 * i, HALF, BF16) for i in range(FC)]
        G1 = ([V_(XNBF + 2 * i, HALF, BF16) for i in range(16)] +
              [V_(R_A + 32 + 2 * (i - 16), HALF, BF16) for i in range(16, 32)])
        G2 = TH

        tmbase = em.now("tm")
        tmarks = {}
        for oc in range(DC):
            w = [("xl", XL_XN2)]
            if oc == 0:
                w += [("tm", em.now("tm")), ("vm", em.now("vm")), ("sm", em.now("sm")),
                      ("sp", SP_ATTNB), ("st", em.now("st"))]
                w += [(f"rl{i}", em.now(f"rl{i}")) for i in range(4)]
            else:
                w.append(("tm", tmbase + 4 * oc))
            em.emit("gpsimd", "dma_start",
                    out=tslab[0].rearrange("p (k n) -> p k n", k=DC),
                    in_=w1t_hbm.rearrange("k p n -> p k n")[:, :, oc * 512:(oc + 1) * 512],
                    waits=w, inc="wt0", incv=16, serial=False)
            em.emit("gpsimd", "dma_start",
                    out=tslab[1].rearrange("p (k n) -> p k n", k=DC),
                    in_=w2t_hbm.rearrange("k p n -> p k n")[:, :, oc * 512:(oc + 1) * 512],
                    inc="wt1", incv=16, serial=False)
            for s4 in range(4):
                occ = oc * 4 + s4
                pst = psA if occ % 2 == 0 else psB
                sil = silu_s if occ % 2 == 0 else silu_s2
                tw = [("wt0", em.now("wt0")), ("wt1", em.now("wt1"))]
                tw.append(("vm", tmarks[occ - 2] if occ >= 2 else em.now("vm")))
                if occ < 2:
                    tw.append(("sm", em.now("sm")))
                first = True
                for k in range(DC):
                    for tc2 in range(2):
                        em.emit("tensor", "matmul",
                                pst[:, tc2 * 512:(tc2 + 1) * 512],
                                tslab[0][:, k * 512 + s4 * 128: k * 512 + (s4 + 1) * 128],
                                xn2_bf[k][:, hf * HALF + tc2 * 512: hf * HALF + (tc2 + 1) * 512],
                                start=(k == 0), stop=(k == DC - 1), waits=tw if first else ())
                        first = False
                for k in range(DC):
                    for tc2 in range(2):
                        em.emit("tensor", "matmul",
                                pst[:, 1024 + tc2 * 512: 1024 + (tc2 + 1) * 512],
                                tslab[1][:, k * 512 + s4 * 128: k * 512 + (s4 + 1) * 128],
                                xn2_bf[k][:, hf * HALF + tc2 * 512: hf * HALF + (tc2 + 1) * 512],
                                start=(k == 0), stop=(k == DC - 1))
                em.emit("tensor", None, inc="tm")
                tmv = em.now("tm")
                em.emit("scalar", "activation", sil, pst[:, 0:1024], AF.Silu,
                        scale=par["w1s"][:, occ:occ + 1], waits=[("tm", tmv)], inc="sm")
                em.emit("vector", "scalar_tensor_tensor", TH[occ], pst[:, 1024:2048],
                        par["w2s"][:, occ:occ + 1], sil, op0=ALU.mult, op1=ALU.mult,
                        waits=[("sm", em.now("sm"))], inc="vm")
                tmarks[occ] = em.now("vm")
        assert em.now("tm") == tmbase + 4 * DC
        VM_TH = em.now("vm")

        def big44(src_tiles, dst_tiles, w_sg, scale_par, src_sem, src_val):
            tmbase2 = em.now("tm")
            gmarks = {}
            for oc in range(FC):
                w = [(src_sem, src_val)]
                if oc == 0:
                    w += [("tm", em.now("tm")), ("sm", em.now("sm")), ("vm", em.now("vm")),
                          ("st", em.now("st"))]
                else:
                    w.append(("tm", tmbase2 + oc - 1))
                wsem = "whA" if oc % 2 == 0 else "whB"
                em.emit("gpsimd", "dma_start",
                        out=hslab[oc % 2].rearrange("p (k n) -> p k n", k=FC),
                        in_=w_sg[oc], waits=w, inc=wsem, incv=16, serial=False)
                pso = 0 if oc % 2 == 0 else 1024
                tw = [(wsem, em.now(wsem)), (src_sem, src_val)]
                tw.append(("sm", gmarks[oc - 2] if oc >= 2 else em.now("sm")))
                first = True
                for k in range(FC):
                    for tc2 in range(2):
                        em.emit("tensor", "matmul",
                                psA[:, pso + tc2 * 512: pso + (tc2 + 1) * 512],
                                hslab[oc % 2][:, k * 128:(k + 1) * 128],
                                src_tiles[k][:, tc2 * 512:(tc2 + 1) * 512],
                                start=(k == 0), stop=(k == FC - 1), waits=tw if first else ())
                        first = False
                em.emit("tensor", None, inc="tm")
                tmv = em.now("tm")
                em.emit("scalar", "activation", dst_tiles[oc], psA[:, pso:pso + 1024],
                        AF.Gelu, scale=scale_par[:, oc:oc + 1], waits=[("tm", tmv)],
                        inc="sm")
                gmarks[oc] = em.now("sm")
            assert em.now("tm") == tmbase2 + FC
            return em.now("sm")

        if upto == "I1" and hf == 0:
            return _stop_here()
        sm_g1 = big44(TH, G1, h1t_hbm, par["h1s"], "vm", VM_TH)
        sm_g2 = big44(G1, G2, h2t_hbm, par["h2s"], "sm", sm_g1)

        if upto == "I2" and hf == 0:
            return _stop_here()
        xh = [V_(R_A + 32 + 4 * i, 1024, F32) for i in range(8)]
        for i in range(8):
            em.emit("sync", "dma_start", out=xh[i],
                    in_=x_hbm[hf * HALF + i * 128: hf * HALF + (i + 1) * 128, :],
                    waits=[("sm", sm_g2)] if i == 0 else (), inc="xl", incv=16)
        XL_XH = em.now("xl")
        out_tm = [V_(XNBF + 4 * i, 1024, F32) for i in range(8)]
        tmbase3 = em.now("tm")
        fmarks = {}
        for oc in range(DC):
            w = [("sm", sm_g2)]
            if oc == 0:
                w += [("tm", em.now("tm")), ("vm", em.now("vm"))]
            else:
                w.append(("tm", tmbase3 + 2 * oc - 1))
            wsem = "whA" if oc % 2 == 0 else "whB"
            em.emit("gpsimd", "dma_start",
                    out=hslab[oc % 2].rearrange("p (k n) -> p k n", k=FC),
                    in_=w3t_hbm[oc], waits=w, inc=wsem, incv=16, serial=False)
            tw = [(wsem, em.now(wsem)), ("vm", em.now("vm"))]
            first = True
            for k in range(FC):
                for tc2 in range(2):
                    em.emit("tensor", "matmul", psA[:, tc2 * 512:(tc2 + 1) * 512],
                            hslab[oc % 2][:, k * 128:(k + 1) * 128],
                            G2[k][:, tc2 * 512:(tc2 + 1) * 512],
                            start=(k == 0), stop=(k == FC - 1), waits=tw if first else ())
                    first = False
            em.emit("tensor", None, inc="tm")
            tmv = em.now("tm")
            rsem = "rlA" if oc % 2 == 0 else "rlB"
            w = [("sp", SP_ATTNB)]
            if oc >= 2:
                w.append(("vm", fmarks[oc - 2]))
            ab = abchb[oc % 2]
            em.emit("sync", "dma_start", out=ab, in_=attnb_sp[hf][oc], waits=w,
                    inc=rsem, incv=16, serial=False)
            rlv = em.now(rsem)
            em.emit("vector", "scalar_tensor_tensor", y3ch, psA[:, 0:1024],
                    par["w3s"][:, oc:oc + 1], ab, op0=ALU.mult, op1=ALU.add,
                    waits=[("tm", tmv), (rsem, rlv)], inc="vm")
            fmarks[oc] = em.now("vm")
            vmv = em.now("vm")
            for p in range(8):
                em.emit("tensor", "transpose",
                        psB[:, 1024 + p * 128: 1024 + (p + 1) * 128],
                        y3ch[:, p * 128:(p + 1) * 128], id_f,
                        waits=[("vm", vmv)] if p == 0 else ())
            em.emit("tensor", None, inc="tm")
            tmv2 = em.now("tm")
            for p in range(8):
                em.emit("vector", "tensor_add", out_tm[p][:, oc * 128:(oc + 1) * 128],
                        psB[:, 1024 + p * 128: 1024 + (p + 1) * 128],
                        xh[p][:, oc * 128:(oc + 1) * 128],
                        waits=[("tm", tmv2), ("xl", XL_XH)] if p == 0 else (),
                        inc="vm" if p == 7 else None)
        assert em.now("tm") == tmbase3 + 2 * DC
        VM_OUT = em.now("vm")
        for i in range(8):
            em.emit("sync", "dma_start",
                    out=out_hbm[hf * HALF + i * 128: hf * HALF + (i + 1) * 128, :],
                    in_=out_tm[i], waits=[("vm", VM_OUT)] if i == 0 else (),
                    inc="st", incv=16)

    em.ewait("sync", ("st", em.now("st")))
    em.ewait("gpsimd", ("st", em.now("st")))
    if em.now("tst"):
        em.ewait("gpsimd", ("tst", em.now("tst")))

    em.realize(nc)
    return nc


def prepare_in_maps(inputs):
    x = np.ascontiguousarray(np.asarray(inputs["L0_attn_x"], np.float32).reshape(B * N, D))

    def t2d(a):
        return np.ascontiguousarray(np.asarray(a, np.float32))

    def slabgrid(w):
        K, NN = w.shape
        return np.ascontiguousarray(
            w.reshape(K // 128, 128, NN // 128, 128).transpose(2, 1, 0, 3))

    def kchunks(w):
        K, NN = w.shape
        return np.ascontiguousarray(w.reshape(K // 128, 128, NN))

    def vec128(v):
        v = t2d(v).ravel()
        return np.ascontiguousarray(v.reshape(-1, 128).T)

    common = {
        "wq": t2d(inputs["L0_attn_Wq"]).reshape(D, D),
        "wk": t2d(inputs["L0_attn_Wk"]).reshape(D, D),
        "wv": t2d(inputs["L0_attn_Wv"]).reshape(D, D),
        "wo": t2d(inputs["L0_attn_Wo"]).reshape(D, D),
        "proj": t2d(inputs["L0_attn_proj"]),
        "w1": slabgrid(t2d(inputs["L0_attn_W1"])),
        "w2": kchunks(t2d(inputs["L0_attn_W2"])),
        "w1t": kchunks(t2d(inputs["L0_ffn_w1_w"]).T),
        "w2t": kchunks(t2d(inputs["L0_ffn_w2_w"]).T),
        "h1t": slabgrid(t2d(inputs["L0_ffn_h1_w"]).T),
        "h2t": slabgrid(t2d(inputs["L0_ffn_h2_w"]).T),
        "w3t": slabgrid(t2d(inputs["L0_ffn_w3_w"]).T),
        "b1v": vec128(inputs["L0_attn_b1"]), "b2v": vec128(inputs["L0_attn_b2"]),
        "ln1g": vec128(inputs["L0_attn_ln1_g"]), "ln1b": vec128(inputs["L0_attn_ln1_b"]),
        "ln2g": vec128(inputs["L0_attn_ln2_g"]), "ln2b": vec128(inputs["L0_attn_ln2_b"]),
        "w1s": vec128(inputs["L0_ffn_w1_s"]), "w2s": vec128(inputs["L0_ffn_w2_s"]),
        "h1s": vec128(inputs["L0_ffn_h1_s"]), "h2s": vec128(inputs["L0_ffn_h2_s"]),
        "w3s": vec128(inputs["L0_ffn_w3_s"]),
    }
    in_maps = []
    for c in range(NCORES):
        m = dict(common)
        m["x"] = np.ascontiguousarray(x[c * T:(c + 1) * T])
        in_maps.append(m)
    return in_maps


def kernel(**inputs):
    in_maps = prepare_in_maps(inputs)
    nc = build()
    res = run_bass_kernel_spmd(nc, in_maps, core_ids=list(range(NCORES)))
    out = np.concatenate([res.results[c]["out"] for c in range(NCORES)], axis=0)
    return out.reshape(B, N, D).astype(np.float32)
